# version 01
# baseline (speedup 1.0000x reference)
import numpy as np
import ml_dtypes
import concourse.bass as bass
import concourse.mybir as mybir
from concourse.bass_utils import run_bass_kernel_spmd
from contextlib import ExitStack

F32 = mybir.dt.float32
BF16 = mybir.dt.bfloat16
AF = mybir.ActivationFunctionType
ALU = mybir.AluOpType
AX = mybir.AxisListType

T = 2048
D = 1024
NT = 16
ND = 8

ENGINES = ("sync", "tensor", "vector", "scalar", "gpsimd")


class Prog:
    def __init__(self, nc, serial=False):
        self.nc = nc
        self.serial = serial
        self.ops = []
        self.key_w = {}
        self.key_r = {}

    def add(self, eng, fn, reads=(), writes=(), dma=False):
        i = len(self.ops)
        deps = set()
        for k in reads:
            j = self.key_w.get(k)
            if j is not None:
                deps.add(j)
        for k in writes:
            j = self.key_w.get(k)
            if j is not None:
                deps.add(j)
            deps.update(self.key_r.get(k, ()))
        if self.serial and i > 0:
            deps.add(i - 1)
        self.ops.append(dict(i=i, eng=eng, fn=fn, deps=deps, dma=dma,
                             marked=dma))
        for k in reads:
            self.key_r.setdefault(k, []).append(i)
        for k in writes:
            self.key_w[k] = i
            self.key_r[k] = []
        return i

    def barrier(self):
        last = {}
        for op in self.ops:
            last[op["eng"]] = op["i"]
        deps = set(last.values())
        for e in ENGINES:
            i = len(self.ops)
            self.ops.append(dict(i=i, eng=e, fn=lambda eng: eng.nop(),
                                 deps=set(deps), dma=False, marked=False))
        self.key_w.clear()
        self.key_r.clear()

    def dma(self, out, in_, reads=(), writes=(), q="sync", **kw):
        self.add(q, lambda e: e.dma_start(out=out, in_=in_, **kw),
                 reads=reads, writes=writes, dma=True)

    def t(self, fn, reads=(), writes=()):
        self.add("tensor", fn, reads, writes)

    def v(self, fn, reads=(), writes=()):
        self.add("vector", fn, reads, writes)

    def s(self, fn, reads=(), writes=()):
        self.add("scalar", fn, reads, writes)

    def p(self, fn, reads=(), writes=()):
        self.add("gpsimd", fn, reads, writes)

    def emit(self, block, sems):
        ops = self.ops

        def skip_edge(pj, op):
            return (pj["eng"] == op["eng"] == "tensor"
                    and not pj["dma"] and not op["dma"])

        for op in ops:
            for j in op["deps"]:
                pj = ops[j]
                if skip_edge(pj, op):
                    continue
                pj["marked"] = True
        pfx = {e: 0 for e in ENGINES}
        order = {e: [] for e in ENGINES}
        for op in ops:
            e = op["eng"]
            if op["marked"]:
                pfx[e] += 16 if op["dma"] else 1
            op["pfx"] = pfx[e]
            order[e].append(op)
        clock = {e: {f: 0 for f in ENGINES} for e in ENGINES}
        last_dma_pfx = {e: 0 for e in ENGINES}
        for op in ops:
            e = op["eng"]
            waits = {}
            for j in op["deps"]:
                pj = ops[j]
                f = pj["eng"]
                if skip_edge(pj, op):
                    continue
                c = pj["pfx"]
                if c > clock[e].get(f, 0):
                    waits[f] = max(waits.get(f, 0), c)
            if op["dma"]:
                c = last_dma_pfx[e]
                if c > clock[e].get(e, 0):
                    waits[e] = max(waits.get(e, 0), c)
                last_dma_pfx[e] = op["pfx"]
            op["waits"] = waits
            for f, c in waits.items():
                clock[e][f] = max(clock[e][f], c)

        for ename in ENGINES:
            if not order[ename]:
                continue

            def mk(my_ops, ename=ename):
                def section(eng):
                    for op in my_ops:
                        for f, c in sorted(op["waits"].items()):
                            eng.wait_ge(sems[f], c)
                        ins = op["fn"](eng)
                        if op["marked"]:
                            ins.then_inc(sems[ename], 16 if op["dma"] else 1)
                    if ename == "sync":
                        eng.wait_ge(sems["sync"], pfx["sync"])
                return section
            getattr(block, ename)(mk(order[ename]))
        return pfx


def const_arrays():
    c = {"c_identb": np.eye(128, dtype=ml_dtypes.bfloat16),
         "c_identf": np.eye(128, dtype=np.float32)}
    seg = np.zeros((128, 128), dtype=ml_dtypes.bfloat16)
    for dc in range(8):
        for dl in range(128):
            seg[dl, dc * 16 + 2 * dc + dl // 64] = 1
    c["c_seg"] = seg
    pos = (np.arange(16)[None, :, None] * 128 + np.arange(128)[:, None, None])
    freqs = (1.0 / 100000.0 ** (np.arange(0, 32, 2, dtype=np.float64) / 32.0))
    ang = pos * freqs[None, None, :]
    c["c_rcos"] = np.cos(ang).reshape(128, 256).astype(ml_dtypes.bfloat16)
    c["c_rsin"] = np.sin(ang).reshape(128, 256).astype(ml_dtypes.bfloat16)
    kk = np.arange(128)[:, None]
    ll = np.arange(128)[None, :]
    c["c_causT"] = np.where(ll >= kk, 0.0, -1e30).astype(ml_dtypes.bfloat16)
    c["c_causU"] = np.where(ll <= kk, 0.0, -1e30).astype(ml_dtypes.bfloat16)
    c["c_cuf"] = np.where(kk <= ll, 1.0, 0.0).astype(np.float32)
    llw = np.arange(128)[:, None]
    kkw = np.arange(136)[None, :]
    band = (kkw >= llw) & (kkw <= llw + 8)
    c["c_bandw"] = np.where(band, 0.0, -1e30).astype(ml_dtypes.bfloat16)
    c["c_band0"] = np.where(band & (kkw >= 8), 0.0, -1e30).astype(ml_dtypes.bfloat16)
    return c


class Ctx:
    def __init__(self, nc, P, stack, debug=False):
        self.nc = nc
        self.P = P
        self.debug = debug
        self.dbg_names = []
        ent = stack.enter_context
        sb = nc.sbuf_tensor
        self.X = ent(sb("X", [128, NT * 1024], F32)).ap()
        self.B1 = ent(sb("B1", [128, 8 * T], BF16)).ap()
        self.B2 = ent(sb("B2", [128, 8 * T], BF16)).ap()
        self.B3 = ent(sb("B3", [128, 5 * T], BF16)).ap()
        self.WA = ent(sb("WA", [128, 8 * T], BF16)).ap()
        self.F1 = ent(sb("F1", [128, 2048], F32)).ap()
        self.F2 = ent(sb("F2", [128, 2048], F32)).ap()
        self.S1 = ent(sb("S1", [128, 2048], F32)).ap()
        self.CB = ent(sb("CB", [128, 256], BF16)).ap()
        self.CF = ent(sb("CF", [128, 132], F32)).ap()
        self.CC = ent(sb("CC", [128, 16], F32)).ap()
        self.CI = ent(sb("CI", [128, 128], F32)).ap()
        self.CS = ent(sb("CS", [128, 128], BF16)).ap()
        self.CRC = ent(sb("CRC", [128, 256], BF16)).ap()
        self.CRS = ent(sb("CRS", [128, 256], BF16)).ap()
        self.CT = ent(sb("CT", [128, 128], BF16)).ap()
        self.CTU = ent(sb("CTU", [128, 128], BF16)).ap()
        self.CUF = ent(sb("CUF", [128, 128], F32)).ap()
        self._ccells = {}
        self.psA = ent(nc.psum_tensor("psA", [128, 2048], F32)).ap()
        self.psB = ent(nc.psum_tensor("psB", [128, 1024], F32)).ap()
        self.psT = ent(nc.psum_tensor("psT", [128, 1024], BF16)).ap()
        self.psS = ent(nc.psum_tensor("psS", [128, 512], F32)).ap()

    @property
    def identb(self):
        return self.CB[:, 0:128]

    @property
    def onescolb(self):
        return self.CB[:, 128:129]

    @property
    def onesrowb(self):
        return self.CB[0:1, 128:256]

    @property
    def onescolf(self):
        return self.CF[:, 0:1]

    @property
    def onesrowf(self):
        return self.CF[0:1, 4:132]

    @property
    def onecellf(self):
        return self.CF[0:1, 0:1]

    def load_consts(self, W):
        P = self.P
        P.dma(self.CB[:, 0:128], W["c_identb"], writes=[("CB",)])
        P.dma(self.CI[:], W["c_identf"], writes=[("CI",)])
        P.dma(self.CS[:], W["c_seg"], writes=[("CS",)])
        P.dma(self.CRC[:], W["c_rcos"], writes=[("CRC",)])
        P.dma(self.CRS[:], W["c_rsin"], writes=[("CRS",)])
        P.dma(self.CT[:], W["c_causT"], writes=[("CT",)])
        P.dma(self.CTU[:], W["c_causU"], writes=[("CTU",)])
        P.dma(self.CUF[:], W["c_cuf"], writes=[("CUF",)])
        P.v(lambda e: e.memset(self.CB[:, 128:256], 1.0), writes=[("CB",)])
        P.v(lambda e: e.memset(self.CF[:], 1.0), writes=[("CF",)])

    def dbg(self, name, ap, reads):
        if not self.debug:
            return
        name = "dbg_" + name
        dt = ap.dtype
        h = self.nc.dram_tensor(name, list(ap.shape), dt, kind="ExternalOutput")
        self.dbg_names.append(name)
        self.P.dma(h.ap(), ap, reads=list(reads))

    def ccell(self, val):
        val = float(val)
        if val not in self._ccells:
            assert len(self._ccells) < 16
            i = len(self._ccells)
            ap = self.CC[:, i:i + 1]
            self._ccells[val] = ap
            self.nc.const_aps.register(F32, val, ap)
            self.P.v(lambda e: e.memset(ap, val), writes=[("CC",)])
        return self._ccells[val]



def tp_group(C, src_tile, dst, dst_lo, n, kr, kw, dst_ap=None, src_ap=None):
    P = C.P
    for k in range(n):
        def tp(e, k=k):
            s = src_tile(k)
            np_ = s.shape[0]
            return e.transpose(C.psT[0:s.shape[1], k * 128:k * 128 + np_],
                               s, C.identb[0:np_, 0:np_])
        P.t(tp, reads=list(kr) + [("CB",)], writes=[("psT", k)])
    d = dst_ap if dst_ap is not None else dst[:, dst_lo:dst_lo + n * 128]
    s = src_ap if src_ap is not None else C.psT[:, 0:n * 128]
    P.v(lambda e, d=d, s=s: e.tensor_copy(d, s),
        reads=[("psT", k) for k in range(8)], writes=list(kw))


def tp_tok2feat(C, src, dst, kr, kw, nt=NT, nd=ND):
    Fs, Ts = nd * 128, nt * 128
    for dc in range(nd):
        for tg in range(0, nt, 8):
            n = min(8, nt - tg)
            tp_group(C,
                     lambda k, dc=dc, tg=tg: src[:, (tg + k) * Fs + dc * 128:
                                                 (tg + k) * Fs + dc * 128 + 128],
                     dst, dc * Ts + tg * 128, n, kr, kw)


def ln_stats(C, x, mu, rstd, keys_x, key_mu, key_rstd, nt=NT, F=1024, eps=1e-5):
    P = C.P
    for t in range(nt):
        xt = x[:, t * F:(t + 1) * F]
        P.v(lambda e, xt=xt, t=t: e.reduce_sum(mu[:, t:t + 1], xt, axis=AX.X),
            reads=keys_x, writes=[key_mu])
        P.s(lambda e, xt=xt, t=t: e.activation(C.F1[:, 0:F], xt, AF.Square,
                                               accum_out=rstd[:, t:t + 1]),
            reads=keys_x, writes=[key_rstd, ("F1", 0)])
    P.v(lambda e: e.tensor_scalar(mu, mu, 1.0 / F, None, ALU.mult),
        reads=[key_mu], writes=[key_mu])
    P.v(lambda e: e.scalar_tensor_tensor(C.F1[:, 0:nt], mu, 1.0, mu,
                                         ALU.mult, ALU.mult),
        reads=[key_mu], writes=[("F1", 0)])
    P.v(lambda e: e.tensor_scalar(rstd, rstd, 1.0 / F, None, ALU.mult),
        reads=[key_rstd], writes=[key_rstd])
    P.v(lambda e: e.tensor_tensor(rstd, rstd, C.F1[:, 0:nt], ALU.subtract),
        reads=[key_rstd, ("F1", 0)], writes=[key_rstd])
    C.ccell(eps)
    C.ccell(-0.5)
    P.s(lambda e: e.activation(rstd, rstd, AF.Ln, bias=eps),
        reads=[key_rstd, ("CC",)], writes=[key_rstd])
    P.s(lambda e: e.activation(rstd, rstd, AF.Exp, scale=-0.5),
        reads=[key_rstd, ("CC",)], writes=[key_rstd])


def load_weight_bf16(C, dst, w_dram, kw, rows, cols, dst_col0=0, cast_eng="p",
                     stage=None, key_stage=None, scale_col=None, key_scale=None):
    P = C.P
    stage = C.F2 if stage is None else stage
    key_stage = key_stage or ("F2",)
    nr = (rows + 127) // 128
    w3 = w_dram.rearrange("(r p) n -> p r n", p=128) if rows > 128 else None
    for r in range(nr):
        c0 = 0
        while c0 < cols:
            cw = min(2048, cols - c0)
            src = w3[:, r, c0:c0 + cw] if w3 is not None else w_dram[:, c0:c0 + cw]
            P.dma(stage[:, 0:cw], src, reads=[key_stage], writes=[key_stage])
            d = dst[:, dst_col0 + r * cols + c0: dst_col0 + r * cols + c0 + cw]
            if scale_col is None:
                getattr(P, cast_eng)(
                    lambda e, d=d, cw=cw: e.tensor_copy(d, stage[:, 0:cw]),
                    reads=[key_stage], writes=list(kw))
            else:
                getattr(P, cast_eng)(
                    lambda e, d=d, cw=cw, r=r: e.tensor_scalar(
                        d, stage[:, 0:cw], scale_col[:, r:r + 1], None, ALU.mult),
                    reads=[key_stage, key_scale], writes=list(kw))
            c0 += cw


def bcast_bias(C, b_dram, dst, kw, n):
    P = C.P
    ps = C.psB
    keys = [("psB", i) for i in range((n + 511) // 512)]
    row = C.F2[0:1, 0:n]
    P.dma(row, b_dram.unsqueeze(0), reads=[("F2",)], writes=[("F2",)])
    c0 = 0
    while c0 < n:
        cw = min(512, n - c0)
        P.t(lambda e, c0=c0, cw=cw: e.matmul(ps[:, c0:c0 + cw], C.onesrowf,
                                             row[0:1, c0:c0 + cw],
                                             start=True, stop=True),
            reads=[("F2",), ("CF",)], writes=[("psB", c0 // 512)])
        c0 += cw
    P.v(lambda e: e.tensor_copy(dst[:, 0:n], ps[:, 0:n]),
        reads=keys, writes=list(kw))




def emit_l0_attn(C, W):
    P = C.P
    X, B1, B2, B3, WA = C.X, C.B1, C.B2, C.B3, C.WA
    S1, F1, F2 = C.S1, C.F1, C.F2
    psA, psB, psS = C.psA, C.psB, C.psS
    KD = float(1024.0 ** -0.5)

    mu = S1[:, 0:16]
    rstd = S1[:, 16:32]
    rsq = S1[:, 32:48]
    rsk = S1[:, 48:56]
    mu2 = S1[:, 56:57]
    rstd2 = S1[:, 57:58]
    bo_b = S1[:, 64:1088]
    gam_b = F1[:, 0:1024]
    bet_b = F1[:, 1024:2048]
    K_WA0, K_WA1 = ("WA", 0), ("WA", 1)

    ln_stats(C, X, mu, rstd, [("X",)], ("mu",), ("rstd",))
    for t in range(NT):
        P.v(lambda e, t=t: e.tensor_scalar(
            B2[:, t * 1024:(t + 1) * 1024], X[:, t * 1024:(t + 1) * 1024],
            mu[:, t:t + 1], rstd[:, t:t + 1], ALU.subtract, ALU.mult),
            reads=[("X",), ("mu",), ("rstd",)], writes=[("B2",)])
    C.dbg("xn", B2[:], [("B2",)])
    C.dbg("mu", mu, [("mu",)])
    C.dbg("rstd", rstd, [("rstd",)])
    tp_tok2feat(C, B2, B1, [("B2",)], [("B1",)])
    C.dbg("xnT", B1[:], [("B1",)])

    def proj_tok(wkey, out_cb):
        load_weight_bf16(C, WA, W[wkey], [K_WA0], 1024, 1024)
        for t in range(NT):
            for h in range(2):
                for dc in range(ND):
                    P.t(lambda e, t=t, h=h, dc=dc: e.matmul(
                        psA[:, h * 512:h * 512 + 512],
                        B1[:, dc * T + t * 128: dc * T + t * 128 + 128],
                        WA[:, dc * 1024 + h * 512: dc * 1024 + h * 512 + 512],
                        start=(dc == 0), stop=(dc == ND - 1)),
                        reads=[("B1",), K_WA0], writes=[("psA", h)])
            out_cb(t)

    def k_out(t):
        P.s(lambda e, t=t: e.activation(B2[:, t * 1024:(t + 1) * 1024],
                                        psA[:, 0:1024], AF.Exp),
            reads=[("psA", 0), ("psA", 1)], writes=[("B2",)])
    proj_tok("Wk", k_out)
    C.dbg("ek", B2[:], [("B2",)])

    def v_slot(t):
        if t < 8:
            return C.B3[:, t * 1024:(t + 1) * 1024]
        return WA[:, 8192 + (t - 8) * 1024: 8192 + (t - 7) * 1024]

    def v_key(t):
        return ("B3",) if t < 8 else K_WA1

    def v_out(t):
        P.v(lambda e, t=t: e.tensor_copy(v_slot(t), psA[:, 0:1024]),
            reads=[("psA", 0), ("psA", 1)], writes=[v_key(t)])
    proj_tok("Wv", v_out)

    for dc in range(ND):
        for t in range(NT):
            P.t(lambda e, t=t, dc=dc: e.matmul(
                psS[:, dc:dc + 1],
                B2[:, t * 1024 + dc * 128: t * 1024 + dc * 128 + 128],
                C.onescolb, start=(t == 0), stop=(t == NT - 1)),
                reads=[("B2",), ("CB",)], writes=[("psS",)])
        for h in range(2):
            for t in range(NT):
                P.t(lambda e, t=t, dc=dc, h=h: e.matmul(
                    psB[:, h * 512:h * 512 + 512],
                    B2[:, t * 1024 + dc * 128: t * 1024 + dc * 128 + 128],
                    v_slot(t)[:, h * 512:h * 512 + 512],
                    start=(t == 0), stop=(t == NT - 1)),
                    reads=[("B2",), v_key(t)], writes=[("psB", h)])
        P.v(lambda e, dc=dc: e.reciprocal(rsk[:, dc:dc + 1], psS[:, dc:dc + 1]),
            reads=[("psS",)], writes=[("rsk",)])
        P.v(lambda e, dc=dc: e.tensor_scalar(
            WA[:, dc * 1024:(dc + 1) * 1024], psB[:, 0:1024],
            rsk[:, dc:dc + 1], None, ALU.mult),
            reads=[("psB", 0), ("psB", 1), ("rsk",)], writes=[K_WA0])
    C.dbg("ctx", WA[:, 0:8192], [K_WA0])

    load_weight_bf16(C, WA, W["Wq"], [K_WA1], 1024, 1024, dst_col0=8192)
    for ec in range(ND):
        for lp in range(4):
            for dc in range(ND):
                P.t(lambda e, ec=ec, lp=lp, dc=dc: e.matmul(
                    psA[:, lp * 512:lp * 512 + 512],
                    WA[:, 8192 + dc * 1024 + ec * 128:
                       8192 + dc * 1024 + ec * 128 + 128],
                    B1[:, dc * T + lp * 512: dc * T + lp * 512 + 512],
                    start=(dc == 0), stop=(dc == ND - 1)),
                    reads=[("B1",), K_WA1], writes=[("psA", lp)])
        P.s(lambda e, ec=ec: e.activation(B2[:, ec * T:(ec + 1) * T],
                                          psA[:, 0:2048], AF.Exp),
            reads=[("psA", h) for h in range(4)], writes=[("B2",)])

    for t in range(NT):
        for dc in range(ND):
            P.t(lambda e, t=t, dc=dc: e.matmul(
                psB[:, t:t + 1],
                B2[:, dc * T + t * 128: dc * T + t * 128 + 128],
                C.onescolb, start=(dc == 0), stop=(dc == ND - 1)),
                reads=[("B2",), ("CB",)], writes=[("psB", 0)])
    P.v(lambda e: e.reciprocal(rsq, psB[:, 0:16]),
        reads=[("psB", 0)], writes=[("rsq",)])
    P.v(lambda e: e.tensor_scalar(rsq, rsq, KD, None, ALU.mult),
        reads=[("rsq",)], writes=[("rsq",)])
    C.dbg("eq", B2[:], [("B2",)])
    C.dbg("rsq", rsq, [("rsq",)])

    for ec in range(ND):
        for lp in range(4):
            for dc in range(ND):
                P.t(lambda e, ec=ec, lp=lp, dc=dc: e.matmul(
                    psA[:, lp * 512:lp * 512 + 512],
                    WA[:, dc * 1024 + ec * 128: dc * 1024 + ec * 128 + 128],
                    B2[:, dc * T + lp * 512: dc * T + lp * 512 + 512],
                    start=(dc == 0), stop=(dc == ND - 1)),
                    reads=[K_WA0, ("B2",)], writes=[("psA", lp)])
        P.v(lambda e, ec=ec: e.tensor_copy(B1[:, ec * T:(ec + 1) * T],
                                           psA[:, 0:2048]),
            reads=[("psA", h) for h in range(4)], writes=[("B1",)])

    C.dbg("attnT", B1[:], [("B1",)])

    load_weight_bf16(C, WA, W["Wo"], [K_WA1], 1024, 1024, dst_col0=8192)
    bcast_bias(C, W["bo"], bo_b, [("bo_b",)], 1024)
    bcast_bias(C, W["gamma"], gam_b, [("F1", 0)], 1024)
    bcast_bias(C, W["beta"], bet_b, [("F1", 1)], 1024)

    for t in range(NT):
        for h in range(2):
            for ec in range(ND):
                P.t(lambda e, t=t, h=h, ec=ec: e.matmul(
                    psA[:, h * 512:h * 512 + 512],
                    B1[:, ec * T + t * 128: ec * T + t * 128 + 128],
                    WA[:, 8192 + ec * 1024 + h * 512:
                       8192 + ec * 1024 + h * 512 + 512],
                    start=(ec == 0), stop=(ec == ND - 1)),
                    reads=[("B1",), K_WA1], writes=[("psA", h)])
        KPS = [("psA", 0), ("psA", 1)]
        xt = X[:, t * 1024:(t + 1) * 1024]
        if t == 0 and C.debug:
            P.v(lambda e: e.tensor_copy(F2[:, 0:1024], psA[:, 0:1024]),
                reads=KPS, writes=[("F2",)])
            C.dbg("ops0", F2[:, 0:1024], [("F2",)])
            C.dbg("bo_b", bo_b, [("bo_b",)])
        P.v(lambda e, t=t: e.scalar_tensor_tensor(
            F2[:, 0:1024], psA[:, 0:1024], rsq[:, t:t + 1], bo_b,
            ALU.mult, ALU.add),
            reads=KPS + [("rsq",), ("bo_b",)], writes=[("F2",)])
        if t == 0:
            C.dbg("t1a", F2[:, 0:1024], [("F2",)])
        P.v(lambda e, t=t, xt=xt: e.tensor_scalar(
            psB[:, 0:1024], xt, mu[:, t:t + 1], rstd[:, t:t + 1],
            ALU.subtract, ALU.mult),
            reads=[("X",), ("mu",), ("rstd",)], writes=[("psB", 0), ("psB", 1)])
        P.v(lambda e: e.tensor_tensor(F2[:, 0:1024], F2[:, 0:1024],
                                      psB[:, 0:1024], ALU.add),
            reads=[("F2",), ("psB", 0), ("psB", 1)], writes=[("F2",)])
        if t == 0:
            C.dbg("t1b", F2[:, 0:1024], [("F2",)])
        P.v(lambda e: e.reduce_sum(mu2, F2[:, 0:1024], axis=AX.X),
            reads=[("F2",)], writes=[("mu2",)])
        P.s(lambda e: e.activation(psB[:, 0:1024], F2[:, 0:1024], AF.Square,
                                   accum_out=rstd2),
            reads=[("F2",)], writes=[("rstd2",), ("psB", 0), ("psB", 1)])
        P.v(lambda e: e.tensor_scalar(mu2, mu2, 1.0 / 1024, None, ALU.mult),
            reads=[("mu2",)], writes=[("mu2",)])
        P.v(lambda e: e.scalar_tensor_tensor(S1[:, 58:59], mu2, 1.0, mu2,
                                             ALU.mult, ALU.mult),
            reads=[("mu2",)], writes=[("s58",)])
        P.v(lambda e: e.tensor_scalar(rstd2, rstd2, 1.0 / 1024, None, ALU.mult),
            reads=[("rstd2",)], writes=[("rstd2",)])
        P.v(lambda e: e.tensor_tensor(rstd2, rstd2, S1[:, 58:59], ALU.subtract),
            reads=[("rstd2",), ("s58",)], writes=[("rstd2",)])
        P.s(lambda e: e.activation(rstd2, rstd2, AF.Ln, bias=1e-5),
            reads=[("rstd2",), ("CC",)], writes=[("rstd2",)])
        P.s(lambda e: e.activation(rstd2, rstd2, AF.Exp, scale=-0.5),
            reads=[("rstd2",), ("CC",)], writes=[("rstd2",)])
        P.v(lambda e: e.tensor_scalar(F2[:, 0:1024], F2[:, 0:1024], mu2, rstd2,
                                      ALU.subtract, ALU.mult),
            reads=[("F2",), ("mu2",), ("rstd2",)], writes=[("F2",)])
        P.v(lambda e: e.tensor_tensor(F2[:, 0:1024], F2[:, 0:1024], gam_b,
                                      ALU.mult),
            reads=[("F2",), ("F1", 0)], writes=[("F2",)])
        P.v(lambda e: e.tensor_tensor(F2[:, 0:1024], F2[:, 0:1024], bet_b,
                                      ALU.add),
            reads=[("F2",), ("F1", 1)], writes=[("F2",)])
        P.v(lambda e, xt=xt: e.tensor_tensor(xt, xt, F2[:, 0:1024], ALU.add),
            reads=[("X",), ("F2",)], writes=[("X",)])




def emit_l0_ffn(C, W):
    P = C.P
    X, B1, B2, B3, WA = C.X, C.B1, C.B2, C.B3, C.WA
    S1, F1, F2 = C.S1, C.F1, C.F2
    psA, psB, psS = C.psA, C.psB, C.psS
    E = 16

    mu = S1[:, 0:16]
    rstd = S1[:, 16:32]
    gate_w = S1[:, 64:192]
    route = S1[:, 192:448]
    gate_bb = S1[:, 448:464]
    wsel = S1[:, 480:736]
    K_WA0, K_WA1 = ("WA", 0), ("WA", 1)

    ln_stats(C, X, mu, rstd, [("X",)], ("mu",), ("rstd",))
    for t in range(NT):
        P.v(lambda e, t=t: e.tensor_scalar(
            B2[:, t * 1024:(t + 1) * 1024], X[:, t * 1024:(t + 1) * 1024],
            mu[:, t:t + 1], rstd[:, t:t + 1], ALU.subtract, ALU.mult),
            reads=[("X",), ("mu",), ("rstd",)], writes=[("B2",)])
    tp_tok2feat(C, B2, B1, [("B2",)], [("B1",)])

    for cc in range(8):
        P.dma(gate_w[:, cc * 16:(cc + 1) * 16],
              W["gate_w"][cc * 128:(cc + 1) * 128, :],
              reads=[("gw",)], writes=[("gw",)])
    row = F2[0:1, 0:16]
    P.dma(row, W["gate_b"].unsqueeze(0), reads=[("F2",)], writes=[("F2",)])
    P.t(lambda e: e.matmul(psB[:, 0:16], C.onesrowf, row, start=True, stop=True),
        reads=[("F2",), ("CF",)], writes=[("psB", 0)])
    P.v(lambda e: e.tensor_copy(gate_bb, psB[:, 0:16]),
        reads=[("psB", 0)], writes=[("gbb",)])

    for dc in range(ND):
        for tg in range(0, NT, 4):
            for k in range(4):
                t = tg + k
                P.v(lambda e, t=t, dc=dc, k=k: e.tensor_scalar(
                    F2[:, k * 128:(k + 1) * 128],
                    X[:, t * 1024 + dc * 128: t * 1024 + dc * 128 + 128],
                    mu[:, t:t + 1], rstd[:, t:t + 1], ALU.subtract, ALU.mult),
                    reads=[("X",), ("mu",), ("rstd",)], writes=[("F2",)])
                P.t(lambda e, k=k: e.transpose(
                    psA[:, 1536 + k * 128:1536 + (k + 1) * 128],
                    F2[:, k * 128:(k + 1) * 128], C.CI),
                    reads=[("F2",), ("CI",)], writes=[("psA", 3)])
            P.v(lambda e, tg=tg: e.tensor_copy(F1[:, tg * 128:(tg + 4) * 128],
                                               psA[:, 1536:2048]),
                reads=[("psA", 3)], writes=[("F1", 0), ("F1", 1)])
        for t in range(NT):
            P.t(lambda e, t=t, dc=dc: e.matmul(
                psS[:, t * 16:(t + 1) * 16],
                F1[:, t * 128:(t + 1) * 128],
                gate_w[:, dc * 16:(dc + 1) * 16],
                start=True, stop=True),
                reads=[("F1", 0), ("F1", 1), ("gw",)], writes=[("psS",)])
        if dc == 0:
            P.v(lambda e: e.tensor_copy(route, psS[:, 0:256]),
                reads=[("psS",)], writes=[("route",)])
        else:
            P.v(lambda e: e.tensor_tensor(route, route, psS[:, 0:256], ALU.add),
                reads=[("psS",), ("route",)], writes=[("route",)])
    P.v(lambda e: e.tensor_tensor(
        route.rearrange("p (t e) -> p t e", t=16),
        route.rearrange("p (t e) -> p t e", t=16),
        gate_bb.unsqueeze(1).broadcast_to([128, 16, 16]), ALU.add),
        reads=[("route",), ("gbb",)], writes=[("route",)])
    P.s(lambda e: e.activation(route, route, AF.Sigmoid),
        reads=[("route",)], writes=[("route",)])

    for t in range(NT):
        rt = route[:, t * 16:(t + 1) * 16]
        m1 = S1[:, 464:465]
        m2 = S1[:, 465:466]
        rr = S1[:, 466:467]
        msk = F2[:, 0:16]
        r2 = F2[:, 16:32]
        P.v(lambda e, rt=rt: e.reduce_max(m1, rt, axis=AX.X),
            reads=[("route",)], writes=[("m1",)])
        P.v(lambda e, rt=rt: e.tensor_scalar(msk, rt, m1, None, ALU.is_ge),
            reads=[("route",), ("m1",)], writes=[("F2",)])
        P.v(lambda e, rt=rt: e.scalar_tensor_tensor(r2, msk, -1e30, rt,
                                                    ALU.mult, ALU.add),
            reads=[("route",), ("F2",)], writes=[("F2",)])
        P.v(lambda e: e.reduce_max(m2, r2, axis=AX.X),
            reads=[("F2",)], writes=[("m2",)])
        P.v(lambda e: e.tensor_tensor(rr, m1, m2, ALU.add),
            reads=[("m1",), ("m2",)], writes=[("rr",)])
        P.v(lambda e: e.reciprocal(rr, rr), reads=[("rr",)], writes=[("rr",)])
        P.v(lambda e, rt=rt: e.tensor_scalar(msk, rt, m2, rr, ALU.is_ge, ALU.mult),
            reads=[("route",), ("m2",), ("rr",)], writes=[("F2",)])
        P.v(lambda e, rt=rt, t=t: e.tensor_tensor(
            wsel[:, t * 16:(t + 1) * 16], rt, msk, ALU.mult),
            reads=[("route",), ("F2",)], writes=[("wsel",)])

    for t in range(NT):
        P.v(lambda e, t=t: e.tensor_scalar(
            psB[:, 0:1024], X[:, t * 1024:(t + 1) * 1024],
            mu[:, t:t + 1], rstd[:, t:t + 1], ALU.subtract, ALU.mult),
            reads=[("X",), ("mu",), ("rstd",)], writes=[("psB", 0), ("psB", 1)])
        P.v(lambda e, t=t: e.tensor_tensor(
            X[:, t * 1024:(t + 1) * 1024], X[:, t * 1024:(t + 1) * 1024],
            psB[:, 0:1024], ALU.add),
            reads=[("psB", 0), ("psB", 1), ("X",)], writes=[("X",)])

    for e_i in range(E + 1):
        if e_i < E:
            wg_d, wu_d, wd_d = (W["experts_wg"][e_i], W["experts_wu"][e_i],
                                W["experts_wd"][e_i])
        else:
            wg_d, wu_d, wd_d = W["shared_wg"], W["shared_wu"], W["shared_wd"]
        load_weight_bf16(C, WA, wg_d, [K_WA0], 1024, 512, dst_col0=0)
        load_weight_bf16(C, WA, wu_d, [K_WA0], 1024, 512, dst_col0=4096)
        load_weight_bf16(C, WA, wd_d, [K_WA1], 512, 1024, dst_col0=8192)
        for ic in range(4):
            for lp in range(4):
                for dc in range(ND):
                    P.t(lambda e, ic=ic, lp=lp, dc=dc: e.matmul(
                        psA[:, lp * 512:(lp + 1) * 512],
                        WA[:, dc * 512 + ic * 128: dc * 512 + ic * 128 + 128],
                        B1[:, dc * T + lp * 512: dc * T + lp * 512 + 512],
                        start=(dc == 0), stop=(dc == ND - 1)),
                        reads=[K_WA0, ("B1",)], writes=[("psA", lp)])
            P.s(lambda e, ic=ic: e.activation(B2[:, ic * T:(ic + 1) * T],
                                              psA[:, 0:2048], AF.Silu),
                reads=[("psA", 0), ("psA", 1), ("psA", 2), ("psA", 3)],
                writes=[("B2",)])
        for ic in range(4):
            for lp in range(4):
                for dc in range(ND):
                    P.t(lambda e, ic=ic, lp=lp, dc=dc: e.matmul(
                        psA[:, lp * 512:(lp + 1) * 512],
                        WA[:, 4096 + dc * 512 + ic * 128:
                           4096 + dc * 512 + ic * 128 + 128],
                        B1[:, dc * T + lp * 512: dc * T + lp * 512 + 512],
                        start=(dc == 0), stop=(dc == ND - 1)),
                        reads=[K_WA0, ("B1",)], writes=[("psA", lp)])
            P.v(lambda e, ic=ic: e.tensor_tensor(
                B3[:, ic * T:(ic + 1) * T], B2[:, ic * T:(ic + 1) * T],
                psA[:, 0:2048], ALU.mult),
                reads=[("psA", 0), ("psA", 1), ("psA", 2), ("psA", 3), ("B2",)],
                writes=[("B3",)])
        for t in range(NT):
            for h in range(2):
                for ic in range(4):
                    P.t(lambda e, t=t, h=h, ic=ic: e.matmul(
                        psB[:, h * 512:(h + 1) * 512],
                        B3[:, ic * T + t * 128: ic * T + t * 128 + 128],
                        WA[:, 8192 + ic * 1024 + h * 512:
                           8192 + ic * 1024 + h * 512 + 512],
                        start=(ic == 0), stop=(ic == 3)),
                        reads=[K_WA1, ("B3",)], writes=[("psB", h)])
            if e_i < E:
                wcell = wsel[:, t * 16 + e_i: t * 16 + e_i + 1]
                P.v(lambda e, t=t, wcell=wcell: e.scalar_tensor_tensor(
                    X[:, t * 1024:(t + 1) * 1024], psB[:, 0:1024], wcell,
                    X[:, t * 1024:(t + 1) * 1024], ALU.mult, ALU.add),
                    reads=[("psB", 0), ("psB", 1), ("wsel",), ("X",)],
                    writes=[("X",)])
            else:
                P.v(lambda e, t=t: e.tensor_tensor(
                    X[:, t * 1024:(t + 1) * 1024],
                    X[:, t * 1024:(t + 1) * 1024], psB[:, 0:1024], ALU.add),
                    reads=[("psB", 0), ("psB", 1), ("X",)], writes=[("X",)])




def emit_l1_attn(C, W):
    P = C.P
    X, B1, B2, B3, WA = C.X, C.B1, C.B2, C.B3, C.WA
    S1, F1, F2 = C.S1, C.F1, C.F2
    psA, psB, psS = C.psA, C.psB, C.psS

    mu = S1[:, 0:16]
    rstd = S1[:, 16:32]
    ksum = S1[:, 32:40]
    bq_col = S1[:, 40:48]
    bk_col = S1[:, 48:56]
    K_WA0, K_WA1 = ("WA", 0), ("WA", 1)
    KPS4 = [("psA", 0), ("psA", 1), ("psA", 2), ("psA", 3)]

    ln_stats(C, X, mu, rstd, [("X",)], ("mu",), ("rstd",))
    for t in range(NT):
        P.v(lambda e, t=t: e.tensor_scalar(
            B2[:, t * 1024:(t + 1) * 1024], X[:, t * 1024:(t + 1) * 1024],
            mu[:, t:t + 1], rstd[:, t:t + 1], ALU.subtract, ALU.mult),
            reads=[("X",), ("mu",), ("rstd",)], writes=[("B2",)])
    tp_tok2feat(C, B2, B1, [("B2",)], [("B1",)])

    def bias_cols(bname, dstcols, key):
        P.dma(F2[0:1, 0:1024], W[bname].unsqueeze(0), reads=[("F2",)],
              writes=[("F2",)])
        for c in range(8):
            P.t(lambda e, c=c: e.matmul(psB[:, c:c + 1],
                                        F2[0:1, c * 128:(c + 1) * 128],
                                        C.onecellf, start=True, stop=True),
                reads=[("F2",), ("CF",)], writes=[("psB", 0)])
        P.v(lambda e: e.tensor_copy(dstcols, psB[:, 0:8]),
            reads=[("psB", 0)], writes=[key])
    bias_cols("bq", bq_col, ("bqc",))
    bias_cols("bk", bk_col, ("bkc",))
    KBROW = ("B3", "brow")

    def bias_row(bname):
        P.dma(F2[0:1, 0:1024], W[bname].unsqueeze(0), reads=[("F2",)],
              writes=[("F2",)])
        P.v(lambda e: e.tensor_copy(B3[0:1, 9216:10240], F2[0:1, 0:1024]),
            reads=[("F2",)], writes=[KBROW])

    def proj_feat(dst_cb, wkey):
        load_weight_bf16(C, WA, W[wkey], [K_WA0], 1024, 1024)
        for ec in range(ND):
            for lp in range(4):
                for dc in range(ND):
                    P.t(lambda e, ec=ec, lp=lp, dc=dc: e.matmul(
                        psA[:, lp * 512:lp * 512 + 512],
                        WA[:, dc * 1024 + ec * 128: dc * 1024 + ec * 128 + 128],
                        B1[:, dc * T + lp * 512: dc * T + lp * 512 + 512],
                        start=(dc == 0), stop=(dc == ND - 1)),
                        reads=[("B1",), K_WA0], writes=[("psA", lp)])
            dst_cb(ec)

    def fm_evac(ec, bcol, dst, dkey):
        P.v(lambda e, ec=ec: e.tensor_scalar(
            F2[:, 0:2048], psA[:, 0:2048], bcol[:, ec:ec + 1], 0.0,
            ALU.add, ALU.min), reads=KPS4 + [("bqc",), ("bkc",)],
            writes=[("F2",)])
        P.s(lambda e: e.activation(F2[:, 0:2048], F2[:, 0:2048], AF.Exp),
            reads=[("F2",)], writes=[("F2",)])
        P.v(lambda e, ec=ec: e.tensor_scalar(
            F1[:, 0:2048], psA[:, 0:2048], bcol[:, ec:ec + 1], 0.0,
            ALU.add, ALU.max), reads=KPS4 + [("bqc",), ("bkc",)],
            writes=[("F1", 0), ("F1", 1)])
        P.v(lambda e, dst=dst: e.tensor_tensor(dst, F1[:, 0:2048],
                                               F2[:, 0:2048], ALU.add),
            reads=[("F1", 0), ("F1", 1), ("F2",)], writes=[dkey])

    def k_out(ec):
        stage = WA[:, 8192:10240]
        fm_evac(ec, bk_col, stage, K_WA1)
        B2v = B2[:].rearrange("p (t f) -> p t f", t=NT)
        for tg in range(0, NT, 8):
            tp_group(C, lambda k, tg=tg: stage[:, (tg + k) * 128:(tg + k + 1) * 128],
                     None, 0, 8, [K_WA1],
                     [("B2",)], dst_ap=B2v[:, tg:tg + 8, ec * 128:(ec + 1) * 128])
    proj_feat(k_out, "wk")

    bias_row("bv")
    load_weight_bf16(C, WA, W["wv"], [K_WA0], 1024, 1024)

    def v_slot(t):
        if t < 8:
            return B3[:, t * 1024:(t + 1) * 1024]
        return WA[:, 8192 + (t - 8) * 1024: 8192 + (t - 7) * 1024]

    def v_key(t):
        return ("B3",) if t < 8 else K_WA1
    for t in range(NT):
        for h in range(2):
            for dc in range(ND):
                P.t(lambda e, t=t, h=h, dc=dc: e.matmul(
                    psA[:, h * 512:h * 512 + 512],
                    B1[:, dc * T + t * 128: dc * T + t * 128 + 128],
                    WA[:, dc * 1024 + h * 512: dc * 1024 + h * 512 + 512],
                    start=(dc == 0), stop=False),
                    reads=[("B1",), K_WA0], writes=[("psA", h)])
            P.t(lambda e, t=t, h=h: e.matmul(
                psA[:, h * 512:h * 512 + 512], C.onesrowb,
                B3[0:1, 9216 + h * 512: 9216 + h * 512 + 512],
                start=False, stop=True),
                reads=[("CB",), KBROW], writes=[("psA", h)])
        P.v(lambda e, t=t: e.tensor_copy(v_slot(t), psA[:, 0:1024]),
            reads=[("psA", 0), ("psA", 1)], writes=[v_key(t)])

    for pg in range(2):
        for pp in range(4):
            p = pg * 4 + pp
            for t in range(NT):
                P.t(lambda e, p=p, pp=pp, t=t: e.matmul(
                    psA[:, pp * 512:pp * 512 + 128],
                    B2[:, t * 1024 + p * 128: t * 1024 + p * 128 + 128],
                    v_slot(t)[:, p * 128:(p + 1) * 128],
                    start=(t == 0), stop=(t == NT - 1)),
                    reads=[("B2",), v_key(t)], writes=[("psA", pp)])
            for t in range(NT):
                P.t(lambda e, p=p, t=t: e.matmul(
                    psS[:, p:p + 1],
                    B2[:, t * 1024 + p * 128: t * 1024 + p * 128 + 128],
                    C.onescolb, start=(t == 0), stop=(t == NT - 1)),
                    reads=[("B2",), ("CB",)], writes=[("psS",)])
        P.v(lambda e, pg=pg: e.tensor_copy(
            B3[:, 8192 + pg * 512: 8192 + (pg + 1) * 512]
            .rearrange("p (b x) -> p b x", b=4),
            psA[:].rearrange("p (b x) -> p b x", b=4)[:, :, 0:128]),
            reads=KPS4, writes=[("B3", "kv")])
    P.v(lambda e: e.tensor_copy(ksum, psS[:, 0:8]),
        reads=[("psS",)], writes=[("ksum",)])

    load_weight_bf16(C, WA, W["wq"], [K_WA0], 1024, 1024)

    def q_out(ec):
        fm_evac(ec, bq_col, B2[:, ec * T:(ec + 1) * T], ("B2",))
    proj_feat(q_out, "wq")

    for lp in range(4):
        for dc in range(ND):
            P.v(lambda e, lp=lp, dc=dc: e.tensor_scalar(
                WA[:, 0:512], B2[:, dc * T + lp * 512: dc * T + lp * 512 + 512],
                ksum[:, dc:dc + 1], None, ALU.mult),
                reads=[("B2",), ("ksum",)], writes=[K_WA0])
            P.t(lambda e, lp=lp, dc=dc: e.matmul(
                psS[0:16, 0:512], C.CS[:, dc * 16:(dc + 1) * 16], WA[:, 0:512],
                start=(dc == 0), stop=(dc == ND - 1)),
                reads=[("CS",), K_WA0], writes=[("psS",)])
        P.v(lambda e, lp=lp: e.tensor_scalar(
            F1[0:16, lp * 512:(lp + 1) * 512], psS[0:16, 0:512], 1e-6, None,
            ALU.add), reads=[("psS",)], writes=[("F1", 0), ("F1", 1)])
    P.v(lambda e: e.reciprocal(F1[0:16, 0:2048], F1[0:16, 0:2048]),
        reads=[("F1", 0), ("F1", 1)], writes=[("F1", 0), ("F1", 1)])

    def slot(c):
        return WA if c < 4 else B3

    def slot_col(c):
        return (c % 4) * 2048

    def slot_key(c):
        return K_WA0 if c < 4 else ("B3",)
    for h in range(16):
        c = h // 2
        r0 = (h % 2) * 64
        P.dma(F2[0:1, 0:2048], F1[h:h + 1, 0:2048],
              reads=[("F1", 0), ("F1", 1), ("F2",)], writes=[("F2",)])
        for lp in range(4):
            P.t(lambda e, h=h, r0=r0, lp=lp: e.matmul(
                psA[r0:r0 + 64, lp * 512:lp * 512 + 512],
                B3[r0:r0 + 64, 8192 + (h // 2) * 128 + r0: 8192 + (h // 2) * 128 + r0 + 64],
                B2[r0:r0 + 64, (h // 2) * T + lp * 512: (h // 2) * T + lp * 512 + 512],
                start=True, stop=True),
                reads=[("B3", "kv"), ("B2",)], writes=[("psA", lp)])
            P.t(lambda e, r0=r0, lp=lp: e.matmul(
                psB[r0:r0 + 64, (lp % 2) * 512:(lp % 2) * 512 + 512],
                C.onesrowf[0:1, 0:64], F2[0:1, lp * 512:(lp + 1) * 512],
                start=True, stop=True),
                reads=[("F2",), ("CF",)], writes=[("psB", lp % 2)])
            P.s(lambda e, h=h, c=c, r0=r0, lp=lp: e.copy(
                slot(c)[r0:r0 + 64, slot_col(c) + lp * 512: slot_col(c) + lp * 512 + 512],
                psA[r0:r0 + 64, lp * 512:lp * 512 + 512]),
                reads=[("psA", lp)], writes=[slot_key(c)])
            P.v(lambda e, h=h, c=c, r0=r0, lp=lp: e.tensor_tensor(
                slot(c)[r0:r0 + 64, slot_col(c) + lp * 512: slot_col(c) + lp * 512 + 512],
                slot(c)[r0:r0 + 64, slot_col(c) + lp * 512: slot_col(c) + lp * 512 + 512],
                psB[r0:r0 + 64, (lp % 2) * 512:(lp % 2) * 512 + 512], ALU.mult),
                reads=[("psB", lp % 2), slot_key(c)], writes=[slot_key(c)])

    load_weight_bf16(C, WA, W["wo"], [K_WA1], 1024, 1024, dst_col0=8192)
    bias_row("bo")
    for t in range(NT):
        for h2 in range(2):
            for c in range(8):
                P.t(lambda e, t=t, h2=h2, c=c: e.matmul(
                    psA[:, h2 * 512:h2 * 512 + 512],
                    slot(c)[:, slot_col(c) + t * 128: slot_col(c) + t * 128 + 128],
                    WA[:, 8192 + c * 1024 + h2 * 512: 8192 + c * 1024 + h2 * 512 + 512],
                    start=(c == 0), stop=False),
                    reads=[slot_key(c), K_WA1], writes=[("psA", h2)])
            P.t(lambda e, t=t, h2=h2: e.matmul(
                psA[:, h2 * 512:h2 * 512 + 512], C.onesrowb,
                B3[0:1, 9216 + h2 * 512: 9216 + h2 * 512 + 512],
                start=False, stop=True),
                reads=[("CB",), KBROW], writes=[("psA", h2)])
        P.v(lambda e, t=t: e.tensor_tensor(
            X[:, t * 1024:(t + 1) * 1024], X[:, t * 1024:(t + 1) * 1024],
            psA[:, 0:1024], ALU.add),
            reads=[("psA", 0), ("psA", 1), ("X",)], writes=[("X",)])




def emit_l1_ffn(C, W):
    P = C.P
    nc = C.nc
    X, B1, B2, B3, WA = C.X, C.B1, C.B2, C.B3, C.WA
    S1, F1, F2 = C.S1, C.F1, C.F2
    psA, psB, psS = C.psA, C.psB, C.psS
    KPS4 = [("psA", h) for h in range(4)]

    mu = S1[:, 0:16]
    rstd = S1[:, 16:32]
    ms1 = S1[:, 32:48]
    rs1 = S1[:, 48:64]
    w1col = S1[:, 64:72]
    w2col = S1[:, 128:136]
    gatew = S1[:, 72:128 + 8] if False else S1[:, 1280:1344]
    ebias = S1[:, 136:144]
    logit = S1[:, 144:272]
    wsel = S1[:, 272:400]
    ms2 = S1[:, 400:416]
    rr2 = S1[:, 416:432]
    x1d = nc.dram_tensor("l1_x1", [T, D], F32, kind="Internal").ap()
    x1v = x1d.rearrange("(t p) f -> p t f", p=128)

    KW = {"kva": ("WA", "kva"), "kvb": ("WA", "kvb"), "lat": ("WA", "lat"),
          "rope": ("WA", "rope"), "stg": ("WA", "stg"), "q": ("WA", "q")}

    def b2k(j):
        return ("B2", j)
    KB2 = [b2k(j) for j in range(8)]

    ln_stats(C, X, mu, rstd, [("X",)], ("mu",), ("rstd",))
    for t in range(NT):
        P.v(lambda e, t=t: e.tensor_scalar(
            F2[:, 0:1024], X[:, t * 1024:(t + 1) * 1024],
            mu[:, t:t + 1], rstd[:, t:t + 1], ALU.subtract, ALU.mult),
            reads=[("X",), ("mu",), ("rstd",)], writes=[("F2",)])
        P.s(lambda e, t=t: e.activation(F1[:, 0:1024], F2[:, 0:1024],
                                        AF.Square, accum_out=ms1[:, t:t + 1]),
            reads=[("F2",)], writes=[("F1", 0), ("ms1",)])
    C.ccell(1e-5)
    C.ccell(-0.5)
    P.v(lambda e: e.tensor_scalar(rs1, ms1, 1.0 / 1024, None, ALU.mult),
        reads=[("ms1",)], writes=[("rs1",)])
    P.s(lambda e: e.activation(rs1, rs1, AF.Ln, bias=1e-5),
        reads=[("rs1",), ("CC",)], writes=[("rs1",)])
    P.s(lambda e: e.activation(rs1, rs1, AF.Exp, scale=-0.5),
        reads=[("rs1",), ("CC",)], writes=[("rs1",)])
    P.v(lambda e: e.tensor_tensor(rs1, rs1, rstd, ALU.mult),
        reads=[("rs1",), ("rstd",)], writes=[("rs1",)])
    for t in range(NT):
        P.v(lambda e, t=t: e.tensor_scalar(
            B2[:, t * 1024:(t + 1) * 1024], X[:, t * 1024:(t + 1) * 1024],
            mu[:, t:t + 1], rs1[:, t:t + 1], ALU.subtract, ALU.mult),
            reads=[("X",), ("mu",), ("rs1",)], writes=[b2k(t // 2)])
    tp_tok2feat(C, B2, B1, KB2, [("B1",)])

    def col_from_row(wname, dst, key, n=8):
        P.dma(F2[0:1, 0:n * 128], W[wname].unsqueeze(0), reads=[("F2",)],
              writes=[("F2",)])
        for cc in range(n):
            P.t(lambda e, cc=cc: e.matmul(psB[:, cc:cc + 1],
                                          F2[0:1, cc * 128:(cc + 1) * 128],
                                          C.onecellf, start=True, stop=True),
                reads=[("F2",), ("CF",)], writes=[("psB", 0)])
        P.v(lambda e, dst=dst, n=n: e.tensor_copy(dst, psB[:, 0:n]),
            reads=[("psB", 0)], writes=[key])
    col_from_row("w_ln1", w1col, ("w1c",))
    col_from_row("w_ln2", w2col, ("w2c",))
    load_weight_bf16(C, WA, W["Wkva"], [KW["kva"]], 1024, 160,
                     scale_col=w1col, key_scale=("w1c",))
    load_weight_bf16(C, WA, W["Wkvb"], [KW["kvb"]], 128, 512, dst_col0=1280)
    load_weight_bf16(C, WA, W["Wq"], [KW["q"]], 1024, 1024, dst_col0=8192,
                     scale_col=w1col, key_scale=("w1c",))

    for oc2 in range(2):
        m = 128 if oc2 == 0 else 32
        for lp in range(4):
            for dc in range(ND):
                P.t(lambda e, oc2=oc2, m=m, lp=lp, dc=dc: e.matmul(
                    psA[0:m, lp * 512:lp * 512 + 512],
                    WA[:, dc * 160 + oc2 * 128: dc * 160 + oc2 * 128 + m],
                    B1[:, dc * T + lp * 512: dc * T + lp * 512 + 512],
                    start=(dc == 0), stop=(dc == ND - 1)),
                    reads=[KW["kva"], ("B1",)], writes=[("psA", lp)])
        dstc = 2048 if oc2 == 0 else 4096
        dkey = KW["lat"] if oc2 == 0 else KW["rope"]
        P.v(lambda e, m=m, dstc=dstc: e.tensor_copy(
            WA[0:m, dstc:dstc + 2048], psA[0:m, 0:2048]),
            reads=KPS4, writes=[dkey])

    VT = B3[:, 0:8448].rearrange("p (t h x) -> p t h x", h=16, x=33)
    P.v(lambda e: e.memset(VT[:, :, :, 32:33], 1.0), writes=[("B3", "v")])
    for c in range(4):
        for lp in range(4):
            P.t(lambda e, c=c, lp=lp: e.matmul(
                psA[:, lp * 512:lp * 512 + 512],
                WA[:, 1280 + c * 128:1280 + (c + 1) * 128],
                WA[:, 2048 + lp * 512: 2048 + lp * 512 + 512],
                start=True, stop=True),
                reads=[KW["kvb"], KW["lat"]], writes=[("psA", lp)])
        P.v(lambda e: e.tensor_copy(WA[:, 6144:8192], psA[:, 0:2048]),
            reads=KPS4, writes=[KW["stg"]])
        for hl in range(4):
            h = 4 * c + hl
            j, r0 = h // 2, (h % 2) * 64
            P.dma(B2[r0:r0 + 32, j * 2048:(j + 1) * 2048],
                  WA[hl * 32:(hl + 1) * 32, 6144:8192],
                  reads=[KW["stg"]], writes=[b2k(j)])
        for tg in range(0, NT, 8):
            tp_group(C, lambda k, tg=tg: WA[:, 6144 + (tg + k) * 128:
                                            6144 + (tg + k + 1) * 128],
                     None, 0, 8, [KW["stg"]], [("B3", "v")],
                     dst_ap=VT[:, tg:tg + 8, 4 * c:4 * c + 4, 0:32],
                     src_ap=C.psT[:, 0:1024].rearrange(
                         "p (t h x) -> p t h x", t=8, h=4))

    RT = B3[:, 8448:8960]
    for tg in range(0, NT, 8):
        tp_group(C, lambda k, tg=tg: WA[0:32, 4096 + (tg + k) * 128:
                                        4096 + (tg + k + 1) * 128],
                 None, 0, 8, [KW["rope"]], [("B3", "rt")],
                 dst_ap=RT.rearrange("p (t x) -> p t x", x=32)[:, tg:tg + 8, :],
                 src_ap=C.psT[:, 0:1024].rearrange(
                     "p (t x) -> p t x", x=128)[:, :, 0:32])
    _rope_rotate(C, RT, ("B3", "rt"))
    for tg in range(0, NT, 8):
        for k in range(8):
            P.t(lambda e, k=k, tg=tg: e.transpose(
                C.psT[0:32, k * 128:(k + 1) * 128],
                RT[:, (tg + k) * 32:(tg + k + 1) * 32], C.identb),
                reads=[("B3", "rt"), ("CB",)], writes=[("psT", k)])
        P.v(lambda e, tg=tg: e.tensor_copy(
            WA[0:32, 4096 + tg * 128:4096 + (tg + 8) * 128],
            C.psT[0:32, 0:1024]),
            reads=[("psT", k) for k in range(8)], writes=[KW["rope"]])
    for j in range(8):
        for r0 in (32, 96):
            P.dma(B2[r0:r0 + 32, j * 2048:(j + 1) * 2048],
                  WA[0:32, 4096:6144], reads=[KW["rope"]], writes=[b2k(j)])

    QS = B3[:, 8960:9984]
    for t in range(NT):
        for hf in range(2):
            for dc in range(ND):
                P.t(lambda e, t=t, hf=hf, dc=dc: e.matmul(
                    psA[:, hf * 512:hf * 512 + 512],
                    B1[:, dc * T + t * 128: dc * T + t * 128 + 128],
                    WA[:, 8192 + dc * 1024 + hf * 512:
                       8192 + dc * 1024 + hf * 512 + 512],
                    start=(dc == 0), stop=(dc == ND - 1)),
                    reads=[("B1",), KW["q"]], writes=[("psA", hf)])
        KQ = [("psA", 0), ("psA", 1)]
        P.s(lambda e: e.copy(F1[:, 0:1024], psA[:, 0:1024]),
            reads=KQ, writes=[("F1", 0)])
        qv = F1[:, 0:1024].rearrange("p (h z) -> p h z", h=16)
        sv = QS.rearrange("p (h z) -> p h z", h=16)
        P.v(lambda e: e.tensor_copy(sv[:, :, 0:32], qv[:, :, 0:32]),
            reads=[("F1", 0)], writes=[("B3", "qs")])
        _rope_rotate_q(C, qv, sv, [("F1", 0)], ("B3", "qs"), t)
        for k in range(8):
            P.t(lambda e, k=k: e.transpose(
                C.psT[:, k * 128:(k + 1) * 128],
                QS[:, k * 128:(k + 1) * 128], C.identb),
                reads=[("B3", "qs"), ("CB",)], writes=[("psT", k)])
        P.v(lambda e, t=t: e.tensor_copy(
            B1[:].rearrange("p (d l) -> p d l", d=8)[:, :, t * 128:(t + 1) * 128],
            C.psT[:, 0:1024].rearrange("p (d x) -> p d x", d=8)),
            reads=[("psT", k) for k in range(8)], writes=[("B1",)])

    C.ccell(0.125)
    for j in range(8):
        for h in (2 * j + 1, 2 * j):
            r0 = (h % 2) * 64
            for lh in range(2):
                nkc = min((lh + 1) * 8, NT)
                for kc in range(nkc):
                    diag = kc * 128 - lh * 1024
                    off = max(0, diag)
                    if off >= 1024:
                        continue
                    if off > 0:
                        P.v(lambda e, off=off: e.memset(
                            WA[:, 6144:6144 + off], 0.0), writes=[KW["stg"]])
                    for nb in range(off // 512, 2):
                        n0 = max(off, nb * 512)
                        n1 = (nb + 1) * 512
                        P.t(lambda e, j=j, h=h, r0=r0, lh=lh, kc=kc, n0=n0, n1=n1: e.matmul(
                            psA[:, n0:n1],
                            B2[r0:r0 + 64, j * 2048 + kc * 128: j * 2048 + kc * 128 + 128],
                            B1[r0:r0 + 64, (h // 2) * T + lh * 1024 + n0:
                               (h // 2) * T + lh * 1024 + n1],
                            start=True, stop=True),
                            reads=[b2k(j), ("B1",)],
                            writes=[("psA", nb)])
                    if 0 <= diag < 1024:
                        P.v(lambda e, diag=diag: e.tensor_tensor(
                            psA[:, diag:diag + 128], psA[:, diag:diag + 128],
                            C.CT[:], ALU.add),
                            reads=[("psA", 0), ("psA", 1), ("CT",)],
                            writes=[("psA", 0), ("psA", 1)])
                    P.s(lambda e, off=off: e.activation(
                        WA[:, 6144 + off:6144 + 1024],
                        psA[:, off:1024], AF.Exp, scale=0.125),
                        reads=[("psA", 0), ("psA", 1), ("CC",)],
                        writes=[KW["stg"]])
                    for nb in range(2):
                        P.t(lambda e, h=h, r0=r0, kc=kc, nb=nb, nkc=nkc: e.matmul(
                            psB[r0:r0 + 33, nb * 512:(nb + 1) * 512],
                            B3[:, kc * 528 + h * 33: kc * 528 + h * 33 + 33],
                            WA[:, 6144 + nb * 512:6144 + (nb + 1) * 512],
                            start=(kc == 0), stop=(kc == nkc - 1)),
                            reads=[("B3", "v"), KW["stg"]],
                            writes=[("psB", nb)])
                P.v(lambda e, r0=r0: e.reciprocal(F2[0:1, 0:1024],
                                                  psB[r0 + 32:r0 + 33, 0:1024]),
                    reads=[("psB", 0), ("psB", 1)], writes=[("F2",)])
                for nb in range(2):
                    P.t(lambda e, r0=r0, nb=nb: e.matmul(
                        psA[r0:r0 + 32, 1024 + nb * 512:1024 + (nb + 1) * 512],
                        C.onesrowf[0:1, 0:32],
                        F2[0:1, nb * 512:(nb + 1) * 512],
                        start=True, stop=True),
                        reads=[("F2",), ("CF",)], writes=[("psA", 2 + nb)])
                P.s(lambda e, j=j, r0=r0, lh=lh: e.copy(
                    B2[r0:r0 + 32, j * 2048 + lh * 1024: j * 2048 + lh * 1024 + 1024],
                    psB[r0:r0 + 32, 0:1024]),
                    reads=[("psB", 0), ("psB", 1)], writes=[b2k(j)])
                P.v(lambda e, j=j, r0=r0, lh=lh: e.tensor_tensor(
                    B2[r0:r0 + 32, j * 2048 + lh * 1024: j * 2048 + lh * 1024 + 1024],
                    B2[r0:r0 + 32, j * 2048 + lh * 1024: j * 2048 + lh * 1024 + 1024],
                    psA[r0:r0 + 32, 1024:2048], ALU.mult),
                    reads=[("psA", 2), ("psA", 3), b2k(j)], writes=[b2k(j)])

    P.p(lambda e: e.memset(WA[:, 8192:16384], 0.0), writes=[KW["q"]])
    wo3 = W["Wo"].rearrange("(c p) n -> p c n", p=64)
    for h in range(16):
        j, r0 = h // 2, (h % 2) * 64
        c0 = 0
        while c0 < 1024:
            cw = min(2048, 1024 - c0)
            P.dma(F2[0:1, 0:cw] if False else F2[0:32, 0:cw],
                  wo3[0:32, h, c0:c0 + cw], reads=[("F2",)], writes=[("F2",)])
            P.p(lambda e, j=j, r0=r0, c0=c0, cw=cw: e.tensor_copy(
                WA[r0:r0 + 32, 8192 + j * 1024 + c0: 8192 + j * 1024 + c0 + cw],
                F2[0:32, 0:cw]),
                reads=[("F2",)], writes=[KW["q"]])
            c0 += cw
    for t in range(NT):
        for hf in range(2):
            for j in range(8):
                P.t(lambda e, t=t, hf=hf, j=j: e.matmul(
                    psA[:, hf * 512:hf * 512 + 512],
                    B2[:, j * 2048 + t * 128: j * 2048 + t * 128 + 128],
                    WA[:, 8192 + j * 1024 + hf * 512:
                       8192 + j * 1024 + hf * 512 + 512],
                    start=(j == 0), stop=(j == 7)),
                    reads=KB2 + [KW["q"]], writes=[("psA", hf)])
        P.v(lambda e, t=t: e.tensor_scalar(
            F1[:, 0:1024], X[:, t * 1024:(t + 1) * 1024],
            mu[:, t:t + 1], rstd[:, t:t + 1], ALU.subtract, ALU.mult),
            reads=[("X",), ("mu",), ("rstd",)], writes=[("F1", 0)])
        P.v(lambda e: e.tensor_tensor(F2[:, 0:1024], F1[:, 0:1024],
                                      psA[:, 0:1024], ALU.add),
            reads=[("F1", 0), ("psA", 0), ("psA", 1)], writes=[("F2",)])
        P.dma(x1v[:, t, :], F2[:, 0:1024], reads=[("F2",)], writes=[("x1",)])
        P.s(lambda e, t=t: e.activation(F1[:, 0:1024], F2[:, 0:1024],
                                        AF.Square, accum_out=ms2[:, t:t + 1]),
            reads=[("F2",)], writes=[("F1", 0), ("ms2",)])

    P.v(lambda e: e.tensor_scalar(rr2, ms2, 1.0 / 1024, None, ALU.mult),
        reads=[("ms2",)], writes=[("rr2",)])
    P.s(lambda e: e.activation(rr2, rr2, AF.Ln, bias=1e-5),
        reads=[("rr2",), ("CC",)], writes=[("rr2",)])
    P.s(lambda e: e.activation(rr2, rr2, AF.Exp, scale=-0.5),
        reads=[("rr2",), ("CC",)], writes=[("rr2",)])
    for cc in range(8):
        P.dma(gatew[:, cc * 8:(cc + 1) * 8],
              W["Wgate"][cc * 128:(cc + 1) * 128, :],
              reads=[("gw",)], writes=[("gw",)])
    for cc in range(8):
        P.v(lambda e, cc=cc: e.tensor_scalar(
            gatew[:, cc * 8:(cc + 1) * 8], gatew[:, cc * 8:(cc + 1) * 8],
            w2col[:, cc:cc + 1], None, ALU.mult),
            reads=[("gw",), ("w2c",)], writes=[("gw",)])
    P.dma(F2[0:1, 0:8], W["expert_bias"].unsqueeze(0), reads=[("F2",)],
          writes=[("F2",)])
    P.t(lambda e: e.matmul(psB[:, 0:8], C.onesrowf, F2[0:1, 0:8],
                           start=True, stop=True),
        reads=[("F2",), ("CF",)], writes=[("psB", 0)])
    P.v(lambda e: e.tensor_copy(ebias, psB[:, 0:8]),
        reads=[("psB", 0)], writes=[("ebias",)])
    for t in range(NT):
        P.dma(F2[:, 0:1024], x1v[:, t, :], reads=[("x1",), ("F2",)],
              writes=[("F2",)])
        P.v(lambda e, t=t: e.tensor_tensor(
            X[:, t * 1024:(t + 1) * 1024], X[:, t * 1024:(t + 1) * 1024],
            F2[:, 0:1024], ALU.add),
            reads=[("F2",), ("X",)], writes=[("X",)])
        P.v(lambda e, t=t: e.tensor_scalar(
            F2[:, 0:1024], F2[:, 0:1024], rr2[:, t:t + 1], None, ALU.mult),
            reads=[("F2",), ("rr2",)], writes=[("F2",)])
        P.v(lambda e, t=t: e.tensor_copy(B2[:, t * 1024:(t + 1) * 1024],
                                         F2[:, 0:1024]),
            reads=[("F2",)], writes=[b2k(t // 2)])
        for k in range(8):
            P.t(lambda e, k=k: e.transpose(
                psA[:, 1536 + (k % 4) * 128:1536 + (k % 4 + 1) * 128],
                F2[:, k * 128:(k + 1) * 128], C.CI),
                reads=[("F2",), ("CI",)], writes=[("psA", 3)])
            if k % 4 == 3:
                P.v(lambda e, k=k: e.tensor_copy(
                    F1[:, (k - 3) * 128:(k + 1) * 128], psA[:, 1536:2048]),
                    reads=[("psA", 3)], writes=[("F1", 0)])
        for dc in range(8):
            P.t(lambda e, t=t, dc=dc: e.matmul(
                psS[:, t * 8:(t + 1) * 8], F1[:, dc * 128:(dc + 1) * 128],
                gatew[:, dc * 8:(dc + 1) * 8],
                start=(dc == 0), stop=(dc == 7)),
                reads=[("F1", 0), ("gw",)], writes=[("psS",)])
    P.v(lambda e: e.tensor_tensor(
        logit.rearrange("p (t e) -> p t e", e=8),
        psS[:, 0:128].rearrange("p (t e) -> p t e", e=8),
        ebias.unsqueeze(1).broadcast_to([128, 16, 8]), ALU.add),
        reads=[("psS",), ("ebias",)], writes=[("logit",)])
    for t in range(NT):
        lt = logit[:, t * 8:(t + 1) * 8]
        m1 = S1[:, 432:433]
        m2 = S1[:, 433:434]
        s1 = S1[:, 434:435]
        s2 = S1[:, 435:436]
        msk1 = F2[:, 0:8]
        msk2 = F2[:, 8:16]
        P.v(lambda e, lt=lt: e.reduce_max(m1, lt, axis=AX.X),
            reads=[("logit",)], writes=[("m1",)])
        P.v(lambda e, lt=lt: e.tensor_scalar(msk1, lt, m1, None, ALU.is_ge),
            reads=[("logit",), ("m1",)], writes=[("F2",)])
        P.v(lambda e, lt=lt: e.scalar_tensor_tensor(
            F2[:, 16:24], msk1, -1e30, lt, ALU.mult, ALU.add),
            reads=[("logit",), ("F2",)], writes=[("F2",)])
        P.v(lambda e: e.reduce_max(m2, F2[:, 16:24], axis=AX.X),
            reads=[("F2",)], writes=[("m2",)])
        P.v(lambda e, lt=lt: e.tensor_scalar(msk2, lt, m2, None, ALU.is_ge),
            reads=[("logit",), ("m2",)], writes=[("F2",)])
        P.v(lambda e: e.tensor_tensor(s1, m1, m2, ALU.subtract),
            reads=[("m1",), ("m2",)], writes=[("s1",)])
        P.s(lambda e: e.activation(s1, s1, AF.Sigmoid),
            reads=[("s1",)], writes=[("s1",)])
        P.v(lambda e: e.tensor_scalar(s2, s1, -1.0, 1.0, ALU.mult, ALU.add),
            reads=[("s1",)], writes=[("s2",)])
        P.v(lambda e: e.tensor_tensor(msk2, msk2, msk1, ALU.subtract),
            reads=[("F2",)], writes=[("F2",)])
        P.v(lambda e, t=t: e.tensor_scalar(
            wsel[:, t * 8:(t + 1) * 8], msk1, s1, None, ALU.mult),
            reads=[("F2",), ("s1",)], writes=[("wsel",)])
        P.v(lambda e, t=t: e.scalar_tensor_tensor(
            wsel[:, t * 8:(t + 1) * 8], msk2, s2, wsel[:, t * 8:(t + 1) * 8],
            ALU.mult, ALU.add),
            reads=[("F2",), ("s2",), ("wsel",)], writes=[("wsel",)])
    tp_tok2feat(C, B2, B1, KB2, [("B1",)])

    for e_i in range(9):
        if e_i < 8:
            wg_d, wu_d, wd_d = W["Wg"][e_i], W["Wu"][e_i], W["Wd"][e_i]
        else:
            wg_d, wu_d, wd_d = W["sWg"][0], W["sWu"][0], W["sWd"][0]
        load_weight_bf16(C, WA, wg_d, [("WA", 0)], 1024, 512, dst_col0=0,
                         scale_col=w2col, key_scale=("w2c",))
        load_weight_bf16(C, WA, wu_d, [("WA", 0)], 1024, 512, dst_col0=4096,
                         scale_col=w2col, key_scale=("w2c",))
        load_weight_bf16(C, WA, wd_d, [("WA", 1)], 512, 1024, dst_col0=8192)
        _moe_expert(C, e_i, 8, wsel, ("wsel",))


def _moe_expert(C, e_i, n_routed, wsel, kwsel):
    P = C.P
    X, B1, B2, B3, WA = C.X, C.B1, C.B2, C.B3, C.WA
    psA, psB = C.psA, C.psB
    KPS4 = [("psA", h) for h in range(4)]
    KB2 = [("B2", j) for j in range(8)]
    for ic in range(4):
        for lp in range(4):
            for dc in range(ND):
                P.t(lambda e, ic=ic, lp=lp, dc=dc: e.matmul(
                    psA[:, lp * 512:(lp + 1) * 512],
                    WA[:, dc * 512 + ic * 128: dc * 512 + ic * 128 + 128],
                    B1[:, dc * T + lp * 512: dc * T + lp * 512 + 512],
                    start=(dc == 0), stop=(dc == ND - 1)),
                    reads=[("WA", 0), ("B1",)], writes=[("psA", lp)])
        P.s(lambda e, ic=ic: e.activation(B2[:, ic * T:(ic + 1) * T],
                                          psA[:, 0:2048], AF.Silu),
            reads=KPS4, writes=[("B2", ic)])
    for ic in range(4):
        for lp in range(4):
            for dc in range(ND):
                P.t(lambda e, ic=ic, lp=lp, dc=dc: e.matmul(
                    psA[:, lp * 512:(lp + 1) * 512],
                    WA[:, 4096 + dc * 512 + ic * 128:
                       4096 + dc * 512 + ic * 128 + 128],
                    B1[:, dc * T + lp * 512: dc * T + lp * 512 + 512],
                    start=(dc == 0), stop=(dc == ND - 1)),
                    reads=[("WA", 0), ("B1",)], writes=[("psA", lp)])
        P.v(lambda e, ic=ic: e.tensor_tensor(
            B3[:, ic * T:(ic + 1) * T], B2[:, ic * T:(ic + 1) * T],
            psA[:, 0:2048], ALU.mult),
            reads=KPS4 + [("B2", ic)], writes=[("B3",), ("B3", "v")])
    for t in range(NT):
        for h in range(2):
            for ic in range(4):
                P.t(lambda e, t=t, h=h, ic=ic: e.matmul(
                    psB[:, h * 512:(h + 1) * 512],
                    B3[:, ic * T + t * 128: ic * T + t * 128 + 128],
                    WA[:, 8192 + ic * 1024 + h * 512:
                       8192 + ic * 1024 + h * 512 + 512],
                    start=(ic == 0), stop=(ic == 3)),
                    reads=[("WA", 1), ("B3",)], writes=[("psB", h)])
        if e_i < n_routed:
            wcell = wsel[:, t * n_routed + e_i: t * n_routed + e_i + 1]
            P.v(lambda e, t=t, wcell=wcell: e.scalar_tensor_tensor(
                X[:, t * 1024:(t + 1) * 1024], psB[:, 0:1024], wcell,
                X[:, t * 1024:(t + 1) * 1024], ALU.mult, ALU.add),
                reads=[("psB", 0), ("psB", 1), kwsel, ("X",)], writes=[("X",)])
        else:
            P.v(lambda e, t=t: e.tensor_tensor(
                X[:, t * 1024:(t + 1) * 1024], X[:, t * 1024:(t + 1) * 1024],
                psB[:, 0:1024], ALU.add),
                reads=[("psB", 0), ("psB", 1), ("X",)], writes=[("X",)])


def _rope_rotate(C, RT, key):
    P = C.P
    ev = RT.rearrange("p (t i w) -> p t i w", i=16, w=2)[:, :, :, 0:1].squeeze(3)
    od = RT.rearrange("p (t i w) -> p t i w", i=16, w=2)[:, :, :, 1:2].squeeze(3)
    cos = C.CRC[:].rearrange("p (t i) -> p t i", i=16)
    sin = C.CRS[:].rearrange("p (t i) -> p t i", i=16)
    F2 = C.F2
    t1 = F2[:, 0:256].rearrange("p (t i) -> p t i", i=16)
    t2 = F2[:, 256:512].rearrange("p (t i) -> p t i", i=16)
    t3 = F2[:, 512:768].rearrange("p (t i) -> p t i", i=16)
    t4 = F2[:, 768:1024].rearrange("p (t i) -> p t i", i=16)
    P.v(lambda e: e.tensor_tensor(t1, ev, cos, ALU.mult),
        reads=[key, ("CRC",)], writes=[("F2",)])
    P.v(lambda e: e.tensor_tensor(t2, od, sin, ALU.mult),
        reads=[key, ("CRS",)], writes=[("F2",)])
    P.v(lambda e: e.tensor_tensor(t3, ev, sin, ALU.mult),
        reads=[key, ("CRS",)], writes=[("F2",)])
    P.v(lambda e: e.tensor_tensor(t4, od, cos, ALU.mult),
        reads=[key, ("CRC",)], writes=[("F2",)])
    P.v(lambda e: e.tensor_tensor(ev, t1, t2, ALU.subtract),
        reads=[("F2",)], writes=[key])
    P.v(lambda e: e.tensor_tensor(od, t3, t4, ALU.add),
        reads=[("F2",)], writes=[key])


def _rope_rotate_q(C, qv, sv, kq, ks, t):
    P = C.P
    F2 = C.F2
    qe = qv[:, :, 32:64].rearrange("p h (i w) -> p h i w", w=2)[:, :, :, 0:1].squeeze(3)
    qo = qv[:, :, 32:64].rearrange("p h (i w) -> p h i w", w=2)[:, :, :, 1:2].squeeze(3)
    se = sv[:, :, 32:64].rearrange("p h (i w) -> p h i w", w=2)[:, :, :, 0:1].squeeze(3)
    so = sv[:, :, 32:64].rearrange("p h (i w) -> p h i w", w=2)[:, :, :, 1:2].squeeze(3)
    cos = C.CRC[:, t * 16:(t + 1) * 16].unsqueeze(1).broadcast_to([128, 16, 16])
    sin = C.CRS[:, t * 16:(t + 1) * 16].unsqueeze(1).broadcast_to([128, 16, 16])
    t1 = F2[:, 0:256].rearrange("p (h i) -> p h i", i=16)
    t2 = F2[:, 256:512].rearrange("p (h i) -> p h i", i=16)
    t3 = F2[:, 512:768].rearrange("p (h i) -> p h i", i=16)
    t4 = F2[:, 768:1024].rearrange("p (h i) -> p h i", i=16)
    P.v(lambda e: e.tensor_tensor(t1, qe, cos, ALU.mult),
        reads=kq + [("CRC",)], writes=[("F2",)])
    P.v(lambda e: e.tensor_tensor(t2, qo, sin, ALU.mult),
        reads=kq + [("CRS",)], writes=[("F2",)])
    P.v(lambda e: e.tensor_tensor(t3, qe, sin, ALU.mult),
        reads=kq + [("CRS",)], writes=[("F2",)])
    P.v(lambda e: e.tensor_tensor(t4, qo, cos, ALU.mult),
        reads=kq + [("CRC",)], writes=[("F2",)])
    P.v(lambda e: e.tensor_tensor(se, t1, t2, ALU.subtract),
        reads=[("F2",)], writes=[ks])
    P.v(lambda e: e.tensor_tensor(so, t3, t4, ALU.add),
        reads=[("F2",)], writes=[ks])




def emit_l2_attn(C, W):
    P = C.P
    nc = C.nc
    X, B1, B2, B3, WA = C.X, C.B1, C.B2, C.B3, C.WA
    S1, F1, F2 = C.S1, C.F1, C.F2
    psA, psB, psS, psT = C.psA, C.psB, C.psS, C.psT

    mu = S1[:, 0:16]
    rstd = S1[:, 16:32]
    Ain = S1[:, 32:64]
    Aex = S1[:, 64:96]
    ecol = S1[:, 96:128]
    wdt = S1[:, 128:160]
    A127 = S1[:, 160:192]
    CW = S1[:, 192:272]
    CBc = S1[:, 272:290]
    DB = S1[:, 416:448]
    dtbB = S1[:, 448:480]
    negA = S1[:, 480:512]
    dt_raw = S1[:, 512:1024]
    dt_sp = S1[:, 1024:1536]
    a_all = S1[:, 1536:2048]
    SEG = 512
    NSEG = 4
    NCH = 18
    xmd = nc.dram_tensor("l2_xm", [T, D], F32, kind="Internal").ap()
    xmv = xmd.rearrange("(t p) f -> p t f", p=128)

    B_f = B3[:, 0:512]
    C_f = B3[:, 512:1024]
    Btk = B3[:, 1024:1152]
    XT = B3[:, 2048:4096]
    DTX = B3[:, 4096:6144]
    SPV = B3[:, 6144:8192]
    GSL = B3[:, 8192:10240]

    KB2 = ("B2",)
    KX = [("X",)]

    ln_stats(C, X, mu, rstd, KX, ("mu",), ("rstd",))
    for t in range(NT):
        P.v(lambda e, t=t: e.tensor_scalar(
            B2[:, t * 1024:(t + 1) * 1024], X[:, t * 1024:(t + 1) * 1024],
            mu[:, t:t + 1], rstd[:, t:t + 1], ALU.subtract, ALU.mult),
            reads=KX + [("mu",), ("rstd",)], writes=[KB2])
    tp_tok2feat(C, B2, B1, [KB2], [("B1",)])

    def col_cells(wname, dst, key, n, scale=1.0):
        P.dma(F2[0:1, 0:n], W[wname].unsqueeze(0), reads=[("F2",)],
              writes=[("F2",)])
        P.t(lambda e, n=n: e.matmul(psB[:, 0:n], C.onesrowf, F2[0:1, 0:n],
                                    start=True, stop=True),
            reads=[("F2",), ("CF",)], writes=[("psB", 0)])
        P.v(lambda e, dst=dst, n=n, scale=scale: e.tensor_scalar(
            dst, psB[:, 0:n], scale, None, ALU.mult),
            reads=[("psB", 0)], writes=[key])

    for d_i, sfx in ((0, "f"), (1, "b")):
        KWI = ("WA", 0)
        KWO = ("WA", 0)
        col_cells("dtb_" + sfx, dtbB, ("dtbB",), 32)
        col_cells("D_" + sfx, DB, ("DB",), 32)
        col_cells("Alog_" + sfx, negA, ("negA",), 32)
        P.s(lambda e: e.activation(negA, negA, AF.Exp),
            reads=[("negA",)], writes=[("negA",)])
        P.v(lambda e: e.tensor_scalar(negA, negA, -1.0, None, ALU.mult),
            reads=[("negA",)], writes=[("negA",)])
        for j in range(4):
            for half in range(2):
                oc_lo, oc_hi = half * 9, min(NCH, half * 9 + 9)
                P.dma(F2[0:1, 0:1152],
                      W["conv_" + sfx][j:j + 1, half * 1152:half * 1152 + 1152],
                      reads=[("F2",)], writes=[("F2",)])
                for oc in range(oc_lo, oc_hi):
                    P.t(lambda e, oc=oc, j=j, half=half: e.matmul(
                        psB[:, oc * 4 + j:oc * 4 + j + 1],
                        F2[0:1, (oc - half * 9) * 128:(oc - half * 9 + 1) * 128],
                        C.onecellf, start=True, stop=True),
                        reads=[("F2",), ("CF",)], writes=[("psB", 0)])
        P.v(lambda e: e.tensor_copy(CW[:, 0:72], psB[:, 0:72]),
            reads=[("psB", 0)], writes=[("CW",)])
        for half in range(2):
            oc_lo, oc_hi = half * 9, min(NCH, half * 9 + 9)
            P.dma(F2[0:1, 0:1152],
                  W["convb_" + sfx].unsqueeze(0)[:, half * 1152:half * 1152 + 1152],
                  reads=[("F2",)], writes=[("F2",)])
            for oc in range(oc_lo, oc_hi):
                P.t(lambda e, oc=oc, half=half: e.matmul(
                    psB[:, oc:oc + 1],
                    F2[0:1, (oc - half * 9) * 128:(oc - half * 9 + 1) * 128],
                    C.onecellf, start=True, stop=True),
                    reads=[("F2",), ("CF",)], writes=[("psB", 0)])
        P.v(lambda e: e.tensor_copy(CBc[:, 0:NCH], psB[:, 0:NCH]),
            reads=[("psB", 0)], writes=[("CBc",)])
        load_weight_bf16(C, WA, W["Wi_" + sfx][:, 4352:4384], [KWI], 1024, 32)
        for t in range(NT):
            for dc in range(ND):
                P.t(lambda e, t=t, dc=dc: e.matmul(
                    psS[:, t * 32:(t + 1) * 32],
                    B1[:, dc * T + t * 128: dc * T + t * 128 + 128],
                    WA[:, dc * 32:(dc + 1) * 32],
                    start=(dc == 0), stop=(dc == ND - 1)),
                    reads=[("B1",), KWI], writes=[("psS",)])
        P.v(lambda e: e.tensor_tensor(
            dt_raw.rearrange("p (c h) -> p c h", h=32),
            psS[:, 0:512].rearrange("p (c h) -> p c h", h=32),
            dtbB.unsqueeze(1).broadcast_to([128, 16, 32]), ALU.add),
            reads=[("psS",), ("dtbB",)], writes=[("dtr",)])
        P.s(lambda e: e.activation(F2[:, 0:512], dt_raw, AF.Abs),
            reads=[("dtr",)], writes=[("F2",)])
        P.s(lambda e: e.activation(F2[:, 0:512], F2[:, 0:512], AF.Exp,
                                   scale=-1.0),
            reads=[("F2",), ("CC",)], writes=[("F2",)])
        P.s(lambda e: e.activation(F2[:, 0:512], F2[:, 0:512], AF.Ln, bias=1.0),
            reads=[("F2",)], writes=[("F2",)])
        P.v(lambda e: e.tensor_scalar(F1[:, 0:512], dt_raw, 0.0, None, ALU.max),
            reads=[("dtr",)], writes=[("F1", 0)])
        P.v(lambda e: e.tensor_tensor(dt_sp, F1[:, 0:512], F2[:, 0:512],
                                      ALU.add),
            reads=[("F1", 0), ("F2",)], writes=[("dtsp",)])
        P.v(lambda e: e.tensor_tensor(
            a_all.rearrange("p (c h) -> p c h", h=32),
            dt_sp.rearrange("p (c h) -> p c h", h=32),
            negA.unsqueeze(1).broadcast_to([128, 16, 32]), ALU.mult),
            reads=[("dtsp",), ("negA",)], writes=[("aall",)])

        load_weight_bf16(C, WA, W["Wo_" + sfx], [KWO], 2048, 1024)

        seg_order = range(NSEG) if d_i == 0 else range(NSEG - 1, -1, -1)
        P.v(lambda e: e.memset(SPV[:], 0.0), writes=[("SPV",)])
        for si, sg in enumerate(seg_order):
            s0 = sg * SEG
            for oc in range(NCH + 16):
                is_z = oc >= NCH
                wcol = (2048 + oc * 128) if oc < NCH else (oc - NCH) * 128
                slot = 8192 + (oc % 2) * 1024
                KSL = ("GSL",)
                load_weight_bf16(C, B3, W["Wi_" + sfx][:, wcol:wcol + 128],
                                 [KSL], 1024, 128, dst_col0=slot)
                for dc in range(ND):
                    P.t(lambda e, oc=oc, dc=dc, slot=slot, s0=s0: e.matmul(
                        psA[:, 0:512],
                        B3[:, slot + dc * 128: slot + dc * 128 + 128],
                        B1[:, dc * T + s0: dc * T + s0 + SEG],
                        start=(dc == 0), stop=(dc == ND - 1)),
                        reads=[("B1",), KSL], writes=[("psA", 0)])
                if is_z:
                    zc = oc - NCH
                    P.s(lambda e, zc=zc: e.copy(
                        B2[:, 8192 + zc * 512:8192 + (zc + 1) * 512],
                        psA[:, 0:512]),
                        reads=[("psA", 0)], writes=[("B2z",), ("B2",)])
                    continue
                P.s(lambda e: e.copy(F1[:, 0:512], psA[:, 0:512]),
                    reads=[("psA", 0)], writes=[("F1", 0)])
                HL = S1[:, 290:290 + 3]
                first_seg = (si == 0)
                hl0 = 290 + oc * 3
                P.v(lambda e, oc=oc: e.tensor_scalar(
                    F2[:, 0:512], F1[:, 0:512], CW[:, oc * 4 + 3:oc * 4 + 4],
                    None, ALU.mult),
                    reads=[("F1", 0), ("CW",)], writes=[("F2",)])
                if d_i == 0:
                    for o in range(1, 4):
                        P.v(lambda e, oc=oc, o=o: e.scalar_tensor_tensor(
                            F2[:, o:512], F1[:, 0:512 - o],
                            CW[:, oc * 4 + 3 - o:oc * 4 + 4 - o],
                            F2[:, o:512], ALU.mult, ALU.add),
                            reads=[("F1", 0), ("CW",), ("F2",)], writes=[("F2",)])
                        if not first_seg:
                            P.v(lambda e, oc=oc, o=o, hl0=hl0: e.scalar_tensor_tensor(
                                F2[:, 0:o], S1[:, hl0 + 3 - o:hl0 + 3],
                                CW[:, oc * 4 + 3 - o:oc * 4 + 4 - o],
                                F2[:, 0:o], ALU.mult, ALU.add),
                                reads=[("HLa",), ("F2",), ("CW",)], writes=[("F2",)])
                else:
                    for o in range(1, 4):
                        P.v(lambda e, oc=oc, o=o: e.scalar_tensor_tensor(
                            F2[:, 0:512 - o], F1[:, o:512],
                            CW[:, oc * 4 + 3 - o:oc * 4 + 4 - o],
                            F2[:, 0:512 - o], ALU.mult, ALU.add),
                            reads=[("F1", 0), ("CW",), ("F2",)], writes=[("F2",)])
                        if not first_seg:
                            P.v(lambda e, oc=oc, o=o, hl0=hl0: e.scalar_tensor_tensor(
                                F2[:, 512 - o:512], S1[:, hl0:hl0 + o],
                                CW[:, oc * 4 + 3 - o:oc * 4 + 4 - o],
                                F2[:, 512 - o:512], ALU.mult, ALU.add),
                                reads=[("HLa",), ("F2",), ("CW",)], writes=[("F2",)])
                if si < NSEG - 1:
                    if d_i == 0:
                        P.v(lambda e, oc=oc: e.tensor_copy(
                            S1[:, 290 + oc * 3:290 + oc * 3 + 3],
                            F1[:, 509:512]),
                            reads=[("F1", 0)], writes=[("HLa",)])
                    else:
                        P.v(lambda e, oc=oc: e.tensor_copy(
                            S1[:, 290 + oc * 3:290 + oc * 3 + 3],
                            F1[:, 0:3]),
                            reads=[("F1", 0)], writes=[("HLa",)])
                dst = (B2[:, oc * 512:(oc + 1) * 512] if oc < 16 else
                       (B_f if oc == 16 else C_f))
                dks = ([("B2x",), ("B2",)] if oc < 16 else
                       ([("Bf",)] if oc == 16 else [("Cf",)]))
                P.s(lambda e, oc=oc, dst=dst: e.activation(
                    dst, F2[:, 0:512], AF.Silu, bias=CBc[:, oc:oc + 1]),
                    reads=[("F2",), ("CBc",)], writes=dks)

            ch_order = range(4) if d_i == 0 else range(3, -1, -1)
            for c_l in ch_order:
                cg = sg * 4 + c_l
                tc0 = c_l * 128
                KA = ("Ain",)
                P.t(lambda e, cg=cg: e.matmul(
                    psS[:, 0:32], C.CUF, a_all[:, cg * 32:(cg + 1) * 32],
                    start=True, stop=True),
                    reads=[("aall",), ("CUF",)], writes=[("psS",)])
                P.t(lambda e, cg=cg: e.matmul(
                    psS[:, 32:64], C.CF[:, 0:128],
                    a_all[:, cg * 32:(cg + 1) * 32],
                    start=True, stop=True),
                    reads=[("aall",), ("CF",)], writes=[("psS",)])
                P.v(lambda e: e.tensor_copy(Ain, psS[:, 0:32]),
                    reads=[("psS",)], writes=[KA])
                P.v(lambda e: e.tensor_copy(A127, psS[:, 32:64]),
                    reads=[("psS",)], writes=[("A127",)])
                P.v(lambda e, cg=cg: e.tensor_tensor(
                    Aex, Ain, a_all[:, cg * 32:(cg + 1) * 32], ALU.subtract),
                    reads=[KA, ("aall",)], writes=[("Aex",)])
                if d_i == 0:
                    P.s(lambda e: e.activation(ecol, Ain, AF.Exp),
                        reads=[KA], writes=[("ecol",)])
                    P.v(lambda e: e.tensor_tensor(F2[:, 0:32], A127, Ain,
                                                  ALU.subtract),
                        reads=[("A127",), KA], writes=[("F2",)])
                    P.s(lambda e: e.activation(F2[:, 0:32], F2[:, 0:32], AF.Exp),
                        reads=[("F2",)], writes=[("F2",)])
                else:
                    P.v(lambda e: e.tensor_tensor(F2[:, 32:64], A127, Aex,
                                                  ALU.subtract),
                        reads=[("A127",), ("Aex",)], writes=[("F2",)])
                    P.s(lambda e: e.activation(ecol, F2[:, 32:64], AF.Exp),
                        reads=[("F2",)], writes=[("ecol",)])
                    P.s(lambda e: e.activation(F2[:, 0:32], Aex, AF.Exp),
                        reads=[("Aex",)], writes=[("F2",)])
                P.v(lambda e, cg=cg: e.tensor_tensor(
                    wdt, F2[:, 0:32], dt_sp[:, cg * 32:(cg + 1) * 32], ALU.mult),
                    reads=[("F2",), ("dtsp",)], writes=[("wdt",)])
                for g in range(2):
                    tp_group(C, lambda k, g=g, tc0=tc0: B2[:, (g * 8 + k) * 512
                                                           + tc0:(g * 8 + k) * 512 + tc0 + 128],
                             XT, g * 1024, 8, [("B2x",)], [("XT",)])
                P.v(lambda e, cg=cg: e.tensor_tensor(
                    DTX.rearrange("p (h z) -> p h z", h=32),
                    XT.rearrange("p (h z) -> p h z", h=32),
                    dt_sp[:, cg * 32:(cg + 1) * 32].unsqueeze(2)
                    .broadcast_to([128, 32, 64]), ALU.mult),
                    reads=[("XT",), ("dtsp",)], writes=[("DTX",)])
                P.t(lambda e, tc0=tc0: e.matmul(
                    psS[:, 128:256], B_f[:, tc0:tc0 + 128],
                    C_f[:, tc0:tc0 + 128], start=True, stop=True),
                    reads=[("Bf",), ("Cf",)], writes=[("psS",)])
                P.v(lambda e: e.tensor_copy(B3[:, 1152:1280], psS[:, 128:256]),
                    reads=[("psS",)], writes=[("SC",)])
                src = Ain if d_i == 0 else Aex
                P.t(lambda e, src=src: e.transpose(
                    C.psA[0:32, 1536:1664], src, C.CI),
                    reads=[("Ain",), ("Aex",), ("CI",)], writes=[("psA", 3)])
                P.v(lambda e: e.tensor_copy(F1[0:32, 0:128],
                                            psA[0:32, 1536:1664]),
                    reads=[("psA", 3)], writes=[("F1", 0)])
                for hg in range(2):
                    P.dma(F2[0:1, 0:2048], F1[hg * 16:(hg + 1) * 16, 0:128],
                          reads=[("F1", 0), ("F2",)], writes=[("F2",)])
                    for nb in range(4):
                        P.t(lambda e, nb=nb: e.matmul(
                            psA[:, nb * 512:(nb + 1) * 512], C.onesrowf,
                            F2[0:1, nb * 512:(nb + 1) * 512],
                            start=True, stop=True),
                            reads=[("F2",), ("CF",)], writes=[("psA", nb)])
                    KPS4 = [("psA", n) for n in range(4)]
                    cv = src[:, hg * 16:(hg + 1) * 16].unsqueeze(2) \
                        .broadcast_to([128, 16, 128])
                    if d_i == 0:
                        P.v(lambda e, cv=cv: e.tensor_tensor(
                            F2[:, 0:2048].rearrange("p (h t) -> p h t", h=16),
                            psA[:, 0:2048].rearrange("p (h t) -> p h t", h=16),
                            cv, ALU.subtract),
                            reads=KPS4 + [("Ain",), ("Aex",)], writes=[("F2",)])
                    else:
                        P.v(lambda e, cv=cv: e.tensor_tensor(
                            F2[:, 0:2048].rearrange("p (h t) -> p h t", h=16),
                            cv,
                            psA[:, 0:2048].rearrange("p (h t) -> p h t", h=16),
                            ALU.subtract),
                            reads=KPS4 + [("Ain",), ("Aex",)], writes=[("F2",)])
                    msk = C.CT if d_i == 0 else C.CTU
                    P.v(lambda e, msk=msk: e.tensor_tensor(
                        F2[:, 0:2048].rearrange("p (h t) -> p h t", h=16),
                        F2[:, 0:2048].rearrange("p (h t) -> p h t", h=16),
                        msk[:].unsqueeze(1).broadcast_to([128, 16, 128]),
                        ALU.add),
                        reads=[("F2",), ("CT",), ("CTU",)], writes=[("F2",)])
                    P.s(lambda e: e.activation(GSL[:, 0:2048], F2[:, 0:2048],
                                               AF.Exp),
                        reads=[("F2",)], writes=[("GSL",)])
                    P.v(lambda e: e.tensor_tensor(
                        GSL[:, 0:2048].rearrange("p (h t) -> p h t", h=16),
                        GSL[:, 0:2048].rearrange("p (h t) -> p h t", h=16),
                        B3[:, 1152:1280].unsqueeze(1).broadcast_to([128, 16, 128]),
                        ALU.mult),
                        reads=[("GSL",), ("SC",)], writes=[("GSL",)])
                    for nb in range(2):
                        P.t(lambda e, hg=hg, nb=nb, tc0=tc0: e.matmul(
                            psB[:, nb * 512:(nb + 1) * 512],
                            C_f[:, tc0:tc0 + 128],
                            SPV[:, hg * 1024 + nb * 512: hg * 1024 + (nb + 1) * 512],
                            start=True, stop=True),
                            reads=[("Cf",), ("SPV",)], writes=[("psB", nb)])
                    P.v(lambda e, hg=hg: e.tensor_tensor(
                        F2[:, 0:1024].rearrange("p (h z) -> p h z", h=16),
                        psB[:, 0:1024].rearrange("p (h z) -> p h z", h=16),
                        ecol[:, hg * 16:(hg + 1) * 16].unsqueeze(2)
                        .broadcast_to([128, 16, 64]), ALU.mult),
                        reads=[("psB", 0), ("psB", 1), ("ecol",)],
                        writes=[("F2",)])
                    for hl in range(16):
                        h = hg * 16 + hl
                        P.t(lambda e, h=h, hl=hl: e.matmul(
                            psB[:, hl * 64:(hl + 1) * 64],
                            GSL[:, hl * 128:(hl + 1) * 128],
                            DTX[:, h * 64:(h + 1) * 64],
                            start=True, stop=True),
                            reads=[("GSL",), ("DTX",)], writes=[("psB", hl // 8)])
                    P.v(lambda e: e.tensor_tensor(
                        F2[:, 1024:2048], F2[:, 0:1024], psB[:, 0:1024],
                        ALU.add),
                        reads=[("psB", 0), ("psB", 1), ("F2",)], writes=[("F2",)])
                    P.v(lambda e, hg=hg: e.tensor_tensor(
                        F1[:, 1024:2048].rearrange("p (h z) -> p h z", h=16),
                        XT[:, hg * 1024:(hg + 1) * 1024]
                        .rearrange("p (h z) -> p h z", h=16),
                        DB[:, hg * 16:(hg + 1) * 16].unsqueeze(2)
                        .broadcast_to([128, 16, 64]), ALU.mult),
                        reads=[("XT",), ("DB",)], writes=[("F1", 1)])
                    P.v(lambda e: e.tensor_tensor(
                        F2[:, 1024:2048], F2[:, 1024:2048], F1[:, 1024:2048],
                        ALU.add),
                        reads=[("F2",), ("F1", 1)], writes=[("F2",)])
                    P.v(lambda e: e.tensor_copy(GSL[:, 0:1024], F2[:, 1024:2048]),
                        reads=[("F2",)], writes=[("GSL",)])
                    P.s(lambda e, hg=hg, tc0=tc0: e.activation(
                        F1[:, 1024:2048].rearrange("p (ch x) -> p ch x", ch=8),
                        B2[:].rearrange("p (ch x) -> p ch x", ch=32)
                        [:, 16 + hg * 8:16 + (hg + 1) * 8, tc0:tc0 + 128],
                        AF.Silu),
                        reads=[("B2z",)], writes=[("F1", 1)])
                    for k in range(8):
                        P.t(lambda e, k=k: e.transpose(
                            psT[:, k * 128:(k + 1) * 128],
                            GSL[:, k * 128:(k + 1) * 128], C.identb),
                            reads=[("GSL",), ("CB",)], writes=[("psT", k)])
                    for k in range(8):
                        P.v(lambda e, k=k: e.tensor_tensor(
                            GSL[:, 1024 + k * 128:1024 + (k + 1) * 128],
                            psT[:, k * 128:(k + 1) * 128],
                            F1[:, 1024 + k * 128:1024 + (k + 1) * 128], ALU.mult),
                            reads=[("psT", k), ("F1", 1)], writes=[("GSL", 1)])
                    for hf in range(2):
                        for k in range(8):
                            P.t(lambda e, hg=hg, hf=hf, k=k: e.matmul(
                                psA[:, hf * 512:(hf + 1) * 512],
                                GSL[:, 1024 + k * 128:1024 + (k + 1) * 128],
                                WA[:, (hg * 8 + k) * 1024 + hf * 512:
                                   (hg * 8 + k) * 1024 + hf * 512 + 512],
                                start=(k == 0), stop=(k == 7)),
                                reads=[("GSL", 1), KWO], writes=[("psA", hf)])
                    first = (d_i == 0 and hg == 0)
                    tgl = cg
                    if first:
                        P.v(lambda e: e.tensor_copy(F2[:, 0:1024], psA[:, 0:1024]),
                            reads=[("psA", 0), ("psA", 1)], writes=[("F2",)])
                    else:
                        P.dma(F2[:, 0:1024], xmv[:, tgl, :],
                              reads=[("xm",), ("F2",)], writes=[("F2",)])
                        P.v(lambda e: e.tensor_tensor(
                            F2[:, 0:1024], F2[:, 0:1024], psA[:, 0:1024],
                            ALU.add),
                            reads=[("psA", 0), ("psA", 1), ("F2",)],
                            writes=[("F2",)])
                    P.dma(xmv[:, tgl, :], F2[:, 0:1024], reads=[("F2",)],
                          writes=[("xm",)])
                P.v(lambda e, cg=cg: e.tensor_tensor(
                    DTX.rearrange("p (h z) -> p h z", h=32),
                    XT.rearrange("p (h z) -> p h z", h=32),
                    wdt.unsqueeze(2).broadcast_to([128, 32, 64]), ALU.mult),
                    reads=[("XT",), ("wdt",)], writes=[("DTX",)])
                P.t(lambda e, tc0=tc0: e.transpose(
                    psT[:, 0:128], B_f[:, tc0:tc0 + 128], C.identb),
                    reads=[("Bf",), ("CB",)], writes=[("psT", 0)])
                P.v(lambda e: e.tensor_copy(Btk, psT[:, 0:128]),
                    reads=[("psT", k) for k in range(8)], writes=[("Btk",)])
                P.s(lambda e: e.activation(F2[:, 0:32], A127, AF.Exp),
                    reads=[("A127",)], writes=[("F2",)])
                P.v(lambda e: e.tensor_tensor(
                    SPV[:].rearrange("p (h z) -> p h z", h=32),
                    SPV[:].rearrange("p (h z) -> p h z", h=32),
                    F2[:, 0:32].unsqueeze(2).broadcast_to([128, 32, 64]),
                    ALU.mult),
                    reads=[("SPV",), ("F2",)], writes=[("SPV",)])
                for nb in range(4):
                    P.t(lambda e, nb=nb: e.matmul(
                        psA[:, nb * 512:(nb + 1) * 512], Btk,
                        DTX[:, nb * 512:(nb + 1) * 512],
                        start=True, stop=True),
                        reads=[("Btk",), ("DTX",)], writes=[("psA", nb)])
                P.v(lambda e: e.tensor_tensor(SPV[:], SPV[:], psA[:, 0:2048],
                                              ALU.add),
                    reads=[("SPV",)] + [("psA", n) for n in range(4)],
                    writes=[("SPV",)])

    load_weight_bf16(C, WA, W["Wm"], [("WA", 0)], 1024, 1024)
    for t in range(NT):
        P.dma(F2[:, 0:1024], xmv[:, t, :], reads=[("xm",), ("F2",)],
              writes=[("F2",)])
        P.v(lambda e: e.tensor_copy(GSL[:, 0:1024], F2[:, 0:1024]),
            reads=[("F2",)], writes=[("GSL",)])
        for k in range(8):
            P.t(lambda e, k=k: e.transpose(psT[:, k * 128:(k + 1) * 128],
                                           GSL[:, k * 128:(k + 1) * 128],
                                           C.identb),
                reads=[("GSL",), ("CB",)], writes=[("psT", k)])
        P.v(lambda e, t=t: e.tensor_copy(
            B1[:].rearrange("p (d l) -> p d l", d=8)[:, :, t * 128:(t + 1) * 128],
            psT[:, 0:1024].rearrange("p (d x) -> p d x", d=8)),
            reads=[("psT", k) for k in range(8)], writes=[("B1",)])
    bob = F1[:, 1024:2048]
    bcast_bias(C, W["bm"], bob, [("F1", 1)], 1024)
    for t in range(NT):
        for hf in range(2):
            for dc in range(ND):
                P.t(lambda e, t=t, hf=hf, dc=dc: e.matmul(
                    psA[:, hf * 512:(hf + 1) * 512],
                    B1[:, dc * T + t * 128: dc * T + t * 128 + 128],
                    WA[:, dc * 1024 + hf * 512: dc * 1024 + hf * 512 + 512],
                    start=(dc == 0), stop=(dc == ND - 1)),
                    reads=[("B1",), ("WA", 0)], writes=[("psA", hf)])
        P.v(lambda e: e.tensor_tensor(F1[:, 0:1024], psA[:, 0:1024], bob,
                                      ALU.add),
            reads=[("psA", 0), ("psA", 1), ("F1", 1)], writes=[("F1", 0)])
        P.s(lambda e: e.activation(F1[:, 0:1024], F1[:, 0:1024], AF.Tanh),
            reads=[("F1", 0)], writes=[("F1", 0)])
        P.dma(F2[:, 0:1024], xmv[:, t, :], reads=[("xm",), ("F2",)],
              writes=[("F2",)])
        P.v(lambda e: e.scalar_tensor_tensor(F2[:, 0:1024], F1[:, 0:1024],
                                             0.1, F2[:, 0:1024],
                                             ALU.mult, ALU.add),
            reads=[("F1", 0), ("F2",)], writes=[("F2",)])
        P.v(lambda e, t=t: e.tensor_tensor(
            X[:, t * 1024:(t + 1) * 1024], X[:, t * 1024:(t + 1) * 1024],
            F2[:, 0:1024], ALU.add),
            reads=[("F2",), ("X",)], writes=[("X",)])




def emit_l2_ffn(C, W):
    P = C.P
    X, B1, B2, B3, WA = C.X, C.B1, C.B2, C.B3, C.WA
    S1, F1, F2 = C.S1, C.F1, C.F2
    psA, psB, psS = C.psA, C.psB, C.psS

    mu = S1[:, 0:16]
    rstd = S1[:, 16:32]
    brB = S1[:, 32:40]
    logit = S1[:, 512:640]
    Xs = S1[:, 640:768]
    gates = S1[:, 768:896]
    tau = S1[:, 896:912]
    flo = S1[:, 912:928]
    mx = S1[:, 928:944]
    b1c = S1[:, 944:976]
    b2B = S1[:, 1024:2048]
    KV3 = ("B2", 0)

    ln_stats(C, X, mu, rstd, [("X",)], ("mu",), ("rstd",))
    for t in range(NT):
        P.v(lambda e, t=t: e.tensor_scalar(
            B2[:, t * 1024:(t + 1) * 1024], X[:, t * 1024:(t + 1) * 1024],
            mu[:, t:t + 1], rstd[:, t:t + 1], ALU.subtract, ALU.mult),
            reads=[("X",), ("mu",), ("rstd",)], writes=[KV3])
    tp_tok2feat(C, B2, B1, [KV3], [("B1",)])

    load_weight_bf16(C, B3, W["Wr"], [("B3", "wr")], 1024, 8)
    bcast_bias(C, W["br"], brB, [("brB",)], 8)
    for t in range(NT):
        for dc in range(ND):
            P.t(lambda e, t=t, dc=dc: e.matmul(
                psS[:, t * 8:(t + 1) * 8],
                B1[:, dc * T + t * 128: dc * T + t * 128 + 128],
                B3[:, dc * 8:(dc + 1) * 8],
                start=(dc == 0), stop=(dc == ND - 1)),
                reads=[("B1",), ("B3", "wr")], writes=[("psS",)])
    P.v(lambda e: e.tensor_tensor(
        logit.rearrange("p (t e) -> p t e", e=8),
        psS[:, 0:128].rearrange("p (t e) -> p t e", e=8),
        brB.unsqueeze(1).broadcast_to([128, 16, 8]), ALU.add),
        reads=[("psS",), ("brB",)], writes=[("logit",)])

    L3v = lambda ap: ap.rearrange("p (t e) -> p t e", e=8)
    P.v(lambda e: e.tensor_scalar(Xs, logit, 0.5, None, ALU.mult),
        reads=[("logit",)], writes=[("Xs",)])
    P.v(lambda e: e.reduce_max(mx, L3v(Xs), axis=AX.X),
        reads=[("Xs",)], writes=[("mx",)])
    P.v(lambda e: e.tensor_scalar(tau, mx, -1.0, None, ALU.add),
        reads=[("mx",)], writes=[("tau",)])

    def fval(dst_f, tau_ap, kt, kdst):
        P.v(lambda e, tau_ap=tau_ap: e.tensor_tensor(
            L3v(F2[:, 0:128]), L3v(Xs),
            tau_ap.unsqueeze(2).broadcast_to([128, 16, 8]), ALU.subtract),
            reads=[("Xs",), kt], writes=[("F2",)])
        P.v(lambda e: e.tensor_scalar(F2[:, 0:128], F2[:, 0:128], 0.0, None,
                                      ALU.max),
            reads=[("F2",)], writes=[("F2",)])
        P.v(lambda e: e.tensor_tensor(F2[:, 128:256], F2[:, 0:128],
                                      F2[:, 0:128], ALU.mult),
            reads=[("F2",)], writes=[("F2",)])
        P.v(lambda e, dst_f=dst_f, kdst=kdst: e.reduce_sum(
            dst_f, L3v(F2[:, 128:256]), axis=AX.X),
            reads=[("F2",)], writes=[kdst])
        P.v(lambda e, dst_f=dst_f, kdst=kdst: e.tensor_scalar(
            dst_f, dst_f, -1.0, None, ALU.add),
            reads=[kdst], writes=[kdst])
    fval(flo, tau, ("tau",), ("flo",))
    dm = 1.0 - 8.0 ** -0.5
    fm = S1[:, 976:992]
    for it in range(25):
        dm *= 0.5
        if it == 24:
            P.v(lambda e, dm=dm: e.tensor_scalar(tau, tau, dm, None, ALU.add),
                reads=[("tau",)], writes=[("tau",)])
            break
        P.v(lambda e, dm=dm: e.tensor_scalar(S1[:, 992:1008], tau, dm, None,
                                             ALU.add),
            reads=[("tau",)], writes=[("taum",)])
        fval(fm, S1[:, 992:1008], ("taum",), ("fm",))
        P.v(lambda e: e.tensor_tensor(fm, fm, flo, ALU.mult),
            reads=[("fm",), ("flo",)], writes=[("fm2",)])
        P.v(lambda e: e.tensor_scalar(fm, fm, 0.0, None, ALU.is_ge),
            reads=[("fm2",)], writes=[("fm2",)])
        P.v(lambda e, dm=dm: e.scalar_tensor_tensor(tau, fm, dm, tau,
                                                    ALU.mult, ALU.add),
            reads=[("fm2",), ("tau",)], writes=[("tau",)])
    P.v(lambda e: e.tensor_tensor(
        L3v(F2[:, 0:128]), L3v(Xs),
        tau.unsqueeze(2).broadcast_to([128, 16, 8]), ALU.subtract),
        reads=[("Xs",), ("tau",)], writes=[("F2",)])
    P.v(lambda e: e.tensor_scalar(F2[:, 0:128], F2[:, 0:128], 0.0, None,
                                  ALU.max),
        reads=[("F2",)], writes=[("F2",)])
    P.v(lambda e: e.tensor_tensor(gates, F2[:, 0:128], F2[:, 0:128], ALU.mult),
        reads=[("F2",)], writes=[("gates",)])
    P.v(lambda e: e.reduce_sum(mx, L3v(gates), axis=AX.X),
        reads=[("gates",)], writes=[("mx",)])
    P.v(lambda e: e.reciprocal(mx, mx), reads=[("mx",)], writes=[("mx",)])
    P.v(lambda e: e.tensor_tensor(
        L3v(gates), L3v(gates), mx.unsqueeze(2).broadcast_to([128, 16, 8]),
        ALU.mult),
        reads=[("gates",), ("mx",)], writes=[("gates",)])
    P.v(lambda e: e.tensor_scalar(F2[:, 0:128], gates, 1e-9, None, ALU.is_gt),
        reads=[("gates",)], writes=[("F2",)])
    P.v(lambda e: e.tensor_tensor(gates, gates, F2[:, 0:128], ALU.mult),
        reads=[("gates",), ("F2",)], writes=[("gates",)])

    for e_i in range(8):
        b1d = W["b1"][e_i]
        P.dma(F2[0:1, 0:2048], b1d[0:2048].unsqueeze(0), reads=[("F2",)],
              writes=[("F2",)])
        for cc in range(16):
            P.t(lambda e, cc=cc: e.matmul(
                psB[:, cc:cc + 1], F2[0:1, cc * 128:(cc + 1) * 128],
                C.onecellf, start=True, stop=True),
                reads=[("F2",), ("CF",)], writes=[("psB", 0)])
        P.dma(F2[0:1, 0:2048], b1d[2048:4096].unsqueeze(0), reads=[("F2",)],
              writes=[("F2",)])
        for cc in range(16):
            P.t(lambda e, cc=cc: e.matmul(
                psB[:, 16 + cc:17 + cc], F2[0:1, cc * 128:(cc + 1) * 128],
                C.onecellf, start=True, stop=True),
                reads=[("F2",), ("CF",)], writes=[("psB", 0)])
        P.v(lambda e: e.tensor_copy(b1c, psB[:, 0:32]),
            reads=[("psB", 0)], writes=[("b1c",)])
        bcast_bias(C, W["b2"][e_i], b2B, [("b2B",)], 1024)
        for half in range(4):
            for fc in range(8):
                fg = half * 8 + fc
                slot = (fc % 2) * 1024
                KSL = ("WA", 0)
                load_weight_bf16(C, WA, W["W1"][e_i][:, fg * 128:(fg + 1) * 128],
                                 [KSL], 1024, 128, dst_col0=slot)
                for lp in range(4):
                    for dc in range(ND):
                        P.t(lambda e, fg=fg, slot=slot, lp=lp, dc=dc: e.matmul(
                            psA[:, lp * 512:(lp + 1) * 512],
                            WA[:, slot + dc * 128: slot + dc * 128 + 128],
                            B1[:, dc * T + lp * 512: dc * T + lp * 512 + 512],
                            start=(dc == 0), stop=(dc == ND - 1)),
                            reads=[("B1",), KSL], writes=[("psA", lp)])
                P.s(lambda e, fc=fc, fg=fg: e.activation(
                    B2[:, fc * T:(fc + 1) * T], psA[:, 0:2048], AF.Relu,
                    bias=b1c[:, fg:fg + 1]),
                    reads=[("psA", n) for n in range(4)] + [("b1c",)],
                    writes=[("B2", fc)])
            load_weight_bf16(C, WA, W["W2"][e_i][half * 1024:(half + 1) * 1024, :],
                             [("WA", 0)], 1024, 1024)
            for t in range(NT):
                for fc in range(8):
                    for nb in range(2):
                        P.t(lambda e, t=t, fc=fc, nb=nb: e.matmul(
                            psB[:, nb * 512:(nb + 1) * 512],
                            B2[:, fc * T + t * 128: fc * T + t * 128 + 128],
                            WA[:, fc * 1024 + nb * 512: fc * 1024 + nb * 512 + 512],
                            start=(fc == 0), stop=(fc == 7)),
                            reads=[("B2", fc), ("WA", 0)], writes=[("psB", nb)])
                gc = gates[:, t * 8 + e_i: t * 8 + e_i + 1]
                P.v(lambda e, t=t, gc=gc: e.scalar_tensor_tensor(
                    X[:, t * 1024:(t + 1) * 1024], psB[:, 0:1024], gc,
                    X[:, t * 1024:(t + 1) * 1024], ALU.mult, ALU.add),
                    reads=[("psB", 0), ("psB", 1), ("gates",), ("X",)],
                    writes=[("X",)])
                if half == 3:
                    P.v(lambda e, t=t, gc=gc: e.scalar_tensor_tensor(
                        X[:, t * 1024:(t + 1) * 1024], b2B, gc,
                        X[:, t * 1024:(t + 1) * 1024], ALU.mult, ALU.add),
                        reads=[("b2B",), ("gates",), ("X",)], writes=[("X",)])


L2_FFN_W = [("Wr", "L2_ffn_Wr", (1024, 8)),
            ("br", "L2_ffn_br", (8,)),
            ("W1", "L2_ffn_W1", (8, 1024, 4096)),
            ("b1", "L2_ffn_b1", (8, 4096)),
            ("W2", "L2_ffn_W2", (8, 4096, 1024)),
            ("b2", "L2_ffn_b2", (8, 1024))]




def emit_l3_attn(C, W):
    P = C.P
    X, B1, B2, B3, WA = C.X, C.B1, C.B2, C.B3, C.WA
    S1, F1, F2 = C.S1, C.F1, C.F2
    psA, psB, psS, psT = C.psA, C.psB, C.psS, C.psT

    mu = S1[:, 0:16]
    rstd = S1[:, 16:32]
    nmx = S1[:, 32:33]
    nms = S1[:, 33:34]
    ssum = S1[:, 34:35]
    SCL = 1.0 / (2.0 * 0.28)
    VT0 = 0
    S80 = 5120
    MSK = 9744

    ln_stats(C, X, mu, rstd, [("X",)], ("mu",), ("rstd",))
    for t in range(NT):
        P.v(lambda e, t=t: e.tensor_scalar(
            B2[:, t * 1024:(t + 1) * 1024], X[:, t * 1024:(t + 1) * 1024],
            mu[:, t:t + 1], rstd[:, t:t + 1], ALU.subtract, ALU.mult),
            reads=[("X",), ("mu",), ("rstd",)], writes=[("B2", "q")])
    tp_tok2feat(C, B2, B1, [("B2", "q")], [("B1",)])
    P.dma(B3[:, MSK:MSK + 136], W["c_bandw"], writes=[("msk",)])
    P.dma(B3[:, MSK + 136:MSK + 272], W["c_band0"], writes=[("msk",)])
    P.v(lambda e: e.memset(B2[:, 4096:8256], 0.0), writes=[("B2", "k")])
    P.v(lambda e: e.memset(B3[:, 0:1024], 0.0), writes=[("vtk",)])
    C.ccell(1.0 / (2.0 * 0.28))

    for qr in range(4):
        q0 = qr * 512
        for wname, kw in (("Wq", ("B2", "q")), ("Wk", ("B2", "k")),
                          ("Wv", ("B2", "v"))):
            load_weight_bf16(C, WA, W[wname], [("WA", 0)], 1024, 1024)
            for ec in range(ND):
                for dc in range(ND):
                    P.t(lambda e, ec=ec, dc=dc, q0=q0: e.matmul(
                        psA[:, (ec % 4) * 512:(ec % 4) * 512 + 512],
                        WA[:, dc * 1024 + ec * 128: dc * 1024 + ec * 128 + 128],
                        B1[:, dc * T + q0: dc * T + q0 + 512],
                        start=(dc == 0), stop=(dc == ND - 1)),
                        reads=[("B1",), ("WA", 0)], writes=[("psA", ec % 4)])
                if wname == "Wq":
                    P.v(lambda e, ec=ec: e.tensor_copy(
                        B2[:, ec * 512:(ec + 1) * 512],
                        psA[:, (ec % 4) * 512:(ec % 4) * 512 + 512]),
                        reads=[("psA", ec % 4)], writes=[("B2", "q")])
                elif wname == "Wk":
                    P.v(lambda e, ec=ec: e.tensor_copy(
                        B2[:, 4096 + ec * 520 + 8:4096 + (ec + 1) * 520],
                        psA[:, (ec % 4) * 512:(ec % 4) * 512 + 512]),
                        reads=[("psA", ec % 4)], writes=[("B2", "k")])
                else:
                    P.v(lambda e, ec=ec: e.tensor_copy(
                        B2[:, 8256:8768],
                        psA[:, (ec % 4) * 512:(ec % 4) * 512 + 512]),
                        reads=[("psA", ec % 4)], writes=[("B2", "v")])
                    for tt in range(4):
                        P.t(lambda e, ec=ec, tt=tt: e.transpose(
                            psT[:, tt * 128:(tt + 1) * 128],
                            B2[:, 8256 + tt * 128:8256 + (tt + 1) * 128],
                            C.identb),
                            reads=[("B2", "v"), ("CB",)], writes=[("psT", tt)])
                    for tt in range(4):
                        P.v(lambda e, ec=ec, tt=tt: e.tensor_copy(
                            B3[:, VT0 + (tt + 1) * 1024 + ec * 128:
                               VT0 + (tt + 1) * 1024 + ec * 128 + 128],
                            psT[:, tt * 128:(tt + 1) * 128]),
                            reads=[("psT", k) for k in range(8)],
                            writes=[("vtk",)])
        vt = B3[:, VT0:VT0 + 5120].rearrange("p (t z) -> p t z", z=1024)
        s8 = B3[:, S80:S80 + 4096].rearrange("p (t z) -> p t z", z=1024)
        P.dma(s8[8:128, :, :], vt[0:120, 1:5, :], reads=[("vtk",)],
              writes=[("s8",)])
        P.dma(s8[0:8, :, :], vt[120:128, 0:4, :], reads=[("vtk",)],
              writes=[("s8",)])
        for cl in range(4):
            for h in range(16):
                r0 = (h % 2) * 64
                jc = h // 2
                P.t(lambda e, cl=cl, h=h, r0=r0, jc=jc: e.matmul(
                    psA[:, 0:136],
                    B2[r0:r0 + 64, jc * 512 + cl * 128: jc * 512 + cl * 128 + 128],
                    B2[r0:r0 + 64, 4096 + jc * 520 + cl * 128:
                       4096 + jc * 520 + cl * 128 + 136],
                    start=True, stop=True),
                    reads=[("B2", "q"), ("B2", "k")], writes=[("psA", 0)])
                m0 = MSK + (136 if (qr == 0 and cl == 0) else 0)
                P.v(lambda e, m0=m0: e.tensor_tensor(
                    psA[:, 0:136], psA[:, 0:136], B3[:, m0:m0 + 136], ALU.add),
                    reads=[("psA", 0), ("msk",)], writes=[("psA", 0)])
                P.v(lambda e: e.reduce_max(nmx, psA[:, 0:136], axis=AX.X,
                                           negate=True),
                    reads=[("psA", 0)], writes=[("nmx",)])
                P.v(lambda e: e.tensor_scalar(nms, nmx, SCL, None, ALU.mult),
                    reads=[("nmx",)], writes=[("nms",)])
                P.s(lambda e: e.activation(F2[:, 0:136], psA[:, 0:136],
                                           AF.Exp, bias=nms, scale=SCL,
                                           accum_out=ssum),
                    reads=[("psA", 0), ("nms",), ("CC",)],
                    writes=[("F2",), ("ssum",)])
                P.v(lambda e: e.reciprocal(ssum, ssum),
                    reads=[("ssum",)], writes=[("ssum",)])
                P.v(lambda e: e.tensor_scalar(B3[:, 9216:9352], F2[:, 0:136],
                                              ssum, None, ALU.mult),
                    reads=[("F2",), ("ssum",)], writes=[("wts",)])
                P.t(lambda e: e.transpose(psT[:, 0:128],
                                          B3[:, 9224:9352], C.identb),
                    reads=[("wts",), ("CB",)], writes=[("psT", 0)])
                P.t(lambda e: e.transpose(psT[0:8, 128:256],
                                          B3[:, 9216:9224], C.identb),
                    reads=[("wts",), ("CB",)], writes=[("psT", 1)])
                P.v(lambda e: e.tensor_copy(B3[:, 9352:9480], psT[:, 0:128]),
                    reads=[("psT", k) for k in range(8)], writes=[("wta",)])
                P.v(lambda e: e.tensor_copy(B3[0:8, 9480:9608],
                                            psT[0:8, 128:256]),
                    reads=[("psT", k) for k in range(8)], writes=[("wtb",)])
                P.t(lambda e, cl=cl, h=h, r0=r0: e.matmul(
                    psB[r0:r0 + 64, 0:128],
                    B3[:, VT0 + (cl + 1) * 1024 + h * 64:
                       VT0 + (cl + 1) * 1024 + h * 64 + 64],
                    B3[:, 9352:9480], start=True, stop=False),
                    reads=[("vtk",), ("wta",)], writes=[("psB", 0)])
                P.t(lambda e, cl=cl, h=h, r0=r0: e.matmul(
                    psB[r0:r0 + 64, 0:128],
                    B3[0:8, S80 + cl * 1024 + h * 64:
                       S80 + cl * 1024 + h * 64 + 64],
                    B3[0:8, 9480:9608], start=False, stop=True),
                    reads=[("s8",), ("wtb",)], writes=[("psB", 0)])
                if h % 2 == 1:
                    P.s(lambda e, cl=cl, jc=jc, q0=q0: e.copy(
                        B2[:, 8768 + jc * 512 + cl * 128:
                           8768 + jc * 512 + cl * 128 + 128],
                        psB[:, 0:128]),
                        reads=[("psB", 0)], writes=[("B2", "y")])
        load_weight_bf16(C, WA, W["Wo"], [("WA", 0)], 1024, 1024)
        for tt in range(4):
            t = qr * 4 + tt
            for hf in range(2):
                for yc in range(8):
                    P.t(lambda e, tt=tt, hf=hf, yc=yc: e.matmul(
                        psA[:, hf * 512:hf * 512 + 512],
                        B2[:, 8768 + yc * 512 + tt * 128:
                           8768 + yc * 512 + tt * 128 + 128],
                        WA[:, yc * 1024 + hf * 512: yc * 1024 + hf * 512 + 512],
                        start=(yc == 0), stop=(yc == 7)),
                        reads=[("B2", "y"), ("WA", 0)], writes=[("psA", hf)])
            P.v(lambda e, t=t: e.tensor_tensor(
                X[:, t * 1024:(t + 1) * 1024], X[:, t * 1024:(t + 1) * 1024],
                psA[:, 0:1024], ALU.add),
                reads=[("psA", 0), ("psA", 1), ("X",)], writes=[("X",)])
        if qr < 3:
            for jc in range(8):
                P.v(lambda e, jc=jc: e.tensor_copy(
                    B2[:, 4096 + jc * 520:4096 + jc * 520 + 8],
                    B2[:, 4096 + jc * 520 + 512:4096 + (jc + 1) * 520]),
                    reads=[("B2", "k")], writes=[("B2", "k")])
            P.v(lambda e: e.tensor_copy(B3[:, 0:1024], B3[:, 4096:5120]),
                reads=[("vtk",)], writes=[("vtk",)])


L3_ATTN_W = [("Wq", "L3_attn_Wq", (1024, 1024)),
             ("Wk", "L3_attn_Wk", (1024, 1024)),
             ("Wv", "L3_attn_Wv", (1024, 1024)),
             ("Wo", "L3_attn_Wo", (1024, 1024)),
             ("c_bandw", "c_bandw", (128, 136)),
             ("c_band0", "c_band0", (128, 136))]




def emit_l3_ffn(C, W):
    P = C.P
    nc = C.nc
    X, B1, B2, B3, WA = C.X, C.B1, C.B2, C.B3, C.WA
    S1, F1, F2 = C.S1, C.F1, C.F2
    psA, psB, psS, psT = C.psA, C.psB, C.psS, C.psT

    mu = S1[:, 0:16]
    rstd = S1[:, 16:32]
    qb = S1[:, 32:40]
    kb = S1[:, 40:48]
    vb = S1[:, 48:56]
    SCL = float(1024.0 ** -0.5)
    wB = F1[:, 0:1024]
    bB = F2[:, 1024:2048]
    bB = S1[:, 1024:2048]
    qkvd = nc.dram_tensor("l3f_qkv", [3, 8, 128, T], BF16, kind="Internal").ap()

    ln_stats(C, X, mu, rstd, [("X",)], ("mu",), ("rstd",))
    bcast_bias(C, W["norm_w"], wB, [("F1", 0)], 1024)
    bcast_bias(C, W["norm_b"], bB, [("bB",)], 1024)
    for t in range(NT):
        P.v(lambda e, t=t: e.tensor_scalar(
            F2[:, 0:1024], X[:, t * 1024:(t + 1) * 1024],
            mu[:, t:t + 1], rstd[:, t:t + 1], ALU.subtract, ALU.mult),
            reads=[("X",), ("mu",), ("rstd",)], writes=[("F2",)])
        P.v(lambda e: e.tensor_tensor(F2[:, 0:1024], F2[:, 0:1024], wB,
                                      ALU.mult),
            reads=[("F2",), ("F1", 0)], writes=[("F2",)])
        P.v(lambda e, t=t: e.tensor_tensor(
            B2[:, t * 1024:(t + 1) * 1024], F2[:, 0:1024], bB, ALU.add),
            reads=[("F2",), ("bB",)], writes=[("B2", "s")])
    tp_tok2feat(C, B2, B1, [("B2", "s")], [("B1",)])

    def bias_cols(bname, dst, key):
        P.dma(F2[0:1, 0:1024], W[bname].unsqueeze(0), reads=[("F2",)],
              writes=[("F2",)])
        for cc in range(8):
            P.t(lambda e, cc=cc: e.matmul(psB[:, cc:cc + 1],
                                          F2[0:1, cc * 128:(cc + 1) * 128],
                                          C.onecellf, start=True, stop=True),
                reads=[("F2",), ("CF",)], writes=[("psB", 0)])
        P.v(lambda e, dst=dst: e.tensor_copy(dst, psB[:, 0:8]),
            reads=[("psB", 0)], writes=[key])
    C.ccell(SCL)
    bias_cols("q_b", qb, ("qb",))
    bias_cols("k_b", kb, ("kb",))
    bias_cols("v_b", vb, ("vb",))

    for wi, (wname, bcol, bkey) in enumerate(
            (("q_w", qb, ("qb",)), ("k_w", kb, ("kb",)), ("v_w", vb, ("vb",)))):
        load_weight_bf16(C, WA, W[wname], [("WA", 0)], 1024, 1024)
        for r in range(8):
            for g in range(0, 8, 8):
                for k in range(8):
                    P.t(lambda e, r=r, k=k: e.transpose(
                        psT[:, k * 128:(k + 1) * 128],
                        WA[:, r * 1024 + k * 128: r * 1024 + k * 128 + 128],
                        C.identb),
                        reads=[("WA", 0), ("CB",)], writes=[("psT", k)])
                P.v(lambda e, r=r: e.tensor_copy(
                    WA[:].rearrange("p (c l) -> p c l", c=8)
                    [:, :, 8192 // 8 + r * 128:8192 // 8 + r * 128 + 128]
                    if False else
                    WA[:, 8192:8192 + 8 * 1024].rearrange(
                        "p (c z) -> p c z", c=8)[:, :, r * 128:r * 128 + 128],
                    psT[:, 0:1024].rearrange("p (c z) -> p c z", c=8)),
                    reads=[("psT", k) for k in range(8)], writes=[("WA", 1)])
        for e_s in range(4):
            m = 1024 >> e_s
            nmc = m // 128
            for ec in range(ND):
                for mc in range(nmc):
                    P.t(lambda e, e_s=e_s, ec=ec, mc=mc, nmc=nmc: e.matmul(
                        psA[:, (ec % 4) * 512:(ec % 4) * 512 + 512],
                        WA[:, 8192 + mc * 1024 + ec * 128:
                           8192 + mc * 1024 + ec * 128 + 128],
                        B1[:, mc * T + e_s * 512: mc * T + e_s * 512 + 512],
                        start=(mc == 0), stop=(mc == nmc - 1)),
                        reads=[("B1",), ("WA", 1)], writes=[("psA", ec % 4)])
                P.v(lambda e, ec=ec, bcol=bcol: e.tensor_scalar(
                    B3[:, ec * 512:(ec + 1) * 512],
                    psA[:, (ec % 4) * 512:(ec % 4) * 512 + 512],
                    bcol[:, ec:ec + 1], None, ALU.add),
                    reads=[("psA", ec % 4), bkey], writes=[("B3", "E")])
            for dp in range(4):
                P.dma(qkvd[wi].rearrange("c p l -> p c l")
                      [:, dp * 2:dp * 2 + 2, e_s * 512:e_s * 512 + 512],
                      B3[:, dp * 1024:(dp + 1) * 1024].rearrange(
                          "p (c z) -> p c z", c=2),
                      reads=[("B3", "E")], writes=[("qkv",)])

    for h in range(8):
        P.dma(B2[:, 0:2048], qkvd[0, h].rearrange("p l -> p l"),
              reads=[("qkv",), ("B2", "s")], writes=[("B2", "s")])
        P.dma(B2[:, 2048:4096], qkvd[1, h], reads=[("qkv",), ("B2", "s")],
              writes=[("B2", "s")])
        P.dma(B2[:, 4096:6144], qkvd[2, h], reads=[("qkv",), ("B2", "s")],
              writes=[("B2", "s")])
        if h == 0:
            C.dbg("qTh", B2[:, 0:2048], [("B2", "s")])
            C.dbg("kTh", B2[:, 2048:4096], [("B2", "s")])
        for g in range(2):
            tp_group(C, lambda k, g=g: B2[:, 4096 + (g * 8 + k) * 128:
                                          4096 + (g * 8 + k + 1) * 128],
                     B2, 6144 + g * 1024, 8, [("B2", "s")], [("B2", "vt")])
        for lh in range(2):
            l0 = lh * 1024
            for kc in range(NT):
                for nb in range(2):
                    P.t(lambda e, kc=kc, nb=nb, l0=l0: e.matmul(
                        psA[:, nb * 512:(nb + 1) * 512],
                        B2[:, 2048 + kc * 128:2048 + (kc + 1) * 128],
                        B2[:, l0 + nb * 512:l0 + nb * 512 + 512],
                        start=True, stop=True),
                        reads=[("B2", "s")], writes=[("psA", nb)])
                P.s(lambda e: e.activation(B3[:, 0:1024], psA[:, 0:1024],
                                           AF.Exp, scale=SCL),
                    reads=[("psA", 0), ("psA", 1), ("CC",)], writes=[("B3", "E")])
                for nb in range(2):
                    P.t(lambda e, kc=kc, nb=nb: e.matmul(
                        psB[:, nb * 512:(nb + 1) * 512],
                        B2[:, 6144 + kc * 128:6144 + (kc + 1) * 128],
                        B3[:, nb * 512:nb * 512 + 512],
                        start=(kc == 0), stop=(kc == NT - 1)),
                        reads=[("B2", "vt"), ("B3", "E")], writes=[("psB", nb)])
                for nb in range(2):
                    P.t(lambda e, kc=kc, nb=nb: e.matmul(
                        psA[0:1, 1024 + nb * 512:1024 + (nb + 1) * 512],
                        C.onescolb, B3[:, nb * 512:nb * 512 + 512],
                        start=(kc == 0), stop=(kc == NT - 1)),
                        reads=[("CB",), ("B3", "E")], writes=[("psA", 2 + nb)])
            P.v(lambda e: e.tensor_copy(F2[0:1, 0:1024], psA[0:1, 1024:2048]),
                reads=[("psA", 2), ("psA", 3)], writes=[("F2",)])
            P.v(lambda e: e.reciprocal(F2[0:1, 0:1024], F2[0:1, 0:1024]),
                reads=[("F2",)], writes=[("F2",)])
            for nb in range(2):
                P.t(lambda e, nb=nb: e.matmul(
                    psA[:, 1024 + nb * 512:1024 + (nb + 1) * 512], C.onesrowf,
                    F2[0:1, nb * 512:(nb + 1) * 512], start=True, stop=True),
                    reads=[("F2",), ("CF",)], writes=[("psA", 2 + nb)])
            P.s(lambda e, h=h, l0=l0: e.copy(
                WA[:, h * 2048 + l0: h * 2048 + l0 + 1024], psB[:, 0:1024]),
                reads=[("psB", 0), ("psB", 1)], writes=[("WA", 0)])
            P.v(lambda e, h=h, l0=l0: e.tensor_tensor(
                WA[:, h * 2048 + l0: h * 2048 + l0 + 1024],
                WA[:, h * 2048 + l0: h * 2048 + l0 + 1024],
                psA[:, 1024:2048], ALU.mult),
                reads=[("psA", 2), ("psA", 3), ("WA", 0)], writes=[("WA", 0)])

    C.dbg("yT0", WA[:, 0:2048], [("WA", 0)])
    load_weight_bf16(C, B2, W["o_w"], [("B2", "s")], 1024, 1024)
    for r in range(8):
        for k in range(8):
            P.t(lambda e, r=r, k=k: e.transpose(
                psT[:, k * 128:(k + 1) * 128],
                B2[:, r * 1024 + k * 128: r * 1024 + k * 128 + 128],
                C.identb),
                reads=[("B2", "s"), ("CB",)], writes=[("psT", k)])
        P.v(lambda e, r=r: e.tensor_copy(
            B2[:, 8192:16384].rearrange("p (c z) -> p c z", c=8)
            [:, :, r * 128:r * 128 + 128],
            psT[:, 0:1024].rearrange("p (c z) -> p c z", c=8)),
            reads=[("psT", k) for k in range(8)], writes=[("B2", "owt")])
    obB = F1[:, 1024:2048]
    bcast_bias(C, W["o_b"], obB, [("F1", 1)], 1024)
    for e_s in range(4):
        m = 1024 >> e_s
        nmc = m // 128
        for tt in range(4):
            t = e_s * 4 + tt
            nnb = max(1, m // 512)
            for nb in range(nnb):
                nw = min(512, m)
                for mc in range(nmc):
                    P.t(lambda e, mc=mc, nb=nb, nw=nw, t=t, nmc=nmc: e.matmul(
                        psA[:, nb * 512:nb * 512 + nw],
                        WA[:].rearrange("p (hh z) -> p hh z", hh=8)
                        [:, mc // 2, (mc % 2) * 1024 + 0:0] if False else
                        WA[:, mc * 2048 // 2:0] if False else
                        _yt_slice(C, mc, t),
                        B2[:, 8192 + mc * 1024 + nb * 512:
                           8192 + mc * 1024 + nb * 512 + nw],
                        start=(mc == 0), stop=(mc == nmc - 1)),
                        reads=[("WA", 0), ("B2", "owt")], writes=[("psA", nb)])
            P.v(lambda e, t=t: e.tensor_scalar(
                F2[:, 0:1024], X[:, t * 1024:(t + 1) * 1024],
                mu[:, t:t + 1], rstd[:, t:t + 1], ALU.subtract, ALU.mult),
                reads=[("X",), ("mu",), ("rstd",)], writes=[("F2",)])
            P.v(lambda e, m=m: e.tensor_tensor(
                F2[:, 0:m], F2[:, 0:m], obB[:, 0:m], ALU.add),
                reads=[("F2",), ("F1", 1)], writes=[("F2",)])
            P.v(lambda e, m=m: e.tensor_tensor(
                F2[:, 0:m], F2[:, 0:m], psA[:, 0:m], ALU.add),
                reads=[("F2",), ("psA", 0), ("psA", 1)], writes=[("F2",)])
            P.v(lambda e, t=t: e.tensor_tensor(
                X[:, t * 1024:(t + 1) * 1024], X[:, t * 1024:(t + 1) * 1024],
                F2[:, 0:1024], ALU.add),
                reads=[("F2",), ("X",)], writes=[("X",)])


def _yt_slice(C, mc, t):
    l0 = (t // 8) * 0
    lh = 1 if t >= 8 else 0
    col = mc * 2048 + lh * 1024 + (t - lh * 8) * 128
    return C.WA[:, col:col + 128]


L3_FFN_W = [("norm_w", "L3_ffn_norm_w", (1024,)),
            ("norm_b", "L3_ffn_norm_b", (1024,)),
            ("q_w", "L3_ffn_q_w", (1024, 1024)),
            ("q_b", "L3_ffn_q_b", (1024,)),
            ("k_w", "L3_ffn_k_w", (1024, 1024)),
            ("k_b", "L3_ffn_k_b", (1024,)),
            ("v_w", "L3_ffn_v_w", (1024, 1024)),
            ("v_b", "L3_ffn_v_b", (1024,)),
            ("o_w", "L3_ffn_o_w", (1024, 1024)),
            ("o_b", "L3_ffn_o_b", (1024,))]




def build_core_program(block_list, serial=False, debug=False):
    nc = bass.Bass("TRN2", target_bir_lowering=False)
    P = Prog(nc, serial=serial)
    stack = ExitStack()
    dram = {}

    x_in = nc.dram_tensor("x_in", [T, D], F32, kind="ExternalInput").ap()
    x_out = nc.dram_tensor("x_out", [T, D], F32, kind="ExternalOutput").ap()

    with stack:
        C = Ctx(nc, P, stack, debug=debug)
        consts = const_arrays()
        for cn, arr in consts.items():
            dram[cn] = nc.dram_tensor(cn, list(arr.shape),
                                      BF16 if arr.dtype == ml_dtypes.bfloat16 else F32,
                                      kind="ExternalInput").ap()
        C.load_consts(dram)
        xi3 = x_in.rearrange("(t p) f -> p t f", p=128)
        for t in range(NT):
            P.dma(C.X[:, t * 1024:(t + 1) * 1024], xi3[:, t, :], writes=[("X",)])
        for emit_fn, wspec in block_list:
            P.barrier()
            Wd = {}
            for short, full, shape in wspec:
                if full not in dram:
                    dram[full] = nc.dram_tensor(full, list(shape), F32,
                                                kind="ExternalInput").ap()
                Wd[short] = dram[full]
            emit_fn(C, Wd)
        P.barrier()
        import os
        trunc = int(os.environ.get("MK_TRUNC", "0"))
        if trunc:
            P.ops = P.ops[:trunc]
            P.key_w = {k: v for k, v in P.key_w.items() if v < trunc}
            P.key_r = {k: [i for i in v if i < trunc]
                       for k, v in P.key_r.items()}
        xo3 = x_out.rearrange("(t p) f -> p t f", p=128)
        for t in range(NT):
            P.dma(xo3[:, t, :], C.X[:, t * 1024:(t + 1) * 1024], reads=[("X",)])

        sems = {}
        sem_stack = ExitStack()
        with sem_stack:
            for e in ENGINES:
                sems[e] = sem_stack.enter_context(nc.semaphore("sem_" + e))
            with nc.Block() as block:
                P.emit(block, sems)
    return nc


L0_ATTN_W = [("Wq", "L0_attn_Wq", (1024, 1024)),
             ("Wk", "L0_attn_Wk", (1024, 1024)),
             ("Wv", "L0_attn_Wv", (1024, 1024)),
             ("Wo", "L0_attn_Wo", (1024, 1024)),
             ("bo", "L0_attn_bo", (1024,)),
             ("gamma", "L0_attn_gamma", (1024,)),
             ("beta", "L0_attn_beta", (1024,))]


BLOCK_LIST = []


def run_blocks(inputs, block_list, serial=False, n_cores=8, x_init=None,
               trace=False):
    nc = build_core_program(block_list, serial=serial)
    if x_init is None:
        x_init = inputs["L0_attn_x"]
    x = np.ascontiguousarray(np.asarray(x_init, dtype=np.float32))
    consts = const_arrays()
    in_maps = []
    for b in range(n_cores):
        m = {"x_in": x[b]}
        m.update(consts)
        for _, wspec in block_list:
            for short, full, shape in wspec:
                if full.startswith("c_"):
                    continue
                m[full] = np.ascontiguousarray(np.asarray(inputs[full], np.float32))
        in_maps.append(m)
    res = run_bass_kernel_spmd(nc, in_maps, core_ids=list(range(n_cores)),
                               trace=trace)
    if trace:
        print("exec_time_ns:", res.exec_time_ns)
    return np.stack([r["x_out"] for r in res.results])


L1_ATTN_W = [("wq", "L1_attn_wq", (1024, 1024)),
             ("bq", "L1_attn_bq", (1024,)),
             ("wk", "L1_attn_wk", (1024, 1024)),
             ("bk", "L1_attn_bk", (1024,)),
             ("wv", "L1_attn_wv", (1024, 1024)),
             ("bv", "L1_attn_bv", (1024,)),
             ("wo", "L1_attn_wo", (1024, 1024)),
             ("bo", "L1_attn_bo", (1024,))]

L0_FFN_W = [("gate_w", "L0_ffn_gate_w", (1024, 16)),
            ("gate_b", "L0_ffn_gate_b", (16,)),
            ("shared_wg", "L0_ffn_shared_wg", (1024, 512)),
            ("shared_wu", "L0_ffn_shared_wu", (1024, 512)),
            ("shared_wd", "L0_ffn_shared_wd", (512, 1024)),
            ("experts_wg", "L0_ffn_experts_wg", (16, 1024, 512)),
            ("experts_wu", "L0_ffn_experts_wu", (16, 1024, 512)),
            ("experts_wd", "L0_ffn_experts_wd", (16, 512, 1024))]

L1_FFN_W = [("w_ln1", "L1_ffn_w_ln1", (1024,)),
            ("Wq", "L1_ffn_Wq", (1024, 1024)),
            ("Wkva", "L1_ffn_Wkva", (1024, 160)),
            ("Wkvb", "L1_ffn_Wkvb", (128, 512)),
            ("Wo", "L1_ffn_Wo", (1024, 1024)),
            ("w_ln2", "L1_ffn_w_ln2", (1024,)),
            ("Wgate", "L1_ffn_Wgate", (1024, 8)),
            ("expert_bias", "L1_ffn_expert_bias", (8,)),
            ("Wg", "L1_ffn_Wg", (8, 1024, 512)),
            ("Wu", "L1_ffn_Wu", (8, 1024, 512)),
            ("Wd", "L1_ffn_Wd", (8, 512, 1024)),
            ("sWg", "L1_ffn_sWg", (1, 1024, 512)),
            ("sWu", "L1_ffn_sWu", (1, 1024, 512)),
            ("sWd", "L1_ffn_sWd", (1, 512, 1024))]

L2_ATTN_W = ([("Wi_f", "L2_attn_Wi_f", (1024, 4384)),
              ("conv_f", "L2_attn_conv_f", (4, 2304)),
              ("convb_f", "L2_attn_convb_f", (2304,)),
              ("Alog_f", "L2_attn_Alog_f", (32,)),
              ("D_f", "L2_attn_D_f", (32,)),
              ("dtb_f", "L2_attn_dtb_f", (32,)),
              ("Wo_f", "L2_attn_Wo_f", (2048, 1024)),
              ("Wi_b", "L2_attn_Wi_b", (1024, 4384)),
              ("conv_b", "L2_attn_conv_b", (4, 2304)),
              ("convb_b", "L2_attn_convb_b", (2304,)),
              ("Alog_b", "L2_attn_Alog_b", (32,)),
              ("D_b", "L2_attn_D_b", (32,)),
              ("dtb_b", "L2_attn_dtb_b", (32,)),
              ("Wo_b", "L2_attn_Wo_b", (2048, 1024)),
              ("Wm", "L2_attn_Wm", (1024, 1024)),
              ("bm", "L2_attn_bm", (1024,))])

BLOCK_LIST = [(emit_l0_attn, L0_ATTN_W), (emit_l0_ffn, L0_FFN_W),
              (emit_l1_attn, L1_ATTN_W), (emit_l1_ffn, L1_FFN_W),
              (emit_l2_attn, L2_ATTN_W), (emit_l2_ffn, L2_FFN_W),
              (emit_l3_attn, L3_ATTN_W), (emit_l3_ffn, L3_FFN_W)]




def kernel(**inputs):
    nc = build_core_program(BLOCK_LIST)
    x = np.ascontiguousarray(np.asarray(inputs["L0_attn_x"], dtype=np.float32))
    consts = const_arrays()
    in_maps = []
    for b in range(8):
        m = {"x_in": x[b]}
        m.update(consts)
        for _, wspec in BLOCK_LIST:
            for short, full, shape in wspec:
                if full.startswith("c_"):
                    continue
                m[full] = np.ascontiguousarray(np.asarray(inputs[full],
                                                          np.float32))
        in_maps.append(m)
    res = run_bass_kernel_spmd(nc, in_maps, core_ids=list(range(8)))
    return np.stack([r["x_out"] for r in res.results]).astype(np.float32)


# version 02
# speedup vs baseline: 1.2593x; 1.2593x over previous
import numpy as np
import ml_dtypes
import concourse.bass as bass
import concourse.mybir as mybir
from concourse.bass_utils import run_bass_kernel_spmd
from contextlib import ExitStack

F32 = mybir.dt.float32
BF16 = mybir.dt.bfloat16
AF = mybir.ActivationFunctionType
ALU = mybir.AluOpType
AX = mybir.AxisListType

T = 2048
D = 1024
NT = 16
ND = 8

ENGINES = ("sync", "tensor", "vector", "scalar", "gpsimd")


class Prog:
    def __init__(self, nc, serial=False):
        self.nc = nc
        self.serial = serial
        self.ops = []
        self.key_w = {}
        self.key_r = {}

    def add(self, eng, fn, reads=(), writes=(), dma=False):
        i = len(self.ops)
        deps = set()
        for k in reads:
            j = self.key_w.get(k)
            if j is not None:
                deps.add(j)
        for k in writes:
            j = self.key_w.get(k)
            if j is not None:
                deps.add(j)
            deps.update(self.key_r.get(k, ()))
        if self.serial and i > 0:
            deps.add(i - 1)
        self.ops.append(dict(i=i, eng=eng, fn=fn, deps=deps, dma=dma,
                             marked=dma))
        for k in reads:
            self.key_r.setdefault(k, []).append(i)
        for k in writes:
            self.key_w[k] = i
            self.key_r[k] = []
        return i

    def barrier(self):
        last = {}
        for op in self.ops:
            last[op["eng"]] = op["i"]
        deps = set(last.values())
        for e in ENGINES:
            i = len(self.ops)
            self.ops.append(dict(i=i, eng=e, fn=lambda eng: eng.nop(),
                                 deps=set(deps), dma=False, marked=False))
        self.key_w.clear()
        self.key_r.clear()

    def dma(self, out, in_, reads=(), writes=(), q="sync", **kw):
        self.add(q, lambda e: e.dma_start(out=out, in_=in_, **kw),
                 reads=reads, writes=writes, dma=True)

    def t(self, fn, reads=(), writes=()):
        self.add("tensor", fn, reads, writes)

    def v(self, fn, reads=(), writes=()):
        self.add("vector", fn, reads, writes)

    def s(self, fn, reads=(), writes=()):
        self.add("scalar", fn, reads, writes)

    def p(self, fn, reads=(), writes=()):
        self.add("gpsimd", fn, reads, writes)

    def emit(self, block, sems):
        ops = self.ops

        def skip_edge(pj, op):
            return (pj["eng"] == op["eng"] == "tensor"
                    and not pj["dma"] and not op["dma"])

        for op in ops:
            for j in op["deps"]:
                pj = ops[j]
                if skip_edge(pj, op):
                    continue
                pj["marked"] = True
        pfx = {e: 0 for e in ENGINES}
        order = {e: [] for e in ENGINES}
        for op in ops:
            e = op["eng"]
            if op["marked"]:
                pfx[e] += 16 if op["dma"] else 1
            op["pfx"] = pfx[e]
            order[e].append(op)
        clock = {e: {f: 0 for f in ENGINES} for e in ENGINES}
        last_dma_pfx = {e: 0 for e in ENGINES}
        for op in ops:
            e = op["eng"]
            waits = {}
            for j in op["deps"]:
                pj = ops[j]
                f = pj["eng"]
                if skip_edge(pj, op):
                    continue
                c = pj["pfx"]
                if c > clock[e].get(f, 0):
                    waits[f] = max(waits.get(f, 0), c)
            if op["dma"]:
                c = last_dma_pfx[e]
                if c > clock[e].get(e, 0):
                    waits[e] = max(waits.get(e, 0), c)
                last_dma_pfx[e] = op["pfx"]
            op["waits"] = waits
            for f, c in waits.items():
                clock[e][f] = max(clock[e][f], c)

        for ename in ENGINES:
            if not order[ename]:
                continue

            def mk(my_ops, ename=ename):
                def section(eng):
                    for op in my_ops:
                        for f, c in sorted(op["waits"].items()):
                            eng.wait_ge(sems[f], c)
                        ins = op["fn"](eng)
                        if op["marked"]:
                            ins.then_inc(sems[ename], 16 if op["dma"] else 1)
                    if ename == "sync":
                        eng.wait_ge(sems["sync"], pfx["sync"])
                return section
            getattr(block, ename)(mk(order[ename]))
        return pfx


def const_arrays():
    c = {"c_identb": np.eye(128, dtype=ml_dtypes.bfloat16),
         "c_identf": np.eye(128, dtype=np.float32)}
    seg = np.zeros((128, 128), dtype=ml_dtypes.bfloat16)
    for dc in range(8):
        for dl in range(128):
            seg[dl, dc * 16 + 2 * dc + dl // 64] = 1
    c["c_seg"] = seg
    pos = (np.arange(16)[None, :, None] * 128 + np.arange(128)[:, None, None])
    freqs = (1.0 / 100000.0 ** (np.arange(0, 32, 2, dtype=np.float64) / 32.0))
    ang = pos * freqs[None, None, :]
    c["c_rcos"] = np.cos(ang).reshape(128, 256).astype(ml_dtypes.bfloat16)
    c["c_rsin"] = np.sin(ang).reshape(128, 256).astype(ml_dtypes.bfloat16)
    kk = np.arange(128)[:, None]
    ll = np.arange(128)[None, :]
    c["c_causT"] = np.where(ll >= kk, 0.0, -1e30).astype(ml_dtypes.bfloat16)
    c["c_causU"] = np.where(ll <= kk, 0.0, -1e30).astype(ml_dtypes.bfloat16)
    c["c_cuf"] = np.where(kk <= ll, 1.0, 0.0).astype(np.float32)
    llw = np.arange(128)[:, None]
    kkw = np.arange(136)[None, :]
    band = (kkw >= llw) & (kkw <= llw + 8)
    c["c_bandw"] = np.where(band, 0.0, -1e30).astype(ml_dtypes.bfloat16)
    c["c_band0"] = np.where(band & (kkw >= 8), 0.0, -1e30).astype(ml_dtypes.bfloat16)
    return c


class Ctx:
    def __init__(self, nc, P, stack, debug=False):
        self.nc = nc
        self.P = P
        self.debug = debug
        self.dbg_names = []
        ent = stack.enter_context
        sb = nc.sbuf_tensor
        self.X = ent(sb("X", [128, NT * 1024], F32)).ap()
        self.B1 = ent(sb("B1", [128, 8 * T], BF16)).ap()
        self.B2 = ent(sb("B2", [128, 8 * T], BF16)).ap()
        self.B3 = ent(sb("B3", [128, 5 * T], BF16)).ap()
        self.WA = ent(sb("WA", [128, 8 * T], BF16)).ap()
        self.F1 = ent(sb("F1", [128, 2048], F32)).ap()
        self.F2 = ent(sb("F2", [128, 2048], F32)).ap()
        self.S1 = ent(sb("S1", [128, 2048], F32)).ap()
        self.CB = ent(sb("CB", [128, 256], BF16)).ap()
        self.CF = ent(sb("CF", [128, 132], F32)).ap()
        self.CC = ent(sb("CC", [128, 16], F32)).ap()
        self.CI = ent(sb("CI", [128, 128], F32)).ap()
        self.CS = ent(sb("CS", [128, 128], BF16)).ap()
        self.CRC = ent(sb("CRC", [128, 256], BF16)).ap()
        self.CRS = ent(sb("CRS", [128, 256], BF16)).ap()
        self.CT = ent(sb("CT", [128, 128], BF16)).ap()
        self.CTU = ent(sb("CTU", [128, 128], BF16)).ap()
        self.CUF = ent(sb("CUF", [128, 128], F32)).ap()
        self._ccells = {}
        self.psA = ent(nc.psum_tensor("psA", [128, 2048], F32)).ap()
        self.psB = ent(nc.psum_tensor("psB", [128, 1024], F32)).ap()
        self.psT = ent(nc.psum_tensor("psT", [128, 1024], BF16)).ap()
        self.psS = ent(nc.psum_tensor("psS", [128, 512], F32)).ap()

    @property
    def identb(self):
        return self.CB[:, 0:128]

    @property
    def onescolb(self):
        return self.CB[:, 128:129]

    @property
    def onesrowb(self):
        return self.CB[0:1, 128:256]

    @property
    def onescolf(self):
        return self.CF[:, 0:1]

    @property
    def onesrowf(self):
        return self.CF[0:1, 4:132]

    @property
    def onecellf(self):
        return self.CF[0:1, 0:1]

    def load_consts(self, W):
        P = self.P
        P.dma(self.CB[:, 0:128], W["c_identb"], writes=[("CB",)])
        P.dma(self.CI[:], W["c_identf"], writes=[("CI",)])
        P.dma(self.CS[:], W["c_seg"], writes=[("CS",)])
        P.dma(self.CRC[:], W["c_rcos"], writes=[("CRC",)])
        P.dma(self.CRS[:], W["c_rsin"], writes=[("CRS",)])
        P.dma(self.CT[:], W["c_causT"], writes=[("CT",)])
        P.dma(self.CTU[:], W["c_causU"], writes=[("CTU",)])
        P.dma(self.CUF[:], W["c_cuf"], writes=[("CUF",)])
        P.v(lambda e: e.memset(self.CB[:, 128:256], 1.0), writes=[("CB",)])
        P.v(lambda e: e.memset(self.CF[:], 1.0), writes=[("CF",)])

    def dbg(self, name, ap, reads):
        if not self.debug:
            return
        name = "dbg_" + name
        dt = ap.dtype
        h = self.nc.dram_tensor(name, list(ap.shape), dt, kind="ExternalOutput")
        self.dbg_names.append(name)
        self.P.dma(h.ap(), ap, reads=list(reads))

    def ccell(self, val):
        val = float(val)
        if val not in self._ccells:
            assert len(self._ccells) < 16
            i = len(self._ccells)
            ap = self.CC[:, i:i + 1]
            self._ccells[val] = ap
            self.nc.const_aps.register(F32, val, ap)
            self.P.v(lambda e: e.memset(ap, val), writes=[("CC",)])
        return self._ccells[val]



def tp_group(C, src_tile, dst, dst_lo, n, kr, kw, dst_ap=None, src_ap=None):
    P = C.P
    for k in range(n):
        def tp(e, k=k):
            s = src_tile(k)
            np_ = s.shape[0]
            return e.transpose(C.psT[0:s.shape[1], k * 128:k * 128 + np_],
                               s, C.identb[0:np_, 0:np_])
        P.t(tp, reads=list(kr) + [("CB",)], writes=[("psT", k)])
    d = dst_ap if dst_ap is not None else dst[:, dst_lo:dst_lo + n * 128]
    s = src_ap if src_ap is not None else C.psT[:, 0:n * 128]
    P.v(lambda e, d=d, s=s: e.tensor_copy(d, s),
        reads=[("psT", k) for k in range(8)], writes=list(kw))


def tp_tok2feat(C, src, dst, kr, kw, nt=NT, nd=ND):
    Fs, Ts = nd * 128, nt * 128
    for dc in range(nd):
        for tg in range(0, nt, 8):
            n = min(8, nt - tg)
            tp_group(C,
                     lambda k, dc=dc, tg=tg: src[:, (tg + k) * Fs + dc * 128:
                                                 (tg + k) * Fs + dc * 128 + 128],
                     dst, dc * Ts + tg * 128, n, kr, kw)


def ln_stats(C, x, mu, rstd, keys_x, key_mu, key_rstd, nt=NT, F=1024, eps=1e-5):
    P = C.P
    for t in range(nt):
        xt = x[:, t * F:(t + 1) * F]
        P.v(lambda e, xt=xt, t=t: e.reduce_sum(mu[:, t:t + 1], xt, axis=AX.X),
            reads=keys_x, writes=[key_mu])
        P.s(lambda e, xt=xt, t=t: e.activation(C.F1[:, 0:F], xt, AF.Square,
                                               accum_out=rstd[:, t:t + 1]),
            reads=keys_x, writes=[key_rstd, ("F1", 0)])
    P.v(lambda e: e.tensor_scalar(mu, mu, 1.0 / F, None, ALU.mult),
        reads=[key_mu], writes=[key_mu])
    P.v(lambda e: e.scalar_tensor_tensor(C.F1[:, 0:nt], mu, 1.0, mu,
                                         ALU.mult, ALU.mult),
        reads=[key_mu], writes=[("F1", 0)])
    P.v(lambda e: e.tensor_scalar(rstd, rstd, 1.0 / F, None, ALU.mult),
        reads=[key_rstd], writes=[key_rstd])
    P.v(lambda e: e.tensor_tensor(rstd, rstd, C.F1[:, 0:nt], ALU.subtract),
        reads=[key_rstd, ("F1", 0)], writes=[key_rstd])
    C.ccell(eps)
    C.ccell(-0.5)
    P.s(lambda e: e.activation(rstd, rstd, AF.Ln, bias=eps),
        reads=[key_rstd, ("CC",)], writes=[key_rstd])
    P.s(lambda e: e.activation(rstd, rstd, AF.Exp, scale=-0.5),
        reads=[key_rstd, ("CC",)], writes=[key_rstd])


def load_weight_bf16(C, dst, w_dram, kw, rows, cols, dst_col0=0, cast_eng="p",
                     stage=None, key_stage=None, scale_col=None, key_scale=None):
    P = C.P
    stage = C.F2 if stage is None else stage
    key_stage = key_stage or ("F2",)
    nr = (rows + 127) // 128
    w3 = w_dram.rearrange("(r p) n -> p r n", p=128) if rows > 128 else None
    if w3 is not None and nr * cols <= 2048:
        P.dma(stage[:, 0:nr * cols].rearrange("p (r n) -> p r n", r=nr),
              w3, reads=[key_stage], writes=[key_stage])
        for r in range(nr):
            d = dst[:, dst_col0 + r * cols: dst_col0 + (r + 1) * cols]
            src = stage[:, r * cols:(r + 1) * cols]
            if scale_col is None:
                getattr(P, cast_eng)(
                    lambda e, d=d, src=src: e.tensor_copy(d, src),
                    reads=[key_stage], writes=list(kw))
            else:
                getattr(P, cast_eng)(
                    lambda e, d=d, src=src, r=r: e.tensor_scalar(
                        d, src, scale_col[:, r:r + 1], None, ALU.mult),
                    reads=[key_stage, key_scale], writes=list(kw))
        return
    for r in range(nr):
        c0 = 0
        while c0 < cols:
            cw = min(2048, cols - c0)
            src = w3[:, r, c0:c0 + cw] if w3 is not None else w_dram[:, c0:c0 + cw]
            P.dma(stage[:, 0:cw], src, reads=[key_stage], writes=[key_stage])
            d = dst[:, dst_col0 + r * cols + c0: dst_col0 + r * cols + c0 + cw]
            if scale_col is None:
                getattr(P, cast_eng)(
                    lambda e, d=d, cw=cw: e.tensor_copy(d, stage[:, 0:cw]),
                    reads=[key_stage], writes=list(kw))
            else:
                getattr(P, cast_eng)(
                    lambda e, d=d, cw=cw, r=r: e.tensor_scalar(
                        d, stage[:, 0:cw], scale_col[:, r:r + 1], None, ALU.mult),
                    reads=[key_stage, key_scale], writes=list(kw))
            c0 += cw


def bcast_bias(C, b_dram, dst, kw, n):
    P = C.P
    ps = C.psB
    keys = [("psB", i) for i in range((n + 511) // 512)]
    row = C.F2[0:1, 0:n]
    P.dma(row, b_dram.unsqueeze(0), reads=[("F2",)], writes=[("F2",)])
    c0 = 0
    while c0 < n:
        cw = min(512, n - c0)
        P.t(lambda e, c0=c0, cw=cw: e.matmul(ps[:, c0:c0 + cw], C.onesrowf,
                                             row[0:1, c0:c0 + cw],
                                             start=True, stop=True),
            reads=[("F2",), ("CF",)], writes=[("psB", c0 // 512)])
        c0 += cw
    P.v(lambda e: e.tensor_copy(dst[:, 0:n], ps[:, 0:n]),
        reads=keys, writes=list(kw))




def emit_l0_attn(C, W):
    P = C.P
    X, B1, B2, B3, WA = C.X, C.B1, C.B2, C.B3, C.WA
    S1, F1, F2 = C.S1, C.F1, C.F2
    psA, psB, psS = C.psA, C.psB, C.psS
    KD = float(1024.0 ** -0.5)

    mu = S1[:, 0:16]
    rstd = S1[:, 16:32]
    rsq = S1[:, 32:48]
    rsk = S1[:, 48:56]
    mu2 = S1[:, 56:57]
    rstd2 = S1[:, 57:58]
    bo_b = S1[:, 64:1088]
    gam_b = F1[:, 0:1024]
    bet_b = F1[:, 1024:2048]
    K_WA0, K_WA1 = ("WA", 0), ("WA", 1)

    ln_stats(C, X, mu, rstd, [("X",)], ("mu",), ("rstd",))
    for t in range(NT):
        P.v(lambda e, t=t: e.tensor_scalar(
            B2[:, t * 1024:(t + 1) * 1024], X[:, t * 1024:(t + 1) * 1024],
            mu[:, t:t + 1], rstd[:, t:t + 1], ALU.subtract, ALU.mult),
            reads=[("X",), ("mu",), ("rstd",)], writes=[("B2",)])
    C.dbg("xn", B2[:], [("B2",)])
    C.dbg("mu", mu, [("mu",)])
    C.dbg("rstd", rstd, [("rstd",)])
    tp_tok2feat(C, B2, B1, [("B2",)], [("B1",)])
    C.dbg("xnT", B1[:], [("B1",)])

    def proj_tok(wkey, out_cb):
        load_weight_bf16(C, WA, W[wkey], [K_WA0], 1024, 1024)
        for t in range(NT):
            for h in range(2):
                for dc in range(ND):
                    P.t(lambda e, t=t, h=h, dc=dc: e.matmul(
                        psA[:, h * 512:h * 512 + 512],
                        B1[:, dc * T + t * 128: dc * T + t * 128 + 128],
                        WA[:, dc * 1024 + h * 512: dc * 1024 + h * 512 + 512],
                        start=(dc == 0), stop=(dc == ND - 1)),
                        reads=[("B1",), K_WA0], writes=[("psA", h)])
            out_cb(t)

    def k_out(t):
        P.s(lambda e, t=t: e.activation(B2[:, t * 1024:(t + 1) * 1024],
                                        psA[:, 0:1024], AF.Exp),
            reads=[("psA", 0), ("psA", 1)], writes=[("B2",)])
    proj_tok("Wk", k_out)
    C.dbg("ek", B2[:], [("B2",)])

    def v_slot(t):
        if t < 8:
            return C.B3[:, t * 1024:(t + 1) * 1024]
        return WA[:, 8192 + (t - 8) * 1024: 8192 + (t - 7) * 1024]

    def v_key(t):
        return ("B3",) if t < 8 else K_WA1

    def v_out(t):
        P.v(lambda e, t=t: e.tensor_copy(v_slot(t), psA[:, 0:1024]),
            reads=[("psA", 0), ("psA", 1)], writes=[v_key(t)])
    proj_tok("Wv", v_out)

    for dc in range(ND):
        for t in range(NT):
            P.t(lambda e, t=t, dc=dc: e.matmul(
                psS[:, dc:dc + 1],
                B2[:, t * 1024 + dc * 128: t * 1024 + dc * 128 + 128],
                C.onescolb, start=(t == 0), stop=(t == NT - 1)),
                reads=[("B2",), ("CB",)], writes=[("psS",)])
        for h in range(2):
            for t in range(NT):
                P.t(lambda e, t=t, dc=dc, h=h: e.matmul(
                    psB[:, h * 512:h * 512 + 512],
                    B2[:, t * 1024 + dc * 128: t * 1024 + dc * 128 + 128],
                    v_slot(t)[:, h * 512:h * 512 + 512],
                    start=(t == 0), stop=(t == NT - 1)),
                    reads=[("B2",), v_key(t)], writes=[("psB", h)])
        P.v(lambda e, dc=dc: e.reciprocal(rsk[:, dc:dc + 1], psS[:, dc:dc + 1]),
            reads=[("psS",)], writes=[("rsk",)])
        P.v(lambda e, dc=dc: e.tensor_scalar(
            WA[:, dc * 1024:(dc + 1) * 1024], psB[:, 0:1024],
            rsk[:, dc:dc + 1], None, ALU.mult),
            reads=[("psB", 0), ("psB", 1), ("rsk",)], writes=[K_WA0])
    C.dbg("ctx", WA[:, 0:8192], [K_WA0])

    load_weight_bf16(C, WA, W["Wq"], [K_WA1], 1024, 1024, dst_col0=8192)
    for ec in range(ND):
        for lp in range(4):
            for dc in range(ND):
                P.t(lambda e, ec=ec, lp=lp, dc=dc: e.matmul(
                    psA[:, lp * 512:lp * 512 + 512],
                    WA[:, 8192 + dc * 1024 + ec * 128:
                       8192 + dc * 1024 + ec * 128 + 128],
                    B1[:, dc * T + lp * 512: dc * T + lp * 512 + 512],
                    start=(dc == 0), stop=(dc == ND - 1)),
                    reads=[("B1",), K_WA1], writes=[("psA", lp)])
        P.s(lambda e, ec=ec: e.activation(B2[:, ec * T:(ec + 1) * T],
                                          psA[:, 0:2048], AF.Exp),
            reads=[("psA", h) for h in range(4)], writes=[("B2",)])

    for t in range(NT):
        for dc in range(ND):
            P.t(lambda e, t=t, dc=dc: e.matmul(
                psB[:, t:t + 1],
                B2[:, dc * T + t * 128: dc * T + t * 128 + 128],
                C.onescolb, start=(dc == 0), stop=(dc == ND - 1)),
                reads=[("B2",), ("CB",)], writes=[("psB", 0)])
    P.v(lambda e: e.reciprocal(rsq, psB[:, 0:16]),
        reads=[("psB", 0)], writes=[("rsq",)])
    P.v(lambda e: e.tensor_scalar(rsq, rsq, KD, None, ALU.mult),
        reads=[("rsq",)], writes=[("rsq",)])
    C.dbg("eq", B2[:], [("B2",)])
    C.dbg("rsq", rsq, [("rsq",)])

    for ec in range(ND):
        for lp in range(4):
            for dc in range(ND):
                P.t(lambda e, ec=ec, lp=lp, dc=dc: e.matmul(
                    psA[:, lp * 512:lp * 512 + 512],
                    WA[:, dc * 1024 + ec * 128: dc * 1024 + ec * 128 + 128],
                    B2[:, dc * T + lp * 512: dc * T + lp * 512 + 512],
                    start=(dc == 0), stop=(dc == ND - 1)),
                    reads=[K_WA0, ("B2",)], writes=[("psA", lp)])
        P.v(lambda e, ec=ec: e.tensor_copy(B1[:, ec * T:(ec + 1) * T],
                                           psA[:, 0:2048]),
            reads=[("psA", h) for h in range(4)], writes=[("B1",)])

    C.dbg("attnT", B1[:], [("B1",)])

    load_weight_bf16(C, WA, W["Wo"], [K_WA1], 1024, 1024, dst_col0=8192)
    bcast_bias(C, W["bo"], bo_b, [("bo_b",)], 1024)
    bcast_bias(C, W["gamma"], gam_b, [("F1", 0)], 1024)
    bcast_bias(C, W["beta"], bet_b, [("F1", 1)], 1024)

    for t in range(NT):
        for h in range(2):
            for ec in range(ND):
                P.t(lambda e, t=t, h=h, ec=ec: e.matmul(
                    psA[:, h * 512:h * 512 + 512],
                    B1[:, ec * T + t * 128: ec * T + t * 128 + 128],
                    WA[:, 8192 + ec * 1024 + h * 512:
                       8192 + ec * 1024 + h * 512 + 512],
                    start=(ec == 0), stop=(ec == ND - 1)),
                    reads=[("B1",), K_WA1], writes=[("psA", h)])
        KPS = [("psA", 0), ("psA", 1)]
        xt = X[:, t * 1024:(t + 1) * 1024]
        if t == 0 and C.debug:
            P.v(lambda e: e.tensor_copy(F2[:, 0:1024], psA[:, 0:1024]),
                reads=KPS, writes=[("F2",)])
            C.dbg("ops0", F2[:, 0:1024], [("F2",)])
            C.dbg("bo_b", bo_b, [("bo_b",)])
        P.v(lambda e, t=t: e.scalar_tensor_tensor(
            F2[:, 0:1024], psA[:, 0:1024], rsq[:, t:t + 1], bo_b,
            ALU.mult, ALU.add),
            reads=KPS + [("rsq",), ("bo_b",)], writes=[("F2",)])
        if t == 0:
            C.dbg("t1a", F2[:, 0:1024], [("F2",)])
        P.v(lambda e, t=t, xt=xt: e.tensor_scalar(
            psB[:, 0:1024], xt, mu[:, t:t + 1], rstd[:, t:t + 1],
            ALU.subtract, ALU.mult),
            reads=[("X",), ("mu",), ("rstd",)], writes=[("psB", 0), ("psB", 1)])
        P.v(lambda e: e.tensor_tensor(F2[:, 0:1024], F2[:, 0:1024],
                                      psB[:, 0:1024], ALU.add),
            reads=[("F2",), ("psB", 0), ("psB", 1)], writes=[("F2",)])
        if t == 0:
            C.dbg("t1b", F2[:, 0:1024], [("F2",)])
        P.v(lambda e: e.reduce_sum(mu2, F2[:, 0:1024], axis=AX.X),
            reads=[("F2",)], writes=[("mu2",)])
        P.s(lambda e: e.activation(psB[:, 0:1024], F2[:, 0:1024], AF.Square,
                                   accum_out=rstd2),
            reads=[("F2",)], writes=[("rstd2",), ("psB", 0), ("psB", 1)])
        P.v(lambda e: e.tensor_scalar(mu2, mu2, 1.0 / 1024, None, ALU.mult),
            reads=[("mu2",)], writes=[("mu2",)])
        P.v(lambda e: e.scalar_tensor_tensor(S1[:, 58:59], mu2, 1.0, mu2,
                                             ALU.mult, ALU.mult),
            reads=[("mu2",)], writes=[("s58",)])
        P.v(lambda e: e.tensor_scalar(rstd2, rstd2, 1.0 / 1024, None, ALU.mult),
            reads=[("rstd2",)], writes=[("rstd2",)])
        P.v(lambda e: e.tensor_tensor(rstd2, rstd2, S1[:, 58:59], ALU.subtract),
            reads=[("rstd2",), ("s58",)], writes=[("rstd2",)])
        P.s(lambda e: e.activation(rstd2, rstd2, AF.Ln, bias=1e-5),
            reads=[("rstd2",), ("CC",)], writes=[("rstd2",)])
        P.s(lambda e: e.activation(rstd2, rstd2, AF.Exp, scale=-0.5),
            reads=[("rstd2",), ("CC",)], writes=[("rstd2",)])
        P.v(lambda e: e.tensor_scalar(F2[:, 0:1024], F2[:, 0:1024], mu2, rstd2,
                                      ALU.subtract, ALU.mult),
            reads=[("F2",), ("mu2",), ("rstd2",)], writes=[("F2",)])
        P.v(lambda e: e.tensor_tensor(F2[:, 0:1024], F2[:, 0:1024], gam_b,
                                      ALU.mult),
            reads=[("F2",), ("F1", 0)], writes=[("F2",)])
        P.v(lambda e: e.tensor_tensor(F2[:, 0:1024], F2[:, 0:1024], bet_b,
                                      ALU.add),
            reads=[("F2",), ("F1", 1)], writes=[("F2",)])
        P.v(lambda e, xt=xt: e.tensor_tensor(xt, xt, F2[:, 0:1024], ALU.add),
            reads=[("X",), ("F2",)], writes=[("X",)])




def emit_l0_ffn(C, W):
    P = C.P
    X, B1, B2, B3, WA = C.X, C.B1, C.B2, C.B3, C.WA
    S1, F1, F2 = C.S1, C.F1, C.F2
    psA, psB, psS = C.psA, C.psB, C.psS
    E = 16

    mu = S1[:, 0:16]
    rstd = S1[:, 16:32]
    gate_w = S1[:, 64:192]
    route = S1[:, 192:448]
    gate_bb = S1[:, 448:464]
    wsel = S1[:, 480:736]
    K_WA0, K_WA1 = ("WA", 0), ("WA", 1)

    ln_stats(C, X, mu, rstd, [("X",)], ("mu",), ("rstd",))
    for t in range(NT):
        P.v(lambda e, t=t: e.tensor_scalar(
            B2[:, t * 1024:(t + 1) * 1024], X[:, t * 1024:(t + 1) * 1024],
            mu[:, t:t + 1], rstd[:, t:t + 1], ALU.subtract, ALU.mult),
            reads=[("X",), ("mu",), ("rstd",)], writes=[("B2",)])
    tp_tok2feat(C, B2, B1, [("B2",)], [("B1",)])

    for cc in range(8):
        P.dma(gate_w[:, cc * 16:(cc + 1) * 16],
              W["gate_w"][cc * 128:(cc + 1) * 128, :],
              reads=[("gw",)], writes=[("gw",)])
    row = F2[0:1, 0:16]
    P.dma(row, W["gate_b"].unsqueeze(0), reads=[("F2",)], writes=[("F2",)])
    P.t(lambda e: e.matmul(psB[:, 0:16], C.onesrowf, row, start=True, stop=True),
        reads=[("F2",), ("CF",)], writes=[("psB", 0)])
    P.v(lambda e: e.tensor_copy(gate_bb, psB[:, 0:16]),
        reads=[("psB", 0)], writes=[("gbb",)])

    for dc in range(ND):
        for tg in range(0, NT, 4):
            for k in range(4):
                t = tg + k
                P.v(lambda e, t=t, dc=dc, k=k: e.tensor_scalar(
                    F2[:, k * 128:(k + 1) * 128],
                    X[:, t * 1024 + dc * 128: t * 1024 + dc * 128 + 128],
                    mu[:, t:t + 1], rstd[:, t:t + 1], ALU.subtract, ALU.mult),
                    reads=[("X",), ("mu",), ("rstd",)], writes=[("F2",)])
                P.t(lambda e, k=k: e.transpose(
                    psA[:, 1536 + k * 128:1536 + (k + 1) * 128],
                    F2[:, k * 128:(k + 1) * 128], C.CI),
                    reads=[("F2",), ("CI",)], writes=[("psA", 3)])
            P.v(lambda e, tg=tg: e.tensor_copy(F1[:, tg * 128:(tg + 4) * 128],
                                               psA[:, 1536:2048]),
                reads=[("psA", 3)], writes=[("F1", 0), ("F1", 1)])
        for t in range(NT):
            P.t(lambda e, t=t, dc=dc: e.matmul(
                psS[:, t * 16:(t + 1) * 16],
                F1[:, t * 128:(t + 1) * 128],
                gate_w[:, dc * 16:(dc + 1) * 16],
                start=True, stop=True),
                reads=[("F1", 0), ("F1", 1), ("gw",)], writes=[("psS",)])
        if dc == 0:
            P.v(lambda e: e.tensor_copy(route, psS[:, 0:256]),
                reads=[("psS",)], writes=[("route",)])
        else:
            P.v(lambda e: e.tensor_tensor(route, route, psS[:, 0:256], ALU.add),
                reads=[("psS",), ("route",)], writes=[("route",)])
    P.v(lambda e: e.tensor_tensor(
        route.rearrange("p (t e) -> p t e", t=16),
        route.rearrange("p (t e) -> p t e", t=16),
        gate_bb.unsqueeze(1).broadcast_to([128, 16, 16]), ALU.add),
        reads=[("route",), ("gbb",)], writes=[("route",)])
    P.s(lambda e: e.activation(route, route, AF.Sigmoid),
        reads=[("route",)], writes=[("route",)])

    for t in range(NT):
        rt = route[:, t * 16:(t + 1) * 16]
        m1 = S1[:, 464:465]
        m2 = S1[:, 465:466]
        rr = S1[:, 466:467]
        msk = F2[:, 0:16]
        r2 = F2[:, 16:32]
        P.v(lambda e, rt=rt: e.reduce_max(m1, rt, axis=AX.X),
            reads=[("route",)], writes=[("m1",)])
        P.v(lambda e, rt=rt: e.tensor_scalar(msk, rt, m1, None, ALU.is_ge),
            reads=[("route",), ("m1",)], writes=[("F2",)])
        P.v(lambda e, rt=rt: e.scalar_tensor_tensor(r2, msk, -1e30, rt,
                                                    ALU.mult, ALU.add),
            reads=[("route",), ("F2",)], writes=[("F2",)])
        P.v(lambda e: e.reduce_max(m2, r2, axis=AX.X),
            reads=[("F2",)], writes=[("m2",)])
        P.v(lambda e: e.tensor_tensor(rr, m1, m2, ALU.add),
            reads=[("m1",), ("m2",)], writes=[("rr",)])
        P.v(lambda e: e.reciprocal(rr, rr), reads=[("rr",)], writes=[("rr",)])
        P.v(lambda e, rt=rt: e.tensor_scalar(msk, rt, m2, rr, ALU.is_ge, ALU.mult),
            reads=[("route",), ("m2",), ("rr",)], writes=[("F2",)])
        P.v(lambda e, rt=rt, t=t: e.tensor_tensor(
            wsel[:, t * 16:(t + 1) * 16], rt, msk, ALU.mult),
            reads=[("route",), ("F2",)], writes=[("wsel",)])

    for t in range(NT):
        P.v(lambda e, t=t: e.tensor_scalar(
            psB[:, 0:1024], X[:, t * 1024:(t + 1) * 1024],
            mu[:, t:t + 1], rstd[:, t:t + 1], ALU.subtract, ALU.mult),
            reads=[("X",), ("mu",), ("rstd",)], writes=[("psB", 0), ("psB", 1)])
        P.v(lambda e, t=t: e.tensor_tensor(
            X[:, t * 1024:(t + 1) * 1024], X[:, t * 1024:(t + 1) * 1024],
            psB[:, 0:1024], ALU.add),
            reads=[("psB", 0), ("psB", 1), ("X",)], writes=[("X",)])

    for e_i in range(E + 1):
        if e_i < E:
            wg_d, wu_d, wd_d = (W["experts_wg"][e_i], W["experts_wu"][e_i],
                                W["experts_wd"][e_i])
        else:
            wg_d, wu_d, wd_d = W["shared_wg"], W["shared_wu"], W["shared_wd"]
        load_weight_bf16(C, WA, wg_d, [K_WA0], 1024, 512, dst_col0=0)
        load_weight_bf16(C, WA, wu_d, [K_WA0], 1024, 512, dst_col0=4096)
        load_weight_bf16(C, WA, wd_d, [K_WA1], 512, 1024, dst_col0=8192)
        for ic in range(4):
            for lp in range(4):
                for dc in range(ND):
                    P.t(lambda e, ic=ic, lp=lp, dc=dc: e.matmul(
                        psA[:, lp * 512:(lp + 1) * 512],
                        WA[:, dc * 512 + ic * 128: dc * 512 + ic * 128 + 128],
                        B1[:, dc * T + lp * 512: dc * T + lp * 512 + 512],
                        start=(dc == 0), stop=(dc == ND - 1)),
                        reads=[K_WA0, ("B1",)], writes=[("psA", lp)])
            P.s(lambda e, ic=ic: e.activation(B2[:, ic * T:(ic + 1) * T],
                                              psA[:, 0:2048], AF.Silu),
                reads=[("psA", 0), ("psA", 1), ("psA", 2), ("psA", 3)],
                writes=[("B2",)])
        for ic in range(4):
            for lp in range(4):
                for dc in range(ND):
                    P.t(lambda e, ic=ic, lp=lp, dc=dc: e.matmul(
                        psA[:, lp * 512:(lp + 1) * 512],
                        WA[:, 4096 + dc * 512 + ic * 128:
                           4096 + dc * 512 + ic * 128 + 128],
                        B1[:, dc * T + lp * 512: dc * T + lp * 512 + 512],
                        start=(dc == 0), stop=(dc == ND - 1)),
                        reads=[K_WA0, ("B1",)], writes=[("psA", lp)])
            P.v(lambda e, ic=ic: e.tensor_tensor(
                B3[:, ic * T:(ic + 1) * T], B2[:, ic * T:(ic + 1) * T],
                psA[:, 0:2048], ALU.mult),
                reads=[("psA", 0), ("psA", 1), ("psA", 2), ("psA", 3), ("B2",)],
                writes=[("B3",)])
        for t in range(NT):
            for h in range(2):
                for ic in range(4):
                    P.t(lambda e, t=t, h=h, ic=ic: e.matmul(
                        psB[:, h * 512:(h + 1) * 512],
                        B3[:, ic * T + t * 128: ic * T + t * 128 + 128],
                        WA[:, 8192 + ic * 1024 + h * 512:
                           8192 + ic * 1024 + h * 512 + 512],
                        start=(ic == 0), stop=(ic == 3)),
                        reads=[K_WA1, ("B3",)], writes=[("psB", h)])
            if e_i < E:
                wcell = wsel[:, t * 16 + e_i: t * 16 + e_i + 1]
                P.v(lambda e, t=t, wcell=wcell: e.scalar_tensor_tensor(
                    X[:, t * 1024:(t + 1) * 1024], psB[:, 0:1024], wcell,
                    X[:, t * 1024:(t + 1) * 1024], ALU.mult, ALU.add),
                    reads=[("psB", 0), ("psB", 1), ("wsel",), ("X",)],
                    writes=[("X",)])
            else:
                P.v(lambda e, t=t: e.tensor_tensor(
                    X[:, t * 1024:(t + 1) * 1024],
                    X[:, t * 1024:(t + 1) * 1024], psB[:, 0:1024], ALU.add),
                    reads=[("psB", 0), ("psB", 1), ("X",)], writes=[("X",)])




def emit_l1_attn(C, W):
    P = C.P
    X, B1, B2, B3, WA = C.X, C.B1, C.B2, C.B3, C.WA
    S1, F1, F2 = C.S1, C.F1, C.F2
    psA, psB, psS = C.psA, C.psB, C.psS

    mu = S1[:, 0:16]
    rstd = S1[:, 16:32]
    ksum = S1[:, 32:40]
    bq_col = S1[:, 40:48]
    bk_col = S1[:, 48:56]
    K_WA0, K_WA1 = ("WA", 0), ("WA", 1)
    KPS4 = [("psA", 0), ("psA", 1), ("psA", 2), ("psA", 3)]

    ln_stats(C, X, mu, rstd, [("X",)], ("mu",), ("rstd",))
    for t in range(NT):
        P.v(lambda e, t=t: e.tensor_scalar(
            B2[:, t * 1024:(t + 1) * 1024], X[:, t * 1024:(t + 1) * 1024],
            mu[:, t:t + 1], rstd[:, t:t + 1], ALU.subtract, ALU.mult),
            reads=[("X",), ("mu",), ("rstd",)], writes=[("B2",)])
    tp_tok2feat(C, B2, B1, [("B2",)], [("B1",)])

    def bias_cols(bname, dstcols, key):
        P.dma(F2[0:1, 0:1024], W[bname].unsqueeze(0), reads=[("F2",)],
              writes=[("F2",)])
        for c in range(8):
            P.t(lambda e, c=c: e.matmul(psB[:, c:c + 1],
                                        F2[0:1, c * 128:(c + 1) * 128],
                                        C.onecellf, start=True, stop=True),
                reads=[("F2",), ("CF",)], writes=[("psB", 0)])
        P.v(lambda e: e.tensor_copy(dstcols, psB[:, 0:8]),
            reads=[("psB", 0)], writes=[key])
    bias_cols("bq", bq_col, ("bqc",))
    bias_cols("bk", bk_col, ("bkc",))
    KBROW = ("B3", "brow")

    def bias_row(bname):
        P.dma(F2[0:1, 0:1024], W[bname].unsqueeze(0), reads=[("F2",)],
              writes=[("F2",)])
        P.v(lambda e: e.tensor_copy(B3[0:1, 9216:10240], F2[0:1, 0:1024]),
            reads=[("F2",)], writes=[KBROW])

    def proj_feat(dst_cb, wkey):
        load_weight_bf16(C, WA, W[wkey], [K_WA0], 1024, 1024)
        for ec in range(ND):
            for lp in range(4):
                for dc in range(ND):
                    P.t(lambda e, ec=ec, lp=lp, dc=dc: e.matmul(
                        psA[:, lp * 512:lp * 512 + 512],
                        WA[:, dc * 1024 + ec * 128: dc * 1024 + ec * 128 + 128],
                        B1[:, dc * T + lp * 512: dc * T + lp * 512 + 512],
                        start=(dc == 0), stop=(dc == ND - 1)),
                        reads=[("B1",), K_WA0], writes=[("psA", lp)])
            dst_cb(ec)

    def fm_evac(ec, bcol, dst, dkey):
        P.v(lambda e, ec=ec: e.tensor_scalar(
            F2[:, 0:2048], psA[:, 0:2048], bcol[:, ec:ec + 1], 0.0,
            ALU.add, ALU.min), reads=KPS4 + [("bqc",), ("bkc",)],
            writes=[("F2",)])
        P.s(lambda e: e.activation(F2[:, 0:2048], F2[:, 0:2048], AF.Exp),
            reads=[("F2",)], writes=[("F2",)])
        P.v(lambda e, ec=ec: e.tensor_scalar(
            F1[:, 0:2048], psA[:, 0:2048], bcol[:, ec:ec + 1], 0.0,
            ALU.add, ALU.max), reads=KPS4 + [("bqc",), ("bkc",)],
            writes=[("F1", 0), ("F1", 1)])
        P.v(lambda e, dst=dst: e.tensor_tensor(dst, F1[:, 0:2048],
                                               F2[:, 0:2048], ALU.add),
            reads=[("F1", 0), ("F1", 1), ("F2",)], writes=[dkey])

    def k_out(ec):
        stage = WA[:, 8192:10240]
        fm_evac(ec, bk_col, stage, K_WA1)
        B2v = B2[:].rearrange("p (t f) -> p t f", t=NT)
        for tg in range(0, NT, 8):
            tp_group(C, lambda k, tg=tg: stage[:, (tg + k) * 128:(tg + k + 1) * 128],
                     None, 0, 8, [K_WA1],
                     [("B2",)], dst_ap=B2v[:, tg:tg + 8, ec * 128:(ec + 1) * 128])
    proj_feat(k_out, "wk")

    bias_row("bv")
    load_weight_bf16(C, WA, W["wv"], [K_WA0], 1024, 1024)

    def v_slot(t):
        if t < 8:
            return B3[:, t * 1024:(t + 1) * 1024]
        return WA[:, 8192 + (t - 8) * 1024: 8192 + (t - 7) * 1024]

    def v_key(t):
        return ("B3",) if t < 8 else K_WA1
    for t in range(NT):
        for h in range(2):
            for dc in range(ND):
                P.t(lambda e, t=t, h=h, dc=dc: e.matmul(
                    psA[:, h * 512:h * 512 + 512],
                    B1[:, dc * T + t * 128: dc * T + t * 128 + 128],
                    WA[:, dc * 1024 + h * 512: dc * 1024 + h * 512 + 512],
                    start=(dc == 0), stop=False),
                    reads=[("B1",), K_WA0], writes=[("psA", h)])
            P.t(lambda e, t=t, h=h: e.matmul(
                psA[:, h * 512:h * 512 + 512], C.onesrowb,
                B3[0:1, 9216 + h * 512: 9216 + h * 512 + 512],
                start=False, stop=True),
                reads=[("CB",), KBROW], writes=[("psA", h)])
        P.v(lambda e, t=t: e.tensor_copy(v_slot(t), psA[:, 0:1024]),
            reads=[("psA", 0), ("psA", 1)], writes=[v_key(t)])

    for pg in range(2):
        for pp in range(4):
            p = pg * 4 + pp
            for t in range(NT):
                P.t(lambda e, p=p, pp=pp, t=t: e.matmul(
                    psA[:, pp * 512:pp * 512 + 128],
                    B2[:, t * 1024 + p * 128: t * 1024 + p * 128 + 128],
                    v_slot(t)[:, p * 128:(p + 1) * 128],
                    start=(t == 0), stop=(t == NT - 1)),
                    reads=[("B2",), v_key(t)], writes=[("psA", pp)])
            for t in range(NT):
                P.t(lambda e, p=p, t=t: e.matmul(
                    psS[:, p:p + 1],
                    B2[:, t * 1024 + p * 128: t * 1024 + p * 128 + 128],
                    C.onescolb, start=(t == 0), stop=(t == NT - 1)),
                    reads=[("B2",), ("CB",)], writes=[("psS",)])
        P.v(lambda e, pg=pg: e.tensor_copy(
            B3[:, 8192 + pg * 512: 8192 + (pg + 1) * 512]
            .rearrange("p (b x) -> p b x", b=4),
            psA[:].rearrange("p (b x) -> p b x", b=4)[:, :, 0:128]),
            reads=KPS4, writes=[("B3", "kv")])
    P.v(lambda e: e.tensor_copy(ksum, psS[:, 0:8]),
        reads=[("psS",)], writes=[("ksum",)])

    load_weight_bf16(C, WA, W["wq"], [K_WA0], 1024, 1024)

    def q_out(ec):
        fm_evac(ec, bq_col, B2[:, ec * T:(ec + 1) * T], ("B2",))
    proj_feat(q_out, "wq")

    for lp in range(4):
        for dc in range(ND):
            P.v(lambda e, lp=lp, dc=dc: e.tensor_scalar(
                WA[:, 0:512], B2[:, dc * T + lp * 512: dc * T + lp * 512 + 512],
                ksum[:, dc:dc + 1], None, ALU.mult),
                reads=[("B2",), ("ksum",)], writes=[K_WA0])
            P.t(lambda e, lp=lp, dc=dc: e.matmul(
                psS[0:16, 0:512], C.CS[:, dc * 16:(dc + 1) * 16], WA[:, 0:512],
                start=(dc == 0), stop=(dc == ND - 1)),
                reads=[("CS",), K_WA0], writes=[("psS",)])
        P.v(lambda e, lp=lp: e.tensor_scalar(
            F1[0:16, lp * 512:(lp + 1) * 512], psS[0:16, 0:512], 1e-6, None,
            ALU.add), reads=[("psS",)], writes=[("F1", 0), ("F1", 1)])
    P.v(lambda e: e.reciprocal(F1[0:16, 0:2048], F1[0:16, 0:2048]),
        reads=[("F1", 0), ("F1", 1)], writes=[("F1", 0), ("F1", 1)])

    def slot(c):
        return WA if c < 4 else B3

    def slot_col(c):
        return (c % 4) * 2048

    def slot_key(c):
        return K_WA0 if c < 4 else ("B3",)
    for h in range(16):
        c = h // 2
        r0 = (h % 2) * 64
        P.dma(F2[0:1, 0:2048], F1[h:h + 1, 0:2048],
              reads=[("F1", 0), ("F1", 1), ("F2",)], writes=[("F2",)])
        for lp in range(4):
            P.t(lambda e, h=h, r0=r0, lp=lp: e.matmul(
                psA[r0:r0 + 64, lp * 512:lp * 512 + 512],
                B3[r0:r0 + 64, 8192 + (h // 2) * 128 + r0: 8192 + (h // 2) * 128 + r0 + 64],
                B2[r0:r0 + 64, (h // 2) * T + lp * 512: (h // 2) * T + lp * 512 + 512],
                start=True, stop=True),
                reads=[("B3", "kv"), ("B2",)], writes=[("psA", lp)])
            P.t(lambda e, r0=r0, lp=lp: e.matmul(
                psB[r0:r0 + 64, (lp % 2) * 512:(lp % 2) * 512 + 512],
                C.onesrowf[0:1, 0:64], F2[0:1, lp * 512:(lp + 1) * 512],
                start=True, stop=True),
                reads=[("F2",), ("CF",)], writes=[("psB", lp % 2)])
            P.s(lambda e, h=h, c=c, r0=r0, lp=lp: e.copy(
                slot(c)[r0:r0 + 64, slot_col(c) + lp * 512: slot_col(c) + lp * 512 + 512],
                psA[r0:r0 + 64, lp * 512:lp * 512 + 512]),
                reads=[("psA", lp)], writes=[slot_key(c)])
            P.v(lambda e, h=h, c=c, r0=r0, lp=lp: e.tensor_tensor(
                slot(c)[r0:r0 + 64, slot_col(c) + lp * 512: slot_col(c) + lp * 512 + 512],
                slot(c)[r0:r0 + 64, slot_col(c) + lp * 512: slot_col(c) + lp * 512 + 512],
                psB[r0:r0 + 64, (lp % 2) * 512:(lp % 2) * 512 + 512], ALU.mult),
                reads=[("psB", lp % 2), slot_key(c)], writes=[slot_key(c)])

    load_weight_bf16(C, WA, W["wo"], [K_WA1], 1024, 1024, dst_col0=8192)
    bias_row("bo")
    for t in range(NT):
        for h2 in range(2):
            for c in range(8):
                P.t(lambda e, t=t, h2=h2, c=c: e.matmul(
                    psA[:, h2 * 512:h2 * 512 + 512],
                    slot(c)[:, slot_col(c) + t * 128: slot_col(c) + t * 128 + 128],
                    WA[:, 8192 + c * 1024 + h2 * 512: 8192 + c * 1024 + h2 * 512 + 512],
                    start=(c == 0), stop=False),
                    reads=[slot_key(c), K_WA1], writes=[("psA", h2)])
            P.t(lambda e, t=t, h2=h2: e.matmul(
                psA[:, h2 * 512:h2 * 512 + 512], C.onesrowb,
                B3[0:1, 9216 + h2 * 512: 9216 + h2 * 512 + 512],
                start=False, stop=True),
                reads=[("CB",), KBROW], writes=[("psA", h2)])
        P.v(lambda e, t=t: e.tensor_tensor(
            X[:, t * 1024:(t + 1) * 1024], X[:, t * 1024:(t + 1) * 1024],
            psA[:, 0:1024], ALU.add),
            reads=[("psA", 0), ("psA", 1), ("X",)], writes=[("X",)])




def emit_l1_ffn(C, W):
    P = C.P
    nc = C.nc
    X, B1, B2, B3, WA = C.X, C.B1, C.B2, C.B3, C.WA
    S1, F1, F2 = C.S1, C.F1, C.F2
    psA, psB, psS = C.psA, C.psB, C.psS
    KPS4 = [("psA", h) for h in range(4)]

    mu = S1[:, 0:16]
    rstd = S1[:, 16:32]
    ms1 = S1[:, 32:48]
    rs1 = S1[:, 48:64]
    w1col = S1[:, 64:72]
    w2col = S1[:, 128:136]
    gatew = S1[:, 72:128 + 8] if False else S1[:, 1280:1344]
    ebias = S1[:, 136:144]
    logit = S1[:, 144:272]
    wsel = S1[:, 272:400]
    ms2 = S1[:, 400:416]
    rr2 = S1[:, 416:432]
    x1d = nc.dram_tensor("l1_x1", [T, D], F32, kind="Internal").ap()
    x1v = x1d.rearrange("(t p) f -> p t f", p=128)

    KW = {"kva": ("WA", "kva"), "kvb": ("WA", "kvb"), "lat": ("WA", "lat"),
          "rope": ("WA", "rope"), "stg": ("WA", "stg"), "q": ("WA", "q")}

    def b2k(j):
        return ("B2", j)
    KB2 = [b2k(j) for j in range(8)]

    ln_stats(C, X, mu, rstd, [("X",)], ("mu",), ("rstd",))
    for t in range(NT):
        P.v(lambda e, t=t: e.tensor_scalar(
            F2[:, 0:1024], X[:, t * 1024:(t + 1) * 1024],
            mu[:, t:t + 1], rstd[:, t:t + 1], ALU.subtract, ALU.mult),
            reads=[("X",), ("mu",), ("rstd",)], writes=[("F2",)])
        P.s(lambda e, t=t: e.activation(F1[:, 0:1024], F2[:, 0:1024],
                                        AF.Square, accum_out=ms1[:, t:t + 1]),
            reads=[("F2",)], writes=[("F1", 0), ("ms1",)])
    C.ccell(1e-5)
    C.ccell(-0.5)
    P.v(lambda e: e.tensor_scalar(rs1, ms1, 1.0 / 1024, None, ALU.mult),
        reads=[("ms1",)], writes=[("rs1",)])
    P.s(lambda e: e.activation(rs1, rs1, AF.Ln, bias=1e-5),
        reads=[("rs1",), ("CC",)], writes=[("rs1",)])
    P.s(lambda e: e.activation(rs1, rs1, AF.Exp, scale=-0.5),
        reads=[("rs1",), ("CC",)], writes=[("rs1",)])
    P.v(lambda e: e.tensor_tensor(rs1, rs1, rstd, ALU.mult),
        reads=[("rs1",), ("rstd",)], writes=[("rs1",)])
    for t in range(NT):
        P.v(lambda e, t=t: e.tensor_scalar(
            B2[:, t * 1024:(t + 1) * 1024], X[:, t * 1024:(t + 1) * 1024],
            mu[:, t:t + 1], rs1[:, t:t + 1], ALU.subtract, ALU.mult),
            reads=[("X",), ("mu",), ("rs1",)], writes=[b2k(t // 2)])
    tp_tok2feat(C, B2, B1, KB2, [("B1",)])

    def col_from_row(wname, dst, key, n=8):
        P.dma(F2[0:1, 0:n * 128], W[wname].unsqueeze(0), reads=[("F2",)],
              writes=[("F2",)])
        for cc in range(n):
            P.t(lambda e, cc=cc: e.matmul(psB[:, cc:cc + 1],
                                          F2[0:1, cc * 128:(cc + 1) * 128],
                                          C.onecellf, start=True, stop=True),
                reads=[("F2",), ("CF",)], writes=[("psB", 0)])
        P.v(lambda e, dst=dst, n=n: e.tensor_copy(dst, psB[:, 0:n]),
            reads=[("psB", 0)], writes=[key])
    col_from_row("w_ln1", w1col, ("w1c",))
    col_from_row("w_ln2", w2col, ("w2c",))
    load_weight_bf16(C, WA, W["Wkva"], [KW["kva"]], 1024, 160,
                     scale_col=w1col, key_scale=("w1c",))
    load_weight_bf16(C, WA, W["Wkvb"], [KW["kvb"]], 128, 512, dst_col0=1280)
    load_weight_bf16(C, WA, W["Wq"], [KW["q"]], 1024, 1024, dst_col0=8192,
                     scale_col=w1col, key_scale=("w1c",))

    for oc2 in range(2):
        m = 128 if oc2 == 0 else 32
        for lp in range(4):
            for dc in range(ND):
                P.t(lambda e, oc2=oc2, m=m, lp=lp, dc=dc: e.matmul(
                    psA[0:m, lp * 512:lp * 512 + 512],
                    WA[:, dc * 160 + oc2 * 128: dc * 160 + oc2 * 128 + m],
                    B1[:, dc * T + lp * 512: dc * T + lp * 512 + 512],
                    start=(dc == 0), stop=(dc == ND - 1)),
                    reads=[KW["kva"], ("B1",)], writes=[("psA", lp)])
        dstc = 2048 if oc2 == 0 else 4096
        dkey = KW["lat"] if oc2 == 0 else KW["rope"]
        P.v(lambda e, m=m, dstc=dstc: e.tensor_copy(
            WA[0:m, dstc:dstc + 2048], psA[0:m, 0:2048]),
            reads=KPS4, writes=[dkey])

    VT = B3[:, 0:8448].rearrange("p (t h x) -> p t h x", h=16, x=33)
    P.v(lambda e: e.memset(VT[:, :, :, 32:33], 1.0), writes=[("B3", "v")])
    for c in range(4):
        for lp in range(4):
            P.t(lambda e, c=c, lp=lp: e.matmul(
                psA[:, lp * 512:lp * 512 + 512],
                WA[:, 1280 + c * 128:1280 + (c + 1) * 128],
                WA[:, 2048 + lp * 512: 2048 + lp * 512 + 512],
                start=True, stop=True),
                reads=[KW["kvb"], KW["lat"]], writes=[("psA", lp)])
        P.v(lambda e: e.tensor_copy(WA[:, 6144:8192], psA[:, 0:2048]),
            reads=KPS4, writes=[KW["stg"]])
        for hl in range(4):
            h = 4 * c + hl
            j, r0 = h // 2, (h % 2) * 64
            P.dma(B2[r0:r0 + 32, j * 2048:(j + 1) * 2048],
                  WA[hl * 32:(hl + 1) * 32, 6144:8192],
                  reads=[KW["stg"]], writes=[b2k(j)])
        for tg in range(0, NT, 8):
            tp_group(C, lambda k, tg=tg: WA[:, 6144 + (tg + k) * 128:
                                            6144 + (tg + k + 1) * 128],
                     None, 0, 8, [KW["stg"]], [("B3", "v")],
                     dst_ap=VT[:, tg:tg + 8, 4 * c:4 * c + 4, 0:32],
                     src_ap=C.psT[:, 0:1024].rearrange(
                         "p (t h x) -> p t h x", t=8, h=4))

    RT = B3[:, 8448:8960]
    for tg in range(0, NT, 8):
        tp_group(C, lambda k, tg=tg: WA[0:32, 4096 + (tg + k) * 128:
                                        4096 + (tg + k + 1) * 128],
                 None, 0, 8, [KW["rope"]], [("B3", "rt")],
                 dst_ap=RT.rearrange("p (t x) -> p t x", x=32)[:, tg:tg + 8, :],
                 src_ap=C.psT[:, 0:1024].rearrange(
                     "p (t x) -> p t x", x=128)[:, :, 0:32])
    _rope_rotate(C, RT, ("B3", "rt"))
    for tg in range(0, NT, 8):
        for k in range(8):
            P.t(lambda e, k=k, tg=tg: e.transpose(
                C.psT[0:32, k * 128:(k + 1) * 128],
                RT[:, (tg + k) * 32:(tg + k + 1) * 32], C.identb),
                reads=[("B3", "rt"), ("CB",)], writes=[("psT", k)])
        P.v(lambda e, tg=tg: e.tensor_copy(
            WA[0:32, 4096 + tg * 128:4096 + (tg + 8) * 128],
            C.psT[0:32, 0:1024]),
            reads=[("psT", k) for k in range(8)], writes=[KW["rope"]])
    for j in range(8):
        for r0 in (32, 96):
            P.dma(B2[r0:r0 + 32, j * 2048:(j + 1) * 2048],
                  WA[0:32, 4096:6144], reads=[KW["rope"]], writes=[b2k(j)])

    QS = B3[:, 8960:9984]
    for t in range(NT):
        for hf in range(2):
            for dc in range(ND):
                P.t(lambda e, t=t, hf=hf, dc=dc: e.matmul(
                    psA[:, hf * 512:hf * 512 + 512],
                    B1[:, dc * T + t * 128: dc * T + t * 128 + 128],
                    WA[:, 8192 + dc * 1024 + hf * 512:
                       8192 + dc * 1024 + hf * 512 + 512],
                    start=(dc == 0), stop=(dc == ND - 1)),
                    reads=[("B1",), KW["q"]], writes=[("psA", hf)])
        KQ = [("psA", 0), ("psA", 1)]
        P.s(lambda e: e.copy(F1[:, 0:1024], psA[:, 0:1024]),
            reads=KQ, writes=[("F1", 0)])
        qv = F1[:, 0:1024].rearrange("p (h z) -> p h z", h=16)
        sv = QS.rearrange("p (h z) -> p h z", h=16)
        P.v(lambda e: e.tensor_copy(sv[:, :, 0:32], qv[:, :, 0:32]),
            reads=[("F1", 0)], writes=[("B3", "qs")])
        _rope_rotate_q(C, qv, sv, [("F1", 0)], ("B3", "qs"), t)
        for k in range(8):
            P.t(lambda e, k=k: e.transpose(
                C.psT[:, k * 128:(k + 1) * 128],
                QS[:, k * 128:(k + 1) * 128], C.identb),
                reads=[("B3", "qs"), ("CB",)], writes=[("psT", k)])
        P.v(lambda e, t=t: e.tensor_copy(
            B1[:].rearrange("p (d l) -> p d l", d=8)[:, :, t * 128:(t + 1) * 128],
            C.psT[:, 0:1024].rearrange("p (d x) -> p d x", d=8)),
            reads=[("psT", k) for k in range(8)], writes=[("B1",)])

    C.ccell(0.125)
    for j in range(8):
        for h in (2 * j + 1, 2 * j):
            r0 = (h % 2) * 64
            for lh in range(2):
                nkc = min((lh + 1) * 8, NT)
                for kc in range(nkc):
                    diag = kc * 128 - lh * 1024
                    off = max(0, diag)
                    if off >= 1024:
                        continue
                    if off > 0:
                        P.v(lambda e, off=off: e.memset(
                            WA[:, 6144:6144 + off], 0.0), writes=[KW["stg"]])
                    for nb in range(off // 512, 2):
                        n0 = max(off, nb * 512)
                        n1 = (nb + 1) * 512
                        P.t(lambda e, j=j, h=h, r0=r0, lh=lh, kc=kc, n0=n0, n1=n1: e.matmul(
                            psA[:, n0:n1],
                            B2[r0:r0 + 64, j * 2048 + kc * 128: j * 2048 + kc * 128 + 128],
                            B1[r0:r0 + 64, (h // 2) * T + lh * 1024 + n0:
                               (h // 2) * T + lh * 1024 + n1],
                            start=True, stop=True),
                            reads=[b2k(j), ("B1",)],
                            writes=[("psA", nb)])
                    if 0 <= diag < 1024:
                        P.v(lambda e, diag=diag: e.tensor_tensor(
                            psA[:, diag:diag + 128], psA[:, diag:diag + 128],
                            C.CT[:], ALU.add),
                            reads=[("psA", 0), ("psA", 1), ("CT",)],
                            writes=[("psA", 0), ("psA", 1)])
                    P.s(lambda e, off=off: e.activation(
                        WA[:, 6144 + off:6144 + 1024],
                        psA[:, off:1024], AF.Exp, scale=0.125),
                        reads=[("psA", 0), ("psA", 1), ("CC",)],
                        writes=[KW["stg"]])
                    for nb in range(2):
                        P.t(lambda e, h=h, r0=r0, kc=kc, nb=nb, nkc=nkc: e.matmul(
                            psB[r0:r0 + 33, nb * 512:(nb + 1) * 512],
                            B3[:, kc * 528 + h * 33: kc * 528 + h * 33 + 33],
                            WA[:, 6144 + nb * 512:6144 + (nb + 1) * 512],
                            start=(kc == 0), stop=(kc == nkc - 1)),
                            reads=[("B3", "v"), KW["stg"]],
                            writes=[("psB", nb)])
                P.v(lambda e, r0=r0: e.reciprocal(F2[0:1, 0:1024],
                                                  psB[r0 + 32:r0 + 33, 0:1024]),
                    reads=[("psB", 0), ("psB", 1)], writes=[("F2",)])
                for nb in range(2):
                    P.t(lambda e, r0=r0, nb=nb: e.matmul(
                        psA[r0:r0 + 32, 1024 + nb * 512:1024 + (nb + 1) * 512],
                        C.onesrowf[0:1, 0:32],
                        F2[0:1, nb * 512:(nb + 1) * 512],
                        start=True, stop=True),
                        reads=[("F2",), ("CF",)], writes=[("psA", 2 + nb)])
                P.s(lambda e, j=j, r0=r0, lh=lh: e.copy(
                    B2[r0:r0 + 32, j * 2048 + lh * 1024: j * 2048 + lh * 1024 + 1024],
                    psB[r0:r0 + 32, 0:1024]),
                    reads=[("psB", 0), ("psB", 1)], writes=[b2k(j)])
                P.v(lambda e, j=j, r0=r0, lh=lh: e.tensor_tensor(
                    B2[r0:r0 + 32, j * 2048 + lh * 1024: j * 2048 + lh * 1024 + 1024],
                    B2[r0:r0 + 32, j * 2048 + lh * 1024: j * 2048 + lh * 1024 + 1024],
                    psA[r0:r0 + 32, 1024:2048], ALU.mult),
                    reads=[("psA", 2), ("psA", 3), b2k(j)], writes=[b2k(j)])

    P.p(lambda e: e.memset(WA[:, 8192:16384], 0.0), writes=[KW["q"]])
    wo3 = W["Wo"].rearrange("(c p) n -> p c n", p=64)
    for h in range(16):
        j, r0 = h // 2, (h % 2) * 64
        c0 = 0
        while c0 < 1024:
            cw = min(2048, 1024 - c0)
            P.dma(F2[0:1, 0:cw] if False else F2[0:32, 0:cw],
                  wo3[0:32, h, c0:c0 + cw], reads=[("F2",)], writes=[("F2",)])
            P.p(lambda e, j=j, r0=r0, c0=c0, cw=cw: e.tensor_copy(
                WA[r0:r0 + 32, 8192 + j * 1024 + c0: 8192 + j * 1024 + c0 + cw],
                F2[0:32, 0:cw]),
                reads=[("F2",)], writes=[KW["q"]])
            c0 += cw
    for t in range(NT):
        for hf in range(2):
            for j in range(8):
                P.t(lambda e, t=t, hf=hf, j=j: e.matmul(
                    psA[:, hf * 512:hf * 512 + 512],
                    B2[:, j * 2048 + t * 128: j * 2048 + t * 128 + 128],
                    WA[:, 8192 + j * 1024 + hf * 512:
                       8192 + j * 1024 + hf * 512 + 512],
                    start=(j == 0), stop=(j == 7)),
                    reads=KB2 + [KW["q"]], writes=[("psA", hf)])
        P.v(lambda e, t=t: e.tensor_scalar(
            F1[:, 0:1024], X[:, t * 1024:(t + 1) * 1024],
            mu[:, t:t + 1], rstd[:, t:t + 1], ALU.subtract, ALU.mult),
            reads=[("X",), ("mu",), ("rstd",)], writes=[("F1", 0)])
        P.v(lambda e: e.tensor_tensor(F2[:, 0:1024], F1[:, 0:1024],
                                      psA[:, 0:1024], ALU.add),
            reads=[("F1", 0), ("psA", 0), ("psA", 1)], writes=[("F2",)])
        P.dma(x1v[:, t, :], F2[:, 0:1024], reads=[("F2",)], writes=[("x1",)])
        P.s(lambda e, t=t: e.activation(F1[:, 0:1024], F2[:, 0:1024],
                                        AF.Square, accum_out=ms2[:, t:t + 1]),
            reads=[("F2",)], writes=[("F1", 0), ("ms2",)])

    P.v(lambda e: e.tensor_scalar(rr2, ms2, 1.0 / 1024, None, ALU.mult),
        reads=[("ms2",)], writes=[("rr2",)])
    P.s(lambda e: e.activation(rr2, rr2, AF.Ln, bias=1e-5),
        reads=[("rr2",), ("CC",)], writes=[("rr2",)])
    P.s(lambda e: e.activation(rr2, rr2, AF.Exp, scale=-0.5),
        reads=[("rr2",), ("CC",)], writes=[("rr2",)])
    for cc in range(8):
        P.dma(gatew[:, cc * 8:(cc + 1) * 8],
              W["Wgate"][cc * 128:(cc + 1) * 128, :],
              reads=[("gw",)], writes=[("gw",)])
    for cc in range(8):
        P.v(lambda e, cc=cc: e.tensor_scalar(
            gatew[:, cc * 8:(cc + 1) * 8], gatew[:, cc * 8:(cc + 1) * 8],
            w2col[:, cc:cc + 1], None, ALU.mult),
            reads=[("gw",), ("w2c",)], writes=[("gw",)])
    P.dma(F2[0:1, 0:8], W["expert_bias"].unsqueeze(0), reads=[("F2",)],
          writes=[("F2",)])
    P.t(lambda e: e.matmul(psB[:, 0:8], C.onesrowf, F2[0:1, 0:8],
                           start=True, stop=True),
        reads=[("F2",), ("CF",)], writes=[("psB", 0)])
    P.v(lambda e: e.tensor_copy(ebias, psB[:, 0:8]),
        reads=[("psB", 0)], writes=[("ebias",)])
    for t in range(NT):
        P.dma(F2[:, 0:1024], x1v[:, t, :], reads=[("x1",), ("F2",)],
              writes=[("F2",)])
        P.v(lambda e, t=t: e.tensor_tensor(
            X[:, t * 1024:(t + 1) * 1024], X[:, t * 1024:(t + 1) * 1024],
            F2[:, 0:1024], ALU.add),
            reads=[("F2",), ("X",)], writes=[("X",)])
        P.v(lambda e, t=t: e.tensor_scalar(
            F2[:, 0:1024], F2[:, 0:1024], rr2[:, t:t + 1], None, ALU.mult),
            reads=[("F2",), ("rr2",)], writes=[("F2",)])
        P.v(lambda e, t=t: e.tensor_copy(B2[:, t * 1024:(t + 1) * 1024],
                                         F2[:, 0:1024]),
            reads=[("F2",)], writes=[b2k(t // 2)])
        for k in range(8):
            P.t(lambda e, k=k: e.transpose(
                psA[:, 1536 + (k % 4) * 128:1536 + (k % 4 + 1) * 128],
                F2[:, k * 128:(k + 1) * 128], C.CI),
                reads=[("F2",), ("CI",)], writes=[("psA", 3)])
            if k % 4 == 3:
                P.v(lambda e, k=k: e.tensor_copy(
                    F1[:, (k - 3) * 128:(k + 1) * 128], psA[:, 1536:2048]),
                    reads=[("psA", 3)], writes=[("F1", 0)])
        for dc in range(8):
            P.t(lambda e, t=t, dc=dc: e.matmul(
                psS[:, t * 8:(t + 1) * 8], F1[:, dc * 128:(dc + 1) * 128],
                gatew[:, dc * 8:(dc + 1) * 8],
                start=(dc == 0), stop=(dc == 7)),
                reads=[("F1", 0), ("gw",)], writes=[("psS",)])
    P.v(lambda e: e.tensor_tensor(
        logit.rearrange("p (t e) -> p t e", e=8),
        psS[:, 0:128].rearrange("p (t e) -> p t e", e=8),
        ebias.unsqueeze(1).broadcast_to([128, 16, 8]), ALU.add),
        reads=[("psS",), ("ebias",)], writes=[("logit",)])
    for t in range(NT):
        lt = logit[:, t * 8:(t + 1) * 8]
        m1 = S1[:, 432:433]
        m2 = S1[:, 433:434]
        s1 = S1[:, 434:435]
        s2 = S1[:, 435:436]
        msk1 = F2[:, 0:8]
        msk2 = F2[:, 8:16]
        P.v(lambda e, lt=lt: e.reduce_max(m1, lt, axis=AX.X),
            reads=[("logit",)], writes=[("m1",)])
        P.v(lambda e, lt=lt: e.tensor_scalar(msk1, lt, m1, None, ALU.is_ge),
            reads=[("logit",), ("m1",)], writes=[("F2",)])
        P.v(lambda e, lt=lt: e.scalar_tensor_tensor(
            F2[:, 16:24], msk1, -1e30, lt, ALU.mult, ALU.add),
            reads=[("logit",), ("F2",)], writes=[("F2",)])
        P.v(lambda e: e.reduce_max(m2, F2[:, 16:24], axis=AX.X),
            reads=[("F2",)], writes=[("m2",)])
        P.v(lambda e, lt=lt: e.tensor_scalar(msk2, lt, m2, None, ALU.is_ge),
            reads=[("logit",), ("m2",)], writes=[("F2",)])
        P.v(lambda e: e.tensor_tensor(s1, m1, m2, ALU.subtract),
            reads=[("m1",), ("m2",)], writes=[("s1",)])
        P.s(lambda e: e.activation(s1, s1, AF.Sigmoid),
            reads=[("s1",)], writes=[("s1",)])
        P.v(lambda e: e.tensor_scalar(s2, s1, -1.0, 1.0, ALU.mult, ALU.add),
            reads=[("s1",)], writes=[("s2",)])
        P.v(lambda e: e.tensor_tensor(msk2, msk2, msk1, ALU.subtract),
            reads=[("F2",)], writes=[("F2",)])
        P.v(lambda e, t=t: e.tensor_scalar(
            wsel[:, t * 8:(t + 1) * 8], msk1, s1, None, ALU.mult),
            reads=[("F2",), ("s1",)], writes=[("wsel",)])
        P.v(lambda e, t=t: e.scalar_tensor_tensor(
            wsel[:, t * 8:(t + 1) * 8], msk2, s2, wsel[:, t * 8:(t + 1) * 8],
            ALU.mult, ALU.add),
            reads=[("F2",), ("s2",), ("wsel",)], writes=[("wsel",)])
    tp_tok2feat(C, B2, B1, KB2, [("B1",)])

    for e_i in range(9):
        if e_i < 8:
            wg_d, wu_d, wd_d = W["Wg"][e_i], W["Wu"][e_i], W["Wd"][e_i]
        else:
            wg_d, wu_d, wd_d = W["sWg"][0], W["sWu"][0], W["sWd"][0]
        load_weight_bf16(C, WA, wg_d, [("WA", 0)], 1024, 512, dst_col0=0,
                         scale_col=w2col, key_scale=("w2c",))
        load_weight_bf16(C, WA, wu_d, [("WA", 0)], 1024, 512, dst_col0=4096,
                         scale_col=w2col, key_scale=("w2c",))
        load_weight_bf16(C, WA, wd_d, [("WA", 1)], 512, 1024, dst_col0=8192)
        _moe_expert(C, e_i, 8, wsel, ("wsel",))


def _moe_expert(C, e_i, n_routed, wsel, kwsel):
    P = C.P
    X, B1, B2, B3, WA = C.X, C.B1, C.B2, C.B3, C.WA
    psA, psB = C.psA, C.psB
    KPS4 = [("psA", h) for h in range(4)]
    KB2 = [("B2", j) for j in range(8)]
    for ic in range(4):
        for lp in range(4):
            for dc in range(ND):
                P.t(lambda e, ic=ic, lp=lp, dc=dc: e.matmul(
                    psA[:, lp * 512:(lp + 1) * 512],
                    WA[:, dc * 512 + ic * 128: dc * 512 + ic * 128 + 128],
                    B1[:, dc * T + lp * 512: dc * T + lp * 512 + 512],
                    start=(dc == 0), stop=(dc == ND - 1)),
                    reads=[("WA", 0), ("B1",)], writes=[("psA", lp)])
        P.s(lambda e, ic=ic: e.activation(B2[:, ic * T:(ic + 1) * T],
                                          psA[:, 0:2048], AF.Silu),
            reads=KPS4, writes=[("B2", ic)])
    for ic in range(4):
        for lp in range(4):
            for dc in range(ND):
                P.t(lambda e, ic=ic, lp=lp, dc=dc: e.matmul(
                    psA[:, lp * 512:(lp + 1) * 512],
                    WA[:, 4096 + dc * 512 + ic * 128:
                       4096 + dc * 512 + ic * 128 + 128],
                    B1[:, dc * T + lp * 512: dc * T + lp * 512 + 512],
                    start=(dc == 0), stop=(dc == ND - 1)),
                    reads=[("WA", 0), ("B1",)], writes=[("psA", lp)])
        P.v(lambda e, ic=ic: e.tensor_tensor(
            B3[:, ic * T:(ic + 1) * T], B2[:, ic * T:(ic + 1) * T],
            psA[:, 0:2048], ALU.mult),
            reads=KPS4 + [("B2", ic)], writes=[("B3",), ("B3", "v")])
    for t in range(NT):
        for h in range(2):
            for ic in range(4):
                P.t(lambda e, t=t, h=h, ic=ic: e.matmul(
                    psB[:, h * 512:(h + 1) * 512],
                    B3[:, ic * T + t * 128: ic * T + t * 128 + 128],
                    WA[:, 8192 + ic * 1024 + h * 512:
                       8192 + ic * 1024 + h * 512 + 512],
                    start=(ic == 0), stop=(ic == 3)),
                    reads=[("WA", 1), ("B3",)], writes=[("psB", h)])
        if e_i < n_routed:
            wcell = wsel[:, t * n_routed + e_i: t * n_routed + e_i + 1]
            P.v(lambda e, t=t, wcell=wcell: e.scalar_tensor_tensor(
                X[:, t * 1024:(t + 1) * 1024], psB[:, 0:1024], wcell,
                X[:, t * 1024:(t + 1) * 1024], ALU.mult, ALU.add),
                reads=[("psB", 0), ("psB", 1), kwsel, ("X",)], writes=[("X",)])
        else:
            P.v(lambda e, t=t: e.tensor_tensor(
                X[:, t * 1024:(t + 1) * 1024], X[:, t * 1024:(t + 1) * 1024],
                psB[:, 0:1024], ALU.add),
                reads=[("psB", 0), ("psB", 1), ("X",)], writes=[("X",)])


def _rope_rotate(C, RT, key):
    P = C.P
    ev = RT.rearrange("p (t i w) -> p t i w", i=16, w=2)[:, :, :, 0:1].squeeze(3)
    od = RT.rearrange("p (t i w) -> p t i w", i=16, w=2)[:, :, :, 1:2].squeeze(3)
    cos = C.CRC[:].rearrange("p (t i) -> p t i", i=16)
    sin = C.CRS[:].rearrange("p (t i) -> p t i", i=16)
    F2 = C.F2
    t1 = F2[:, 0:256].rearrange("p (t i) -> p t i", i=16)
    t2 = F2[:, 256:512].rearrange("p (t i) -> p t i", i=16)
    t3 = F2[:, 512:768].rearrange("p (t i) -> p t i", i=16)
    t4 = F2[:, 768:1024].rearrange("p (t i) -> p t i", i=16)
    P.v(lambda e: e.tensor_tensor(t1, ev, cos, ALU.mult),
        reads=[key, ("CRC",)], writes=[("F2",)])
    P.v(lambda e: e.tensor_tensor(t2, od, sin, ALU.mult),
        reads=[key, ("CRS",)], writes=[("F2",)])
    P.v(lambda e: e.tensor_tensor(t3, ev, sin, ALU.mult),
        reads=[key, ("CRS",)], writes=[("F2",)])
    P.v(lambda e: e.tensor_tensor(t4, od, cos, ALU.mult),
        reads=[key, ("CRC",)], writes=[("F2",)])
    P.v(lambda e: e.tensor_tensor(ev, t1, t2, ALU.subtract),
        reads=[("F2",)], writes=[key])
    P.v(lambda e: e.tensor_tensor(od, t3, t4, ALU.add),
        reads=[("F2",)], writes=[key])


def _rope_rotate_q(C, qv, sv, kq, ks, t):
    P = C.P
    F2 = C.F2
    qe = qv[:, :, 32:64].rearrange("p h (i w) -> p h i w", w=2)[:, :, :, 0:1].squeeze(3)
    qo = qv[:, :, 32:64].rearrange("p h (i w) -> p h i w", w=2)[:, :, :, 1:2].squeeze(3)
    se = sv[:, :, 32:64].rearrange("p h (i w) -> p h i w", w=2)[:, :, :, 0:1].squeeze(3)
    so = sv[:, :, 32:64].rearrange("p h (i w) -> p h i w", w=2)[:, :, :, 1:2].squeeze(3)
    cos = C.CRC[:, t * 16:(t + 1) * 16].unsqueeze(1).broadcast_to([128, 16, 16])
    sin = C.CRS[:, t * 16:(t + 1) * 16].unsqueeze(1).broadcast_to([128, 16, 16])
    t1 = F2[:, 0:256].rearrange("p (h i) -> p h i", i=16)
    t2 = F2[:, 256:512].rearrange("p (h i) -> p h i", i=16)
    t3 = F2[:, 512:768].rearrange("p (h i) -> p h i", i=16)
    t4 = F2[:, 768:1024].rearrange("p (h i) -> p h i", i=16)
    P.v(lambda e: e.tensor_tensor(t1, qe, cos, ALU.mult),
        reads=kq + [("CRC",)], writes=[("F2",)])
    P.v(lambda e: e.tensor_tensor(t2, qo, sin, ALU.mult),
        reads=kq + [("CRS",)], writes=[("F2",)])
    P.v(lambda e: e.tensor_tensor(t3, qe, sin, ALU.mult),
        reads=kq + [("CRS",)], writes=[("F2",)])
    P.v(lambda e: e.tensor_tensor(t4, qo, cos, ALU.mult),
        reads=kq + [("CRC",)], writes=[("F2",)])
    P.v(lambda e: e.tensor_tensor(se, t1, t2, ALU.subtract),
        reads=[("F2",)], writes=[ks])
    P.v(lambda e: e.tensor_tensor(so, t3, t4, ALU.add),
        reads=[("F2",)], writes=[ks])




def emit_l2_attn(C, W):
    P = C.P
    nc = C.nc
    X, B1, B2, B3, WA = C.X, C.B1, C.B2, C.B3, C.WA
    S1, F1, F2 = C.S1, C.F1, C.F2
    psA, psB, psS, psT = C.psA, C.psB, C.psS, C.psT

    mu = S1[:, 0:16]
    rstd = S1[:, 16:32]
    Ain = S1[:, 32:64]
    Aex = S1[:, 64:96]
    ecol = S1[:, 96:128]
    wdt = S1[:, 128:160]
    A127 = S1[:, 160:192]
    CW = S1[:, 192:272]
    CBc = S1[:, 272:290]
    DB = S1[:, 416:448]
    dtbB = S1[:, 448:480]
    negA = S1[:, 480:512]
    dt_raw = S1[:, 512:1024]
    dt_sp = S1[:, 1024:1536]
    a_all = S1[:, 1536:2048]
    SEG = 512
    NSEG = 4
    NCH = 18
    xmd = nc.dram_tensor("l2_xm", [T, D], F32, kind="Internal").ap()
    xmv = xmd.rearrange("(t p) f -> p t f", p=128)

    B_f = B3[:, 0:512]
    C_f = B3[:, 512:1024]
    Btk = B3[:, 1024:1152]
    XT = B3[:, 2048:4096]
    DTX = B3[:, 4096:6144]
    SPV = B3[:, 6144:8192]
    GSL = B3[:, 8192:10240]

    KB2 = ("B2",)
    KX = [("X",)]

    ln_stats(C, X, mu, rstd, KX, ("mu",), ("rstd",))
    for t in range(NT):
        P.v(lambda e, t=t: e.tensor_scalar(
            B2[:, t * 1024:(t + 1) * 1024], X[:, t * 1024:(t + 1) * 1024],
            mu[:, t:t + 1], rstd[:, t:t + 1], ALU.subtract, ALU.mult),
            reads=KX + [("mu",), ("rstd",)], writes=[KB2])
    tp_tok2feat(C, B2, B1, [KB2], [("B1",)])

    def col_cells(wname, dst, key, n, scale=1.0):
        P.dma(F2[0:1, 0:n], W[wname].unsqueeze(0), reads=[("F2",)],
              writes=[("F2",)])
        P.t(lambda e, n=n: e.matmul(psB[:, 0:n], C.onesrowf, F2[0:1, 0:n],
                                    start=True, stop=True),
            reads=[("F2",), ("CF",)], writes=[("psB", 0)])
        P.v(lambda e, dst=dst, n=n, scale=scale: e.tensor_scalar(
            dst, psB[:, 0:n], scale, None, ALU.mult),
            reads=[("psB", 0)], writes=[key])

    for d_i, sfx in ((0, "f"), (1, "b")):
        KWI = ("WA", 0)
        KWO = ("WA", 0)
        col_cells("dtb_" + sfx, dtbB, ("dtbB",), 32)
        col_cells("D_" + sfx, DB, ("DB",), 32)
        col_cells("Alog_" + sfx, negA, ("negA",), 32)
        P.s(lambda e: e.activation(negA, negA, AF.Exp),
            reads=[("negA",)], writes=[("negA",)])
        P.v(lambda e: e.tensor_scalar(negA, negA, -1.0, None, ALU.mult),
            reads=[("negA",)], writes=[("negA",)])
        for j in range(4):
            for half in range(2):
                oc_lo, oc_hi = half * 9, min(NCH, half * 9 + 9)
                P.dma(F2[0:1, 0:1152],
                      W["conv_" + sfx][j:j + 1, half * 1152:half * 1152 + 1152],
                      reads=[("F2",)], writes=[("F2",)])
                for oc in range(oc_lo, oc_hi):
                    P.t(lambda e, oc=oc, j=j, half=half: e.matmul(
                        psB[:, oc * 4 + j:oc * 4 + j + 1],
                        F2[0:1, (oc - half * 9) * 128:(oc - half * 9 + 1) * 128],
                        C.onecellf, start=True, stop=True),
                        reads=[("F2",), ("CF",)], writes=[("psB", 0)])
        P.v(lambda e: e.tensor_copy(CW[:, 0:72], psB[:, 0:72]),
            reads=[("psB", 0)], writes=[("CW",)])
        for half in range(2):
            oc_lo, oc_hi = half * 9, min(NCH, half * 9 + 9)
            P.dma(F2[0:1, 0:1152],
                  W["convb_" + sfx].unsqueeze(0)[:, half * 1152:half * 1152 + 1152],
                  reads=[("F2",)], writes=[("F2",)])
            for oc in range(oc_lo, oc_hi):
                P.t(lambda e, oc=oc, half=half: e.matmul(
                    psB[:, oc:oc + 1],
                    F2[0:1, (oc - half * 9) * 128:(oc - half * 9 + 1) * 128],
                    C.onecellf, start=True, stop=True),
                    reads=[("F2",), ("CF",)], writes=[("psB", 0)])
        P.v(lambda e: e.tensor_copy(CBc[:, 0:NCH], psB[:, 0:NCH]),
            reads=[("psB", 0)], writes=[("CBc",)])
        load_weight_bf16(C, WA, W["Wi_" + sfx][:, 4352:4384], [KWI], 1024, 32)
        for t in range(NT):
            for dc in range(ND):
                P.t(lambda e, t=t, dc=dc: e.matmul(
                    psS[:, t * 32:(t + 1) * 32],
                    B1[:, dc * T + t * 128: dc * T + t * 128 + 128],
                    WA[:, dc * 32:(dc + 1) * 32],
                    start=(dc == 0), stop=(dc == ND - 1)),
                    reads=[("B1",), KWI], writes=[("psS",)])
        P.v(lambda e: e.tensor_tensor(
            dt_raw.rearrange("p (c h) -> p c h", h=32),
            psS[:, 0:512].rearrange("p (c h) -> p c h", h=32),
            dtbB.unsqueeze(1).broadcast_to([128, 16, 32]), ALU.add),
            reads=[("psS",), ("dtbB",)], writes=[("dtr",)])
        P.s(lambda e: e.activation(F2[:, 0:512], dt_raw, AF.Abs),
            reads=[("dtr",)], writes=[("F2",)])
        P.s(lambda e: e.activation(F2[:, 0:512], F2[:, 0:512], AF.Exp,
                                   scale=-1.0),
            reads=[("F2",), ("CC",)], writes=[("F2",)])
        P.s(lambda e: e.activation(F2[:, 0:512], F2[:, 0:512], AF.Ln, bias=1.0),
            reads=[("F2",)], writes=[("F2",)])
        P.v(lambda e: e.tensor_scalar(F1[:, 0:512], dt_raw, 0.0, None, ALU.max),
            reads=[("dtr",)], writes=[("F1", 0)])
        P.v(lambda e: e.tensor_tensor(dt_sp, F1[:, 0:512], F2[:, 0:512],
                                      ALU.add),
            reads=[("F1", 0), ("F2",)], writes=[("dtsp",)])
        P.v(lambda e: e.tensor_tensor(
            a_all.rearrange("p (c h) -> p c h", h=32),
            dt_sp.rearrange("p (c h) -> p c h", h=32),
            negA.unsqueeze(1).broadcast_to([128, 16, 32]), ALU.mult),
            reads=[("dtsp",), ("negA",)], writes=[("aall",)])

        load_weight_bf16(C, WA, W["Wo_" + sfx], [KWO], 2048, 1024)

        seg_order = range(NSEG) if d_i == 0 else range(NSEG - 1, -1, -1)
        P.v(lambda e: e.memset(SPV[:], 0.0), writes=[("SPV",)])
        for si, sg in enumerate(seg_order):
            s0 = sg * SEG
            for oc in range(NCH + 16):
                is_z = oc >= NCH
                wcol = (2048 + oc * 128) if oc < NCH else (oc - NCH) * 128
                slot = 8192 + (oc % 2) * 1024
                KSL = ("GSL",)
                load_weight_bf16(C, B3, W["Wi_" + sfx][:, wcol:wcol + 128],
                                 [KSL], 1024, 128, dst_col0=slot)
                for dc in range(ND):
                    P.t(lambda e, oc=oc, dc=dc, slot=slot, s0=s0: e.matmul(
                        psA[:, 0:512],
                        B3[:, slot + dc * 128: slot + dc * 128 + 128],
                        B1[:, dc * T + s0: dc * T + s0 + SEG],
                        start=(dc == 0), stop=(dc == ND - 1)),
                        reads=[("B1",), KSL], writes=[("psA", 0)])
                if is_z:
                    zc = oc - NCH
                    P.s(lambda e, zc=zc: e.copy(
                        B2[:, 8192 + zc * 512:8192 + (zc + 1) * 512],
                        psA[:, 0:512]),
                        reads=[("psA", 0)], writes=[("B2z",), ("B2",)])
                    continue
                P.s(lambda e: e.copy(F1[:, 0:512], psA[:, 0:512]),
                    reads=[("psA", 0)], writes=[("F1", 0)])
                HL = S1[:, 290:290 + 3]
                first_seg = (si == 0)
                hl0 = 290 + oc * 3
                P.v(lambda e, oc=oc: e.tensor_scalar(
                    F2[:, 0:512], F1[:, 0:512], CW[:, oc * 4 + 3:oc * 4 + 4],
                    None, ALU.mult),
                    reads=[("F1", 0), ("CW",)], writes=[("F2",)])
                if d_i == 0:
                    for o in range(1, 4):
                        P.v(lambda e, oc=oc, o=o: e.scalar_tensor_tensor(
                            F2[:, o:512], F1[:, 0:512 - o],
                            CW[:, oc * 4 + 3 - o:oc * 4 + 4 - o],
                            F2[:, o:512], ALU.mult, ALU.add),
                            reads=[("F1", 0), ("CW",), ("F2",)], writes=[("F2",)])
                        if not first_seg:
                            P.v(lambda e, oc=oc, o=o, hl0=hl0: e.scalar_tensor_tensor(
                                F2[:, 0:o], S1[:, hl0 + 3 - o:hl0 + 3],
                                CW[:, oc * 4 + 3 - o:oc * 4 + 4 - o],
                                F2[:, 0:o], ALU.mult, ALU.add),
                                reads=[("HLa",), ("F2",), ("CW",)], writes=[("F2",)])
                else:
                    for o in range(1, 4):
                        P.v(lambda e, oc=oc, o=o: e.scalar_tensor_tensor(
                            F2[:, 0:512 - o], F1[:, o:512],
                            CW[:, oc * 4 + 3 - o:oc * 4 + 4 - o],
                            F2[:, 0:512 - o], ALU.mult, ALU.add),
                            reads=[("F1", 0), ("CW",), ("F2",)], writes=[("F2",)])
                        if not first_seg:
                            P.v(lambda e, oc=oc, o=o, hl0=hl0: e.scalar_tensor_tensor(
                                F2[:, 512 - o:512], S1[:, hl0:hl0 + o],
                                CW[:, oc * 4 + 3 - o:oc * 4 + 4 - o],
                                F2[:, 512 - o:512], ALU.mult, ALU.add),
                                reads=[("HLa",), ("F2",), ("CW",)], writes=[("F2",)])
                if si < NSEG - 1:
                    if d_i == 0:
                        P.v(lambda e, oc=oc: e.tensor_copy(
                            S1[:, 290 + oc * 3:290 + oc * 3 + 3],
                            F1[:, 509:512]),
                            reads=[("F1", 0)], writes=[("HLa",)])
                    else:
                        P.v(lambda e, oc=oc: e.tensor_copy(
                            S1[:, 290 + oc * 3:290 + oc * 3 + 3],
                            F1[:, 0:3]),
                            reads=[("F1", 0)], writes=[("HLa",)])
                dst = (B2[:, oc * 512:(oc + 1) * 512] if oc < 16 else
                       (B_f if oc == 16 else C_f))
                dks = ([("B2x",), ("B2",)] if oc < 16 else
                       ([("Bf",)] if oc == 16 else [("Cf",)]))
                P.s(lambda e, oc=oc, dst=dst: e.activation(
                    dst, F2[:, 0:512], AF.Silu, bias=CBc[:, oc:oc + 1]),
                    reads=[("F2",), ("CBc",)], writes=dks)

            ch_order = range(4) if d_i == 0 else range(3, -1, -1)
            for c_l in ch_order:
                cg = sg * 4 + c_l
                tc0 = c_l * 128
                KA = ("Ain",)
                P.t(lambda e, cg=cg: e.matmul(
                    psS[:, 0:32], C.CUF, a_all[:, cg * 32:(cg + 1) * 32],
                    start=True, stop=True),
                    reads=[("aall",), ("CUF",)], writes=[("psS",)])
                P.t(lambda e, cg=cg: e.matmul(
                    psS[:, 32:64], C.CF[:, 0:128],
                    a_all[:, cg * 32:(cg + 1) * 32],
                    start=True, stop=True),
                    reads=[("aall",), ("CF",)], writes=[("psS",)])
                P.v(lambda e: e.tensor_copy(Ain, psS[:, 0:32]),
                    reads=[("psS",)], writes=[KA])
                P.v(lambda e: e.tensor_copy(A127, psS[:, 32:64]),
                    reads=[("psS",)], writes=[("A127",)])
                P.v(lambda e, cg=cg: e.tensor_tensor(
                    Aex, Ain, a_all[:, cg * 32:(cg + 1) * 32], ALU.subtract),
                    reads=[KA, ("aall",)], writes=[("Aex",)])
                if d_i == 0:
                    P.s(lambda e: e.activation(ecol, Ain, AF.Exp),
                        reads=[KA], writes=[("ecol",)])
                    P.v(lambda e: e.tensor_tensor(F2[:, 0:32], A127, Ain,
                                                  ALU.subtract),
                        reads=[("A127",), KA], writes=[("F2",)])
                    P.s(lambda e: e.activation(F2[:, 0:32], F2[:, 0:32], AF.Exp),
                        reads=[("F2",)], writes=[("F2",)])
                else:
                    P.v(lambda e: e.tensor_tensor(F2[:, 32:64], A127, Aex,
                                                  ALU.subtract),
                        reads=[("A127",), ("Aex",)], writes=[("F2",)])
                    P.s(lambda e: e.activation(ecol, F2[:, 32:64], AF.Exp),
                        reads=[("F2",)], writes=[("ecol",)])
                    P.s(lambda e: e.activation(F2[:, 0:32], Aex, AF.Exp),
                        reads=[("Aex",)], writes=[("F2",)])
                P.v(lambda e, cg=cg: e.tensor_tensor(
                    wdt, F2[:, 0:32], dt_sp[:, cg * 32:(cg + 1) * 32], ALU.mult),
                    reads=[("F2",), ("dtsp",)], writes=[("wdt",)])
                for g in range(2):
                    tp_group(C, lambda k, g=g, tc0=tc0: B2[:, (g * 8 + k) * 512
                                                           + tc0:(g * 8 + k) * 512 + tc0 + 128],
                             XT, g * 1024, 8, [("B2x",)], [("XT",)])
                P.v(lambda e, cg=cg: e.tensor_tensor(
                    DTX.rearrange("p (h z) -> p h z", h=32),
                    XT.rearrange("p (h z) -> p h z", h=32),
                    dt_sp[:, cg * 32:(cg + 1) * 32].unsqueeze(2)
                    .broadcast_to([128, 32, 64]), ALU.mult),
                    reads=[("XT",), ("dtsp",)], writes=[("DTX",)])
                P.t(lambda e, tc0=tc0: e.matmul(
                    psS[:, 128:256], B_f[:, tc0:tc0 + 128],
                    C_f[:, tc0:tc0 + 128], start=True, stop=True),
                    reads=[("Bf",), ("Cf",)], writes=[("psS",)])
                P.v(lambda e: e.tensor_copy(B3[:, 1152:1280], psS[:, 128:256]),
                    reads=[("psS",)], writes=[("SC",)])
                src = Ain if d_i == 0 else Aex
                P.t(lambda e, src=src: e.transpose(
                    C.psA[0:32, 1536:1664], src, C.CI),
                    reads=[("Ain",), ("Aex",), ("CI",)], writes=[("psA", 3)])
                P.v(lambda e: e.tensor_copy(F1[0:32, 0:128],
                                            psA[0:32, 1536:1664]),
                    reads=[("psA", 3)], writes=[("F1", 0)])
                for hg in range(2):
                    P.dma(F2[0:1, 0:2048], F1[hg * 16:(hg + 1) * 16, 0:128],
                          reads=[("F1", 0), ("F2",)], writes=[("F2",)])
                    for nb in range(4):
                        P.t(lambda e, nb=nb: e.matmul(
                            psA[:, nb * 512:(nb + 1) * 512], C.onesrowf,
                            F2[0:1, nb * 512:(nb + 1) * 512],
                            start=True, stop=True),
                            reads=[("F2",), ("CF",)], writes=[("psA", nb)])
                    KPS4 = [("psA", n) for n in range(4)]
                    cv = src[:, hg * 16:(hg + 1) * 16].unsqueeze(2) \
                        .broadcast_to([128, 16, 128])
                    if d_i == 0:
                        P.v(lambda e, cv=cv: e.tensor_tensor(
                            F2[:, 0:2048].rearrange("p (h t) -> p h t", h=16),
                            psA[:, 0:2048].rearrange("p (h t) -> p h t", h=16),
                            cv, ALU.subtract),
                            reads=KPS4 + [("Ain",), ("Aex",)], writes=[("F2",)])
                    else:
                        P.v(lambda e, cv=cv: e.tensor_tensor(
                            F2[:, 0:2048].rearrange("p (h t) -> p h t", h=16),
                            cv,
                            psA[:, 0:2048].rearrange("p (h t) -> p h t", h=16),
                            ALU.subtract),
                            reads=KPS4 + [("Ain",), ("Aex",)], writes=[("F2",)])
                    msk = C.CT if d_i == 0 else C.CTU
                    P.v(lambda e, msk=msk: e.tensor_tensor(
                        F2[:, 0:2048].rearrange("p (h t) -> p h t", h=16),
                        F2[:, 0:2048].rearrange("p (h t) -> p h t", h=16),
                        msk[:].unsqueeze(1).broadcast_to([128, 16, 128]),
                        ALU.add),
                        reads=[("F2",), ("CT",), ("CTU",)], writes=[("F2",)])
                    P.s(lambda e: e.activation(GSL[:, 0:2048], F2[:, 0:2048],
                                               AF.Exp),
                        reads=[("F2",)], writes=[("GSL",)])
                    P.v(lambda e: e.tensor_tensor(
                        GSL[:, 0:2048].rearrange("p (h t) -> p h t", h=16),
                        GSL[:, 0:2048].rearrange("p (h t) -> p h t", h=16),
                        B3[:, 1152:1280].unsqueeze(1).broadcast_to([128, 16, 128]),
                        ALU.mult),
                        reads=[("GSL",), ("SC",)], writes=[("GSL",)])
                    for nb in range(2):
                        P.t(lambda e, hg=hg, nb=nb, tc0=tc0: e.matmul(
                            psB[:, nb * 512:(nb + 1) * 512],
                            C_f[:, tc0:tc0 + 128],
                            SPV[:, hg * 1024 + nb * 512: hg * 1024 + (nb + 1) * 512],
                            start=True, stop=True),
                            reads=[("Cf",), ("SPV",)], writes=[("psB", nb)])
                    P.v(lambda e, hg=hg: e.tensor_tensor(
                        F2[:, 0:1024].rearrange("p (h z) -> p h z", h=16),
                        psB[:, 0:1024].rearrange("p (h z) -> p h z", h=16),
                        ecol[:, hg * 16:(hg + 1) * 16].unsqueeze(2)
                        .broadcast_to([128, 16, 64]), ALU.mult),
                        reads=[("psB", 0), ("psB", 1), ("ecol",)],
                        writes=[("F2",)])
                    for hl in range(16):
                        h = hg * 16 + hl
                        P.t(lambda e, h=h, hl=hl: e.matmul(
                            psB[:, hl * 64:(hl + 1) * 64],
                            GSL[:, hl * 128:(hl + 1) * 128],
                            DTX[:, h * 64:(h + 1) * 64],
                            start=True, stop=True),
                            reads=[("GSL",), ("DTX",)], writes=[("psB", hl // 8)])
                    P.v(lambda e: e.tensor_tensor(
                        F2[:, 1024:2048], F2[:, 0:1024], psB[:, 0:1024],
                        ALU.add),
                        reads=[("psB", 0), ("psB", 1), ("F2",)], writes=[("F2",)])
                    P.v(lambda e, hg=hg: e.tensor_tensor(
                        F1[:, 1024:2048].rearrange("p (h z) -> p h z", h=16),
                        XT[:, hg * 1024:(hg + 1) * 1024]
                        .rearrange("p (h z) -> p h z", h=16),
                        DB[:, hg * 16:(hg + 1) * 16].unsqueeze(2)
                        .broadcast_to([128, 16, 64]), ALU.mult),
                        reads=[("XT",), ("DB",)], writes=[("F1", 1)])
                    P.v(lambda e: e.tensor_tensor(
                        F2[:, 1024:2048], F2[:, 1024:2048], F1[:, 1024:2048],
                        ALU.add),
                        reads=[("F2",), ("F1", 1)], writes=[("F2",)])
                    P.v(lambda e: e.tensor_copy(GSL[:, 0:1024], F2[:, 1024:2048]),
                        reads=[("F2",)], writes=[("GSL",)])
                    P.s(lambda e, hg=hg, tc0=tc0: e.activation(
                        F1[:, 1024:2048].rearrange("p (ch x) -> p ch x", ch=8),
                        B2[:].rearrange("p (ch x) -> p ch x", ch=32)
                        [:, 16 + hg * 8:16 + (hg + 1) * 8, tc0:tc0 + 128],
                        AF.Silu),
                        reads=[("B2z",)], writes=[("F1", 1)])
                    for k in range(8):
                        P.t(lambda e, k=k: e.transpose(
                            psT[:, k * 128:(k + 1) * 128],
                            GSL[:, k * 128:(k + 1) * 128], C.identb),
                            reads=[("GSL",), ("CB",)], writes=[("psT", k)])
                    for k in range(8):
                        P.v(lambda e, k=k: e.tensor_tensor(
                            GSL[:, 1024 + k * 128:1024 + (k + 1) * 128],
                            psT[:, k * 128:(k + 1) * 128],
                            F1[:, 1024 + k * 128:1024 + (k + 1) * 128], ALU.mult),
                            reads=[("psT", k), ("F1", 1)], writes=[("GSL", 1)])
                    for hf in range(2):
                        for k in range(8):
                            P.t(lambda e, hg=hg, hf=hf, k=k: e.matmul(
                                psA[:, hf * 512:(hf + 1) * 512],
                                GSL[:, 1024 + k * 128:1024 + (k + 1) * 128],
                                WA[:, (hg * 8 + k) * 1024 + hf * 512:
                                   (hg * 8 + k) * 1024 + hf * 512 + 512],
                                start=(k == 0), stop=(k == 7)),
                                reads=[("GSL", 1), KWO], writes=[("psA", hf)])
                    first = (d_i == 0 and hg == 0)
                    tgl = cg
                    if first:
                        P.v(lambda e: e.tensor_copy(F2[:, 0:1024], psA[:, 0:1024]),
                            reads=[("psA", 0), ("psA", 1)], writes=[("F2",)])
                    else:
                        P.dma(F2[:, 0:1024], xmv[:, tgl, :],
                              reads=[("xm",), ("F2",)], writes=[("F2",)])
                        P.v(lambda e: e.tensor_tensor(
                            F2[:, 0:1024], F2[:, 0:1024], psA[:, 0:1024],
                            ALU.add),
                            reads=[("psA", 0), ("psA", 1), ("F2",)],
                            writes=[("F2",)])
                    P.dma(xmv[:, tgl, :], F2[:, 0:1024], reads=[("F2",)],
                          writes=[("xm",)])
                P.v(lambda e, cg=cg: e.tensor_tensor(
                    DTX.rearrange("p (h z) -> p h z", h=32),
                    XT.rearrange("p (h z) -> p h z", h=32),
                    wdt.unsqueeze(2).broadcast_to([128, 32, 64]), ALU.mult),
                    reads=[("XT",), ("wdt",)], writes=[("DTX",)])
                P.t(lambda e, tc0=tc0: e.transpose(
                    psT[:, 0:128], B_f[:, tc0:tc0 + 128], C.identb),
                    reads=[("Bf",), ("CB",)], writes=[("psT", 0)])
                P.v(lambda e: e.tensor_copy(Btk, psT[:, 0:128]),
                    reads=[("psT", k) for k in range(8)], writes=[("Btk",)])
                P.s(lambda e: e.activation(F2[:, 0:32], A127, AF.Exp),
                    reads=[("A127",)], writes=[("F2",)])
                P.v(lambda e: e.tensor_tensor(
                    SPV[:].rearrange("p (h z) -> p h z", h=32),
                    SPV[:].rearrange("p (h z) -> p h z", h=32),
                    F2[:, 0:32].unsqueeze(2).broadcast_to([128, 32, 64]),
                    ALU.mult),
                    reads=[("SPV",), ("F2",)], writes=[("SPV",)])
                for nb in range(4):
                    P.t(lambda e, nb=nb: e.matmul(
                        psA[:, nb * 512:(nb + 1) * 512], Btk,
                        DTX[:, nb * 512:(nb + 1) * 512],
                        start=True, stop=True),
                        reads=[("Btk",), ("DTX",)], writes=[("psA", nb)])
                P.v(lambda e: e.tensor_tensor(SPV[:], SPV[:], psA[:, 0:2048],
                                              ALU.add),
                    reads=[("SPV",)] + [("psA", n) for n in range(4)],
                    writes=[("SPV",)])

    load_weight_bf16(C, WA, W["Wm"], [("WA", 0)], 1024, 1024)
    for t in range(NT):
        P.dma(F2[:, 0:1024], xmv[:, t, :], reads=[("xm",), ("F2",)],
              writes=[("F2",)])
        P.v(lambda e: e.tensor_copy(GSL[:, 0:1024], F2[:, 0:1024]),
            reads=[("F2",)], writes=[("GSL",)])
        for k in range(8):
            P.t(lambda e, k=k: e.transpose(psT[:, k * 128:(k + 1) * 128],
                                           GSL[:, k * 128:(k + 1) * 128],
                                           C.identb),
                reads=[("GSL",), ("CB",)], writes=[("psT", k)])
        P.v(lambda e, t=t: e.tensor_copy(
            B1[:].rearrange("p (d l) -> p d l", d=8)[:, :, t * 128:(t + 1) * 128],
            psT[:, 0:1024].rearrange("p (d x) -> p d x", d=8)),
            reads=[("psT", k) for k in range(8)], writes=[("B1",)])
    bob = F1[:, 1024:2048]
    bcast_bias(C, W["bm"], bob, [("F1", 1)], 1024)
    for t in range(NT):
        for hf in range(2):
            for dc in range(ND):
                P.t(lambda e, t=t, hf=hf, dc=dc: e.matmul(
                    psA[:, hf * 512:(hf + 1) * 512],
                    B1[:, dc * T + t * 128: dc * T + t * 128 + 128],
                    WA[:, dc * 1024 + hf * 512: dc * 1024 + hf * 512 + 512],
                    start=(dc == 0), stop=(dc == ND - 1)),
                    reads=[("B1",), ("WA", 0)], writes=[("psA", hf)])
        P.v(lambda e: e.tensor_tensor(F1[:, 0:1024], psA[:, 0:1024], bob,
                                      ALU.add),
            reads=[("psA", 0), ("psA", 1), ("F1", 1)], writes=[("F1", 0)])
        P.s(lambda e: e.activation(F1[:, 0:1024], F1[:, 0:1024], AF.Tanh),
            reads=[("F1", 0)], writes=[("F1", 0)])
        P.dma(F2[:, 0:1024], xmv[:, t, :], reads=[("xm",), ("F2",)],
              writes=[("F2",)])
        P.v(lambda e: e.scalar_tensor_tensor(F2[:, 0:1024], F1[:, 0:1024],
                                             0.1, F2[:, 0:1024],
                                             ALU.mult, ALU.add),
            reads=[("F1", 0), ("F2",)], writes=[("F2",)])
        P.v(lambda e, t=t: e.tensor_tensor(
            X[:, t * 1024:(t + 1) * 1024], X[:, t * 1024:(t + 1) * 1024],
            F2[:, 0:1024], ALU.add),
            reads=[("F2",), ("X",)], writes=[("X",)])




def emit_l2_ffn(C, W):
    P = C.P
    X, B1, B2, B3, WA = C.X, C.B1, C.B2, C.B3, C.WA
    S1, F1, F2 = C.S1, C.F1, C.F2
    psA, psB, psS = C.psA, C.psB, C.psS

    mu = S1[:, 0:16]
    rstd = S1[:, 16:32]
    brB = S1[:, 32:40]
    logit = S1[:, 512:640]
    Xs = S1[:, 640:768]
    gates = S1[:, 768:896]
    tau = S1[:, 896:912]
    flo = S1[:, 912:928]
    mx = S1[:, 928:944]
    b1c = S1[:, 944:976]
    b2B = S1[:, 1024:2048]
    KV3 = ("B2", 0)

    ln_stats(C, X, mu, rstd, [("X",)], ("mu",), ("rstd",))
    for t in range(NT):
        P.v(lambda e, t=t: e.tensor_scalar(
            B2[:, t * 1024:(t + 1) * 1024], X[:, t * 1024:(t + 1) * 1024],
            mu[:, t:t + 1], rstd[:, t:t + 1], ALU.subtract, ALU.mult),
            reads=[("X",), ("mu",), ("rstd",)], writes=[KV3])
    tp_tok2feat(C, B2, B1, [KV3], [("B1",)])

    load_weight_bf16(C, B3, W["Wr"], [("B3", "wr")], 1024, 8)
    bcast_bias(C, W["br"], brB, [("brB",)], 8)
    for t in range(NT):
        for dc in range(ND):
            P.t(lambda e, t=t, dc=dc: e.matmul(
                psS[:, t * 8:(t + 1) * 8],
                B1[:, dc * T + t * 128: dc * T + t * 128 + 128],
                B3[:, dc * 8:(dc + 1) * 8],
                start=(dc == 0), stop=(dc == ND - 1)),
                reads=[("B1",), ("B3", "wr")], writes=[("psS",)])
    P.v(lambda e: e.tensor_tensor(
        logit.rearrange("p (t e) -> p t e", e=8),
        psS[:, 0:128].rearrange("p (t e) -> p t e", e=8),
        brB.unsqueeze(1).broadcast_to([128, 16, 8]), ALU.add),
        reads=[("psS",), ("brB",)], writes=[("logit",)])

    L3v = lambda ap: ap.rearrange("p (t e) -> p t e", e=8)
    P.v(lambda e: e.tensor_scalar(Xs, logit, 0.5, None, ALU.mult),
        reads=[("logit",)], writes=[("Xs",)])
    P.v(lambda e: e.reduce_max(mx, L3v(Xs), axis=AX.X),
        reads=[("Xs",)], writes=[("mx",)])
    P.v(lambda e: e.tensor_scalar(tau, mx, -1.0, None, ALU.add),
        reads=[("mx",)], writes=[("tau",)])

    def fval(dst_f, tau_ap, kt, kdst):
        P.v(lambda e, tau_ap=tau_ap: e.tensor_tensor(
            L3v(F2[:, 0:128]), L3v(Xs),
            tau_ap.unsqueeze(2).broadcast_to([128, 16, 8]), ALU.subtract),
            reads=[("Xs",), kt], writes=[("F2",)])
        P.v(lambda e: e.tensor_scalar(F2[:, 0:128], F2[:, 0:128], 0.0, None,
                                      ALU.max),
            reads=[("F2",)], writes=[("F2",)])
        P.v(lambda e: e.tensor_tensor(F2[:, 128:256], F2[:, 0:128],
                                      F2[:, 0:128], ALU.mult),
            reads=[("F2",)], writes=[("F2",)])
        P.v(lambda e, dst_f=dst_f, kdst=kdst: e.reduce_sum(
            dst_f, L3v(F2[:, 128:256]), axis=AX.X),
            reads=[("F2",)], writes=[kdst])
        P.v(lambda e, dst_f=dst_f, kdst=kdst: e.tensor_scalar(
            dst_f, dst_f, -1.0, None, ALU.add),
            reads=[kdst], writes=[kdst])
    fval(flo, tau, ("tau",), ("flo",))
    dm = 1.0 - 8.0 ** -0.5
    fm = S1[:, 976:992]
    for it in range(25):
        dm *= 0.5
        if it == 24:
            P.v(lambda e, dm=dm: e.tensor_scalar(tau, tau, dm, None, ALU.add),
                reads=[("tau",)], writes=[("tau",)])
            break
        P.v(lambda e, dm=dm: e.tensor_scalar(S1[:, 992:1008], tau, dm, None,
                                             ALU.add),
            reads=[("tau",)], writes=[("taum",)])
        fval(fm, S1[:, 992:1008], ("taum",), ("fm",))
        P.v(lambda e: e.tensor_tensor(fm, fm, flo, ALU.mult),
            reads=[("fm",), ("flo",)], writes=[("fm2",)])
        P.v(lambda e: e.tensor_scalar(fm, fm, 0.0, None, ALU.is_ge),
            reads=[("fm2",)], writes=[("fm2",)])
        P.v(lambda e, dm=dm: e.scalar_tensor_tensor(tau, fm, dm, tau,
                                                    ALU.mult, ALU.add),
            reads=[("fm2",), ("tau",)], writes=[("tau",)])
    P.v(lambda e: e.tensor_tensor(
        L3v(F2[:, 0:128]), L3v(Xs),
        tau.unsqueeze(2).broadcast_to([128, 16, 8]), ALU.subtract),
        reads=[("Xs",), ("tau",)], writes=[("F2",)])
    P.v(lambda e: e.tensor_scalar(F2[:, 0:128], F2[:, 0:128], 0.0, None,
                                  ALU.max),
        reads=[("F2",)], writes=[("F2",)])
    P.v(lambda e: e.tensor_tensor(gates, F2[:, 0:128], F2[:, 0:128], ALU.mult),
        reads=[("F2",)], writes=[("gates",)])
    P.v(lambda e: e.reduce_sum(mx, L3v(gates), axis=AX.X),
        reads=[("gates",)], writes=[("mx",)])
    P.v(lambda e: e.reciprocal(mx, mx), reads=[("mx",)], writes=[("mx",)])
    P.v(lambda e: e.tensor_tensor(
        L3v(gates), L3v(gates), mx.unsqueeze(2).broadcast_to([128, 16, 8]),
        ALU.mult),
        reads=[("gates",), ("mx",)], writes=[("gates",)])
    P.v(lambda e: e.tensor_scalar(F2[:, 0:128], gates, 1e-9, None, ALU.is_gt),
        reads=[("gates",)], writes=[("F2",)])
    P.v(lambda e: e.tensor_tensor(gates, gates, F2[:, 0:128], ALU.mult),
        reads=[("gates",), ("F2",)], writes=[("gates",)])

    for e_i in range(8):
        b1d = W["b1"][e_i]
        P.dma(F2[0:1, 0:2048], b1d[0:2048].unsqueeze(0), reads=[("F2",)],
              writes=[("F2",)])
        for cc in range(16):
            P.t(lambda e, cc=cc: e.matmul(
                psB[:, cc:cc + 1], F2[0:1, cc * 128:(cc + 1) * 128],
                C.onecellf, start=True, stop=True),
                reads=[("F2",), ("CF",)], writes=[("psB", 0)])
        P.dma(F2[0:1, 0:2048], b1d[2048:4096].unsqueeze(0), reads=[("F2",)],
              writes=[("F2",)])
        for cc in range(16):
            P.t(lambda e, cc=cc: e.matmul(
                psB[:, 16 + cc:17 + cc], F2[0:1, cc * 128:(cc + 1) * 128],
                C.onecellf, start=True, stop=True),
                reads=[("F2",), ("CF",)], writes=[("psB", 0)])
        P.v(lambda e: e.tensor_copy(b1c, psB[:, 0:32]),
            reads=[("psB", 0)], writes=[("b1c",)])
        bcast_bias(C, W["b2"][e_i], b2B, [("b2B",)], 1024)
        for half in range(4):
            for fc in range(8):
                fg = half * 8 + fc
                slot = (fc % 2) * 1024
                KSL = ("WA", 0)
                load_weight_bf16(C, WA, W["W1"][e_i][:, fg * 128:(fg + 1) * 128],
                                 [KSL], 1024, 128, dst_col0=slot)
                for lp in range(4):
                    for dc in range(ND):
                        P.t(lambda e, fg=fg, slot=slot, lp=lp, dc=dc: e.matmul(
                            psA[:, lp * 512:(lp + 1) * 512],
                            WA[:, slot + dc * 128: slot + dc * 128 + 128],
                            B1[:, dc * T + lp * 512: dc * T + lp * 512 + 512],
                            start=(dc == 0), stop=(dc == ND - 1)),
                            reads=[("B1",), KSL], writes=[("psA", lp)])
                P.s(lambda e, fc=fc, fg=fg: e.activation(
                    B2[:, fc * T:(fc + 1) * T], psA[:, 0:2048], AF.Relu,
                    bias=b1c[:, fg:fg + 1]),
                    reads=[("psA", n) for n in range(4)] + [("b1c",)],
                    writes=[("B2", fc)])
            load_weight_bf16(C, WA, W["W2"][e_i][half * 1024:(half + 1) * 1024, :],
                             [("WA", 0)], 1024, 1024)
            for t in range(NT):
                for fc in range(8):
                    for nb in range(2):
                        P.t(lambda e, t=t, fc=fc, nb=nb: e.matmul(
                            psB[:, nb * 512:(nb + 1) * 512],
                            B2[:, fc * T + t * 128: fc * T + t * 128 + 128],
                            WA[:, fc * 1024 + nb * 512: fc * 1024 + nb * 512 + 512],
                            start=(fc == 0), stop=(fc == 7)),
                            reads=[("B2", fc), ("WA", 0)], writes=[("psB", nb)])
                gc = gates[:, t * 8 + e_i: t * 8 + e_i + 1]
                P.v(lambda e, t=t, gc=gc: e.scalar_tensor_tensor(
                    X[:, t * 1024:(t + 1) * 1024], psB[:, 0:1024], gc,
                    X[:, t * 1024:(t + 1) * 1024], ALU.mult, ALU.add),
                    reads=[("psB", 0), ("psB", 1), ("gates",), ("X",)],
                    writes=[("X",)])
                if half == 3:
                    P.v(lambda e, t=t, gc=gc: e.scalar_tensor_tensor(
                        X[:, t * 1024:(t + 1) * 1024], b2B, gc,
                        X[:, t * 1024:(t + 1) * 1024], ALU.mult, ALU.add),
                        reads=[("b2B",), ("gates",), ("X",)], writes=[("X",)])


L2_FFN_W = [("Wr", "L2_ffn_Wr", (1024, 8)),
            ("br", "L2_ffn_br", (8,)),
            ("W1", "L2_ffn_W1", (8, 1024, 4096)),
            ("b1", "L2_ffn_b1", (8, 4096)),
            ("W2", "L2_ffn_W2", (8, 4096, 1024)),
            ("b2", "L2_ffn_b2", (8, 1024))]




def emit_l3_attn(C, W):
    P = C.P
    X, B1, B2, B3, WA = C.X, C.B1, C.B2, C.B3, C.WA
    S1, F1, F2 = C.S1, C.F1, C.F2
    psA, psB, psS, psT = C.psA, C.psB, C.psS, C.psT

    mu = S1[:, 0:16]
    rstd = S1[:, 16:32]
    nmx = S1[:, 32:33]
    nms = S1[:, 33:34]
    ssum = S1[:, 34:35]
    SCL = 1.0 / (2.0 * 0.28)
    VT0 = 0
    S80 = 5120
    MSK = 9744

    ln_stats(C, X, mu, rstd, [("X",)], ("mu",), ("rstd",))
    for t in range(NT):
        P.v(lambda e, t=t: e.tensor_scalar(
            B2[:, t * 1024:(t + 1) * 1024], X[:, t * 1024:(t + 1) * 1024],
            mu[:, t:t + 1], rstd[:, t:t + 1], ALU.subtract, ALU.mult),
            reads=[("X",), ("mu",), ("rstd",)], writes=[("B2", "q")])
    tp_tok2feat(C, B2, B1, [("B2", "q")], [("B1",)])
    P.dma(B3[:, MSK:MSK + 136], W["c_bandw"], writes=[("msk",)])
    P.dma(B3[:, MSK + 136:MSK + 272], W["c_band0"], writes=[("msk",)])
    P.v(lambda e: e.memset(B2[:, 4096:8256], 0.0), writes=[("B2", "k")])
    P.v(lambda e: e.memset(B3[:, 0:1024], 0.0), writes=[("vtk",)])
    C.ccell(1.0 / (2.0 * 0.28))

    for qr in range(4):
        q0 = qr * 512
        for wname, kw in (("Wq", ("B2", "q")), ("Wk", ("B2", "k")),
                          ("Wv", ("B2", "v"))):
            load_weight_bf16(C, WA, W[wname], [("WA", 0)], 1024, 1024)
            for ec in range(ND):
                for dc in range(ND):
                    P.t(lambda e, ec=ec, dc=dc, q0=q0: e.matmul(
                        psA[:, (ec % 4) * 512:(ec % 4) * 512 + 512],
                        WA[:, dc * 1024 + ec * 128: dc * 1024 + ec * 128 + 128],
                        B1[:, dc * T + q0: dc * T + q0 + 512],
                        start=(dc == 0), stop=(dc == ND - 1)),
                        reads=[("B1",), ("WA", 0)], writes=[("psA", ec % 4)])
                if wname == "Wq":
                    P.v(lambda e, ec=ec: e.tensor_copy(
                        B2[:, ec * 512:(ec + 1) * 512],
                        psA[:, (ec % 4) * 512:(ec % 4) * 512 + 512]),
                        reads=[("psA", ec % 4)], writes=[("B2", "q")])
                elif wname == "Wk":
                    P.v(lambda e, ec=ec: e.tensor_copy(
                        B2[:, 4096 + ec * 520 + 8:4096 + (ec + 1) * 520],
                        psA[:, (ec % 4) * 512:(ec % 4) * 512 + 512]),
                        reads=[("psA", ec % 4)], writes=[("B2", "k")])
                else:
                    P.v(lambda e, ec=ec: e.tensor_copy(
                        B2[:, 8256:8768],
                        psA[:, (ec % 4) * 512:(ec % 4) * 512 + 512]),
                        reads=[("psA", ec % 4)], writes=[("B2", "v")])
                    for tt in range(4):
                        P.t(lambda e, ec=ec, tt=tt: e.transpose(
                            psT[:, tt * 128:(tt + 1) * 128],
                            B2[:, 8256 + tt * 128:8256 + (tt + 1) * 128],
                            C.identb),
                            reads=[("B2", "v"), ("CB",)], writes=[("psT", tt)])
                    for tt in range(4):
                        P.v(lambda e, ec=ec, tt=tt: e.tensor_copy(
                            B3[:, VT0 + (tt + 1) * 1024 + ec * 128:
                               VT0 + (tt + 1) * 1024 + ec * 128 + 128],
                            psT[:, tt * 128:(tt + 1) * 128]),
                            reads=[("psT", k) for k in range(8)],
                            writes=[("vtk",)])
        vt = B3[:, VT0:VT0 + 5120].rearrange("p (t z) -> p t z", z=1024)
        s8 = B3[:, S80:S80 + 4096].rearrange("p (t z) -> p t z", z=1024)
        P.dma(s8[8:128, :, :], vt[0:120, 1:5, :], reads=[("vtk",)],
              writes=[("s8",)])
        P.dma(s8[0:8, :, :], vt[120:128, 0:4, :], reads=[("vtk",)],
              writes=[("s8",)])
        for cl in range(4):
            for h in range(16):
                r0 = (h % 2) * 64
                jc = h // 2
                P.t(lambda e, cl=cl, h=h, r0=r0, jc=jc: e.matmul(
                    psA[:, 0:136],
                    B2[r0:r0 + 64, jc * 512 + cl * 128: jc * 512 + cl * 128 + 128],
                    B2[r0:r0 + 64, 4096 + jc * 520 + cl * 128:
                       4096 + jc * 520 + cl * 128 + 136],
                    start=True, stop=True),
                    reads=[("B2", "q"), ("B2", "k")], writes=[("psA", 0)])
                m0 = MSK + (136 if (qr == 0 and cl == 0) else 0)
                P.v(lambda e, m0=m0: e.tensor_tensor(
                    psA[:, 0:136], psA[:, 0:136], B3[:, m0:m0 + 136], ALU.add),
                    reads=[("psA", 0), ("msk",)], writes=[("psA", 0)])
                P.v(lambda e: e.reduce_max(nmx, psA[:, 0:136], axis=AX.X,
                                           negate=True),
                    reads=[("psA", 0)], writes=[("nmx",)])
                P.v(lambda e: e.tensor_scalar(nms, nmx, SCL, None, ALU.mult),
                    reads=[("nmx",)], writes=[("nms",)])
                P.s(lambda e: e.activation(F2[:, 0:136], psA[:, 0:136],
                                           AF.Exp, bias=nms, scale=SCL,
                                           accum_out=ssum),
                    reads=[("psA", 0), ("nms",), ("CC",)],
                    writes=[("F2",), ("ssum",)])
                P.v(lambda e: e.reciprocal(ssum, ssum),
                    reads=[("ssum",)], writes=[("ssum",)])
                P.v(lambda e: e.tensor_scalar(B3[:, 9216:9352], F2[:, 0:136],
                                              ssum, None, ALU.mult),
                    reads=[("F2",), ("ssum",)], writes=[("wts",)])
                P.t(lambda e: e.transpose(psT[:, 0:128],
                                          B3[:, 9224:9352], C.identb),
                    reads=[("wts",), ("CB",)], writes=[("psT", 0)])
                P.t(lambda e: e.transpose(psT[0:8, 128:256],
                                          B3[:, 9216:9224], C.identb),
                    reads=[("wts",), ("CB",)], writes=[("psT", 1)])
                P.v(lambda e: e.tensor_copy(B3[:, 9352:9480], psT[:, 0:128]),
                    reads=[("psT", k) for k in range(8)], writes=[("wta",)])
                P.v(lambda e: e.tensor_copy(B3[0:8, 9480:9608],
                                            psT[0:8, 128:256]),
                    reads=[("psT", k) for k in range(8)], writes=[("wtb",)])
                P.t(lambda e, cl=cl, h=h, r0=r0: e.matmul(
                    psB[r0:r0 + 64, 0:128],
                    B3[:, VT0 + (cl + 1) * 1024 + h * 64:
                       VT0 + (cl + 1) * 1024 + h * 64 + 64],
                    B3[:, 9352:9480], start=True, stop=False),
                    reads=[("vtk",), ("wta",)], writes=[("psB", 0)])
                P.t(lambda e, cl=cl, h=h, r0=r0: e.matmul(
                    psB[r0:r0 + 64, 0:128],
                    B3[0:8, S80 + cl * 1024 + h * 64:
                       S80 + cl * 1024 + h * 64 + 64],
                    B3[0:8, 9480:9608], start=False, stop=True),
                    reads=[("s8",), ("wtb",)], writes=[("psB", 0)])
                if h % 2 == 1:
                    P.s(lambda e, cl=cl, jc=jc, q0=q0: e.copy(
                        B2[:, 8768 + jc * 512 + cl * 128:
                           8768 + jc * 512 + cl * 128 + 128],
                        psB[:, 0:128]),
                        reads=[("psB", 0)], writes=[("B2", "y")])
        load_weight_bf16(C, WA, W["Wo"], [("WA", 0)], 1024, 1024)
        for tt in range(4):
            t = qr * 4 + tt
            for hf in range(2):
                for yc in range(8):
                    P.t(lambda e, tt=tt, hf=hf, yc=yc: e.matmul(
                        psA[:, hf * 512:hf * 512 + 512],
                        B2[:, 8768 + yc * 512 + tt * 128:
                           8768 + yc * 512 + tt * 128 + 128],
                        WA[:, yc * 1024 + hf * 512: yc * 1024 + hf * 512 + 512],
                        start=(yc == 0), stop=(yc == 7)),
                        reads=[("B2", "y"), ("WA", 0)], writes=[("psA", hf)])
            P.v(lambda e, t=t: e.tensor_tensor(
                X[:, t * 1024:(t + 1) * 1024], X[:, t * 1024:(t + 1) * 1024],
                psA[:, 0:1024], ALU.add),
                reads=[("psA", 0), ("psA", 1), ("X",)], writes=[("X",)])
        if qr < 3:
            for jc in range(8):
                P.v(lambda e, jc=jc: e.tensor_copy(
                    B2[:, 4096 + jc * 520:4096 + jc * 520 + 8],
                    B2[:, 4096 + jc * 520 + 512:4096 + (jc + 1) * 520]),
                    reads=[("B2", "k")], writes=[("B2", "k")])
            P.v(lambda e: e.tensor_copy(B3[:, 0:1024], B3[:, 4096:5120]),
                reads=[("vtk",)], writes=[("vtk",)])


L3_ATTN_W = [("Wq", "L3_attn_Wq", (1024, 1024)),
             ("Wk", "L3_attn_Wk", (1024, 1024)),
             ("Wv", "L3_attn_Wv", (1024, 1024)),
             ("Wo", "L3_attn_Wo", (1024, 1024)),
             ("c_bandw", "c_bandw", (128, 136)),
             ("c_band0", "c_band0", (128, 136))]




def emit_l3_ffn(C, W):
    P = C.P
    nc = C.nc
    X, B1, B2, B3, WA = C.X, C.B1, C.B2, C.B3, C.WA
    S1, F1, F2 = C.S1, C.F1, C.F2
    psA, psB, psS, psT = C.psA, C.psB, C.psS, C.psT

    mu = S1[:, 0:16]
    rstd = S1[:, 16:32]
    qb = S1[:, 32:40]
    kb = S1[:, 40:48]
    vb = S1[:, 48:56]
    SCL = float(1024.0 ** -0.5)
    wB = F1[:, 0:1024]
    bB = F2[:, 1024:2048]
    bB = S1[:, 1024:2048]
    qkvd = nc.dram_tensor("l3f_qkv", [3, 8, 128, T], BF16, kind="Internal").ap()

    ln_stats(C, X, mu, rstd, [("X",)], ("mu",), ("rstd",))
    bcast_bias(C, W["norm_w"], wB, [("F1", 0)], 1024)
    bcast_bias(C, W["norm_b"], bB, [("bB",)], 1024)
    for t in range(NT):
        P.v(lambda e, t=t: e.tensor_scalar(
            F2[:, 0:1024], X[:, t * 1024:(t + 1) * 1024],
            mu[:, t:t + 1], rstd[:, t:t + 1], ALU.subtract, ALU.mult),
            reads=[("X",), ("mu",), ("rstd",)], writes=[("F2",)])
        P.v(lambda e: e.tensor_tensor(F2[:, 0:1024], F2[:, 0:1024], wB,
                                      ALU.mult),
            reads=[("F2",), ("F1", 0)], writes=[("F2",)])
        P.v(lambda e, t=t: e.tensor_tensor(
            B2[:, t * 1024:(t + 1) * 1024], F2[:, 0:1024], bB, ALU.add),
            reads=[("F2",), ("bB",)], writes=[("B2", "s")])
    tp_tok2feat(C, B2, B1, [("B2", "s")], [("B1",)])

    def bias_cols(bname, dst, key):
        P.dma(F2[0:1, 0:1024], W[bname].unsqueeze(0), reads=[("F2",)],
              writes=[("F2",)])
        for cc in range(8):
            P.t(lambda e, cc=cc: e.matmul(psB[:, cc:cc + 1],
                                          F2[0:1, cc * 128:(cc + 1) * 128],
                                          C.onecellf, start=True, stop=True),
                reads=[("F2",), ("CF",)], writes=[("psB", 0)])
        P.v(lambda e, dst=dst: e.tensor_copy(dst, psB[:, 0:8]),
            reads=[("psB", 0)], writes=[key])
    C.ccell(SCL)
    bias_cols("q_b", qb, ("qb",))
    bias_cols("k_b", kb, ("kb",))
    bias_cols("v_b", vb, ("vb",))

    for wi, (wname, bcol, bkey) in enumerate(
            (("q_w", qb, ("qb",)), ("k_w", kb, ("kb",)), ("v_w", vb, ("vb",)))):
        load_weight_bf16(C, WA, W[wname], [("WA", 0)], 1024, 1024)
        for r in range(8):
            for g in range(0, 8, 8):
                for k in range(8):
                    P.t(lambda e, r=r, k=k: e.transpose(
                        psT[:, k * 128:(k + 1) * 128],
                        WA[:, r * 1024 + k * 128: r * 1024 + k * 128 + 128],
                        C.identb),
                        reads=[("WA", 0), ("CB",)], writes=[("psT", k)])
                P.v(lambda e, r=r: e.tensor_copy(
                    WA[:].rearrange("p (c l) -> p c l", c=8)
                    [:, :, 8192 // 8 + r * 128:8192 // 8 + r * 128 + 128]
                    if False else
                    WA[:, 8192:8192 + 8 * 1024].rearrange(
                        "p (c z) -> p c z", c=8)[:, :, r * 128:r * 128 + 128],
                    psT[:, 0:1024].rearrange("p (c z) -> p c z", c=8)),
                    reads=[("psT", k) for k in range(8)], writes=[("WA", 1)])
        for e_s in range(4):
            m = 1024 >> e_s
            nmc = m // 128
            for ec in range(ND):
                for mc in range(nmc):
                    P.t(lambda e, e_s=e_s, ec=ec, mc=mc, nmc=nmc: e.matmul(
                        psA[:, (ec % 4) * 512:(ec % 4) * 512 + 512],
                        WA[:, 8192 + mc * 1024 + ec * 128:
                           8192 + mc * 1024 + ec * 128 + 128],
                        B1[:, mc * T + e_s * 512: mc * T + e_s * 512 + 512],
                        start=(mc == 0), stop=(mc == nmc - 1)),
                        reads=[("B1",), ("WA", 1)], writes=[("psA", ec % 4)])
                P.v(lambda e, ec=ec, bcol=bcol: e.tensor_scalar(
                    B3[:, ec * 512:(ec + 1) * 512],
                    psA[:, (ec % 4) * 512:(ec % 4) * 512 + 512],
                    bcol[:, ec:ec + 1], None, ALU.add),
                    reads=[("psA", ec % 4), bkey], writes=[("B3", "E")])
            for dp in range(4):
                P.dma(qkvd[wi].rearrange("c p l -> p c l")
                      [:, dp * 2:dp * 2 + 2, e_s * 512:e_s * 512 + 512],
                      B3[:, dp * 1024:(dp + 1) * 1024].rearrange(
                          "p (c z) -> p c z", c=2),
                      reads=[("B3", "E")], writes=[("qkv",)])

    for h in range(8):
        P.dma(B2[:, 0:2048], qkvd[0, h].rearrange("p l -> p l"),
              reads=[("qkv",), ("B2", "s")], writes=[("B2", "s")])
        P.dma(B2[:, 2048:4096], qkvd[1, h], reads=[("qkv",), ("B2", "s")],
              writes=[("B2", "s")])
        P.dma(B2[:, 4096:6144], qkvd[2, h], reads=[("qkv",), ("B2", "s")],
              writes=[("B2", "s")])
        if h == 0:
            C.dbg("qTh", B2[:, 0:2048], [("B2", "s")])
            C.dbg("kTh", B2[:, 2048:4096], [("B2", "s")])
        for g in range(2):
            tp_group(C, lambda k, g=g: B2[:, 4096 + (g * 8 + k) * 128:
                                          4096 + (g * 8 + k + 1) * 128],
                     B2, 6144 + g * 1024, 8, [("B2", "s")], [("B2", "vt")])
        for lh in range(2):
            l0 = lh * 1024
            for kc in range(NT):
                for nb in range(2):
                    P.t(lambda e, kc=kc, nb=nb, l0=l0: e.matmul(
                        psA[:, nb * 512:(nb + 1) * 512],
                        B2[:, 2048 + kc * 128:2048 + (kc + 1) * 128],
                        B2[:, l0 + nb * 512:l0 + nb * 512 + 512],
                        start=True, stop=True),
                        reads=[("B2", "s")], writes=[("psA", nb)])
                P.s(lambda e: e.activation(B3[:, 0:1024], psA[:, 0:1024],
                                           AF.Exp, scale=SCL),
                    reads=[("psA", 0), ("psA", 1), ("CC",)], writes=[("B3", "E")])
                for nb in range(2):
                    P.t(lambda e, kc=kc, nb=nb: e.matmul(
                        psB[:, nb * 512:(nb + 1) * 512],
                        B2[:, 6144 + kc * 128:6144 + (kc + 1) * 128],
                        B3[:, nb * 512:nb * 512 + 512],
                        start=(kc == 0), stop=(kc == NT - 1)),
                        reads=[("B2", "vt"), ("B3", "E")], writes=[("psB", nb)])
                for nb in range(2):
                    P.t(lambda e, kc=kc, nb=nb: e.matmul(
                        psA[0:1, 1024 + nb * 512:1024 + (nb + 1) * 512],
                        C.onescolb, B3[:, nb * 512:nb * 512 + 512],
                        start=(kc == 0), stop=(kc == NT - 1)),
                        reads=[("CB",), ("B3", "E")], writes=[("psA", 2 + nb)])
            P.v(lambda e: e.tensor_copy(F2[0:1, 0:1024], psA[0:1, 1024:2048]),
                reads=[("psA", 2), ("psA", 3)], writes=[("F2",)])
            P.v(lambda e: e.reciprocal(F2[0:1, 0:1024], F2[0:1, 0:1024]),
                reads=[("F2",)], writes=[("F2",)])
            for nb in range(2):
                P.t(lambda e, nb=nb: e.matmul(
                    psA[:, 1024 + nb * 512:1024 + (nb + 1) * 512], C.onesrowf,
                    F2[0:1, nb * 512:(nb + 1) * 512], start=True, stop=True),
                    reads=[("F2",), ("CF",)], writes=[("psA", 2 + nb)])
            P.s(lambda e, h=h, l0=l0: e.copy(
                WA[:, h * 2048 + l0: h * 2048 + l0 + 1024], psB[:, 0:1024]),
                reads=[("psB", 0), ("psB", 1)], writes=[("WA", 0)])
            P.v(lambda e, h=h, l0=l0: e.tensor_tensor(
                WA[:, h * 2048 + l0: h * 2048 + l0 + 1024],
                WA[:, h * 2048 + l0: h * 2048 + l0 + 1024],
                psA[:, 1024:2048], ALU.mult),
                reads=[("psA", 2), ("psA", 3), ("WA", 0)], writes=[("WA", 0)])

    C.dbg("yT0", WA[:, 0:2048], [("WA", 0)])
    load_weight_bf16(C, B2, W["o_w"], [("B2", "s")], 1024, 1024)
    for r in range(8):
        for k in range(8):
            P.t(lambda e, r=r, k=k: e.transpose(
                psT[:, k * 128:(k + 1) * 128],
                B2[:, r * 1024 + k * 128: r * 1024 + k * 128 + 128],
                C.identb),
                reads=[("B2", "s"), ("CB",)], writes=[("psT", k)])
        P.v(lambda e, r=r: e.tensor_copy(
            B2[:, 8192:16384].rearrange("p (c z) -> p c z", c=8)
            [:, :, r * 128:r * 128 + 128],
            psT[:, 0:1024].rearrange("p (c z) -> p c z", c=8)),
            reads=[("psT", k) for k in range(8)], writes=[("B2", "owt")])
    obB = F1[:, 1024:2048]
    bcast_bias(C, W["o_b"], obB, [("F1", 1)], 1024)
    for e_s in range(4):
        m = 1024 >> e_s
        nmc = m // 128
        for tt in range(4):
            t = e_s * 4 + tt
            nnb = max(1, m // 512)
            for nb in range(nnb):
                nw = min(512, m)
                for mc in range(nmc):
                    P.t(lambda e, mc=mc, nb=nb, nw=nw, t=t, nmc=nmc: e.matmul(
                        psA[:, nb * 512:nb * 512 + nw],
                        WA[:].rearrange("p (hh z) -> p hh z", hh=8)
                        [:, mc // 2, (mc % 2) * 1024 + 0:0] if False else
                        WA[:, mc * 2048 // 2:0] if False else
                        _yt_slice(C, mc, t),
                        B2[:, 8192 + mc * 1024 + nb * 512:
                           8192 + mc * 1024 + nb * 512 + nw],
                        start=(mc == 0), stop=(mc == nmc - 1)),
                        reads=[("WA", 0), ("B2", "owt")], writes=[("psA", nb)])
            P.v(lambda e, t=t: e.tensor_scalar(
                F2[:, 0:1024], X[:, t * 1024:(t + 1) * 1024],
                mu[:, t:t + 1], rstd[:, t:t + 1], ALU.subtract, ALU.mult),
                reads=[("X",), ("mu",), ("rstd",)], writes=[("F2",)])
            P.v(lambda e, m=m: e.tensor_tensor(
                F2[:, 0:m], F2[:, 0:m], obB[:, 0:m], ALU.add),
                reads=[("F2",), ("F1", 1)], writes=[("F2",)])
            P.v(lambda e, m=m: e.tensor_tensor(
                F2[:, 0:m], F2[:, 0:m], psA[:, 0:m], ALU.add),
                reads=[("F2",), ("psA", 0), ("psA", 1)], writes=[("F2",)])
            P.v(lambda e, t=t: e.tensor_tensor(
                X[:, t * 1024:(t + 1) * 1024], X[:, t * 1024:(t + 1) * 1024],
                F2[:, 0:1024], ALU.add),
                reads=[("F2",), ("X",)], writes=[("X",)])


def _yt_slice(C, mc, t):
    l0 = (t // 8) * 0
    lh = 1 if t >= 8 else 0
    col = mc * 2048 + lh * 1024 + (t - lh * 8) * 128
    return C.WA[:, col:col + 128]


L3_FFN_W = [("norm_w", "L3_ffn_norm_w", (1024,)),
            ("norm_b", "L3_ffn_norm_b", (1024,)),
            ("q_w", "L3_ffn_q_w", (1024, 1024)),
            ("q_b", "L3_ffn_q_b", (1024,)),
            ("k_w", "L3_ffn_k_w", (1024, 1024)),
            ("k_b", "L3_ffn_k_b", (1024,)),
            ("v_w", "L3_ffn_v_w", (1024, 1024)),
            ("v_b", "L3_ffn_v_b", (1024,)),
            ("o_w", "L3_ffn_o_w", (1024, 1024)),
            ("o_b", "L3_ffn_o_b", (1024,))]




def build_core_program(block_list, serial=False, debug=False):
    nc = bass.Bass("TRN2", target_bir_lowering=False)
    P = Prog(nc, serial=serial)
    stack = ExitStack()
    dram = {}

    x_in = nc.dram_tensor("x_in", [T, D], F32, kind="ExternalInput").ap()
    x_out = nc.dram_tensor("x_out", [T, D], F32, kind="ExternalOutput").ap()

    with stack:
        C = Ctx(nc, P, stack, debug=debug)
        consts = const_arrays()
        for cn, arr in consts.items():
            dram[cn] = nc.dram_tensor(cn, list(arr.shape),
                                      BF16 if arr.dtype == ml_dtypes.bfloat16 else F32,
                                      kind="ExternalInput").ap()
        C.load_consts(dram)
        xi3 = x_in.rearrange("(t p) f -> p t f", p=128)
        for t in range(NT):
            P.dma(C.X[:, t * 1024:(t + 1) * 1024], xi3[:, t, :], writes=[("X",)])
        for emit_fn, wspec in block_list:
            P.barrier()
            Wd = {}
            for short, full, shape in wspec:
                if full not in dram:
                    dram[full] = nc.dram_tensor(full, list(shape), F32,
                                                kind="ExternalInput").ap()
                Wd[short] = dram[full]
            emit_fn(C, Wd)
        P.barrier()
        import os
        trunc = int(os.environ.get("MK_TRUNC", "0"))
        if trunc:
            P.ops = P.ops[:trunc]
            P.key_w = {k: v for k, v in P.key_w.items() if v < trunc}
            P.key_r = {k: [i for i in v if i < trunc]
                       for k, v in P.key_r.items()}
        xo3 = x_out.rearrange("(t p) f -> p t f", p=128)
        for t in range(NT):
            P.dma(xo3[:, t, :], C.X[:, t * 1024:(t + 1) * 1024], reads=[("X",)])

        sems = {}
        sem_stack = ExitStack()
        with sem_stack:
            for e in ENGINES:
                sems[e] = sem_stack.enter_context(nc.semaphore("sem_" + e))
            with nc.Block() as block:
                P.emit(block, sems)
    return nc


L0_ATTN_W = [("Wq", "L0_attn_Wq", (1024, 1024)),
             ("Wk", "L0_attn_Wk", (1024, 1024)),
             ("Wv", "L0_attn_Wv", (1024, 1024)),
             ("Wo", "L0_attn_Wo", (1024, 1024)),
             ("bo", "L0_attn_bo", (1024,)),
             ("gamma", "L0_attn_gamma", (1024,)),
             ("beta", "L0_attn_beta", (1024,))]


BLOCK_LIST = []


def run_blocks(inputs, block_list, serial=False, n_cores=8, x_init=None,
               trace=False):
    nc = build_core_program(block_list, serial=serial)
    if x_init is None:
        x_init = inputs["L0_attn_x"]
    x = np.ascontiguousarray(np.asarray(x_init, dtype=np.float32))
    consts = const_arrays()
    in_maps = []
    for b in range(n_cores):
        m = {"x_in": x[b]}
        m.update(consts)
        for _, wspec in block_list:
            for short, full, shape in wspec:
                if full.startswith("c_"):
                    continue
                m[full] = np.ascontiguousarray(np.asarray(inputs[full], np.float32))
        in_maps.append(m)
    res = run_bass_kernel_spmd(nc, in_maps, core_ids=list(range(n_cores)),
                               trace=trace)
    if trace:
        print("exec_time_ns:", res.exec_time_ns)
    return np.stack([r["x_out"] for r in res.results])


L1_ATTN_W = [("wq", "L1_attn_wq", (1024, 1024)),
             ("bq", "L1_attn_bq", (1024,)),
             ("wk", "L1_attn_wk", (1024, 1024)),
             ("bk", "L1_attn_bk", (1024,)),
             ("wv", "L1_attn_wv", (1024, 1024)),
             ("bv", "L1_attn_bv", (1024,)),
             ("wo", "L1_attn_wo", (1024, 1024)),
             ("bo", "L1_attn_bo", (1024,))]

L0_FFN_W = [("gate_w", "L0_ffn_gate_w", (1024, 16)),
            ("gate_b", "L0_ffn_gate_b", (16,)),
            ("shared_wg", "L0_ffn_shared_wg", (1024, 512)),
            ("shared_wu", "L0_ffn_shared_wu", (1024, 512)),
            ("shared_wd", "L0_ffn_shared_wd", (512, 1024)),
            ("experts_wg", "L0_ffn_experts_wg", (16, 1024, 512)),
            ("experts_wu", "L0_ffn_experts_wu", (16, 1024, 512)),
            ("experts_wd", "L0_ffn_experts_wd", (16, 512, 1024))]

L1_FFN_W = [("w_ln1", "L1_ffn_w_ln1", (1024,)),
            ("Wq", "L1_ffn_Wq", (1024, 1024)),
            ("Wkva", "L1_ffn_Wkva", (1024, 160)),
            ("Wkvb", "L1_ffn_Wkvb", (128, 512)),
            ("Wo", "L1_ffn_Wo", (1024, 1024)),
            ("w_ln2", "L1_ffn_w_ln2", (1024,)),
            ("Wgate", "L1_ffn_Wgate", (1024, 8)),
            ("expert_bias", "L1_ffn_expert_bias", (8,)),
            ("Wg", "L1_ffn_Wg", (8, 1024, 512)),
            ("Wu", "L1_ffn_Wu", (8, 1024, 512)),
            ("Wd", "L1_ffn_Wd", (8, 512, 1024)),
            ("sWg", "L1_ffn_sWg", (1, 1024, 512)),
            ("sWu", "L1_ffn_sWu", (1, 1024, 512)),
            ("sWd", "L1_ffn_sWd", (1, 512, 1024))]

L2_ATTN_W = ([("Wi_f", "L2_attn_Wi_f", (1024, 4384)),
              ("conv_f", "L2_attn_conv_f", (4, 2304)),
              ("convb_f", "L2_attn_convb_f", (2304,)),
              ("Alog_f", "L2_attn_Alog_f", (32,)),
              ("D_f", "L2_attn_D_f", (32,)),
              ("dtb_f", "L2_attn_dtb_f", (32,)),
              ("Wo_f", "L2_attn_Wo_f", (2048, 1024)),
              ("Wi_b", "L2_attn_Wi_b", (1024, 4384)),
              ("conv_b", "L2_attn_conv_b", (4, 2304)),
              ("convb_b", "L2_attn_convb_b", (2304,)),
              ("Alog_b", "L2_attn_Alog_b", (32,)),
              ("D_b", "L2_attn_D_b", (32,)),
              ("dtb_b", "L2_attn_dtb_b", (32,)),
              ("Wo_b", "L2_attn_Wo_b", (2048, 1024)),
              ("Wm", "L2_attn_Wm", (1024, 1024)),
              ("bm", "L2_attn_bm", (1024,))])

BLOCK_LIST = [(emit_l0_attn, L0_ATTN_W), (emit_l0_ffn, L0_FFN_W),
              (emit_l1_attn, L1_ATTN_W), (emit_l1_ffn, L1_FFN_W),
              (emit_l2_attn, L2_ATTN_W), (emit_l2_ffn, L2_FFN_W),
              (emit_l3_attn, L3_ATTN_W), (emit_l3_ffn, L3_FFN_W)]




def kernel(**inputs):
    nc = build_core_program(BLOCK_LIST)
    x = np.ascontiguousarray(np.asarray(inputs["L0_attn_x"], dtype=np.float32))
    consts = const_arrays()
    in_maps = []
    for b in range(8):
        m = {"x_in": x[b]}
        m.update(consts)
        for _, wspec in BLOCK_LIST:
            for short, full, shape in wspec:
                if full.startswith("c_"):
                    continue
                m[full] = np.ascontiguousarray(np.asarray(inputs[full],
                                                          np.float32))
        in_maps.append(m)
    res = run_bass_kernel_spmd(nc, in_maps, core_ids=list(range(8)))
    return np.stack([r["x_out"] for r in res.results]).astype(np.float32)


# version 03 (final)
# speedup vs baseline: 1.3058x; 1.0370x over previous
import numpy as np
import ml_dtypes
import concourse.bass as bass
import concourse.mybir as mybir
from concourse.bass_utils import run_bass_kernel_spmd
from contextlib import ExitStack

F32 = mybir.dt.float32
BF16 = mybir.dt.bfloat16
AF = mybir.ActivationFunctionType
ALU = mybir.AluOpType
AX = mybir.AxisListType

T = 2048
D = 1024
NT = 16
ND = 8

ENGINES = ("sync", "tensor", "vector", "scalar", "gpsimd")


class Prog:
    def __init__(self, nc, serial=False):
        self.nc = nc
        self.serial = serial
        self.ops = []
        self.key_w = {}
        self.key_r = {}

    def add(self, eng, fn, reads=(), writes=(), dma=False):
        i = len(self.ops)
        deps = set()
        for k in reads:
            j = self.key_w.get(k)
            if j is not None:
                deps.add(j)
        for k in writes:
            j = self.key_w.get(k)
            if j is not None:
                deps.add(j)
            deps.update(self.key_r.get(k, ()))
        if self.serial and i > 0:
            deps.add(i - 1)
        self.ops.append(dict(i=i, eng=eng, fn=fn, deps=deps, dma=dma,
                             marked=dma))
        for k in reads:
            self.key_r.setdefault(k, []).append(i)
        for k in writes:
            self.key_w[k] = i
            self.key_r[k] = []
        return i

    def barrier(self):
        last = {}
        for op in self.ops:
            last[op["eng"]] = op["i"]
        deps = set(last.values())
        for e in ENGINES:
            i = len(self.ops)
            self.ops.append(dict(i=i, eng=e, fn=lambda eng: eng.nop(),
                                 deps=set(deps), dma=False, marked=False))
        self.key_w.clear()
        self.key_r.clear()

    def dma(self, out, in_, reads=(), writes=(), q="sync", **kw):
        self.add(q, lambda e: e.dma_start(out=out, in_=in_, **kw),
                 reads=reads, writes=writes, dma=True)

    def t(self, fn, reads=(), writes=()):
        self.add("tensor", fn, reads, writes)

    def v(self, fn, reads=(), writes=()):
        self.add("vector", fn, reads, writes)

    def s(self, fn, reads=(), writes=()):
        self.add("scalar", fn, reads, writes)

    def p(self, fn, reads=(), writes=()):
        self.add("gpsimd", fn, reads, writes)

    def emit(self, block, sems):
        ops = self.ops

        def skip_edge(pj, op):
            return (pj["eng"] == op["eng"] == "tensor"
                    and not pj["dma"] and not op["dma"])

        for op in ops:
            for j in op["deps"]:
                pj = ops[j]
                if skip_edge(pj, op):
                    continue
                pj["marked"] = True
        pfx = {e: 0 for e in ENGINES}
        order = {e: [] for e in ENGINES}
        for op in ops:
            e = op["eng"]
            if op["marked"]:
                pfx[e] += 16 if op["dma"] else 1
            op["pfx"] = pfx[e]
            order[e].append(op)
        clock = {e: {f: 0 for f in ENGINES} for e in ENGINES}
        last_dma_pfx = {e: 0 for e in ENGINES}
        for op in ops:
            e = op["eng"]
            waits = {}
            for j in op["deps"]:
                pj = ops[j]
                f = pj["eng"]
                if skip_edge(pj, op):
                    continue
                c = pj["pfx"]
                if c > clock[e].get(f, 0):
                    waits[f] = max(waits.get(f, 0), c)
            if op["dma"]:
                c = last_dma_pfx[e]
                if c > clock[e].get(e, 0):
                    waits[e] = max(waits.get(e, 0), c)
                last_dma_pfx[e] = op["pfx"]
            op["waits"] = waits
            for f, c in waits.items():
                clock[e][f] = max(clock[e][f], c)

        for ename in ENGINES:
            if not order[ename]:
                continue

            def mk(my_ops, ename=ename):
                def section(eng):
                    for op in my_ops:
                        for f, c in sorted(op["waits"].items()):
                            eng.wait_ge(sems[f], c)
                        ins = op["fn"](eng)
                        if op["marked"]:
                            ins.then_inc(sems[ename], 16 if op["dma"] else 1)
                    if ename == "sync":
                        eng.wait_ge(sems["sync"], pfx["sync"])
                return section
            getattr(block, ename)(mk(order[ename]))
        return pfx


def const_arrays():
    c = {"c_identb": np.eye(128, dtype=ml_dtypes.bfloat16),
         "c_identf": np.eye(128, dtype=np.float32)}
    seg = np.zeros((128, 128), dtype=ml_dtypes.bfloat16)
    for dc in range(8):
        for dl in range(128):
            seg[dl, dc * 16 + 2 * dc + dl // 64] = 1
    c["c_seg"] = seg
    pos = (np.arange(16)[None, :, None] * 128 + np.arange(128)[:, None, None])
    freqs = (1.0 / 100000.0 ** (np.arange(0, 32, 2, dtype=np.float64) / 32.0))
    ang = pos * freqs[None, None, :]
    c["c_rcos"] = np.cos(ang).reshape(128, 256).astype(ml_dtypes.bfloat16)
    c["c_rsin"] = np.sin(ang).reshape(128, 256).astype(ml_dtypes.bfloat16)
    kk = np.arange(128)[:, None]
    ll = np.arange(128)[None, :]
    c["c_causT"] = np.where(ll >= kk, 0.0, -1e30).astype(ml_dtypes.bfloat16)
    c["c_causU"] = np.where(ll <= kk, 0.0, -1e30).astype(ml_dtypes.bfloat16)
    c["c_cuf"] = np.where(kk <= ll, 1.0, 0.0).astype(np.float32)
    llw = np.arange(128)[:, None]
    kkw = np.arange(136)[None, :]
    band = (kkw >= llw) & (kkw <= llw + 8)
    c["c_bandw"] = np.where(band, 0.0, -1e30).astype(ml_dtypes.bfloat16)
    c["c_band0"] = np.where(band & (kkw >= 8), 0.0, -1e30).astype(ml_dtypes.bfloat16)
    return c


class Ctx:
    def __init__(self, nc, P, stack, debug=False):
        self.nc = nc
        self.P = P
        self.debug = debug
        self.dbg_names = []
        ent = stack.enter_context
        sb = nc.sbuf_tensor
        self.X = ent(sb("X", [128, NT * 1024], F32)).ap()
        self.B1 = ent(sb("B1", [128, 8 * T], BF16)).ap()
        self.B2 = ent(sb("B2", [128, 8 * T], BF16)).ap()
        self.B3 = ent(sb("B3", [128, 5 * T], BF16)).ap()
        self.WA = ent(sb("WA", [128, 8 * T], BF16)).ap()
        self.F1 = ent(sb("F1", [128, 2048], F32)).ap()
        self.F2 = ent(sb("F2", [128, 2048], F32)).ap()
        self.S1 = ent(sb("S1", [128, 2048], F32)).ap()
        self.CB = ent(sb("CB", [128, 256], BF16)).ap()
        self.CF = ent(sb("CF", [128, 132], F32)).ap()
        self.CC = ent(sb("CC", [128, 16], F32)).ap()
        self.CI = ent(sb("CI", [128, 128], F32)).ap()
        self.CS = ent(sb("CS", [128, 128], BF16)).ap()
        self.CRC = ent(sb("CRC", [128, 256], BF16)).ap()
        self.CRS = ent(sb("CRS", [128, 256], BF16)).ap()
        self.CT = ent(sb("CT", [128, 128], BF16)).ap()
        self.CTU = ent(sb("CTU", [128, 128], BF16)).ap()
        self.CUF = ent(sb("CUF", [128, 128], F32)).ap()
        self._ccells = {}
        self.psA = ent(nc.psum_tensor("psA", [128, 2048], F32)).ap()
        self.psB = ent(nc.psum_tensor("psB", [128, 1024], F32)).ap()
        self.psT = ent(nc.psum_tensor("psT", [128, 1024], BF16)).ap()
        self.psS = ent(nc.psum_tensor("psS", [128, 512], F32)).ap()

    @property
    def identb(self):
        return self.CB[:, 0:128]

    @property
    def onescolb(self):
        return self.CB[:, 128:129]

    @property
    def onesrowb(self):
        return self.CB[0:1, 128:256]

    @property
    def onescolf(self):
        return self.CF[:, 0:1]

    @property
    def onesrowf(self):
        return self.CF[0:1, 4:132]

    @property
    def onecellf(self):
        return self.CF[0:1, 0:1]

    def load_consts(self, W):
        P = self.P
        P.dma(self.CB[:, 0:128], W["c_identb"], writes=[("CB",)])
        P.dma(self.CI[:], W["c_identf"], writes=[("CI",)])
        P.dma(self.CS[:], W["c_seg"], writes=[("CS",)])
        P.dma(self.CRC[:], W["c_rcos"], writes=[("CRC",)])
        P.dma(self.CRS[:], W["c_rsin"], writes=[("CRS",)])
        P.dma(self.CT[:], W["c_causT"], writes=[("CT",)])
        P.dma(self.CTU[:], W["c_causU"], writes=[("CTU",)])
        P.dma(self.CUF[:], W["c_cuf"], writes=[("CUF",)])
        P.v(lambda e: e.memset(self.CB[:, 128:256], 1.0), writes=[("CB",)])
        P.v(lambda e: e.memset(self.CF[:], 1.0), writes=[("CF",)])

    def dbg(self, name, ap, reads):
        if not self.debug:
            return
        name = "dbg_" + name
        dt = ap.dtype
        h = self.nc.dram_tensor(name, list(ap.shape), dt, kind="ExternalOutput")
        self.dbg_names.append(name)
        self.P.dma(h.ap(), ap, reads=list(reads))

    def ccell(self, val):
        val = float(val)
        if val not in self._ccells:
            assert len(self._ccells) < 16
            i = len(self._ccells)
            ap = self.CC[:, i:i + 1]
            self._ccells[val] = ap
            self.nc.const_aps.register(F32, val, ap)
            self.P.v(lambda e: e.memset(ap, val), writes=[("CC",)])
        return self._ccells[val]



def tp_group(C, src_tile, dst, dst_lo, n, kr, kw, dst_ap=None, src_ap=None):
    P = C.P
    for k in range(n):
        def tp(e, k=k):
            s = src_tile(k)
            np_ = s.shape[0]
            return e.transpose(C.psT[0:s.shape[1], k * 128:k * 128 + np_],
                               s, C.identb[0:np_, 0:np_])
        P.t(tp, reads=list(kr) + [("CB",)], writes=[("psT", k)])
    d = dst_ap if dst_ap is not None else dst[:, dst_lo:dst_lo + n * 128]
    s = src_ap if src_ap is not None else C.psT[:, 0:n * 128]
    P.v(lambda e, d=d, s=s: e.tensor_copy(d, s),
        reads=[("psT", k) for k in range(8)], writes=list(kw))


def tp_tok2feat(C, src, dst, kr, kw, nt=NT, nd=ND):
    Fs, Ts = nd * 128, nt * 128
    for dc in range(nd):
        for tg in range(0, nt, 8):
            n = min(8, nt - tg)
            tp_group(C,
                     lambda k, dc=dc, tg=tg: src[:, (tg + k) * Fs + dc * 128:
                                                 (tg + k) * Fs + dc * 128 + 128],
                     dst, dc * Ts + tg * 128, n, kr, kw)


def ln_stats(C, x, mu, rstd, keys_x, key_mu, key_rstd, nt=NT, F=1024, eps=1e-5):
    P = C.P
    for t in range(nt):
        xt = x[:, t * F:(t + 1) * F]
        P.v(lambda e, xt=xt, t=t: e.reduce_sum(mu[:, t:t + 1], xt, axis=AX.X),
            reads=keys_x, writes=[key_mu])
        P.s(lambda e, xt=xt, t=t: e.activation(C.F1[:, 0:F], xt, AF.Square,
                                               accum_out=rstd[:, t:t + 1]),
            reads=keys_x, writes=[key_rstd, ("F1", 0)])
    P.v(lambda e: e.tensor_scalar(mu, mu, 1.0 / F, None, ALU.mult),
        reads=[key_mu], writes=[key_mu])
    P.v(lambda e: e.scalar_tensor_tensor(C.F1[:, 0:nt], mu, 1.0, mu,
                                         ALU.mult, ALU.mult),
        reads=[key_mu], writes=[("F1", 0)])
    P.v(lambda e: e.tensor_scalar(rstd, rstd, 1.0 / F, None, ALU.mult),
        reads=[key_rstd], writes=[key_rstd])
    P.v(lambda e: e.tensor_tensor(rstd, rstd, C.F1[:, 0:nt], ALU.subtract),
        reads=[key_rstd, ("F1", 0)], writes=[key_rstd])
    C.ccell(eps)
    C.ccell(-0.5)
    P.s(lambda e: e.activation(rstd, rstd, AF.Ln, bias=eps),
        reads=[key_rstd, ("CC",)], writes=[key_rstd])
    P.s(lambda e: e.activation(rstd, rstd, AF.Exp, scale=-0.5),
        reads=[key_rstd, ("CC",)], writes=[key_rstd])


def load_weight_bf16(C, dst, w_dram, kw, rows, cols, dst_col0=0, cast_eng="p",
                     stage=None, key_stage=None, scale_col=None, key_scale=None):
    P = C.P
    stage = C.F2 if stage is None else stage
    key_stage = key_stage or ("F2",)
    nr = (rows + 127) // 128
    w3 = w_dram.rearrange("(r p) n -> p r n", p=128) if rows > 128 else None
    if w3 is not None and nr * cols <= 2048:
        P.dma(stage[:, 0:nr * cols].rearrange("p (r n) -> p r n", r=nr),
              w3, reads=[key_stage], writes=[key_stage])
        for r in range(nr):
            d = dst[:, dst_col0 + r * cols: dst_col0 + (r + 1) * cols]
            src = stage[:, r * cols:(r + 1) * cols]
            if scale_col is None:
                getattr(P, cast_eng)(
                    lambda e, d=d, src=src: e.tensor_copy(d, src),
                    reads=[key_stage], writes=list(kw))
            else:
                getattr(P, cast_eng)(
                    lambda e, d=d, src=src, r=r: e.tensor_scalar(
                        d, src, scale_col[:, r:r + 1], None, ALU.mult),
                    reads=[key_stage, key_scale], writes=list(kw))
        return
    if w3 is not None and cols <= 1024 and scale_col is None and nr % 2 == 0:
        for r0 in range(0, nr, 2):
            P.dma(stage[:, 0:2 * cols].rearrange("p (r n) -> p r n", r=2),
                  w3[:, r0:r0 + 2, :], reads=[key_stage], writes=[key_stage])
            for rr in range(2):
                d = dst[:, dst_col0 + (r0 + rr) * cols:
                        dst_col0 + (r0 + rr + 1) * cols]
                getattr(P, cast_eng)(
                    lambda e, d=d, rr=rr: e.tensor_copy(
                        d, stage[:, rr * cols:(rr + 1) * cols]),
                    reads=[key_stage], writes=list(kw))
        return
    for r in range(nr):
        c0 = 0
        while c0 < cols:
            cw = min(2048, cols - c0)
            src = w3[:, r, c0:c0 + cw] if w3 is not None else w_dram[:, c0:c0 + cw]
            P.dma(stage[:, 0:cw], src, reads=[key_stage], writes=[key_stage])
            d = dst[:, dst_col0 + r * cols + c0: dst_col0 + r * cols + c0 + cw]
            if scale_col is None:
                getattr(P, cast_eng)(
                    lambda e, d=d, cw=cw: e.tensor_copy(d, stage[:, 0:cw]),
                    reads=[key_stage], writes=list(kw))
            else:
                getattr(P, cast_eng)(
                    lambda e, d=d, cw=cw, r=r: e.tensor_scalar(
                        d, stage[:, 0:cw], scale_col[:, r:r + 1], None, ALU.mult),
                    reads=[key_stage, key_scale], writes=list(kw))
            c0 += cw


def bcast_bias(C, b_dram, dst, kw, n):
    P = C.P
    ps = C.psB
    keys = [("psB", i) for i in range((n + 511) // 512)]
    row = C.F2[0:1, 0:n]
    P.dma(row, b_dram.unsqueeze(0), reads=[("F2",)], writes=[("F2",)])
    c0 = 0
    while c0 < n:
        cw = min(512, n - c0)
        P.t(lambda e, c0=c0, cw=cw: e.matmul(ps[:, c0:c0 + cw], C.onesrowf,
                                             row[0:1, c0:c0 + cw],
                                             start=True, stop=True),
            reads=[("F2",), ("CF",)], writes=[("psB", c0 // 512)])
        c0 += cw
    P.v(lambda e: e.tensor_copy(dst[:, 0:n], ps[:, 0:n]),
        reads=keys, writes=list(kw))




def emit_l0_attn(C, W):
    P = C.P
    X, B1, B2, B3, WA = C.X, C.B1, C.B2, C.B3, C.WA
    S1, F1, F2 = C.S1, C.F1, C.F2
    psA, psB, psS = C.psA, C.psB, C.psS
    KD = float(1024.0 ** -0.5)

    mu = S1[:, 0:16]
    rstd = S1[:, 16:32]
    rsq = S1[:, 32:48]
    rsk = S1[:, 48:56]
    mu2 = S1[:, 56:57]
    rstd2 = S1[:, 57:58]
    bo_b = S1[:, 64:1088]
    gam_b = F1[:, 0:1024]
    bet_b = F1[:, 1024:2048]
    K_WA0, K_WA1 = ("WA", 0), ("WA", 1)

    ln_stats(C, X, mu, rstd, [("X",)], ("mu",), ("rstd",))
    for t in range(NT):
        P.v(lambda e, t=t: e.tensor_scalar(
            B2[:, t * 1024:(t + 1) * 1024], X[:, t * 1024:(t + 1) * 1024],
            mu[:, t:t + 1], rstd[:, t:t + 1], ALU.subtract, ALU.mult),
            reads=[("X",), ("mu",), ("rstd",)], writes=[("B2",)])
    C.dbg("xn", B2[:], [("B2",)])
    C.dbg("mu", mu, [("mu",)])
    C.dbg("rstd", rstd, [("rstd",)])
    tp_tok2feat(C, B2, B1, [("B2",)], [("B1",)])
    C.dbg("xnT", B1[:], [("B1",)])

    def proj_tok(wkey, out_cb):
        load_weight_bf16(C, WA, W[wkey], [K_WA0], 1024, 1024)
        for t in range(NT):
            for h in range(2):
                for dc in range(ND):
                    P.t(lambda e, t=t, h=h, dc=dc: e.matmul(
                        psA[:, h * 512:h * 512 + 512],
                        B1[:, dc * T + t * 128: dc * T + t * 128 + 128],
                        WA[:, dc * 1024 + h * 512: dc * 1024 + h * 512 + 512],
                        start=(dc == 0), stop=(dc == ND - 1)),
                        reads=[("B1",), K_WA0], writes=[("psA", h)])
            out_cb(t)

    def k_out(t):
        P.s(lambda e, t=t: e.activation(B2[:, t * 1024:(t + 1) * 1024],
                                        psA[:, 0:1024], AF.Exp),
            reads=[("psA", 0), ("psA", 1)], writes=[("B2",)])
    proj_tok("Wk", k_out)
    C.dbg("ek", B2[:], [("B2",)])

    def v_slot(t):
        if t < 8:
            return C.B3[:, t * 1024:(t + 1) * 1024]
        return WA[:, 8192 + (t - 8) * 1024: 8192 + (t - 7) * 1024]

    def v_key(t):
        return ("B3",) if t < 8 else K_WA1

    def v_out(t):
        P.v(lambda e, t=t: e.tensor_copy(v_slot(t), psA[:, 0:1024]),
            reads=[("psA", 0), ("psA", 1)], writes=[v_key(t)])
    proj_tok("Wv", v_out)

    for dc in range(ND):
        for t in range(NT):
            P.t(lambda e, t=t, dc=dc: e.matmul(
                psS[:, dc:dc + 1],
                B2[:, t * 1024 + dc * 128: t * 1024 + dc * 128 + 128],
                C.onescolb, start=(t == 0), stop=(t == NT - 1)),
                reads=[("B2",), ("CB",)], writes=[("psS",)])
        for h in range(2):
            for t in range(NT):
                P.t(lambda e, t=t, dc=dc, h=h: e.matmul(
                    psB[:, h * 512:h * 512 + 512],
                    B2[:, t * 1024 + dc * 128: t * 1024 + dc * 128 + 128],
                    v_slot(t)[:, h * 512:h * 512 + 512],
                    start=(t == 0), stop=(t == NT - 1)),
                    reads=[("B2",), v_key(t)], writes=[("psB", h)])
        P.v(lambda e, dc=dc: e.reciprocal(rsk[:, dc:dc + 1], psS[:, dc:dc + 1]),
            reads=[("psS",)], writes=[("rsk",)])
        P.v(lambda e, dc=dc: e.tensor_scalar(
            WA[:, dc * 1024:(dc + 1) * 1024], psB[:, 0:1024],
            rsk[:, dc:dc + 1], None, ALU.mult),
            reads=[("psB", 0), ("psB", 1), ("rsk",)], writes=[K_WA0])
    C.dbg("ctx", WA[:, 0:8192], [K_WA0])

    load_weight_bf16(C, WA, W["Wq"], [K_WA1], 1024, 1024, dst_col0=8192)
    for ec in range(ND):
        for lp in range(4):
            for dc in range(ND):
                P.t(lambda e, ec=ec, lp=lp, dc=dc: e.matmul(
                    psA[:, lp * 512:lp * 512 + 512],
                    WA[:, 8192 + dc * 1024 + ec * 128:
                       8192 + dc * 1024 + ec * 128 + 128],
                    B1[:, dc * T + lp * 512: dc * T + lp * 512 + 512],
                    start=(dc == 0), stop=(dc == ND - 1)),
                    reads=[("B1",), K_WA1], writes=[("psA", lp)])
        P.s(lambda e, ec=ec: e.activation(B2[:, ec * T:(ec + 1) * T],
                                          psA[:, 0:2048], AF.Exp),
            reads=[("psA", h) for h in range(4)], writes=[("B2",)])

    for t in range(NT):
        for dc in range(ND):
            P.t(lambda e, t=t, dc=dc: e.matmul(
                psB[:, t:t + 1],
                B2[:, dc * T + t * 128: dc * T + t * 128 + 128],
                C.onescolb, start=(dc == 0), stop=(dc == ND - 1)),
                reads=[("B2",), ("CB",)], writes=[("psB", 0)])
    P.v(lambda e: e.reciprocal(rsq, psB[:, 0:16]),
        reads=[("psB", 0)], writes=[("rsq",)])
    P.v(lambda e: e.tensor_scalar(rsq, rsq, KD, None, ALU.mult),
        reads=[("rsq",)], writes=[("rsq",)])
    C.dbg("eq", B2[:], [("B2",)])
    C.dbg("rsq", rsq, [("rsq",)])

    for ec in range(ND):
        for lp in range(4):
            for dc in range(ND):
                P.t(lambda e, ec=ec, lp=lp, dc=dc: e.matmul(
                    psA[:, lp * 512:lp * 512 + 512],
                    WA[:, dc * 1024 + ec * 128: dc * 1024 + ec * 128 + 128],
                    B2[:, dc * T + lp * 512: dc * T + lp * 512 + 512],
                    start=(dc == 0), stop=(dc == ND - 1)),
                    reads=[K_WA0, ("B2",)], writes=[("psA", lp)])
        P.v(lambda e, ec=ec: e.tensor_copy(B1[:, ec * T:(ec + 1) * T],
                                           psA[:, 0:2048]),
            reads=[("psA", h) for h in range(4)], writes=[("B1",)])

    C.dbg("attnT", B1[:], [("B1",)])

    load_weight_bf16(C, WA, W["Wo"], [K_WA1], 1024, 1024, dst_col0=8192)
    bcast_bias(C, W["bo"], bo_b, [("bo_b",)], 1024)
    bcast_bias(C, W["gamma"], gam_b, [("F1", 0)], 1024)
    bcast_bias(C, W["beta"], bet_b, [("F1", 1)], 1024)

    for t in range(NT):
        for h in range(2):
            for ec in range(ND):
                P.t(lambda e, t=t, h=h, ec=ec: e.matmul(
                    psA[:, h * 512:h * 512 + 512],
                    B1[:, ec * T + t * 128: ec * T + t * 128 + 128],
                    WA[:, 8192 + ec * 1024 + h * 512:
                       8192 + ec * 1024 + h * 512 + 512],
                    start=(ec == 0), stop=(ec == ND - 1)),
                    reads=[("B1",), K_WA1], writes=[("psA", h)])
        KPS = [("psA", 0), ("psA", 1)]
        xt = X[:, t * 1024:(t + 1) * 1024]
        if t == 0 and C.debug:
            P.v(lambda e: e.tensor_copy(F2[:, 0:1024], psA[:, 0:1024]),
                reads=KPS, writes=[("F2",)])
            C.dbg("ops0", F2[:, 0:1024], [("F2",)])
            C.dbg("bo_b", bo_b, [("bo_b",)])
        P.v(lambda e, t=t: e.scalar_tensor_tensor(
            F2[:, 0:1024], psA[:, 0:1024], rsq[:, t:t + 1], bo_b,
            ALU.mult, ALU.add),
            reads=KPS + [("rsq",), ("bo_b",)], writes=[("F2",)])
        if t == 0:
            C.dbg("t1a", F2[:, 0:1024], [("F2",)])
        P.v(lambda e, t=t, xt=xt: e.tensor_scalar(
            psB[:, 0:1024], xt, mu[:, t:t + 1], rstd[:, t:t + 1],
            ALU.subtract, ALU.mult),
            reads=[("X",), ("mu",), ("rstd",)], writes=[("psB", 0), ("psB", 1)])
        P.v(lambda e: e.tensor_tensor(F2[:, 0:1024], F2[:, 0:1024],
                                      psB[:, 0:1024], ALU.add),
            reads=[("F2",), ("psB", 0), ("psB", 1)], writes=[("F2",)])
        if t == 0:
            C.dbg("t1b", F2[:, 0:1024], [("F2",)])
        P.v(lambda e: e.reduce_sum(mu2, F2[:, 0:1024], axis=AX.X),
            reads=[("F2",)], writes=[("mu2",)])
        P.s(lambda e: e.activation(psB[:, 0:1024], F2[:, 0:1024], AF.Square,
                                   accum_out=rstd2),
            reads=[("F2",)], writes=[("rstd2",), ("psB", 0), ("psB", 1)])
        P.v(lambda e: e.tensor_scalar(mu2, mu2, 1.0 / 1024, None, ALU.mult),
            reads=[("mu2",)], writes=[("mu2",)])
        P.v(lambda e: e.scalar_tensor_tensor(S1[:, 58:59], mu2, 1.0, mu2,
                                             ALU.mult, ALU.mult),
            reads=[("mu2",)], writes=[("s58",)])
        P.v(lambda e: e.tensor_scalar(rstd2, rstd2, 1.0 / 1024, None, ALU.mult),
            reads=[("rstd2",)], writes=[("rstd2",)])
        P.v(lambda e: e.tensor_tensor(rstd2, rstd2, S1[:, 58:59], ALU.subtract),
            reads=[("rstd2",), ("s58",)], writes=[("rstd2",)])
        P.s(lambda e: e.activation(rstd2, rstd2, AF.Ln, bias=1e-5),
            reads=[("rstd2",), ("CC",)], writes=[("rstd2",)])
        P.s(lambda e: e.activation(rstd2, rstd2, AF.Exp, scale=-0.5),
            reads=[("rstd2",), ("CC",)], writes=[("rstd2",)])
        P.v(lambda e: e.tensor_scalar(F2[:, 0:1024], F2[:, 0:1024], mu2, rstd2,
                                      ALU.subtract, ALU.mult),
            reads=[("F2",), ("mu2",), ("rstd2",)], writes=[("F2",)])
        P.v(lambda e: e.tensor_tensor(F2[:, 0:1024], F2[:, 0:1024], gam_b,
                                      ALU.mult),
            reads=[("F2",), ("F1", 0)], writes=[("F2",)])
        P.v(lambda e: e.tensor_tensor(F2[:, 0:1024], F2[:, 0:1024], bet_b,
                                      ALU.add),
            reads=[("F2",), ("F1", 1)], writes=[("F2",)])
        P.v(lambda e, xt=xt: e.tensor_tensor(xt, xt, F2[:, 0:1024], ALU.add),
            reads=[("X",), ("F2",)], writes=[("X",)])




def emit_l0_ffn(C, W):
    P = C.P
    X, B1, B2, B3, WA = C.X, C.B1, C.B2, C.B3, C.WA
    S1, F1, F2 = C.S1, C.F1, C.F2
    psA, psB, psS = C.psA, C.psB, C.psS
    E = 16

    mu = S1[:, 0:16]
    rstd = S1[:, 16:32]
    gate_w = S1[:, 64:192]
    route = S1[:, 192:448]
    gate_bb = S1[:, 448:464]
    wsel = S1[:, 480:736]
    K_WA0, K_WA1 = ("WA", 0), ("WA", 1)

    ln_stats(C, X, mu, rstd, [("X",)], ("mu",), ("rstd",))
    for t in range(NT):
        P.v(lambda e, t=t: e.tensor_scalar(
            B2[:, t * 1024:(t + 1) * 1024], X[:, t * 1024:(t + 1) * 1024],
            mu[:, t:t + 1], rstd[:, t:t + 1], ALU.subtract, ALU.mult),
            reads=[("X",), ("mu",), ("rstd",)], writes=[("B2",)])
    tp_tok2feat(C, B2, B1, [("B2",)], [("B1",)])

    for cc in range(8):
        P.dma(gate_w[:, cc * 16:(cc + 1) * 16],
              W["gate_w"][cc * 128:(cc + 1) * 128, :],
              reads=[("gw",)], writes=[("gw",)])
    row = F2[0:1, 0:16]
    P.dma(row, W["gate_b"].unsqueeze(0), reads=[("F2",)], writes=[("F2",)])
    P.t(lambda e: e.matmul(psB[:, 0:16], C.onesrowf, row, start=True, stop=True),
        reads=[("F2",), ("CF",)], writes=[("psB", 0)])
    P.v(lambda e: e.tensor_copy(gate_bb, psB[:, 0:16]),
        reads=[("psB", 0)], writes=[("gbb",)])

    for dc in range(ND):
        for tg in range(0, NT, 4):
            for k in range(4):
                t = tg + k
                P.v(lambda e, t=t, dc=dc, k=k: e.tensor_scalar(
                    F2[:, k * 128:(k + 1) * 128],
                    X[:, t * 1024 + dc * 128: t * 1024 + dc * 128 + 128],
                    mu[:, t:t + 1], rstd[:, t:t + 1], ALU.subtract, ALU.mult),
                    reads=[("X",), ("mu",), ("rstd",)], writes=[("F2",)])
                P.t(lambda e, k=k: e.transpose(
                    psA[:, 1536 + k * 128:1536 + (k + 1) * 128],
                    F2[:, k * 128:(k + 1) * 128], C.CI),
                    reads=[("F2",), ("CI",)], writes=[("psA", 3)])
            P.v(lambda e, tg=tg: e.tensor_copy(F1[:, tg * 128:(tg + 4) * 128],
                                               psA[:, 1536:2048]),
                reads=[("psA", 3)], writes=[("F1", 0), ("F1", 1)])
        for t in range(NT):
            P.t(lambda e, t=t, dc=dc: e.matmul(
                psS[:, t * 16:(t + 1) * 16],
                F1[:, t * 128:(t + 1) * 128],
                gate_w[:, dc * 16:(dc + 1) * 16],
                start=True, stop=True),
                reads=[("F1", 0), ("F1", 1), ("gw",)], writes=[("psS",)])
        if dc == 0:
            P.v(lambda e: e.tensor_copy(route, psS[:, 0:256]),
                reads=[("psS",)], writes=[("route",)])
        else:
            P.v(lambda e: e.tensor_tensor(route, route, psS[:, 0:256], ALU.add),
                reads=[("psS",), ("route",)], writes=[("route",)])
    P.v(lambda e: e.tensor_tensor(
        route.rearrange("p (t e) -> p t e", t=16),
        route.rearrange("p (t e) -> p t e", t=16),
        gate_bb.unsqueeze(1).broadcast_to([128, 16, 16]), ALU.add),
        reads=[("route",), ("gbb",)], writes=[("route",)])
    P.s(lambda e: e.activation(route, route, AF.Sigmoid),
        reads=[("route",)], writes=[("route",)])

    for t in range(NT):
        rt = route[:, t * 16:(t + 1) * 16]
        m1 = S1[:, 464:465]
        m2 = S1[:, 465:466]
        rr = S1[:, 466:467]
        msk = F2[:, 0:16]
        r2 = F2[:, 16:32]
        P.v(lambda e, rt=rt: e.reduce_max(m1, rt, axis=AX.X),
            reads=[("route",)], writes=[("m1",)])
        P.v(lambda e, rt=rt: e.tensor_scalar(msk, rt, m1, None, ALU.is_ge),
            reads=[("route",), ("m1",)], writes=[("F2",)])
        P.v(lambda e, rt=rt: e.scalar_tensor_tensor(r2, msk, -1e30, rt,
                                                    ALU.mult, ALU.add),
            reads=[("route",), ("F2",)], writes=[("F2",)])
        P.v(lambda e: e.reduce_max(m2, r2, axis=AX.X),
            reads=[("F2",)], writes=[("m2",)])
        P.v(lambda e: e.tensor_tensor(rr, m1, m2, ALU.add),
            reads=[("m1",), ("m2",)], writes=[("rr",)])
        P.v(lambda e: e.reciprocal(rr, rr), reads=[("rr",)], writes=[("rr",)])
        P.v(lambda e, rt=rt: e.tensor_scalar(msk, rt, m2, rr, ALU.is_ge, ALU.mult),
            reads=[("route",), ("m2",), ("rr",)], writes=[("F2",)])
        P.v(lambda e, rt=rt, t=t: e.tensor_tensor(
            wsel[:, t * 16:(t + 1) * 16], rt, msk, ALU.mult),
            reads=[("route",), ("F2",)], writes=[("wsel",)])

    for t in range(NT):
        P.v(lambda e, t=t: e.tensor_scalar(
            psB[:, 0:1024], X[:, t * 1024:(t + 1) * 1024],
            mu[:, t:t + 1], rstd[:, t:t + 1], ALU.subtract, ALU.mult),
            reads=[("X",), ("mu",), ("rstd",)], writes=[("psB", 0), ("psB", 1)])
        P.v(lambda e, t=t: e.tensor_tensor(
            X[:, t * 1024:(t + 1) * 1024], X[:, t * 1024:(t + 1) * 1024],
            psB[:, 0:1024], ALU.add),
            reads=[("psB", 0), ("psB", 1), ("X",)], writes=[("X",)])

    for e_i in range(E + 1):
        if e_i < E:
            wg_d, wu_d, wd_d = (W["experts_wg"][e_i], W["experts_wu"][e_i],
                                W["experts_wd"][e_i])
        else:
            wg_d, wu_d, wd_d = W["shared_wg"], W["shared_wu"], W["shared_wd"]
        load_weight_bf16(C, WA, wg_d, [K_WA0], 1024, 512, dst_col0=0)
        load_weight_bf16(C, WA, wu_d, [K_WA0], 1024, 512, dst_col0=4096)
        load_weight_bf16(C, WA, wd_d, [K_WA1], 512, 1024, dst_col0=8192)
        for ic in range(4):
            for lp in range(4):
                for dc in range(ND):
                    P.t(lambda e, ic=ic, lp=lp, dc=dc: e.matmul(
                        psA[:, lp * 512:(lp + 1) * 512],
                        WA[:, dc * 512 + ic * 128: dc * 512 + ic * 128 + 128],
                        B1[:, dc * T + lp * 512: dc * T + lp * 512 + 512],
                        start=(dc == 0), stop=(dc == ND - 1)),
                        reads=[K_WA0, ("B1",)], writes=[("psA", lp)])
            P.s(lambda e, ic=ic: e.activation(B2[:, ic * T:(ic + 1) * T],
                                              psA[:, 0:2048], AF.Silu),
                reads=[("psA", 0), ("psA", 1), ("psA", 2), ("psA", 3)],
                writes=[("B2",)])
        for ic in range(4):
            for lp in range(4):
                for dc in range(ND):
                    P.t(lambda e, ic=ic, lp=lp, dc=dc: e.matmul(
                        psA[:, lp * 512:(lp + 1) * 512],
                        WA[:, 4096 + dc * 512 + ic * 128:
                           4096 + dc * 512 + ic * 128 + 128],
                        B1[:, dc * T + lp * 512: dc * T + lp * 512 + 512],
                        start=(dc == 0), stop=(dc == ND - 1)),
                        reads=[K_WA0, ("B1",)], writes=[("psA", lp)])
            P.v(lambda e, ic=ic: e.tensor_tensor(
                B3[:, ic * T:(ic + 1) * T], B2[:, ic * T:(ic + 1) * T],
                psA[:, 0:2048], ALU.mult),
                reads=[("psA", 0), ("psA", 1), ("psA", 2), ("psA", 3), ("B2",)],
                writes=[("B3",)])
        for t in range(NT):
            for h in range(2):
                for ic in range(4):
                    P.t(lambda e, t=t, h=h, ic=ic: e.matmul(
                        psB[:, h * 512:(h + 1) * 512],
                        B3[:, ic * T + t * 128: ic * T + t * 128 + 128],
                        WA[:, 8192 + ic * 1024 + h * 512:
                           8192 + ic * 1024 + h * 512 + 512],
                        start=(ic == 0), stop=(ic == 3)),
                        reads=[K_WA1, ("B3",)], writes=[("psB", h)])
            if e_i < E:
                wcell = wsel[:, t * 16 + e_i: t * 16 + e_i + 1]
                P.v(lambda e, t=t, wcell=wcell: e.scalar_tensor_tensor(
                    X[:, t * 1024:(t + 1) * 1024], psB[:, 0:1024], wcell,
                    X[:, t * 1024:(t + 1) * 1024], ALU.mult, ALU.add),
                    reads=[("psB", 0), ("psB", 1), ("wsel",), ("X",)],
                    writes=[("X",)])
            else:
                P.v(lambda e, t=t: e.tensor_tensor(
                    X[:, t * 1024:(t + 1) * 1024],
                    X[:, t * 1024:(t + 1) * 1024], psB[:, 0:1024], ALU.add),
                    reads=[("psB", 0), ("psB", 1), ("X",)], writes=[("X",)])




def emit_l1_attn(C, W):
    P = C.P
    X, B1, B2, B3, WA = C.X, C.B1, C.B2, C.B3, C.WA
    S1, F1, F2 = C.S1, C.F1, C.F2
    psA, psB, psS = C.psA, C.psB, C.psS

    mu = S1[:, 0:16]
    rstd = S1[:, 16:32]
    ksum = S1[:, 32:40]
    bq_col = S1[:, 40:48]
    bk_col = S1[:, 48:56]
    K_WA0, K_WA1 = ("WA", 0), ("WA", 1)
    KPS4 = [("psA", 0), ("psA", 1), ("psA", 2), ("psA", 3)]

    ln_stats(C, X, mu, rstd, [("X",)], ("mu",), ("rstd",))
    for t in range(NT):
        P.v(lambda e, t=t: e.tensor_scalar(
            B2[:, t * 1024:(t + 1) * 1024], X[:, t * 1024:(t + 1) * 1024],
            mu[:, t:t + 1], rstd[:, t:t + 1], ALU.subtract, ALU.mult),
            reads=[("X",), ("mu",), ("rstd",)], writes=[("B2",)])
    tp_tok2feat(C, B2, B1, [("B2",)], [("B1",)])

    def bias_cols(bname, dstcols, key):
        P.dma(F2[0:1, 0:1024], W[bname].unsqueeze(0), reads=[("F2",)],
              writes=[("F2",)])
        for c in range(8):
            P.t(lambda e, c=c: e.matmul(psB[:, c:c + 1],
                                        F2[0:1, c * 128:(c + 1) * 128],
                                        C.onecellf, start=True, stop=True),
                reads=[("F2",), ("CF",)], writes=[("psB", 0)])
        P.v(lambda e: e.tensor_copy(dstcols, psB[:, 0:8]),
            reads=[("psB", 0)], writes=[key])
    bias_cols("bq", bq_col, ("bqc",))
    bias_cols("bk", bk_col, ("bkc",))
    KBROW = ("B3", "brow")

    def bias_row(bname):
        P.dma(F2[0:1, 0:1024], W[bname].unsqueeze(0), reads=[("F2",)],
              writes=[("F2",)])
        P.v(lambda e: e.tensor_copy(B3[0:1, 9216:10240], F2[0:1, 0:1024]),
            reads=[("F2",)], writes=[KBROW])

    def proj_feat(dst_cb, wkey):
        load_weight_bf16(C, WA, W[wkey], [K_WA0], 1024, 1024)
        for ec in range(ND):
            for lp in range(4):
                for dc in range(ND):
                    P.t(lambda e, ec=ec, lp=lp, dc=dc: e.matmul(
                        psA[:, lp * 512:lp * 512 + 512],
                        WA[:, dc * 1024 + ec * 128: dc * 1024 + ec * 128 + 128],
                        B1[:, dc * T + lp * 512: dc * T + lp * 512 + 512],
                        start=(dc == 0), stop=(dc == ND - 1)),
                        reads=[("B1",), K_WA0], writes=[("psA", lp)])
            dst_cb(ec)

    def fm_evac(ec, bcol, dst, dkey):
        P.v(lambda e, ec=ec: e.tensor_scalar(
            F2[:, 0:2048], psA[:, 0:2048], bcol[:, ec:ec + 1], 0.0,
            ALU.add, ALU.min), reads=KPS4 + [("bqc",), ("bkc",)],
            writes=[("F2",)])
        P.s(lambda e: e.activation(F2[:, 0:2048], F2[:, 0:2048], AF.Exp),
            reads=[("F2",)], writes=[("F2",)])
        P.v(lambda e, ec=ec: e.tensor_scalar(
            F1[:, 0:2048], psA[:, 0:2048], bcol[:, ec:ec + 1], 0.0,
            ALU.add, ALU.max), reads=KPS4 + [("bqc",), ("bkc",)],
            writes=[("F1", 0), ("F1", 1)])
        P.v(lambda e, dst=dst: e.tensor_tensor(dst, F1[:, 0:2048],
                                               F2[:, 0:2048], ALU.add),
            reads=[("F1", 0), ("F1", 1), ("F2",)], writes=[dkey])

    def k_out(ec):
        stage = WA[:, 8192:10240]
        fm_evac(ec, bk_col, stage, K_WA1)
        B2v = B2[:].rearrange("p (t f) -> p t f", t=NT)
        for tg in range(0, NT, 8):
            tp_group(C, lambda k, tg=tg: stage[:, (tg + k) * 128:(tg + k + 1) * 128],
                     None, 0, 8, [K_WA1],
                     [("B2",)], dst_ap=B2v[:, tg:tg + 8, ec * 128:(ec + 1) * 128])
    proj_feat(k_out, "wk")

    bias_row("bv")
    load_weight_bf16(C, WA, W["wv"], [K_WA0], 1024, 1024)

    def v_slot(t):
        if t < 8:
            return B3[:, t * 1024:(t + 1) * 1024]
        return WA[:, 8192 + (t - 8) * 1024: 8192 + (t - 7) * 1024]

    def v_key(t):
        return ("B3",) if t < 8 else K_WA1
    for t in range(NT):
        for h in range(2):
            for dc in range(ND):
                P.t(lambda e, t=t, h=h, dc=dc: e.matmul(
                    psA[:, h * 512:h * 512 + 512],
                    B1[:, dc * T + t * 128: dc * T + t * 128 + 128],
                    WA[:, dc * 1024 + h * 512: dc * 1024 + h * 512 + 512],
                    start=(dc == 0), stop=False),
                    reads=[("B1",), K_WA0], writes=[("psA", h)])
            P.t(lambda e, t=t, h=h: e.matmul(
                psA[:, h * 512:h * 512 + 512], C.onesrowb,
                B3[0:1, 9216 + h * 512: 9216 + h * 512 + 512],
                start=False, stop=True),
                reads=[("CB",), KBROW], writes=[("psA", h)])
        P.v(lambda e, t=t: e.tensor_copy(v_slot(t), psA[:, 0:1024]),
            reads=[("psA", 0), ("psA", 1)], writes=[v_key(t)])

    for pg in range(2):
        for pp in range(4):
            p = pg * 4 + pp
            for t in range(NT):
                P.t(lambda e, p=p, pp=pp, t=t: e.matmul(
                    psA[:, pp * 512:pp * 512 + 128],
                    B2[:, t * 1024 + p * 128: t * 1024 + p * 128 + 128],
                    v_slot(t)[:, p * 128:(p + 1) * 128],
                    start=(t == 0), stop=(t == NT - 1)),
                    reads=[("B2",), v_key(t)], writes=[("psA", pp)])
            for t in range(NT):
                P.t(lambda e, p=p, t=t: e.matmul(
                    psS[:, p:p + 1],
                    B2[:, t * 1024 + p * 128: t * 1024 + p * 128 + 128],
                    C.onescolb, start=(t == 0), stop=(t == NT - 1)),
                    reads=[("B2",), ("CB",)], writes=[("psS",)])
        P.v(lambda e, pg=pg: e.tensor_copy(
            B3[:, 8192 + pg * 512: 8192 + (pg + 1) * 512]
            .rearrange("p (b x) -> p b x", b=4),
            psA[:].rearrange("p (b x) -> p b x", b=4)[:, :, 0:128]),
            reads=KPS4, writes=[("B3", "kv")])
    P.v(lambda e: e.tensor_copy(ksum, psS[:, 0:8]),
        reads=[("psS",)], writes=[("ksum",)])

    load_weight_bf16(C, WA, W["wq"], [K_WA0], 1024, 1024)

    def q_out(ec):
        fm_evac(ec, bq_col, B2[:, ec * T:(ec + 1) * T], ("B2",))
    proj_feat(q_out, "wq")

    for lp in range(4):
        for dc in range(ND):
            P.v(lambda e, lp=lp, dc=dc: e.tensor_scalar(
                WA[:, 0:512], B2[:, dc * T + lp * 512: dc * T + lp * 512 + 512],
                ksum[:, dc:dc + 1], None, ALU.mult),
                reads=[("B2",), ("ksum",)], writes=[K_WA0])
            P.t(lambda e, lp=lp, dc=dc: e.matmul(
                psS[0:16, 0:512], C.CS[:, dc * 16:(dc + 1) * 16], WA[:, 0:512],
                start=(dc == 0), stop=(dc == ND - 1)),
                reads=[("CS",), K_WA0], writes=[("psS",)])
        P.v(lambda e, lp=lp: e.tensor_scalar(
            F1[0:16, lp * 512:(lp + 1) * 512], psS[0:16, 0:512], 1e-6, None,
            ALU.add), reads=[("psS",)], writes=[("F1", 0), ("F1", 1)])
    P.v(lambda e: e.reciprocal(F1[0:16, 0:2048], F1[0:16, 0:2048]),
        reads=[("F1", 0), ("F1", 1)], writes=[("F1", 0), ("F1", 1)])

    def slot(c):
        return WA if c < 4 else B3

    def slot_col(c):
        return (c % 4) * 2048

    def slot_key(c):
        return K_WA0 if c < 4 else ("B3",)
    for h in range(16):
        c = h // 2
        r0 = (h % 2) * 64
        P.dma(F2[0:1, 0:2048], F1[h:h + 1, 0:2048],
              reads=[("F1", 0), ("F1", 1), ("F2",)], writes=[("F2",)])
        for lp in range(4):
            P.t(lambda e, h=h, r0=r0, lp=lp: e.matmul(
                psA[r0:r0 + 64, lp * 512:lp * 512 + 512],
                B3[r0:r0 + 64, 8192 + (h // 2) * 128 + r0: 8192 + (h // 2) * 128 + r0 + 64],
                B2[r0:r0 + 64, (h // 2) * T + lp * 512: (h // 2) * T + lp * 512 + 512],
                start=True, stop=True),
                reads=[("B3", "kv"), ("B2",)], writes=[("psA", lp)])
            P.t(lambda e, r0=r0, lp=lp: e.matmul(
                psB[r0:r0 + 64, (lp % 2) * 512:(lp % 2) * 512 + 512],
                C.onesrowf[0:1, 0:64], F2[0:1, lp * 512:(lp + 1) * 512],
                start=True, stop=True),
                reads=[("F2",), ("CF",)], writes=[("psB", lp % 2)])
            P.s(lambda e, h=h, c=c, r0=r0, lp=lp: e.copy(
                slot(c)[r0:r0 + 64, slot_col(c) + lp * 512: slot_col(c) + lp * 512 + 512],
                psA[r0:r0 + 64, lp * 512:lp * 512 + 512]),
                reads=[("psA", lp)], writes=[slot_key(c)])
            P.v(lambda e, h=h, c=c, r0=r0, lp=lp: e.tensor_tensor(
                slot(c)[r0:r0 + 64, slot_col(c) + lp * 512: slot_col(c) + lp * 512 + 512],
                slot(c)[r0:r0 + 64, slot_col(c) + lp * 512: slot_col(c) + lp * 512 + 512],
                psB[r0:r0 + 64, (lp % 2) * 512:(lp % 2) * 512 + 512], ALU.mult),
                reads=[("psB", lp % 2), slot_key(c)], writes=[slot_key(c)])

    load_weight_bf16(C, WA, W["wo"], [K_WA1], 1024, 1024, dst_col0=8192)
    bias_row("bo")
    for t in range(NT):
        for h2 in range(2):
            for c in range(8):
                P.t(lambda e, t=t, h2=h2, c=c: e.matmul(
                    psA[:, h2 * 512:h2 * 512 + 512],
                    slot(c)[:, slot_col(c) + t * 128: slot_col(c) + t * 128 + 128],
                    WA[:, 8192 + c * 1024 + h2 * 512: 8192 + c * 1024 + h2 * 512 + 512],
                    start=(c == 0), stop=False),
                    reads=[slot_key(c), K_WA1], writes=[("psA", h2)])
            P.t(lambda e, t=t, h2=h2: e.matmul(
                psA[:, h2 * 512:h2 * 512 + 512], C.onesrowb,
                B3[0:1, 9216 + h2 * 512: 9216 + h2 * 512 + 512],
                start=False, stop=True),
                reads=[("CB",), KBROW], writes=[("psA", h2)])
        P.v(lambda e, t=t: e.tensor_tensor(
            X[:, t * 1024:(t + 1) * 1024], X[:, t * 1024:(t + 1) * 1024],
            psA[:, 0:1024], ALU.add),
            reads=[("psA", 0), ("psA", 1), ("X",)], writes=[("X",)])




def emit_l1_ffn(C, W):
    P = C.P
    nc = C.nc
    X, B1, B2, B3, WA = C.X, C.B1, C.B2, C.B3, C.WA
    S1, F1, F2 = C.S1, C.F1, C.F2
    psA, psB, psS = C.psA, C.psB, C.psS
    KPS4 = [("psA", h) for h in range(4)]

    mu = S1[:, 0:16]
    rstd = S1[:, 16:32]
    ms1 = S1[:, 32:48]
    rs1 = S1[:, 48:64]
    w1col = S1[:, 64:72]
    w2col = S1[:, 128:136]
    gatew = S1[:, 72:128 + 8] if False else S1[:, 1280:1344]
    ebias = S1[:, 136:144]
    logit = S1[:, 144:272]
    wsel = S1[:, 272:400]
    ms2 = S1[:, 400:416]
    rr2 = S1[:, 416:432]
    x1d = nc.dram_tensor("l1_x1", [T, D], F32, kind="Internal").ap()
    x1v = x1d.rearrange("(t p) f -> p t f", p=128)

    KW = {"kva": ("WA", "kva"), "kvb": ("WA", "kvb"), "lat": ("WA", "lat"),
          "rope": ("WA", "rope"), "stg": ("WA", "stg"), "q": ("WA", "q")}

    def b2k(j):
        return ("B2", j)
    KB2 = [b2k(j) for j in range(8)]

    ln_stats(C, X, mu, rstd, [("X",)], ("mu",), ("rstd",))
    for t in range(NT):
        P.v(lambda e, t=t: e.tensor_scalar(
            F2[:, 0:1024], X[:, t * 1024:(t + 1) * 1024],
            mu[:, t:t + 1], rstd[:, t:t + 1], ALU.subtract, ALU.mult),
            reads=[("X",), ("mu",), ("rstd",)], writes=[("F2",)])
        P.s(lambda e, t=t: e.activation(F1[:, 0:1024], F2[:, 0:1024],
                                        AF.Square, accum_out=ms1[:, t:t + 1]),
            reads=[("F2",)], writes=[("F1", 0), ("ms1",)])
    C.ccell(1e-5)
    C.ccell(-0.5)
    P.v(lambda e: e.tensor_scalar(rs1, ms1, 1.0 / 1024, None, ALU.mult),
        reads=[("ms1",)], writes=[("rs1",)])
    P.s(lambda e: e.activation(rs1, rs1, AF.Ln, bias=1e-5),
        reads=[("rs1",), ("CC",)], writes=[("rs1",)])
    P.s(lambda e: e.activation(rs1, rs1, AF.Exp, scale=-0.5),
        reads=[("rs1",), ("CC",)], writes=[("rs1",)])
    P.v(lambda e: e.tensor_tensor(rs1, rs1, rstd, ALU.mult),
        reads=[("rs1",), ("rstd",)], writes=[("rs1",)])
    for t in range(NT):
        P.v(lambda e, t=t: e.tensor_scalar(
            B2[:, t * 1024:(t + 1) * 1024], X[:, t * 1024:(t + 1) * 1024],
            mu[:, t:t + 1], rs1[:, t:t + 1], ALU.subtract, ALU.mult),
            reads=[("X",), ("mu",), ("rs1",)], writes=[b2k(t // 2)])
    tp_tok2feat(C, B2, B1, KB2, [("B1",)])

    def col_from_row(wname, dst, key, n=8):
        P.dma(F2[0:1, 0:n * 128], W[wname].unsqueeze(0), reads=[("F2",)],
              writes=[("F2",)])
        for cc in range(n):
            P.t(lambda e, cc=cc: e.matmul(psB[:, cc:cc + 1],
                                          F2[0:1, cc * 128:(cc + 1) * 128],
                                          C.onecellf, start=True, stop=True),
                reads=[("F2",), ("CF",)], writes=[("psB", 0)])
        P.v(lambda e, dst=dst, n=n: e.tensor_copy(dst, psB[:, 0:n]),
            reads=[("psB", 0)], writes=[key])
    col_from_row("w_ln1", w1col, ("w1c",))
    col_from_row("w_ln2", w2col, ("w2c",))
    load_weight_bf16(C, WA, W["Wkva"], [KW["kva"]], 1024, 160,
                     scale_col=w1col, key_scale=("w1c",))
    load_weight_bf16(C, WA, W["Wkvb"], [KW["kvb"]], 128, 512, dst_col0=1280)
    load_weight_bf16(C, WA, W["Wq"], [KW["q"]], 1024, 1024, dst_col0=8192,
                     scale_col=w1col, key_scale=("w1c",))

    for oc2 in range(2):
        m = 128 if oc2 == 0 else 32
        for lp in range(4):
            for dc in range(ND):
                P.t(lambda e, oc2=oc2, m=m, lp=lp, dc=dc: e.matmul(
                    psA[0:m, lp * 512:lp * 512 + 512],
                    WA[:, dc * 160 + oc2 * 128: dc * 160 + oc2 * 128 + m],
                    B1[:, dc * T + lp * 512: dc * T + lp * 512 + 512],
                    start=(dc == 0), stop=(dc == ND - 1)),
                    reads=[KW["kva"], ("B1",)], writes=[("psA", lp)])
        dstc = 2048 if oc2 == 0 else 4096
        dkey = KW["lat"] if oc2 == 0 else KW["rope"]
        P.v(lambda e, m=m, dstc=dstc: e.tensor_copy(
            WA[0:m, dstc:dstc + 2048], psA[0:m, 0:2048]),
            reads=KPS4, writes=[dkey])

    VT = B3[:, 0:8448].rearrange("p (t h x) -> p t h x", h=16, x=33)
    P.v(lambda e: e.memset(VT[:, :, :, 32:33], 1.0), writes=[("B3", "v")])
    for c in range(4):
        for lp in range(4):
            P.t(lambda e, c=c, lp=lp: e.matmul(
                psA[:, lp * 512:lp * 512 + 512],
                WA[:, 1280 + c * 128:1280 + (c + 1) * 128],
                WA[:, 2048 + lp * 512: 2048 + lp * 512 + 512],
                start=True, stop=True),
                reads=[KW["kvb"], KW["lat"]], writes=[("psA", lp)])
        P.v(lambda e: e.tensor_copy(WA[:, 6144:8192], psA[:, 0:2048]),
            reads=KPS4, writes=[KW["stg"]])
        for hl in range(4):
            h = 4 * c + hl
            j, r0 = h // 2, (h % 2) * 64
            P.dma(B2[r0:r0 + 32, j * 2048:(j + 1) * 2048],
                  WA[hl * 32:(hl + 1) * 32, 6144:8192],
                  reads=[KW["stg"]], writes=[b2k(j)])
        for tg in range(0, NT, 8):
            tp_group(C, lambda k, tg=tg: WA[:, 6144 + (tg + k) * 128:
                                            6144 + (tg + k + 1) * 128],
                     None, 0, 8, [KW["stg"]], [("B3", "v")],
                     dst_ap=VT[:, tg:tg + 8, 4 * c:4 * c + 4, 0:32],
                     src_ap=C.psT[:, 0:1024].rearrange(
                         "p (t h x) -> p t h x", t=8, h=4))

    RT = B3[:, 8448:8960]
    for tg in range(0, NT, 8):
        tp_group(C, lambda k, tg=tg: WA[0:32, 4096 + (tg + k) * 128:
                                        4096 + (tg + k + 1) * 128],
                 None, 0, 8, [KW["rope"]], [("B3", "rt")],
                 dst_ap=RT.rearrange("p (t x) -> p t x", x=32)[:, tg:tg + 8, :],
                 src_ap=C.psT[:, 0:1024].rearrange(
                     "p (t x) -> p t x", x=128)[:, :, 0:32])
    _rope_rotate(C, RT, ("B3", "rt"))
    for tg in range(0, NT, 8):
        for k in range(8):
            P.t(lambda e, k=k, tg=tg: e.transpose(
                C.psT[0:32, k * 128:(k + 1) * 128],
                RT[:, (tg + k) * 32:(tg + k + 1) * 32], C.identb),
                reads=[("B3", "rt"), ("CB",)], writes=[("psT", k)])
        P.v(lambda e, tg=tg: e.tensor_copy(
            WA[0:32, 4096 + tg * 128:4096 + (tg + 8) * 128],
            C.psT[0:32, 0:1024]),
            reads=[("psT", k) for k in range(8)], writes=[KW["rope"]])
    for j in range(8):
        for r0 in (32, 96):
            P.dma(B2[r0:r0 + 32, j * 2048:(j + 1) * 2048],
                  WA[0:32, 4096:6144], reads=[KW["rope"]], writes=[b2k(j)])

    QS = B3[:, 8960:9984]
    for t in range(NT):
        for hf in range(2):
            for dc in range(ND):
                P.t(lambda e, t=t, hf=hf, dc=dc: e.matmul(
                    psA[:, hf * 512:hf * 512 + 512],
                    B1[:, dc * T + t * 128: dc * T + t * 128 + 128],
                    WA[:, 8192 + dc * 1024 + hf * 512:
                       8192 + dc * 1024 + hf * 512 + 512],
                    start=(dc == 0), stop=(dc == ND - 1)),
                    reads=[("B1",), KW["q"]], writes=[("psA", hf)])
        KQ = [("psA", 0), ("psA", 1)]
        P.s(lambda e: e.copy(F1[:, 0:1024], psA[:, 0:1024]),
            reads=KQ, writes=[("F1", 0)])
        qv = F1[:, 0:1024].rearrange("p (h z) -> p h z", h=16)
        sv = QS.rearrange("p (h z) -> p h z", h=16)
        P.v(lambda e: e.tensor_copy(sv[:, :, 0:32], qv[:, :, 0:32]),
            reads=[("F1", 0)], writes=[("B3", "qs")])
        _rope_rotate_q(C, qv, sv, [("F1", 0)], ("B3", "qs"), t)
        for k in range(8):
            P.t(lambda e, k=k: e.transpose(
                C.psT[:, k * 128:(k + 1) * 128],
                QS[:, k * 128:(k + 1) * 128], C.identb),
                reads=[("B3", "qs"), ("CB",)], writes=[("psT", k)])
        P.v(lambda e, t=t: e.tensor_copy(
            B1[:].rearrange("p (d l) -> p d l", d=8)[:, :, t * 128:(t + 1) * 128],
            C.psT[:, 0:1024].rearrange("p (d x) -> p d x", d=8)),
            reads=[("psT", k) for k in range(8)], writes=[("B1",)])

    C.ccell(0.125)
    for j in range(8):
        for h in (2 * j + 1, 2 * j):
            r0 = (h % 2) * 64
            for lh in range(2):
                nkc = min((lh + 1) * 8, NT)
                for kc in range(nkc):
                    diag = kc * 128 - lh * 1024
                    off = max(0, diag)
                    if off >= 1024:
                        continue
                    if off > 0:
                        P.v(lambda e, off=off: e.memset(
                            WA[:, 6144:6144 + off], 0.0), writes=[KW["stg"]])
                    for nb in range(off // 512, 2):
                        n0 = max(off, nb * 512)
                        n1 = (nb + 1) * 512
                        P.t(lambda e, j=j, h=h, r0=r0, lh=lh, kc=kc, n0=n0, n1=n1: e.matmul(
                            psA[:, n0:n1],
                            B2[r0:r0 + 64, j * 2048 + kc * 128: j * 2048 + kc * 128 + 128],
                            B1[r0:r0 + 64, (h // 2) * T + lh * 1024 + n0:
                               (h // 2) * T + lh * 1024 + n1],
                            start=True, stop=True),
                            reads=[b2k(j), ("B1",)],
                            writes=[("psA", nb)])
                    if 0 <= diag < 1024:
                        P.v(lambda e, diag=diag: e.tensor_tensor(
                            psA[:, diag:diag + 128], psA[:, diag:diag + 128],
                            C.CT[:], ALU.add),
                            reads=[("psA", 0), ("psA", 1), ("CT",)],
                            writes=[("psA", 0), ("psA", 1)])
                    P.s(lambda e, off=off: e.activation(
                        WA[:, 6144 + off:6144 + 1024],
                        psA[:, off:1024], AF.Exp, scale=0.125),
                        reads=[("psA", 0), ("psA", 1), ("CC",)],
                        writes=[KW["stg"]])
                    for nb in range(2):
                        P.t(lambda e, h=h, r0=r0, kc=kc, nb=nb, nkc=nkc: e.matmul(
                            psB[r0:r0 + 33, nb * 512:(nb + 1) * 512],
                            B3[:, kc * 528 + h * 33: kc * 528 + h * 33 + 33],
                            WA[:, 6144 + nb * 512:6144 + (nb + 1) * 512],
                            start=(kc == 0), stop=(kc == nkc - 1)),
                            reads=[("B3", "v"), KW["stg"]],
                            writes=[("psB", nb)])
                P.v(lambda e, r0=r0: e.reciprocal(F2[0:1, 0:1024],
                                                  psB[r0 + 32:r0 + 33, 0:1024]),
                    reads=[("psB", 0), ("psB", 1)], writes=[("F2",)])
                for nb in range(2):
                    P.t(lambda e, r0=r0, nb=nb: e.matmul(
                        psA[r0:r0 + 32, 1024 + nb * 512:1024 + (nb + 1) * 512],
                        C.onesrowf[0:1, 0:32],
                        F2[0:1, nb * 512:(nb + 1) * 512],
                        start=True, stop=True),
                        reads=[("F2",), ("CF",)], writes=[("psA", 2 + nb)])
                P.s(lambda e, j=j, r0=r0, lh=lh: e.copy(
                    B2[r0:r0 + 32, j * 2048 + lh * 1024: j * 2048 + lh * 1024 + 1024],
                    psB[r0:r0 + 32, 0:1024]),
                    reads=[("psB", 0), ("psB", 1)], writes=[b2k(j)])
                P.v(lambda e, j=j, r0=r0, lh=lh: e.tensor_tensor(
                    B2[r0:r0 + 32, j * 2048 + lh * 1024: j * 2048 + lh * 1024 + 1024],
                    B2[r0:r0 + 32, j * 2048 + lh * 1024: j * 2048 + lh * 1024 + 1024],
                    psA[r0:r0 + 32, 1024:2048], ALU.mult),
                    reads=[("psA", 2), ("psA", 3), b2k(j)], writes=[b2k(j)])

    P.p(lambda e: e.memset(WA[:, 8192:16384], 0.0), writes=[KW["q"]])
    wo3 = W["Wo"].rearrange("(c p) n -> p c n", p=64)
    for h in range(16):
        j, r0 = h // 2, (h % 2) * 64
        c0 = 0
        while c0 < 1024:
            cw = min(2048, 1024 - c0)
            P.dma(F2[0:1, 0:cw] if False else F2[0:32, 0:cw],
                  wo3[0:32, h, c0:c0 + cw], reads=[("F2",)], writes=[("F2",)])
            P.p(lambda e, j=j, r0=r0, c0=c0, cw=cw: e.tensor_copy(
                WA[r0:r0 + 32, 8192 + j * 1024 + c0: 8192 + j * 1024 + c0 + cw],
                F2[0:32, 0:cw]),
                reads=[("F2",)], writes=[KW["q"]])
            c0 += cw
    for t in range(NT):
        for hf in range(2):
            for j in range(8):
                P.t(lambda e, t=t, hf=hf, j=j: e.matmul(
                    psA[:, hf * 512:hf * 512 + 512],
                    B2[:, j * 2048 + t * 128: j * 2048 + t * 128 + 128],
                    WA[:, 8192 + j * 1024 + hf * 512:
                       8192 + j * 1024 + hf * 512 + 512],
                    start=(j == 0), stop=(j == 7)),
                    reads=KB2 + [KW["q"]], writes=[("psA", hf)])
        P.v(lambda e, t=t: e.tensor_scalar(
            F1[:, 0:1024], X[:, t * 1024:(t + 1) * 1024],
            mu[:, t:t + 1], rstd[:, t:t + 1], ALU.subtract, ALU.mult),
            reads=[("X",), ("mu",), ("rstd",)], writes=[("F1", 0)])
        P.v(lambda e: e.tensor_tensor(F2[:, 0:1024], F1[:, 0:1024],
                                      psA[:, 0:1024], ALU.add),
            reads=[("F1", 0), ("psA", 0), ("psA", 1)], writes=[("F2",)])
        P.dma(x1v[:, t, :], F2[:, 0:1024], reads=[("F2",)], writes=[("x1",)])
        P.s(lambda e, t=t: e.activation(F1[:, 0:1024], F2[:, 0:1024],
                                        AF.Square, accum_out=ms2[:, t:t + 1]),
            reads=[("F2",)], writes=[("F1", 0), ("ms2",)])

    P.v(lambda e: e.tensor_scalar(rr2, ms2, 1.0 / 1024, None, ALU.mult),
        reads=[("ms2",)], writes=[("rr2",)])
    P.s(lambda e: e.activation(rr2, rr2, AF.Ln, bias=1e-5),
        reads=[("rr2",), ("CC",)], writes=[("rr2",)])
    P.s(lambda e: e.activation(rr2, rr2, AF.Exp, scale=-0.5),
        reads=[("rr2",), ("CC",)], writes=[("rr2",)])
    for cc in range(8):
        P.dma(gatew[:, cc * 8:(cc + 1) * 8],
              W["Wgate"][cc * 128:(cc + 1) * 128, :],
              reads=[("gw",)], writes=[("gw",)])
    for cc in range(8):
        P.v(lambda e, cc=cc: e.tensor_scalar(
            gatew[:, cc * 8:(cc + 1) * 8], gatew[:, cc * 8:(cc + 1) * 8],
            w2col[:, cc:cc + 1], None, ALU.mult),
            reads=[("gw",), ("w2c",)], writes=[("gw",)])
    P.dma(F2[0:1, 0:8], W["expert_bias"].unsqueeze(0), reads=[("F2",)],
          writes=[("F2",)])
    P.t(lambda e: e.matmul(psB[:, 0:8], C.onesrowf, F2[0:1, 0:8],
                           start=True, stop=True),
        reads=[("F2",), ("CF",)], writes=[("psB", 0)])
    P.v(lambda e: e.tensor_copy(ebias, psB[:, 0:8]),
        reads=[("psB", 0)], writes=[("ebias",)])
    for t in range(NT):
        P.dma(F2[:, 0:1024], x1v[:, t, :], reads=[("x1",), ("F2",)],
              writes=[("F2",)])
        P.v(lambda e, t=t: e.tensor_tensor(
            X[:, t * 1024:(t + 1) * 1024], X[:, t * 1024:(t + 1) * 1024],
            F2[:, 0:1024], ALU.add),
            reads=[("F2",), ("X",)], writes=[("X",)])
        P.v(lambda e, t=t: e.tensor_scalar(
            F2[:, 0:1024], F2[:, 0:1024], rr2[:, t:t + 1], None, ALU.mult),
            reads=[("F2",), ("rr2",)], writes=[("F2",)])
        P.v(lambda e, t=t: e.tensor_copy(B2[:, t * 1024:(t + 1) * 1024],
                                         F2[:, 0:1024]),
            reads=[("F2",)], writes=[b2k(t // 2)])
        for k in range(8):
            P.t(lambda e, k=k: e.transpose(
                psA[:, 1536 + (k % 4) * 128:1536 + (k % 4 + 1) * 128],
                F2[:, k * 128:(k + 1) * 128], C.CI),
                reads=[("F2",), ("CI",)], writes=[("psA", 3)])
            if k % 4 == 3:
                P.v(lambda e, k=k: e.tensor_copy(
                    F1[:, (k - 3) * 128:(k + 1) * 128], psA[:, 1536:2048]),
                    reads=[("psA", 3)], writes=[("F1", 0)])
        for dc in range(8):
            P.t(lambda e, t=t, dc=dc: e.matmul(
                psS[:, t * 8:(t + 1) * 8], F1[:, dc * 128:(dc + 1) * 128],
                gatew[:, dc * 8:(dc + 1) * 8],
                start=(dc == 0), stop=(dc == 7)),
                reads=[("F1", 0), ("gw",)], writes=[("psS",)])
    P.v(lambda e: e.tensor_tensor(
        logit.rearrange("p (t e) -> p t e", e=8),
        psS[:, 0:128].rearrange("p (t e) -> p t e", e=8),
        ebias.unsqueeze(1).broadcast_to([128, 16, 8]), ALU.add),
        reads=[("psS",), ("ebias",)], writes=[("logit",)])
    for t in range(NT):
        lt = logit[:, t * 8:(t + 1) * 8]
        m1 = S1[:, 432:433]
        m2 = S1[:, 433:434]
        s1 = S1[:, 434:435]
        s2 = S1[:, 435:436]
        msk1 = F2[:, 0:8]
        msk2 = F2[:, 8:16]
        P.v(lambda e, lt=lt: e.reduce_max(m1, lt, axis=AX.X),
            reads=[("logit",)], writes=[("m1",)])
        P.v(lambda e, lt=lt: e.tensor_scalar(msk1, lt, m1, None, ALU.is_ge),
            reads=[("logit",), ("m1",)], writes=[("F2",)])
        P.v(lambda e, lt=lt: e.scalar_tensor_tensor(
            F2[:, 16:24], msk1, -1e30, lt, ALU.mult, ALU.add),
            reads=[("logit",), ("F2",)], writes=[("F2",)])
        P.v(lambda e: e.reduce_max(m2, F2[:, 16:24], axis=AX.X),
            reads=[("F2",)], writes=[("m2",)])
        P.v(lambda e, lt=lt: e.tensor_scalar(msk2, lt, m2, None, ALU.is_ge),
            reads=[("logit",), ("m2",)], writes=[("F2",)])
        P.v(lambda e: e.tensor_tensor(s1, m1, m2, ALU.subtract),
            reads=[("m1",), ("m2",)], writes=[("s1",)])
        P.s(lambda e: e.activation(s1, s1, AF.Sigmoid),
            reads=[("s1",)], writes=[("s1",)])
        P.v(lambda e: e.tensor_scalar(s2, s1, -1.0, 1.0, ALU.mult, ALU.add),
            reads=[("s1",)], writes=[("s2",)])
        P.v(lambda e: e.tensor_tensor(msk2, msk2, msk1, ALU.subtract),
            reads=[("F2",)], writes=[("F2",)])
        P.v(lambda e, t=t: e.tensor_scalar(
            wsel[:, t * 8:(t + 1) * 8], msk1, s1, None, ALU.mult),
            reads=[("F2",), ("s1",)], writes=[("wsel",)])
        P.v(lambda e, t=t: e.scalar_tensor_tensor(
            wsel[:, t * 8:(t + 1) * 8], msk2, s2, wsel[:, t * 8:(t + 1) * 8],
            ALU.mult, ALU.add),
            reads=[("F2",), ("s2",), ("wsel",)], writes=[("wsel",)])
    tp_tok2feat(C, B2, B1, KB2, [("B1",)])

    for e_i in range(9):
        if e_i < 8:
            wg_d, wu_d, wd_d = W["Wg"][e_i], W["Wu"][e_i], W["Wd"][e_i]
        else:
            wg_d, wu_d, wd_d = W["sWg"][0], W["sWu"][0], W["sWd"][0]
        load_weight_bf16(C, WA, wg_d, [("WA", 0)], 1024, 512, dst_col0=0,
                         scale_col=w2col, key_scale=("w2c",))
        load_weight_bf16(C, WA, wu_d, [("WA", 0)], 1024, 512, dst_col0=4096,
                         scale_col=w2col, key_scale=("w2c",))
        load_weight_bf16(C, WA, wd_d, [("WA", 1)], 512, 1024, dst_col0=8192)
        _moe_expert(C, e_i, 8, wsel, ("wsel",))


def _moe_expert(C, e_i, n_routed, wsel, kwsel):
    P = C.P
    X, B1, B2, B3, WA = C.X, C.B1, C.B2, C.B3, C.WA
    psA, psB = C.psA, C.psB
    KPS4 = [("psA", h) for h in range(4)]
    KB2 = [("B2", j) for j in range(8)]
    for ic in range(4):
        for lp in range(4):
            for dc in range(ND):
                P.t(lambda e, ic=ic, lp=lp, dc=dc: e.matmul(
                    psA[:, lp * 512:(lp + 1) * 512],
                    WA[:, dc * 512 + ic * 128: dc * 512 + ic * 128 + 128],
                    B1[:, dc * T + lp * 512: dc * T + lp * 512 + 512],
                    start=(dc == 0), stop=(dc == ND - 1)),
                    reads=[("WA", 0), ("B1",)], writes=[("psA", lp)])
        for hv in range(2):
            P.s(lambda e, ic=ic, hv=hv: e.activation(
                B2[:, ic * T + hv * 1024:ic * T + (hv + 1) * 1024],
                psA[:, hv * 1024:(hv + 1) * 1024], AF.Silu),
                reads=[("psA", 2 * hv), ("psA", 2 * hv + 1)],
                writes=[("B2", ic)])
    for ic in range(4):
        for lp in range(4):
            for dc in range(ND):
                P.t(lambda e, ic=ic, lp=lp, dc=dc: e.matmul(
                    psA[:, lp * 512:(lp + 1) * 512],
                    WA[:, 4096 + dc * 512 + ic * 128:
                       4096 + dc * 512 + ic * 128 + 128],
                    B1[:, dc * T + lp * 512: dc * T + lp * 512 + 512],
                    start=(dc == 0), stop=(dc == ND - 1)),
                    reads=[("WA", 0), ("B1",)], writes=[("psA", lp)])
        for hv in range(2):
            P.v(lambda e, ic=ic, hv=hv: e.tensor_tensor(
                B3[:, ic * T + hv * 1024:ic * T + (hv + 1) * 1024],
                B2[:, ic * T + hv * 1024:ic * T + (hv + 1) * 1024],
                psA[:, hv * 1024:(hv + 1) * 1024], ALU.mult),
                reads=[("psA", 2 * hv), ("psA", 2 * hv + 1), ("B2", ic)],
                writes=[("B3",), ("B3", "v")])
    for t in range(NT):
        for h in range(2):
            for ic in range(4):
                P.t(lambda e, t=t, h=h, ic=ic: e.matmul(
                    psB[:, h * 512:(h + 1) * 512],
                    B3[:, ic * T + t * 128: ic * T + t * 128 + 128],
                    WA[:, 8192 + ic * 1024 + h * 512:
                       8192 + ic * 1024 + h * 512 + 512],
                    start=(ic == 0), stop=(ic == 3)),
                    reads=[("WA", 1), ("B3",)], writes=[("psB", h)])
        if e_i < n_routed:
            wcell = wsel[:, t * n_routed + e_i: t * n_routed + e_i + 1]
            P.v(lambda e, t=t, wcell=wcell: e.scalar_tensor_tensor(
                X[:, t * 1024:(t + 1) * 1024], psB[:, 0:1024], wcell,
                X[:, t * 1024:(t + 1) * 1024], ALU.mult, ALU.add),
                reads=[("psB", 0), ("psB", 1), kwsel, ("X",)], writes=[("X",)])
        else:
            P.v(lambda e, t=t: e.tensor_tensor(
                X[:, t * 1024:(t + 1) * 1024], X[:, t * 1024:(t + 1) * 1024],
                psB[:, 0:1024], ALU.add),
                reads=[("psB", 0), ("psB", 1), ("X",)], writes=[("X",)])


def _rope_rotate(C, RT, key):
    P = C.P
    ev = RT.rearrange("p (t i w) -> p t i w", i=16, w=2)[:, :, :, 0:1].squeeze(3)
    od = RT.rearrange("p (t i w) -> p t i w", i=16, w=2)[:, :, :, 1:2].squeeze(3)
    cos = C.CRC[:].rearrange("p (t i) -> p t i", i=16)
    sin = C.CRS[:].rearrange("p (t i) -> p t i", i=16)
    F2 = C.F2
    t1 = F2[:, 0:256].rearrange("p (t i) -> p t i", i=16)
    t2 = F2[:, 256:512].rearrange("p (t i) -> p t i", i=16)
    t3 = F2[:, 512:768].rearrange("p (t i) -> p t i", i=16)
    t4 = F2[:, 768:1024].rearrange("p (t i) -> p t i", i=16)
    P.v(lambda e: e.tensor_tensor(t1, ev, cos, ALU.mult),
        reads=[key, ("CRC",)], writes=[("F2",)])
    P.v(lambda e: e.tensor_tensor(t2, od, sin, ALU.mult),
        reads=[key, ("CRS",)], writes=[("F2",)])
    P.v(lambda e: e.tensor_tensor(t3, ev, sin, ALU.mult),
        reads=[key, ("CRS",)], writes=[("F2",)])
    P.v(lambda e: e.tensor_tensor(t4, od, cos, ALU.mult),
        reads=[key, ("CRC",)], writes=[("F2",)])
    P.v(lambda e: e.tensor_tensor(ev, t1, t2, ALU.subtract),
        reads=[("F2",)], writes=[key])
    P.v(lambda e: e.tensor_tensor(od, t3, t4, ALU.add),
        reads=[("F2",)], writes=[key])


def _rope_rotate_q(C, qv, sv, kq, ks, t):
    P = C.P
    F2 = C.F2
    qe = qv[:, :, 32:64].rearrange("p h (i w) -> p h i w", w=2)[:, :, :, 0:1].squeeze(3)
    qo = qv[:, :, 32:64].rearrange("p h (i w) -> p h i w", w=2)[:, :, :, 1:2].squeeze(3)
    se = sv[:, :, 32:64].rearrange("p h (i w) -> p h i w", w=2)[:, :, :, 0:1].squeeze(3)
    so = sv[:, :, 32:64].rearrange("p h (i w) -> p h i w", w=2)[:, :, :, 1:2].squeeze(3)
    cos = C.CRC[:, t * 16:(t + 1) * 16].unsqueeze(1).broadcast_to([128, 16, 16])
    sin = C.CRS[:, t * 16:(t + 1) * 16].unsqueeze(1).broadcast_to([128, 16, 16])
    t1 = F2[:, 0:256].rearrange("p (h i) -> p h i", i=16)
    t2 = F2[:, 256:512].rearrange("p (h i) -> p h i", i=16)
    t3 = F2[:, 512:768].rearrange("p (h i) -> p h i", i=16)
    t4 = F2[:, 768:1024].rearrange("p (h i) -> p h i", i=16)
    P.v(lambda e: e.tensor_tensor(t1, qe, cos, ALU.mult),
        reads=kq + [("CRC",)], writes=[("F2",)])
    P.v(lambda e: e.tensor_tensor(t2, qo, sin, ALU.mult),
        reads=kq + [("CRS",)], writes=[("F2",)])
    P.v(lambda e: e.tensor_tensor(t3, qe, sin, ALU.mult),
        reads=kq + [("CRS",)], writes=[("F2",)])
    P.v(lambda e: e.tensor_tensor(t4, qo, cos, ALU.mult),
        reads=kq + [("CRC",)], writes=[("F2",)])
    P.v(lambda e: e.tensor_tensor(se, t1, t2, ALU.subtract),
        reads=[("F2",)], writes=[ks])
    P.v(lambda e: e.tensor_tensor(so, t3, t4, ALU.add),
        reads=[("F2",)], writes=[ks])




def emit_l2_attn(C, W):
    P = C.P
    nc = C.nc
    X, B1, B2, B3, WA = C.X, C.B1, C.B2, C.B3, C.WA
    S1, F1, F2 = C.S1, C.F1, C.F2
    psA, psB, psS, psT = C.psA, C.psB, C.psS, C.psT

    mu = S1[:, 0:16]
    rstd = S1[:, 16:32]
    Ain = S1[:, 32:64]
    Aex = S1[:, 64:96]
    ecol = S1[:, 96:128]
    wdt = S1[:, 128:160]
    A127 = S1[:, 160:192]
    CW = S1[:, 192:272]
    CBc = S1[:, 272:290]
    DB = S1[:, 416:448]
    dtbB = S1[:, 448:480]
    negA = S1[:, 480:512]
    dt_raw = S1[:, 512:1024]
    dt_sp = S1[:, 1024:1536]
    a_all = S1[:, 1536:2048]
    SEG = 512
    NSEG = 4
    NCH = 18
    xmd = nc.dram_tensor("l2_xm", [T, D], F32, kind="Internal").ap()
    xmv = xmd.rearrange("(t p) f -> p t f", p=128)

    B_f = B3[:, 0:512]
    C_f = B3[:, 512:1024]
    Btk = B3[:, 1024:1152]
    XT = B3[:, 2048:4096]
    DTX = B3[:, 4096:6144]
    SPV = B3[:, 6144:8192]
    GSL = B3[:, 8192:10240]

    KB2 = ("B2",)
    KX = [("X",)]

    ln_stats(C, X, mu, rstd, KX, ("mu",), ("rstd",))
    for t in range(NT):
        P.v(lambda e, t=t: e.tensor_scalar(
            B2[:, t * 1024:(t + 1) * 1024], X[:, t * 1024:(t + 1) * 1024],
            mu[:, t:t + 1], rstd[:, t:t + 1], ALU.subtract, ALU.mult),
            reads=KX + [("mu",), ("rstd",)], writes=[KB2])
    tp_tok2feat(C, B2, B1, [KB2], [("B1",)])

    def col_cells(wname, dst, key, n, scale=1.0):
        P.dma(F2[0:1, 0:n], W[wname].unsqueeze(0), reads=[("F2",)],
              writes=[("F2",)])
        P.t(lambda e, n=n: e.matmul(psB[:, 0:n], C.onesrowf, F2[0:1, 0:n],
                                    start=True, stop=True),
            reads=[("F2",), ("CF",)], writes=[("psB", 0)])
        P.v(lambda e, dst=dst, n=n, scale=scale: e.tensor_scalar(
            dst, psB[:, 0:n], scale, None, ALU.mult),
            reads=[("psB", 0)], writes=[key])

    for d_i, sfx in ((0, "f"), (1, "b")):
        KWI = ("WA", 0)
        KWO = ("WA", 0)
        col_cells("dtb_" + sfx, dtbB, ("dtbB",), 32)
        col_cells("D_" + sfx, DB, ("DB",), 32)
        col_cells("Alog_" + sfx, negA, ("negA",), 32)
        P.s(lambda e: e.activation(negA, negA, AF.Exp),
            reads=[("negA",)], writes=[("negA",)])
        P.v(lambda e: e.tensor_scalar(negA, negA, -1.0, None, ALU.mult),
            reads=[("negA",)], writes=[("negA",)])
        for j in range(4):
            for half in range(2):
                oc_lo, oc_hi = half * 9, min(NCH, half * 9 + 9)
                P.dma(F2[0:1, 0:1152],
                      W["conv_" + sfx][j:j + 1, half * 1152:half * 1152 + 1152],
                      reads=[("F2",)], writes=[("F2",)])
                for oc in range(oc_lo, oc_hi):
                    P.t(lambda e, oc=oc, j=j, half=half: e.matmul(
                        psB[:, oc * 4 + j:oc * 4 + j + 1],
                        F2[0:1, (oc - half * 9) * 128:(oc - half * 9 + 1) * 128],
                        C.onecellf, start=True, stop=True),
                        reads=[("F2",), ("CF",)], writes=[("psB", 0)])
        P.v(lambda e: e.tensor_copy(CW[:, 0:72], psB[:, 0:72]),
            reads=[("psB", 0)], writes=[("CW",)])
        for half in range(2):
            oc_lo, oc_hi = half * 9, min(NCH, half * 9 + 9)
            P.dma(F2[0:1, 0:1152],
                  W["convb_" + sfx].unsqueeze(0)[:, half * 1152:half * 1152 + 1152],
                  reads=[("F2",)], writes=[("F2",)])
            for oc in range(oc_lo, oc_hi):
                P.t(lambda e, oc=oc, half=half: e.matmul(
                    psB[:, oc:oc + 1],
                    F2[0:1, (oc - half * 9) * 128:(oc - half * 9 + 1) * 128],
                    C.onecellf, start=True, stop=True),
                    reads=[("F2",), ("CF",)], writes=[("psB", 0)])
        P.v(lambda e: e.tensor_copy(CBc[:, 0:NCH], psB[:, 0:NCH]),
            reads=[("psB", 0)], writes=[("CBc",)])
        load_weight_bf16(C, WA, W["Wi_" + sfx][:, 4352:4384], [KWI], 1024, 32)
        for t in range(NT):
            for dc in range(ND):
                P.t(lambda e, t=t, dc=dc: e.matmul(
                    psS[:, t * 32:(t + 1) * 32],
                    B1[:, dc * T + t * 128: dc * T + t * 128 + 128],
                    WA[:, dc * 32:(dc + 1) * 32],
                    start=(dc == 0), stop=(dc == ND - 1)),
                    reads=[("B1",), KWI], writes=[("psS",)])
        P.v(lambda e: e.tensor_tensor(
            dt_raw.rearrange("p (c h) -> p c h", h=32),
            psS[:, 0:512].rearrange("p (c h) -> p c h", h=32),
            dtbB.unsqueeze(1).broadcast_to([128, 16, 32]), ALU.add),
            reads=[("psS",), ("dtbB",)], writes=[("dtr",)])
        P.s(lambda e: e.activation(F2[:, 0:512], dt_raw, AF.Abs),
            reads=[("dtr",)], writes=[("F2",)])
        P.s(lambda e: e.activation(F2[:, 0:512], F2[:, 0:512], AF.Exp,
                                   scale=-1.0),
            reads=[("F2",), ("CC",)], writes=[("F2",)])
        P.s(lambda e: e.activation(F2[:, 0:512], F2[:, 0:512], AF.Ln, bias=1.0),
            reads=[("F2",)], writes=[("F2",)])
        P.v(lambda e: e.tensor_scalar(F1[:, 0:512], dt_raw, 0.0, None, ALU.max),
            reads=[("dtr",)], writes=[("F1", 0)])
        P.v(lambda e: e.tensor_tensor(dt_sp, F1[:, 0:512], F2[:, 0:512],
                                      ALU.add),
            reads=[("F1", 0), ("F2",)], writes=[("dtsp",)])
        P.v(lambda e: e.tensor_tensor(
            a_all.rearrange("p (c h) -> p c h", h=32),
            dt_sp.rearrange("p (c h) -> p c h", h=32),
            negA.unsqueeze(1).broadcast_to([128, 16, 32]), ALU.mult),
            reads=[("dtsp",), ("negA",)], writes=[("aall",)])

        load_weight_bf16(C, WA, W["Wo_" + sfx], [KWO], 2048, 1024)

        seg_order = range(NSEG) if d_i == 0 else range(NSEG - 1, -1, -1)
        P.v(lambda e: e.memset(SPV[:], 0.0), writes=[("SPV",)])
        for si, sg in enumerate(seg_order):
            s0 = sg * SEG
            for oc in range(NCH + 16):
                is_z = oc >= NCH
                wcol = (2048 + oc * 128) if oc < NCH else (oc - NCH) * 128
                slot = 8192 + (oc % 2) * 1024
                KSL = ("GSL",)
                load_weight_bf16(C, B3, W["Wi_" + sfx][:, wcol:wcol + 128],
                                 [KSL], 1024, 128, dst_col0=slot)
                for dc in range(ND):
                    P.t(lambda e, oc=oc, dc=dc, slot=slot, s0=s0: e.matmul(
                        psA[:, 0:512],
                        B3[:, slot + dc * 128: slot + dc * 128 + 128],
                        B1[:, dc * T + s0: dc * T + s0 + SEG],
                        start=(dc == 0), stop=(dc == ND - 1)),
                        reads=[("B1",), KSL], writes=[("psA", 0)])
                if is_z:
                    zc = oc - NCH
                    P.s(lambda e, zc=zc: e.copy(
                        B2[:, 8192 + zc * 512:8192 + (zc + 1) * 512],
                        psA[:, 0:512]),
                        reads=[("psA", 0)], writes=[("B2z",), ("B2",)])
                    continue
                P.s(lambda e: e.copy(F1[:, 0:512], psA[:, 0:512]),
                    reads=[("psA", 0)], writes=[("F1", 0)])
                HL = S1[:, 290:290 + 3]
                first_seg = (si == 0)
                hl0 = 290 + oc * 3
                P.v(lambda e, oc=oc: e.tensor_scalar(
                    F2[:, 0:512], F1[:, 0:512], CW[:, oc * 4 + 3:oc * 4 + 4],
                    None, ALU.mult),
                    reads=[("F1", 0), ("CW",)], writes=[("F2",)])
                if d_i == 0:
                    for o in range(1, 4):
                        P.v(lambda e, oc=oc, o=o: e.scalar_tensor_tensor(
                            F2[:, o:512], F1[:, 0:512 - o],
                            CW[:, oc * 4 + 3 - o:oc * 4 + 4 - o],
                            F2[:, o:512], ALU.mult, ALU.add),
                            reads=[("F1", 0), ("CW",), ("F2",)], writes=[("F2",)])
                        if not first_seg:
                            P.v(lambda e, oc=oc, o=o, hl0=hl0: e.scalar_tensor_tensor(
                                F2[:, 0:o], S1[:, hl0 + 3 - o:hl0 + 3],
                                CW[:, oc * 4 + 3 - o:oc * 4 + 4 - o],
                                F2[:, 0:o], ALU.mult, ALU.add),
                                reads=[("HLa",), ("F2",), ("CW",)], writes=[("F2",)])
                else:
                    for o in range(1, 4):
                        P.v(lambda e, oc=oc, o=o: e.scalar_tensor_tensor(
                            F2[:, 0:512 - o], F1[:, o:512],
                            CW[:, oc * 4 + 3 - o:oc * 4 + 4 - o],
                            F2[:, 0:512 - o], ALU.mult, ALU.add),
                            reads=[("F1", 0), ("CW",), ("F2",)], writes=[("F2",)])
                        if not first_seg:
                            P.v(lambda e, oc=oc, o=o, hl0=hl0: e.scalar_tensor_tensor(
                                F2[:, 512 - o:512], S1[:, hl0:hl0 + o],
                                CW[:, oc * 4 + 3 - o:oc * 4 + 4 - o],
                                F2[:, 512 - o:512], ALU.mult, ALU.add),
                                reads=[("HLa",), ("F2",), ("CW",)], writes=[("F2",)])
                if si < NSEG - 1:
                    if d_i == 0:
                        P.v(lambda e, oc=oc: e.tensor_copy(
                            S1[:, 290 + oc * 3:290 + oc * 3 + 3],
                            F1[:, 509:512]),
                            reads=[("F1", 0)], writes=[("HLa",)])
                    else:
                        P.v(lambda e, oc=oc: e.tensor_copy(
                            S1[:, 290 + oc * 3:290 + oc * 3 + 3],
                            F1[:, 0:3]),
                            reads=[("F1", 0)], writes=[("HLa",)])
                dst = (B2[:, oc * 512:(oc + 1) * 512] if oc < 16 else
                       (B_f if oc == 16 else C_f))
                dks = ([("B2x",), ("B2",)] if oc < 16 else
                       ([("Bf",)] if oc == 16 else [("Cf",)]))
                P.s(lambda e, oc=oc, dst=dst: e.activation(
                    dst, F2[:, 0:512], AF.Silu, bias=CBc[:, oc:oc + 1]),
                    reads=[("F2",), ("CBc",)], writes=dks)

            ch_order = range(4) if d_i == 0 else range(3, -1, -1)
            for c_l in ch_order:
                cg = sg * 4 + c_l
                tc0 = c_l * 128
                KA = ("Ain",)
                P.t(lambda e, cg=cg: e.matmul(
                    psS[:, 0:32], C.CUF, a_all[:, cg * 32:(cg + 1) * 32],
                    start=True, stop=True),
                    reads=[("aall",), ("CUF",)], writes=[("psS",)])
                P.t(lambda e, cg=cg: e.matmul(
                    psS[:, 32:64], C.CF[:, 0:128],
                    a_all[:, cg * 32:(cg + 1) * 32],
                    start=True, stop=True),
                    reads=[("aall",), ("CF",)], writes=[("psS",)])
                P.v(lambda e: e.tensor_copy(Ain, psS[:, 0:32]),
                    reads=[("psS",)], writes=[KA])
                P.v(lambda e: e.tensor_copy(A127, psS[:, 32:64]),
                    reads=[("psS",)], writes=[("A127",)])
                P.v(lambda e, cg=cg: e.tensor_tensor(
                    Aex, Ain, a_all[:, cg * 32:(cg + 1) * 32], ALU.subtract),
                    reads=[KA, ("aall",)], writes=[("Aex",)])
                if d_i == 0:
                    P.s(lambda e: e.activation(ecol, Ain, AF.Exp),
                        reads=[KA], writes=[("ecol",)])
                    P.v(lambda e: e.tensor_tensor(F2[:, 0:32], A127, Ain,
                                                  ALU.subtract),
                        reads=[("A127",), KA], writes=[("F2",)])
                    P.s(lambda e: e.activation(F2[:, 0:32], F2[:, 0:32], AF.Exp),
                        reads=[("F2",)], writes=[("F2",)])
                else:
                    P.v(lambda e: e.tensor_tensor(F2[:, 32:64], A127, Aex,
                                                  ALU.subtract),
                        reads=[("A127",), ("Aex",)], writes=[("F2",)])
                    P.s(lambda e: e.activation(ecol, F2[:, 32:64], AF.Exp),
                        reads=[("F2",)], writes=[("ecol",)])
                    P.s(lambda e: e.activation(F2[:, 0:32], Aex, AF.Exp),
                        reads=[("Aex",)], writes=[("F2",)])
                P.v(lambda e, cg=cg: e.tensor_tensor(
                    wdt, F2[:, 0:32], dt_sp[:, cg * 32:(cg + 1) * 32], ALU.mult),
                    reads=[("F2",), ("dtsp",)], writes=[("wdt",)])
                for g in range(2):
                    tp_group(C, lambda k, g=g, tc0=tc0: B2[:, (g * 8 + k) * 512
                                                           + tc0:(g * 8 + k) * 512 + tc0 + 128],
                             XT, g * 1024, 8, [("B2x",)], [("XT",)])
                P.v(lambda e, cg=cg: e.tensor_tensor(
                    DTX.rearrange("p (h z) -> p h z", h=32),
                    XT.rearrange("p (h z) -> p h z", h=32),
                    dt_sp[:, cg * 32:(cg + 1) * 32].unsqueeze(2)
                    .broadcast_to([128, 32, 64]), ALU.mult),
                    reads=[("XT",), ("dtsp",)], writes=[("DTX",)])
                P.t(lambda e, tc0=tc0: e.matmul(
                    psS[:, 128:256], B_f[:, tc0:tc0 + 128],
                    C_f[:, tc0:tc0 + 128], start=True, stop=True),
                    reads=[("Bf",), ("Cf",)], writes=[("psS",)])
                P.v(lambda e: e.tensor_copy(B3[:, 1152:1280], psS[:, 128:256]),
                    reads=[("psS",)], writes=[("SC",)])
                src = Ain if d_i == 0 else Aex
                P.t(lambda e, src=src: e.transpose(
                    C.psA[0:32, 1536:1664], src, C.CI),
                    reads=[("Ain",), ("Aex",), ("CI",)], writes=[("psA", 3)])
                P.v(lambda e: e.tensor_copy(F1[0:32, 0:128],
                                            psA[0:32, 1536:1664]),
                    reads=[("psA", 3)], writes=[("F1", 0)])
                for hg in range(2):
                    P.dma(F2[0:1, 0:2048], F1[hg * 16:(hg + 1) * 16, 0:128],
                          reads=[("F1", 0), ("F2",)], writes=[("F2",)])
                    for nb in range(4):
                        P.t(lambda e, nb=nb: e.matmul(
                            psA[:, nb * 512:(nb + 1) * 512], C.onesrowf,
                            F2[0:1, nb * 512:(nb + 1) * 512],
                            start=True, stop=True),
                            reads=[("F2",), ("CF",)], writes=[("psA", nb)])
                    KPS4 = [("psA", n) for n in range(4)]
                    cv = src[:, hg * 16:(hg + 1) * 16].unsqueeze(2) \
                        .broadcast_to([128, 16, 128])
                    if d_i == 0:
                        P.v(lambda e, cv=cv: e.tensor_tensor(
                            F2[:, 0:2048].rearrange("p (h t) -> p h t", h=16),
                            psA[:, 0:2048].rearrange("p (h t) -> p h t", h=16),
                            cv, ALU.subtract),
                            reads=KPS4 + [("Ain",), ("Aex",)], writes=[("F2",)])
                    else:
                        P.v(lambda e, cv=cv: e.tensor_tensor(
                            F2[:, 0:2048].rearrange("p (h t) -> p h t", h=16),
                            cv,
                            psA[:, 0:2048].rearrange("p (h t) -> p h t", h=16),
                            ALU.subtract),
                            reads=KPS4 + [("Ain",), ("Aex",)], writes=[("F2",)])
                    msk = C.CT if d_i == 0 else C.CTU
                    P.v(lambda e, msk=msk: e.tensor_tensor(
                        F2[:, 0:2048].rearrange("p (h t) -> p h t", h=16),
                        F2[:, 0:2048].rearrange("p (h t) -> p h t", h=16),
                        msk[:].unsqueeze(1).broadcast_to([128, 16, 128]),
                        ALU.add),
                        reads=[("F2",), ("CT",), ("CTU",)], writes=[("F2",)])
                    P.s(lambda e: e.activation(GSL[:, 0:2048], F2[:, 0:2048],
                                               AF.Exp),
                        reads=[("F2",)], writes=[("GSL",)])
                    P.v(lambda e: e.tensor_tensor(
                        GSL[:, 0:2048].rearrange("p (h t) -> p h t", h=16),
                        GSL[:, 0:2048].rearrange("p (h t) -> p h t", h=16),
                        B3[:, 1152:1280].unsqueeze(1).broadcast_to([128, 16, 128]),
                        ALU.mult),
                        reads=[("GSL",), ("SC",)], writes=[("GSL",)])
                    for nb in range(2):
                        P.t(lambda e, hg=hg, nb=nb, tc0=tc0: e.matmul(
                            psB[:, nb * 512:(nb + 1) * 512],
                            C_f[:, tc0:tc0 + 128],
                            SPV[:, hg * 1024 + nb * 512: hg * 1024 + (nb + 1) * 512],
                            start=True, stop=True),
                            reads=[("Cf",), ("SPV",)], writes=[("psB", nb)])
                    P.v(lambda e, hg=hg: e.tensor_tensor(
                        F2[:, 0:1024].rearrange("p (h z) -> p h z", h=16),
                        psB[:, 0:1024].rearrange("p (h z) -> p h z", h=16),
                        ecol[:, hg * 16:(hg + 1) * 16].unsqueeze(2)
                        .broadcast_to([128, 16, 64]), ALU.mult),
                        reads=[("psB", 0), ("psB", 1), ("ecol",)],
                        writes=[("F2",)])
                    for hl in range(16):
                        h = hg * 16 + hl
                        P.t(lambda e, h=h, hl=hl: e.matmul(
                            psB[:, hl * 64:(hl + 1) * 64],
                            GSL[:, hl * 128:(hl + 1) * 128],
                            DTX[:, h * 64:(h + 1) * 64],
                            start=True, stop=True),
                            reads=[("GSL",), ("DTX",)], writes=[("psB", hl // 8)])
                    P.v(lambda e: e.tensor_tensor(
                        F2[:, 1024:2048], F2[:, 0:1024], psB[:, 0:1024],
                        ALU.add),
                        reads=[("psB", 0), ("psB", 1), ("F2",)], writes=[("F2",)])
                    P.v(lambda e, hg=hg: e.tensor_tensor(
                        F1[:, 1024:2048].rearrange("p (h z) -> p h z", h=16),
                        XT[:, hg * 1024:(hg + 1) * 1024]
                        .rearrange("p (h z) -> p h z", h=16),
                        DB[:, hg * 16:(hg + 1) * 16].unsqueeze(2)
                        .broadcast_to([128, 16, 64]), ALU.mult),
                        reads=[("XT",), ("DB",)], writes=[("F1", 1)])
                    P.v(lambda e: e.tensor_tensor(
                        F2[:, 1024:2048], F2[:, 1024:2048], F1[:, 1024:2048],
                        ALU.add),
                        reads=[("F2",), ("F1", 1)], writes=[("F2",)])
                    P.v(lambda e: e.tensor_copy(GSL[:, 0:1024], F2[:, 1024:2048]),
                        reads=[("F2",)], writes=[("GSL",)])
                    P.s(lambda e, hg=hg, tc0=tc0: e.activation(
                        F1[:, 1024:2048].rearrange("p (ch x) -> p ch x", ch=8),
                        B2[:].rearrange("p (ch x) -> p ch x", ch=32)
                        [:, 16 + hg * 8:16 + (hg + 1) * 8, tc0:tc0 + 128],
                        AF.Silu),
                        reads=[("B2z",)], writes=[("F1", 1)])
                    for k in range(8):
                        P.t(lambda e, k=k: e.transpose(
                            psT[:, k * 128:(k + 1) * 128],
                            GSL[:, k * 128:(k + 1) * 128], C.identb),
                            reads=[("GSL",), ("CB",)], writes=[("psT", k)])
                    for k in range(8):
                        P.v(lambda e, k=k: e.tensor_tensor(
                            GSL[:, 1024 + k * 128:1024 + (k + 1) * 128],
                            psT[:, k * 128:(k + 1) * 128],
                            F1[:, 1024 + k * 128:1024 + (k + 1) * 128], ALU.mult),
                            reads=[("psT", k), ("F1", 1)], writes=[("GSL", 1)])
                    for hf in range(2):
                        for k in range(8):
                            P.t(lambda e, hg=hg, hf=hf, k=k: e.matmul(
                                psA[:, hf * 512:(hf + 1) * 512],
                                GSL[:, 1024 + k * 128:1024 + (k + 1) * 128],
                                WA[:, (hg * 8 + k) * 1024 + hf * 512:
                                   (hg * 8 + k) * 1024 + hf * 512 + 512],
                                start=(k == 0), stop=(k == 7)),
                                reads=[("GSL", 1), KWO], writes=[("psA", hf)])
                    first = (d_i == 0 and hg == 0)
                    tgl = cg
                    if first:
                        P.v(lambda e: e.tensor_copy(F2[:, 0:1024], psA[:, 0:1024]),
                            reads=[("psA", 0), ("psA", 1)], writes=[("F2",)])
                    else:
                        P.dma(F2[:, 0:1024], xmv[:, tgl, :],
                              reads=[("xm",), ("F2",)], writes=[("F2",)])
                        P.v(lambda e: e.tensor_tensor(
                            F2[:, 0:1024], F2[:, 0:1024], psA[:, 0:1024],
                            ALU.add),
                            reads=[("psA", 0), ("psA", 1), ("F2",)],
                            writes=[("F2",)])
                    P.dma(xmv[:, tgl, :], F2[:, 0:1024], reads=[("F2",)],
                          writes=[("xm",)])
                P.v(lambda e, cg=cg: e.tensor_tensor(
                    DTX.rearrange("p (h z) -> p h z", h=32),
                    XT.rearrange("p (h z) -> p h z", h=32),
                    wdt.unsqueeze(2).broadcast_to([128, 32, 64]), ALU.mult),
                    reads=[("XT",), ("wdt",)], writes=[("DTX",)])
                P.t(lambda e, tc0=tc0: e.transpose(
                    psT[:, 0:128], B_f[:, tc0:tc0 + 128], C.identb),
                    reads=[("Bf",), ("CB",)], writes=[("psT", 0)])
                P.v(lambda e: e.tensor_copy(Btk, psT[:, 0:128]),
                    reads=[("psT", k) for k in range(8)], writes=[("Btk",)])
                P.s(lambda e: e.activation(F2[:, 0:32], A127, AF.Exp),
                    reads=[("A127",)], writes=[("F2",)])
                P.v(lambda e: e.tensor_tensor(
                    SPV[:].rearrange("p (h z) -> p h z", h=32),
                    SPV[:].rearrange("p (h z) -> p h z", h=32),
                    F2[:, 0:32].unsqueeze(2).broadcast_to([128, 32, 64]),
                    ALU.mult),
                    reads=[("SPV",), ("F2",)], writes=[("SPV",)])
                for nb in range(4):
                    P.t(lambda e, nb=nb: e.matmul(
                        psA[:, nb * 512:(nb + 1) * 512], Btk,
                        DTX[:, nb * 512:(nb + 1) * 512],
                        start=True, stop=True),
                        reads=[("Btk",), ("DTX",)], writes=[("psA", nb)])
                P.v(lambda e: e.tensor_tensor(SPV[:], SPV[:], psA[:, 0:2048],
                                              ALU.add),
                    reads=[("SPV",)] + [("psA", n) for n in range(4)],
                    writes=[("SPV",)])

    load_weight_bf16(C, WA, W["Wm"], [("WA", 0)], 1024, 1024)
    for t in range(NT):
        P.dma(F2[:, 0:1024], xmv[:, t, :], reads=[("xm",), ("F2",)],
              writes=[("F2",)])
        P.v(lambda e: e.tensor_copy(GSL[:, 0:1024], F2[:, 0:1024]),
            reads=[("F2",)], writes=[("GSL",)])
        for k in range(8):
            P.t(lambda e, k=k: e.transpose(psT[:, k * 128:(k + 1) * 128],
                                           GSL[:, k * 128:(k + 1) * 128],
                                           C.identb),
                reads=[("GSL",), ("CB",)], writes=[("psT", k)])
        P.v(lambda e, t=t: e.tensor_copy(
            B1[:].rearrange("p (d l) -> p d l", d=8)[:, :, t * 128:(t + 1) * 128],
            psT[:, 0:1024].rearrange("p (d x) -> p d x", d=8)),
            reads=[("psT", k) for k in range(8)], writes=[("B1",)])
    bob = F1[:, 1024:2048]
    bcast_bias(C, W["bm"], bob, [("F1", 1)], 1024)
    for t in range(NT):
        for hf in range(2):
            for dc in range(ND):
                P.t(lambda e, t=t, hf=hf, dc=dc: e.matmul(
                    psA[:, hf * 512:(hf + 1) * 512],
                    B1[:, dc * T + t * 128: dc * T + t * 128 + 128],
                    WA[:, dc * 1024 + hf * 512: dc * 1024 + hf * 512 + 512],
                    start=(dc == 0), stop=(dc == ND - 1)),
                    reads=[("B1",), ("WA", 0)], writes=[("psA", hf)])
        P.v(lambda e: e.tensor_tensor(F1[:, 0:1024], psA[:, 0:1024], bob,
                                      ALU.add),
            reads=[("psA", 0), ("psA", 1), ("F1", 1)], writes=[("F1", 0)])
        P.s(lambda e: e.activation(F1[:, 0:1024], F1[:, 0:1024], AF.Tanh),
            reads=[("F1", 0)], writes=[("F1", 0)])
        P.dma(F2[:, 0:1024], xmv[:, t, :], reads=[("xm",), ("F2",)],
              writes=[("F2",)])
        P.v(lambda e: e.scalar_tensor_tensor(F2[:, 0:1024], F1[:, 0:1024],
                                             0.1, F2[:, 0:1024],
                                             ALU.mult, ALU.add),
            reads=[("F1", 0), ("F2",)], writes=[("F2",)])
        P.v(lambda e, t=t: e.tensor_tensor(
            X[:, t * 1024:(t + 1) * 1024], X[:, t * 1024:(t + 1) * 1024],
            F2[:, 0:1024], ALU.add),
            reads=[("F2",), ("X",)], writes=[("X",)])




def emit_l2_ffn(C, W):
    P = C.P
    X, B1, B2, B3, WA = C.X, C.B1, C.B2, C.B3, C.WA
    S1, F1, F2 = C.S1, C.F1, C.F2
    psA, psB, psS = C.psA, C.psB, C.psS

    mu = S1[:, 0:16]
    rstd = S1[:, 16:32]
    brB = S1[:, 32:40]
    logit = S1[:, 512:640]
    Xs = S1[:, 640:768]
    gates = S1[:, 768:896]
    tau = S1[:, 896:912]
    flo = S1[:, 912:928]
    mx = S1[:, 928:944]
    b1c = S1[:, 944:976]
    b2B = S1[:, 1024:2048]
    KV3 = ("B2", 0)

    ln_stats(C, X, mu, rstd, [("X",)], ("mu",), ("rstd",))
    for t in range(NT):
        P.v(lambda e, t=t: e.tensor_scalar(
            B2[:, t * 1024:(t + 1) * 1024], X[:, t * 1024:(t + 1) * 1024],
            mu[:, t:t + 1], rstd[:, t:t + 1], ALU.subtract, ALU.mult),
            reads=[("X",), ("mu",), ("rstd",)], writes=[KV3])
    tp_tok2feat(C, B2, B1, [KV3], [("B1",)])

    load_weight_bf16(C, B3, W["Wr"], [("B3", "wr")], 1024, 8)
    bcast_bias(C, W["br"], brB, [("brB",)], 8)
    for t in range(NT):
        for dc in range(ND):
            P.t(lambda e, t=t, dc=dc: e.matmul(
                psS[:, t * 8:(t + 1) * 8],
                B1[:, dc * T + t * 128: dc * T + t * 128 + 128],
                B3[:, dc * 8:(dc + 1) * 8],
                start=(dc == 0), stop=(dc == ND - 1)),
                reads=[("B1",), ("B3", "wr")], writes=[("psS",)])
    P.v(lambda e: e.tensor_tensor(
        logit.rearrange("p (t e) -> p t e", e=8),
        psS[:, 0:128].rearrange("p (t e) -> p t e", e=8),
        brB.unsqueeze(1).broadcast_to([128, 16, 8]), ALU.add),
        reads=[("psS",), ("brB",)], writes=[("logit",)])

    L3v = lambda ap: ap.rearrange("p (t e) -> p t e", e=8)
    P.v(lambda e: e.tensor_scalar(Xs, logit, 0.5, None, ALU.mult),
        reads=[("logit",)], writes=[("Xs",)])
    P.v(lambda e: e.reduce_max(mx, L3v(Xs), axis=AX.X),
        reads=[("Xs",)], writes=[("mx",)])
    P.v(lambda e: e.tensor_scalar(tau, mx, -1.0, None, ALU.add),
        reads=[("mx",)], writes=[("tau",)])

    def fval(dst_f, tau_ap, kt, kdst):
        P.v(lambda e, tau_ap=tau_ap: e.tensor_tensor(
            L3v(F2[:, 0:128]), L3v(Xs),
            tau_ap.unsqueeze(2).broadcast_to([128, 16, 8]), ALU.subtract),
            reads=[("Xs",), kt], writes=[("F2",)])
        P.v(lambda e: e.tensor_scalar(F2[:, 0:128], F2[:, 0:128], 0.0, None,
                                      ALU.max),
            reads=[("F2",)], writes=[("F2",)])
        P.v(lambda e: e.tensor_tensor(F2[:, 128:256], F2[:, 0:128],
                                      F2[:, 0:128], ALU.mult),
            reads=[("F2",)], writes=[("F2",)])
        P.v(lambda e, dst_f=dst_f, kdst=kdst: e.reduce_sum(
            dst_f, L3v(F2[:, 128:256]), axis=AX.X),
            reads=[("F2",)], writes=[kdst])
        P.v(lambda e, dst_f=dst_f, kdst=kdst: e.tensor_scalar(
            dst_f, dst_f, -1.0, None, ALU.add),
            reads=[kdst], writes=[kdst])
    fval(flo, tau, ("tau",), ("flo",))
    dm = 1.0 - 8.0 ** -0.5
    fm = S1[:, 976:992]
    for it in range(25):
        dm *= 0.5
        if it == 24:
            P.v(lambda e, dm=dm: e.tensor_scalar(tau, tau, dm, None, ALU.add),
                reads=[("tau",)], writes=[("tau",)])
            break
        P.v(lambda e, dm=dm: e.tensor_scalar(S1[:, 992:1008], tau, dm, None,
                                             ALU.add),
            reads=[("tau",)], writes=[("taum",)])
        fval(fm, S1[:, 992:1008], ("taum",), ("fm",))
        P.v(lambda e: e.tensor_tensor(fm, fm, flo, ALU.mult),
            reads=[("fm",), ("flo",)], writes=[("fm2",)])
        P.v(lambda e: e.tensor_scalar(fm, fm, 0.0, None, ALU.is_ge),
            reads=[("fm2",)], writes=[("fm2",)])
        P.v(lambda e, dm=dm: e.scalar_tensor_tensor(tau, fm, dm, tau,
                                                    ALU.mult, ALU.add),
            reads=[("fm2",), ("tau",)], writes=[("tau",)])
    P.v(lambda e: e.tensor_tensor(
        L3v(F2[:, 0:128]), L3v(Xs),
        tau.unsqueeze(2).broadcast_to([128, 16, 8]), ALU.subtract),
        reads=[("Xs",), ("tau",)], writes=[("F2",)])
    P.v(lambda e: e.tensor_scalar(F2[:, 0:128], F2[:, 0:128], 0.0, None,
                                  ALU.max),
        reads=[("F2",)], writes=[("F2",)])
    P.v(lambda e: e.tensor_tensor(gates, F2[:, 0:128], F2[:, 0:128], ALU.mult),
        reads=[("F2",)], writes=[("gates",)])
    P.v(lambda e: e.reduce_sum(mx, L3v(gates), axis=AX.X),
        reads=[("gates",)], writes=[("mx",)])
    P.v(lambda e: e.reciprocal(mx, mx), reads=[("mx",)], writes=[("mx",)])
    P.v(lambda e: e.tensor_tensor(
        L3v(gates), L3v(gates), mx.unsqueeze(2).broadcast_to([128, 16, 8]),
        ALU.mult),
        reads=[("gates",), ("mx",)], writes=[("gates",)])
    P.v(lambda e: e.tensor_scalar(F2[:, 0:128], gates, 1e-9, None, ALU.is_gt),
        reads=[("gates",)], writes=[("F2",)])
    P.v(lambda e: e.tensor_tensor(gates, gates, F2[:, 0:128], ALU.mult),
        reads=[("gates",), ("F2",)], writes=[("gates",)])

    for e_i in range(8):
        b1d = W["b1"][e_i]
        P.dma(F2[0:1, 0:2048], b1d[0:2048].unsqueeze(0), reads=[("F2",)],
              writes=[("F2",)])
        for cc in range(16):
            P.t(lambda e, cc=cc: e.matmul(
                psB[:, cc:cc + 1], F2[0:1, cc * 128:(cc + 1) * 128],
                C.onecellf, start=True, stop=True),
                reads=[("F2",), ("CF",)], writes=[("psB", 0)])
        P.dma(F2[0:1, 0:2048], b1d[2048:4096].unsqueeze(0), reads=[("F2",)],
              writes=[("F2",)])
        for cc in range(16):
            P.t(lambda e, cc=cc: e.matmul(
                psB[:, 16 + cc:17 + cc], F2[0:1, cc * 128:(cc + 1) * 128],
                C.onecellf, start=True, stop=True),
                reads=[("F2",), ("CF",)], writes=[("psB", 0)])
        P.v(lambda e: e.tensor_copy(b1c, psB[:, 0:32]),
            reads=[("psB", 0)], writes=[("b1c",)])
        bcast_bias(C, W["b2"][e_i], b2B, [("b2B",)], 1024)
        for half in range(4):
            for fc in range(8):
                fg = half * 8 + fc
                slot = (fc % 2) * 1024
                KSL = ("WA", 0)
                load_weight_bf16(C, WA, W["W1"][e_i][:, fg * 128:(fg + 1) * 128],
                                 [KSL], 1024, 128, dst_col0=slot)
                for lp in range(4):
                    for dc in range(ND):
                        P.t(lambda e, fg=fg, slot=slot, lp=lp, dc=dc: e.matmul(
                            psA[:, lp * 512:(lp + 1) * 512],
                            WA[:, slot + dc * 128: slot + dc * 128 + 128],
                            B1[:, dc * T + lp * 512: dc * T + lp * 512 + 512],
                            start=(dc == 0), stop=(dc == ND - 1)),
                            reads=[("B1",), KSL], writes=[("psA", lp)])
                for hv in range(2):
                    P.s(lambda e, fc=fc, fg=fg, hv=hv: e.activation(
                        B2[:, fc * T + hv * 1024:fc * T + (hv + 1) * 1024],
                        psA[:, hv * 1024:(hv + 1) * 1024], AF.Relu,
                        bias=b1c[:, fg:fg + 1]),
                        reads=[("psA", 2 * hv), ("psA", 2 * hv + 1), ("b1c",)],
                        writes=[("B2", fc)])
            load_weight_bf16(C, WA, W["W2"][e_i][half * 1024:(half + 1) * 1024, :],
                             [("WA", 0)], 1024, 1024)
            for t in range(NT):
                for fc in range(8):
                    for nb in range(2):
                        P.t(lambda e, t=t, fc=fc, nb=nb: e.matmul(
                            psB[:, nb * 512:(nb + 1) * 512],
                            B2[:, fc * T + t * 128: fc * T + t * 128 + 128],
                            WA[:, fc * 1024 + nb * 512: fc * 1024 + nb * 512 + 512],
                            start=(fc == 0), stop=(fc == 7)),
                            reads=[("B2", fc), ("WA", 0)], writes=[("psB", nb)])
                gc = gates[:, t * 8 + e_i: t * 8 + e_i + 1]
                P.v(lambda e, t=t, gc=gc: e.scalar_tensor_tensor(
                    X[:, t * 1024:(t + 1) * 1024], psB[:, 0:1024], gc,
                    X[:, t * 1024:(t + 1) * 1024], ALU.mult, ALU.add),
                    reads=[("psB", 0), ("psB", 1), ("gates",), ("X",)],
                    writes=[("X",)])
                if half == 3:
                    P.v(lambda e, t=t, gc=gc: e.scalar_tensor_tensor(
                        X[:, t * 1024:(t + 1) * 1024], b2B, gc,
                        X[:, t * 1024:(t + 1) * 1024], ALU.mult, ALU.add),
                        reads=[("b2B",), ("gates",), ("X",)], writes=[("X",)])


L2_FFN_W = [("Wr", "L2_ffn_Wr", (1024, 8)),
            ("br", "L2_ffn_br", (8,)),
            ("W1", "L2_ffn_W1", (8, 1024, 4096)),
            ("b1", "L2_ffn_b1", (8, 4096)),
            ("W2", "L2_ffn_W2", (8, 4096, 1024)),
            ("b2", "L2_ffn_b2", (8, 1024))]




def emit_l3_attn(C, W):
    P = C.P
    X, B1, B2, B3, WA = C.X, C.B1, C.B2, C.B3, C.WA
    S1, F1, F2 = C.S1, C.F1, C.F2
    psA, psB, psS, psT = C.psA, C.psB, C.psS, C.psT

    mu = S1[:, 0:16]
    rstd = S1[:, 16:32]
    nmx = S1[:, 32:33]
    nms = S1[:, 33:34]
    ssum = S1[:, 34:35]
    SCL = 1.0 / (2.0 * 0.28)
    VT0 = 0
    S80 = 5120
    MSK = 9744

    ln_stats(C, X, mu, rstd, [("X",)], ("mu",), ("rstd",))
    for t in range(NT):
        P.v(lambda e, t=t: e.tensor_scalar(
            B2[:, t * 1024:(t + 1) * 1024], X[:, t * 1024:(t + 1) * 1024],
            mu[:, t:t + 1], rstd[:, t:t + 1], ALU.subtract, ALU.mult),
            reads=[("X",), ("mu",), ("rstd",)], writes=[("B2", "q")])
    tp_tok2feat(C, B2, B1, [("B2", "q")], [("B1",)])
    P.dma(B3[:, MSK:MSK + 136], W["c_bandw"], writes=[("msk",)])
    P.dma(B3[:, MSK + 136:MSK + 272], W["c_band0"], writes=[("msk",)])
    P.v(lambda e: e.memset(B2[:, 4096:8256], 0.0), writes=[("B2", "k")])
    P.v(lambda e: e.memset(B3[:, 0:1024], 0.0), writes=[("vtk",)])
    C.ccell(1.0 / (2.0 * 0.28))

    for qr in range(4):
        q0 = qr * 512
        for wname, kw in (("Wq", ("B2", "q")), ("Wk", ("B2", "k")),
                          ("Wv", ("B2", "v"))):
            load_weight_bf16(C, WA, W[wname], [("WA", 0)], 1024, 1024)
            for ec in range(ND):
                for dc in range(ND):
                    P.t(lambda e, ec=ec, dc=dc, q0=q0: e.matmul(
                        psA[:, (ec % 4) * 512:(ec % 4) * 512 + 512],
                        WA[:, dc * 1024 + ec * 128: dc * 1024 + ec * 128 + 128],
                        B1[:, dc * T + q0: dc * T + q0 + 512],
                        start=(dc == 0), stop=(dc == ND - 1)),
                        reads=[("B1",), ("WA", 0)], writes=[("psA", ec % 4)])
                if wname == "Wq":
                    P.v(lambda e, ec=ec: e.tensor_copy(
                        B2[:, ec * 512:(ec + 1) * 512],
                        psA[:, (ec % 4) * 512:(ec % 4) * 512 + 512]),
                        reads=[("psA", ec % 4)], writes=[("B2", "q")])
                elif wname == "Wk":
                    P.v(lambda e, ec=ec: e.tensor_copy(
                        B2[:, 4096 + ec * 520 + 8:4096 + (ec + 1) * 520],
                        psA[:, (ec % 4) * 512:(ec % 4) * 512 + 512]),
                        reads=[("psA", ec % 4)], writes=[("B2", "k")])
                else:
                    P.v(lambda e, ec=ec: e.tensor_copy(
                        B2[:, 8256:8768],
                        psA[:, (ec % 4) * 512:(ec % 4) * 512 + 512]),
                        reads=[("psA", ec % 4)], writes=[("B2", "v")])
                    for tt in range(4):
                        P.t(lambda e, ec=ec, tt=tt: e.transpose(
                            psT[:, tt * 128:(tt + 1) * 128],
                            B2[:, 8256 + tt * 128:8256 + (tt + 1) * 128],
                            C.identb),
                            reads=[("B2", "v"), ("CB",)], writes=[("psT", tt)])
                    for tt in range(4):
                        P.v(lambda e, ec=ec, tt=tt: e.tensor_copy(
                            B3[:, VT0 + (tt + 1) * 1024 + ec * 128:
                               VT0 + (tt + 1) * 1024 + ec * 128 + 128],
                            psT[:, tt * 128:(tt + 1) * 128]),
                            reads=[("psT", k) for k in range(8)],
                            writes=[("vtk",)])
        vt = B3[:, VT0:VT0 + 5120].rearrange("p (t z) -> p t z", z=1024)
        s8 = B3[:, S80:S80 + 4096].rearrange("p (t z) -> p t z", z=1024)
        P.dma(s8[8:128, :, :], vt[0:120, 1:5, :], reads=[("vtk",)],
              writes=[("s8",)])
        P.dma(s8[0:8, :, :], vt[120:128, 0:4, :], reads=[("vtk",)],
              writes=[("s8",)])
        for cl in range(4):
            for h in range(16):
                r0 = (h % 2) * 64
                jc = h // 2
                P.t(lambda e, cl=cl, h=h, r0=r0, jc=jc: e.matmul(
                    psA[:, 0:136],
                    B2[r0:r0 + 64, jc * 512 + cl * 128: jc * 512 + cl * 128 + 128],
                    B2[r0:r0 + 64, 4096 + jc * 520 + cl * 128:
                       4096 + jc * 520 + cl * 128 + 136],
                    start=True, stop=True),
                    reads=[("B2", "q"), ("B2", "k")], writes=[("psA", 0)])
                m0 = MSK + (136 if (qr == 0 and cl == 0) else 0)
                P.v(lambda e, m0=m0: e.tensor_tensor(
                    psA[:, 0:136], psA[:, 0:136], B3[:, m0:m0 + 136], ALU.add),
                    reads=[("psA", 0), ("msk",)], writes=[("psA", 0)])
                P.v(lambda e: e.reduce_max(nmx, psA[:, 0:136], axis=AX.X,
                                           negate=True),
                    reads=[("psA", 0)], writes=[("nmx",)])
                P.v(lambda e: e.tensor_scalar(nms, nmx, SCL, None, ALU.mult),
                    reads=[("nmx",)], writes=[("nms",)])
                P.s(lambda e: e.activation(F2[:, 0:136], psA[:, 0:136],
                                           AF.Exp, bias=nms, scale=SCL,
                                           accum_out=ssum),
                    reads=[("psA", 0), ("nms",), ("CC",)],
                    writes=[("F2",), ("ssum",)])
                P.v(lambda e: e.reciprocal(ssum, ssum),
                    reads=[("ssum",)], writes=[("ssum",)])
                P.v(lambda e: e.tensor_scalar(B3[:, 9216:9352], F2[:, 0:136],
                                              ssum, None, ALU.mult),
                    reads=[("F2",), ("ssum",)], writes=[("wts",)])
                P.t(lambda e: e.transpose(psT[:, 0:128],
                                          B3[:, 9224:9352], C.identb),
                    reads=[("wts",), ("CB",)], writes=[("psT", 0)])
                P.t(lambda e: e.transpose(psT[0:8, 128:256],
                                          B3[:, 9216:9224], C.identb),
                    reads=[("wts",), ("CB",)], writes=[("psT", 1)])
                P.v(lambda e: e.tensor_copy(B3[:, 9352:9480], psT[:, 0:128]),
                    reads=[("psT", k) for k in range(8)], writes=[("wta",)])
                P.v(lambda e: e.tensor_copy(B3[0:8, 9480:9608],
                                            psT[0:8, 128:256]),
                    reads=[("psT", k) for k in range(8)], writes=[("wtb",)])
                P.t(lambda e, cl=cl, h=h, r0=r0: e.matmul(
                    psB[r0:r0 + 64, 0:128],
                    B3[:, VT0 + (cl + 1) * 1024 + h * 64:
                       VT0 + (cl + 1) * 1024 + h * 64 + 64],
                    B3[:, 9352:9480], start=True, stop=False),
                    reads=[("vtk",), ("wta",)], writes=[("psB", 0)])
                P.t(lambda e, cl=cl, h=h, r0=r0: e.matmul(
                    psB[r0:r0 + 64, 0:128],
                    B3[0:8, S80 + cl * 1024 + h * 64:
                       S80 + cl * 1024 + h * 64 + 64],
                    B3[0:8, 9480:9608], start=False, stop=True),
                    reads=[("s8",), ("wtb",)], writes=[("psB", 0)])
                if h % 2 == 1:
                    P.s(lambda e, cl=cl, jc=jc, q0=q0: e.copy(
                        B2[:, 8768 + jc * 512 + cl * 128:
                           8768 + jc * 512 + cl * 128 + 128],
                        psB[:, 0:128]),
                        reads=[("psB", 0)], writes=[("B2", "y")])
        load_weight_bf16(C, WA, W["Wo"], [("WA", 0)], 1024, 1024)
        for tt in range(4):
            t = qr * 4 + tt
            for hf in range(2):
                for yc in range(8):
                    P.t(lambda e, tt=tt, hf=hf, yc=yc: e.matmul(
                        psA[:, hf * 512:hf * 512 + 512],
                        B2[:, 8768 + yc * 512 + tt * 128:
                           8768 + yc * 512 + tt * 128 + 128],
                        WA[:, yc * 1024 + hf * 512: yc * 1024 + hf * 512 + 512],
                        start=(yc == 0), stop=(yc == 7)),
                        reads=[("B2", "y"), ("WA", 0)], writes=[("psA", hf)])
            P.v(lambda e, t=t: e.tensor_tensor(
                X[:, t * 1024:(t + 1) * 1024], X[:, t * 1024:(t + 1) * 1024],
                psA[:, 0:1024], ALU.add),
                reads=[("psA", 0), ("psA", 1), ("X",)], writes=[("X",)])
        if qr < 3:
            for jc in range(8):
                P.v(lambda e, jc=jc: e.tensor_copy(
                    B2[:, 4096 + jc * 520:4096 + jc * 520 + 8],
                    B2[:, 4096 + jc * 520 + 512:4096 + (jc + 1) * 520]),
                    reads=[("B2", "k")], writes=[("B2", "k")])
            P.v(lambda e: e.tensor_copy(B3[:, 0:1024], B3[:, 4096:5120]),
                reads=[("vtk",)], writes=[("vtk",)])


L3_ATTN_W = [("Wq", "L3_attn_Wq", (1024, 1024)),
             ("Wk", "L3_attn_Wk", (1024, 1024)),
             ("Wv", "L3_attn_Wv", (1024, 1024)),
             ("Wo", "L3_attn_Wo", (1024, 1024)),
             ("c_bandw", "c_bandw", (128, 136)),
             ("c_band0", "c_band0", (128, 136))]




def emit_l3_ffn(C, W):
    P = C.P
    nc = C.nc
    X, B1, B2, B3, WA = C.X, C.B1, C.B2, C.B3, C.WA
    S1, F1, F2 = C.S1, C.F1, C.F2
    psA, psB, psS, psT = C.psA, C.psB, C.psS, C.psT

    mu = S1[:, 0:16]
    rstd = S1[:, 16:32]
    qb = S1[:, 32:40]
    kb = S1[:, 40:48]
    vb = S1[:, 48:56]
    SCL = float(1024.0 ** -0.5)
    wB = F1[:, 0:1024]
    bB = F2[:, 1024:2048]
    bB = S1[:, 1024:2048]
    qkvd = nc.dram_tensor("l3f_qkv", [3, 8, 128, T], BF16, kind="Internal").ap()

    ln_stats(C, X, mu, rstd, [("X",)], ("mu",), ("rstd",))
    bcast_bias(C, W["norm_w"], wB, [("F1", 0)], 1024)
    bcast_bias(C, W["norm_b"], bB, [("bB",)], 1024)
    for t in range(NT):
        P.v(lambda e, t=t: e.tensor_scalar(
            F2[:, 0:1024], X[:, t * 1024:(t + 1) * 1024],
            mu[:, t:t + 1], rstd[:, t:t + 1], ALU.subtract, ALU.mult),
            reads=[("X",), ("mu",), ("rstd",)], writes=[("F2",)])
        P.v(lambda e: e.tensor_tensor(F2[:, 0:1024], F2[:, 0:1024], wB,
                                      ALU.mult),
            reads=[("F2",), ("F1", 0)], writes=[("F2",)])
        P.v(lambda e, t=t: e.tensor_tensor(
            B2[:, t * 1024:(t + 1) * 1024], F2[:, 0:1024], bB, ALU.add),
            reads=[("F2",), ("bB",)], writes=[("B2", "s")])
    tp_tok2feat(C, B2, B1, [("B2", "s")], [("B1",)])

    def bias_cols(bname, dst, key):
        P.dma(F2[0:1, 0:1024], W[bname].unsqueeze(0), reads=[("F2",)],
              writes=[("F2",)])
        for cc in range(8):
            P.t(lambda e, cc=cc: e.matmul(psB[:, cc:cc + 1],
                                          F2[0:1, cc * 128:(cc + 1) * 128],
                                          C.onecellf, start=True, stop=True),
                reads=[("F2",), ("CF",)], writes=[("psB", 0)])
        P.v(lambda e, dst=dst: e.tensor_copy(dst, psB[:, 0:8]),
            reads=[("psB", 0)], writes=[key])
    C.ccell(SCL)
    bias_cols("q_b", qb, ("qb",))
    bias_cols("k_b", kb, ("kb",))
    bias_cols("v_b", vb, ("vb",))

    for wi, (wname, bcol, bkey) in enumerate(
            (("q_w", qb, ("qb",)), ("k_w", kb, ("kb",)), ("v_w", vb, ("vb",)))):
        load_weight_bf16(C, WA, W[wname], [("WA", 0)], 1024, 1024)
        for r in range(8):
            for g in range(0, 8, 8):
                for k in range(8):
                    P.t(lambda e, r=r, k=k: e.transpose(
                        psT[:, k * 128:(k + 1) * 128],
                        WA[:, r * 1024 + k * 128: r * 1024 + k * 128 + 128],
                        C.identb),
                        reads=[("WA", 0), ("CB",)], writes=[("psT", k)])
                P.v(lambda e, r=r: e.tensor_copy(
                    WA[:].rearrange("p (c l) -> p c l", c=8)
                    [:, :, 8192 // 8 + r * 128:8192 // 8 + r * 128 + 128]
                    if False else
                    WA[:, 8192:8192 + 8 * 1024].rearrange(
                        "p (c z) -> p c z", c=8)[:, :, r * 128:r * 128 + 128],
                    psT[:, 0:1024].rearrange("p (c z) -> p c z", c=8)),
                    reads=[("psT", k) for k in range(8)], writes=[("WA", 1)])
        for e_s in range(4):
            m = 1024 >> e_s
            nmc = m // 128
            for ec in range(ND):
                for mc in range(nmc):
                    P.t(lambda e, e_s=e_s, ec=ec, mc=mc, nmc=nmc: e.matmul(
                        psA[:, (ec % 4) * 512:(ec % 4) * 512 + 512],
                        WA[:, 8192 + mc * 1024 + ec * 128:
                           8192 + mc * 1024 + ec * 128 + 128],
                        B1[:, mc * T + e_s * 512: mc * T + e_s * 512 + 512],
                        start=(mc == 0), stop=(mc == nmc - 1)),
                        reads=[("B1",), ("WA", 1)], writes=[("psA", ec % 4)])
                P.v(lambda e, ec=ec, bcol=bcol: e.tensor_scalar(
                    B3[:, ec * 512:(ec + 1) * 512],
                    psA[:, (ec % 4) * 512:(ec % 4) * 512 + 512],
                    bcol[:, ec:ec + 1], None, ALU.add),
                    reads=[("psA", ec % 4), bkey], writes=[("B3", "E")])
            for dp in range(4):
                P.dma(qkvd[wi].rearrange("c p l -> p c l")
                      [:, dp * 2:dp * 2 + 2, e_s * 512:e_s * 512 + 512],
                      B3[:, dp * 1024:(dp + 1) * 1024].rearrange(
                          "p (c z) -> p c z", c=2),
                      reads=[("B3", "E")], writes=[("qkv",)])

    for h in range(8):
        P.dma(B2[:, 0:2048], qkvd[0, h].rearrange("p l -> p l"),
              reads=[("qkv",), ("B2", "s")], writes=[("B2", "s")])
        P.dma(B2[:, 2048:4096], qkvd[1, h], reads=[("qkv",), ("B2", "s")],
              writes=[("B2", "s")])
        P.dma(B2[:, 4096:6144], qkvd[2, h], reads=[("qkv",), ("B2", "s")],
              writes=[("B2", "s")])
        if h == 0:
            C.dbg("qTh", B2[:, 0:2048], [("B2", "s")])
            C.dbg("kTh", B2[:, 2048:4096], [("B2", "s")])
        for g in range(2):
            tp_group(C, lambda k, g=g: B2[:, 4096 + (g * 8 + k) * 128:
                                          4096 + (g * 8 + k + 1) * 128],
                     B2, 6144 + g * 1024, 8, [("B2", "s")], [("B2", "vt")])
        for lh in range(2):
            l0 = lh * 1024
            for kc in range(NT):
                for nb in range(2):
                    P.t(lambda e, kc=kc, nb=nb, l0=l0: e.matmul(
                        psA[:, nb * 512:(nb + 1) * 512],
                        B2[:, 2048 + kc * 128:2048 + (kc + 1) * 128],
                        B2[:, l0 + nb * 512:l0 + nb * 512 + 512],
                        start=True, stop=True),
                        reads=[("B2", "s")], writes=[("psA", nb)])
                P.s(lambda e: e.activation(B3[:, 0:1024], psA[:, 0:1024],
                                           AF.Exp, scale=SCL),
                    reads=[("psA", 0), ("psA", 1), ("CC",)], writes=[("B3", "E")])
                for nb in range(2):
                    P.t(lambda e, kc=kc, nb=nb: e.matmul(
                        psB[:, nb * 512:(nb + 1) * 512],
                        B2[:, 6144 + kc * 128:6144 + (kc + 1) * 128],
                        B3[:, nb * 512:nb * 512 + 512],
                        start=(kc == 0), stop=(kc == NT - 1)),
                        reads=[("B2", "vt"), ("B3", "E")], writes=[("psB", nb)])
                for nb in range(2):
                    P.t(lambda e, kc=kc, nb=nb: e.matmul(
                        psA[0:1, 1024 + nb * 512:1024 + (nb + 1) * 512],
                        C.onescolb, B3[:, nb * 512:nb * 512 + 512],
                        start=(kc == 0), stop=(kc == NT - 1)),
                        reads=[("CB",), ("B3", "E")], writes=[("psA", 2 + nb)])
            P.v(lambda e: e.tensor_copy(F2[0:1, 0:1024], psA[0:1, 1024:2048]),
                reads=[("psA", 2), ("psA", 3)], writes=[("F2",)])
            P.v(lambda e: e.reciprocal(F2[0:1, 0:1024], F2[0:1, 0:1024]),
                reads=[("F2",)], writes=[("F2",)])
            for nb in range(2):
                P.t(lambda e, nb=nb: e.matmul(
                    psA[:, 1024 + nb * 512:1024 + (nb + 1) * 512], C.onesrowf,
                    F2[0:1, nb * 512:(nb + 1) * 512], start=True, stop=True),
                    reads=[("F2",), ("CF",)], writes=[("psA", 2 + nb)])
            P.s(lambda e, h=h, l0=l0: e.copy(
                WA[:, h * 2048 + l0: h * 2048 + l0 + 1024], psB[:, 0:1024]),
                reads=[("psB", 0), ("psB", 1)], writes=[("WA", 0)])
            P.v(lambda e, h=h, l0=l0: e.tensor_tensor(
                WA[:, h * 2048 + l0: h * 2048 + l0 + 1024],
                WA[:, h * 2048 + l0: h * 2048 + l0 + 1024],
                psA[:, 1024:2048], ALU.mult),
                reads=[("psA", 2), ("psA", 3), ("WA", 0)], writes=[("WA", 0)])

    C.dbg("yT0", WA[:, 0:2048], [("WA", 0)])
    load_weight_bf16(C, B2, W["o_w"], [("B2", "s")], 1024, 1024)
    for r in range(8):
        for k in range(8):
            P.t(lambda e, r=r, k=k: e.transpose(
                psT[:, k * 128:(k + 1) * 128],
                B2[:, r * 1024 + k * 128: r * 1024 + k * 128 + 128],
                C.identb),
                reads=[("B2", "s"), ("CB",)], writes=[("psT", k)])
        P.v(lambda e, r=r: e.tensor_copy(
            B2[:, 8192:16384].rearrange("p (c z) -> p c z", c=8)
            [:, :, r * 128:r * 128 + 128],
            psT[:, 0:1024].rearrange("p (c z) -> p c z", c=8)),
            reads=[("psT", k) for k in range(8)], writes=[("B2", "owt")])
    obB = F1[:, 1024:2048]
    bcast_bias(C, W["o_b"], obB, [("F1", 1)], 1024)
    for e_s in range(4):
        m = 1024 >> e_s
        nmc = m // 128
        for tt in range(4):
            t = e_s * 4 + tt
            nnb = max(1, m // 512)
            for nb in range(nnb):
                nw = min(512, m)
                for mc in range(nmc):
                    P.t(lambda e, mc=mc, nb=nb, nw=nw, t=t, nmc=nmc: e.matmul(
                        psA[:, nb * 512:nb * 512 + nw],
                        WA[:].rearrange("p (hh z) -> p hh z", hh=8)
                        [:, mc // 2, (mc % 2) * 1024 + 0:0] if False else
                        WA[:, mc * 2048 // 2:0] if False else
                        _yt_slice(C, mc, t),
                        B2[:, 8192 + mc * 1024 + nb * 512:
                           8192 + mc * 1024 + nb * 512 + nw],
                        start=(mc == 0), stop=(mc == nmc - 1)),
                        reads=[("WA", 0), ("B2", "owt")], writes=[("psA", nb)])
            P.v(lambda e, t=t: e.tensor_scalar(
                F2[:, 0:1024], X[:, t * 1024:(t + 1) * 1024],
                mu[:, t:t + 1], rstd[:, t:t + 1], ALU.subtract, ALU.mult),
                reads=[("X",), ("mu",), ("rstd",)], writes=[("F2",)])
            P.v(lambda e, m=m: e.tensor_tensor(
                F2[:, 0:m], F2[:, 0:m], obB[:, 0:m], ALU.add),
                reads=[("F2",), ("F1", 1)], writes=[("F2",)])
            P.v(lambda e, m=m: e.tensor_tensor(
                F2[:, 0:m], F2[:, 0:m], psA[:, 0:m], ALU.add),
                reads=[("F2",), ("psA", 0), ("psA", 1)], writes=[("F2",)])
            P.v(lambda e, t=t: e.tensor_tensor(
                X[:, t * 1024:(t + 1) * 1024], X[:, t * 1024:(t + 1) * 1024],
                F2[:, 0:1024], ALU.add),
                reads=[("F2",), ("X",)], writes=[("X",)])


def _yt_slice(C, mc, t):
    l0 = (t // 8) * 0
    lh = 1 if t >= 8 else 0
    col = mc * 2048 + lh * 1024 + (t - lh * 8) * 128
    return C.WA[:, col:col + 128]


L3_FFN_W = [("norm_w", "L3_ffn_norm_w", (1024,)),
            ("norm_b", "L3_ffn_norm_b", (1024,)),
            ("q_w", "L3_ffn_q_w", (1024, 1024)),
            ("q_b", "L3_ffn_q_b", (1024,)),
            ("k_w", "L3_ffn_k_w", (1024, 1024)),
            ("k_b", "L3_ffn_k_b", (1024,)),
            ("v_w", "L3_ffn_v_w", (1024, 1024)),
            ("v_b", "L3_ffn_v_b", (1024,)),
            ("o_w", "L3_ffn_o_w", (1024, 1024)),
            ("o_b", "L3_ffn_o_b", (1024,))]




def build_core_program(block_list, serial=False, debug=False):
    nc = bass.Bass("TRN2", target_bir_lowering=False)
    P = Prog(nc, serial=serial)
    stack = ExitStack()
    dram = {}

    x_in = nc.dram_tensor("x_in", [T, D], F32, kind="ExternalInput").ap()
    x_out = nc.dram_tensor("x_out", [T, D], F32, kind="ExternalOutput").ap()

    with stack:
        C = Ctx(nc, P, stack, debug=debug)
        consts = const_arrays()
        for cn, arr in consts.items():
            dram[cn] = nc.dram_tensor(cn, list(arr.shape),
                                      BF16 if arr.dtype == ml_dtypes.bfloat16 else F32,
                                      kind="ExternalInput").ap()
        C.load_consts(dram)
        xi3 = x_in.rearrange("(t p) f -> p t f", p=128)
        for t in range(NT):
            P.dma(C.X[:, t * 1024:(t + 1) * 1024], xi3[:, t, :], writes=[("X",)])
        for emit_fn, wspec in block_list:
            P.barrier()
            Wd = {}
            for short, full, shape in wspec:
                if full not in dram:
                    dram[full] = nc.dram_tensor(full, list(shape), F32,
                                                kind="ExternalInput").ap()
                Wd[short] = dram[full]
            emit_fn(C, Wd)
        P.barrier()
        import os
        trunc = int(os.environ.get("MK_TRUNC", "0"))
        if trunc:
            P.ops = P.ops[:trunc]
            P.key_w = {k: v for k, v in P.key_w.items() if v < trunc}
            P.key_r = {k: [i for i in v if i < trunc]
                       for k, v in P.key_r.items()}
        xo3 = x_out.rearrange("(t p) f -> p t f", p=128)
        for t in range(NT):
            P.dma(xo3[:, t, :], C.X[:, t * 1024:(t + 1) * 1024], reads=[("X",)])

        sems = {}
        sem_stack = ExitStack()
        with sem_stack:
            for e in ENGINES:
                sems[e] = sem_stack.enter_context(nc.semaphore("sem_" + e))
            with nc.Block() as block:
                P.emit(block, sems)
    return nc


L0_ATTN_W = [("Wq", "L0_attn_Wq", (1024, 1024)),
             ("Wk", "L0_attn_Wk", (1024, 1024)),
             ("Wv", "L0_attn_Wv", (1024, 1024)),
             ("Wo", "L0_attn_Wo", (1024, 1024)),
             ("bo", "L0_attn_bo", (1024,)),
             ("gamma", "L0_attn_gamma", (1024,)),
             ("beta", "L0_attn_beta", (1024,))]


BLOCK_LIST = []


def run_blocks(inputs, block_list, serial=False, n_cores=8, x_init=None,
               trace=False):
    nc = build_core_program(block_list, serial=serial)
    if x_init is None:
        x_init = inputs["L0_attn_x"]
    x = np.ascontiguousarray(np.asarray(x_init, dtype=np.float32))
    consts = const_arrays()
    in_maps = []
    for b in range(n_cores):
        m = {"x_in": x[b]}
        m.update(consts)
        for _, wspec in block_list:
            for short, full, shape in wspec:
                if full.startswith("c_"):
                    continue
                m[full] = np.ascontiguousarray(np.asarray(inputs[full], np.float32))
        in_maps.append(m)
    res = run_bass_kernel_spmd(nc, in_maps, core_ids=list(range(n_cores)),
                               trace=trace)
    if trace:
        print("exec_time_ns:", res.exec_time_ns)
    return np.stack([r["x_out"] for r in res.results])


L1_ATTN_W = [("wq", "L1_attn_wq", (1024, 1024)),
             ("bq", "L1_attn_bq", (1024,)),
             ("wk", "L1_attn_wk", (1024, 1024)),
             ("bk", "L1_attn_bk", (1024,)),
             ("wv", "L1_attn_wv", (1024, 1024)),
             ("bv", "L1_attn_bv", (1024,)),
             ("wo", "L1_attn_wo", (1024, 1024)),
             ("bo", "L1_attn_bo", (1024,))]

L0_FFN_W = [("gate_w", "L0_ffn_gate_w", (1024, 16)),
            ("gate_b", "L0_ffn_gate_b", (16,)),
            ("shared_wg", "L0_ffn_shared_wg", (1024, 512)),
            ("shared_wu", "L0_ffn_shared_wu", (1024, 512)),
            ("shared_wd", "L0_ffn_shared_wd", (512, 1024)),
            ("experts_wg", "L0_ffn_experts_wg", (16, 1024, 512)),
            ("experts_wu", "L0_ffn_experts_wu", (16, 1024, 512)),
            ("experts_wd", "L0_ffn_experts_wd", (16, 512, 1024))]

L1_FFN_W = [("w_ln1", "L1_ffn_w_ln1", (1024,)),
            ("Wq", "L1_ffn_Wq", (1024, 1024)),
            ("Wkva", "L1_ffn_Wkva", (1024, 160)),
            ("Wkvb", "L1_ffn_Wkvb", (128, 512)),
            ("Wo", "L1_ffn_Wo", (1024, 1024)),
            ("w_ln2", "L1_ffn_w_ln2", (1024,)),
            ("Wgate", "L1_ffn_Wgate", (1024, 8)),
            ("expert_bias", "L1_ffn_expert_bias", (8,)),
            ("Wg", "L1_ffn_Wg", (8, 1024, 512)),
            ("Wu", "L1_ffn_Wu", (8, 1024, 512)),
            ("Wd", "L1_ffn_Wd", (8, 512, 1024)),
            ("sWg", "L1_ffn_sWg", (1, 1024, 512)),
            ("sWu", "L1_ffn_sWu", (1, 1024, 512)),
            ("sWd", "L1_ffn_sWd", (1, 512, 1024))]

L2_ATTN_W = ([("Wi_f", "L2_attn_Wi_f", (1024, 4384)),
              ("conv_f", "L2_attn_conv_f", (4, 2304)),
              ("convb_f", "L2_attn_convb_f", (2304,)),
              ("Alog_f", "L2_attn_Alog_f", (32,)),
              ("D_f", "L2_attn_D_f", (32,)),
              ("dtb_f", "L2_attn_dtb_f", (32,)),
              ("Wo_f", "L2_attn_Wo_f", (2048, 1024)),
              ("Wi_b", "L2_attn_Wi_b", (1024, 4384)),
              ("conv_b", "L2_attn_conv_b", (4, 2304)),
              ("convb_b", "L2_attn_convb_b", (2304,)),
              ("Alog_b", "L2_attn_Alog_b", (32,)),
              ("D_b", "L2_attn_D_b", (32,)),
              ("dtb_b", "L2_attn_dtb_b", (32,)),
              ("Wo_b", "L2_attn_Wo_b", (2048, 1024)),
              ("Wm", "L2_attn_Wm", (1024, 1024)),
              ("bm", "L2_attn_bm", (1024,))])

BLOCK_LIST = [(emit_l0_attn, L0_ATTN_W), (emit_l0_ffn, L0_FFN_W),
              (emit_l1_attn, L1_ATTN_W), (emit_l1_ffn, L1_FFN_W),
              (emit_l2_attn, L2_ATTN_W), (emit_l2_ffn, L2_FFN_W),
              (emit_l3_attn, L3_ATTN_W), (emit_l3_ffn, L3_FFN_W)]




def kernel(**inputs):
    nc = build_core_program(BLOCK_LIST)
    x = np.ascontiguousarray(np.asarray(inputs["L0_attn_x"], dtype=np.float32))
    consts = const_arrays()
    in_maps = []
    for b in range(8):
        m = {"x_in": x[b]}
        m.update(consts)
        for _, wspec in BLOCK_LIST:
            for short, full, shape in wspec:
                if full.startswith("c_"):
                    continue
                m[full] = np.ascontiguousarray(np.asarray(inputs[full],
                                                          np.float32))
        in_maps.append(m)
    res = run_bass_kernel_spmd(nc, in_maps, core_ids=list(range(8)))
    return np.stack([r["x_out"] for r in res.results]).astype(np.float32)
